# version 24
# baseline (speedup 1.0000x reference)
import numpy as np
import ml_dtypes
from contextlib import ExitStack

import concourse.bass as bass
import concourse.mybir as mybir
from concourse.bass_utils import run_bass_kernel_spmd

F32 = mybir.dt.float32
BF16 = mybir.dt.bfloat16
AF = mybir.ActivationFunctionType
ALU = mybir.AluOpType
AX = mybir.AxisListType

NCORES = 8
SEQS = 2
T = 2048
D = 1024
NC = 8
TT = 16
TS = 4
NH = 16
HD = 64



class PB:
    ENGS = ("tensor", "vector", "scalar", "gpsimd", "sync")

    def __init__(self, nc, block, ctx):
        self.nc = nc
        self.block = block
        self.ctx = ctx
        self.buf = {e: [] for e in self.ENGS}
        self.sems = {}
        self.cnt = {}
        self.dirty = {e: False for e in self.ENGS}

    def sem(self, name):
        if name not in self.sems:
            self.sems[name] = self.ctx.enter_context(self.nc.semaphore(name))
            self.cnt[name] = 0
        return self.sems[name]

    CHAIN = {"vector": "mvec", "scalar": "mscl", "gpsimd": "mgps"}

    def emit(self, eng, fn, inc=None, n=1):
        ch = self.CHAIN.get(eng)
        if ch is not None and inc is None:
            inc, n = ch, 1
        if ch is not None:
            h = self.sem(ch)
            v = self.cnt[ch]
            if v > 0:
                self.buf[eng].append(lambda E, h=h, v=v: E.wait_ge(h, v))
        if inc is not None:
            h2 = self.sem(inc)
            self.cnt[inc] += n
            self.buf[eng].append(lambda E, fn=fn, h=h2, n=n: fn(E).then_inc(h2, n))
        else:
            self.buf[eng].append(fn)
        if eng != "sync":
            self.dirty[eng] = True

    def wait(self, eng, sem, val=None):
        if val is None:
            val = self.cnt.get(sem, 0)
        if val <= 0:
            return
        h = self.sem(sem)
        self.buf[eng].append(lambda E, h=h, v=val: E.wait_ge(h, v))

    def dma(self, queue, out, in_, inc, **kw):
        self.emit(queue, lambda E, o=out, i=in_, kw=kw: E.dma_start(out=o, in_=i, **kw),
                  inc=inc, n=16)

    def flush(self):
        for eng in self.ENGS:
            ops = self.buf[eng]
            if not ops:
                continue
            self.buf[eng] = []

            def section(E, ops=ops):
                for op in ops:
                    op(E)
            getattr(self.block, eng)(section)

    def barrier(self, name=""):
        active = [e for e in ("tensor", "vector", "scalar", "gpsimd")
                  if self.dirty[e]]
        for e in active:
            self.emit(e, lambda E: E.nop(), inc="m" + e)
        targets = {e: self.cnt["m" + e] for e in active}
        dma_sems = [s for s in self.cnt if s.startswith("q")]
        for e in self.ENGS:
            for o in active:
                if o != e:
                    self.wait(e, "m" + o, targets[o])
            for s in dma_sems:
                if self.cnt[s] > 0:
                    self.wait(e, s, self.cnt[s])
        for e in active:
            self.dirty[e] = False
        self.flush()




def build_program(dbg=(), phases=("P1", "P2", "P3", "P4", "P5", "P6")):
    nc = bass.Bass("TRN2", target_bir_lowering=False, debug=False)

    def din(name, shape, dt=F32):
        return nc.dram_tensor(name, shape, dt, kind="ExternalInput").ap()

    x_in = din("x", [SEQS, T, D])
    wqkv_oc = din("wqkv_oc", [24, 128, 1024])
    wproj = din("wproj", [128, NC * 1024])
    gate_w = din("gate_w", [16, D])
    gate_wT = din("gate_wT", [128, 128])
    wg_e = din("wg_e", [16, 128, NC * 512])
    wu_e = din("wu_e", [16, 128, NC * 512])
    wd_e = din("wd_e", [16, 128, 4 * 1024])
    sw_g = din("sw_g", [128, NC * 512])
    sw_u = din("sw_u", [128, NC * 512])
    sw_d = din("sw_d", [128, 4 * 1024])
    maa_w1 = din("maa_w1", [128, NC * 160])
    maa_w2 = din("maa_w2", [5, 32, D])
    dec_w1 = din("dec_w1", [128, NC * 64])
    dec_w2 = din("dec_w2", [64, D])
    WrT = din("WrT", [128, NC * 1024])
    WkT = din("WkT", [128, NC * 1024])
    WvT = din("WvT", [128, NC * 1024])
    WgT = din("WgT", [128, NC * 1024])
    WoT = din("WoT", [128, NC * 1024])
    w1 = din("w1", [4096, D])
    w1t_g = din("w1t_g", [8, 2, 128, NC * 256])
    w2 = din("w2", [D, 4096])
    w2t_g = din("w2t_g", [8, 2, 128, 2 * 1024])
    packA = din("packA", [128, 120])
    ehot = din("ehot", [16, 2048], BF16)
    ident_f32 = din("ident_f32", [128, 128])
    lion_masks = din("lion_masks", [4, 128, 512], mybir.dt.uint8)

    out = nc.dram_tensor("out", [SEQS, T, D], F32, kind="ExternalOutput").ap()
    xdr = nc.dram_tensor("xdr", [SEQS, T, D], F32, kind="Internal").ap()
    vdr = nc.dram_tensor("vdr", [NC, T, 128], BF16, kind="Internal").ap()
    gdr = nc.dram_tensor("gdr", [NC, 128, T], BF16, kind="Internal").ap()

    dbg_out = {}
    for name, shape in dbg:
        dbg_out[name] = nc.dram_tensor("dbg_" + name, shape, F32,
                                       kind="ExternalOutput").ap()

    ctx = ExitStack()

    def sb(name, shape, dt):
        return ctx.enter_context(nc.sbuf_tensor(name, shape, dt))

    packA_s = sb("packA_s", [128, 120], F32)
    cvals = sb("cvals", [128, 8], F32)
    ident = sb("ident", [128, 128], F32)
    gwT_s = sb("gwT_s", [128, 128], F32)
    gwT_hi = sb("gwT_hi", [128, 128], BF16)
    gwT_lo = sb("gwT_lo", [128, 128], BF16)
    invgw = sb("invgw", [16, 1], F32)
    alpha1 = sb("alpha1", [128, 32], F32)
    alpha2 = sb("alpha2", [128, 8], F32)
    ones1 = sb("ones1", [1, 128], F32)
    onescol = sb("onescol", [128, 1], F32)

    COL_NORMW = 0
    COL_MAA = 8
    COL_TDEC = 56
    COL_LNW = 64
    COL_LNB = 72
    COL_B1 = 80
    COL_B2 = 112

    def pcol(c):
        return packA_s[:, c:c + 1]

    pgrp = [ctx.enter_context(nc.psum_tensor(f"pg{i}", [128, 2048], F32))
            for i in range(2)]
    pbank = [pgrp[i // 4][:, 512 * (i % 4):512 * (i % 4 + 1)] for i in range(8)]

    block = ctx.enter_context(nc.Block())
    pb = PB(nc, block, ctx)

    uid = [0]

    def usem(tag):
        uid[0] += 1
        return f"u{uid[0]}_{tag}"

    def sbt(stack, name, shape, dt):
        uid[0] += 1
        return stack.enter_context(nc.sbuf_tensor(f"{name}_{uid[0]}", shape, dt))

    pb.dma("sync", packA_s[:], packA, inc="qS")
    pb.dma("sync", ident[:], ident_f32, inc="qS")
    pb.dma("sync", gwT_s[:], gate_wT, inc="qS")

    for i, v in enumerate([1e-5, 1e-6, 6.4e-4, 1024.0, 0.125, -1.0, -0.5]):
        pb.emit("gpsimd", lambda E, i=i, v=v: E.memset(cvals[:, i:i + 1], v))
        nc.const_aps.register(F32, v, cvals[:, i:i + 1])

    pb.emit("gpsimd", lambda E: E.memset(ones1[:], 1.0))
    pb.emit("gpsimd", lambda E: E.memset(onescol[:], 1.0))

    pb.barrier("preamble")

    xnT = sb("xnT", [128, NC * T], BF16)
    res_seq = {}

    def emit_P1(s, src_x):
        st = ExitStack()
        hrmsT = res_seq["hrmsT"] = ctx.enter_context(
            nc.sbuf_tensor(f"hrmsT{s}", [128, NC * T], BF16)) \
            if "hrmsT" not in res_seq else res_seq["hrmsT"]
        combT = res_seq["combT"] = ctx.enter_context(
            nc.sbuf_tensor(f"combT{s}", [16, T], BF16)) \
            if "combT" not in res_seq else res_seq["combT"]

        xbuf = sbt(st, "p1_x", [128, 2 * 1024], F32)
        xsq = sbt(st, "p1_sq", [128, 1024], F32)
        stat = sbt(st, "p1_st", [128, 8], F32)
        xntt = sbt(st, "p1_xn", [128, 1024], F32)
        xnrm = sbt(st, "p1_xr", [128, 1024], F32)
        xnTf = sbt(st, "p1_xnTf", [128, 1024], F32)
        xh = sbt(st, "p1_xh", [128, 1024], BF16)
        xl = sbt(st, "p1_xl", [128, 1024], BF16)
        logitsT = sbt(st, "p1_lg", [16, T], F32)
        rowt = sbt(st, "p1_rt", [1, 128], F32)
        dbg_praw = sbt(st, "p1_dpr", [16, 128], F32)
        dbg_rbc = sbt(st, "p1_drb", [16, 128], F32)
        rowt_hi = sbt(st, "p1_rth", [1, 128], BF16)
        rowt_lo = sbt(st, "p1_rtl", [1, 128], BF16)
        ones1b = sbt(st, "p1_o1b", [1, 16], BF16)
        pb.emit("gpsimd", lambda E: E.memset(ones1b[:], 1.0))
        rbc = sbt(st, "p1_rbc", [16, 128], F32)

        if s == 0:
            presq = xbuf[0:16, 0:1024]
            pb.dma("sync", presq, gate_w, inc="qS")
            pb.wait("vector", "qS")
            pb.wait("scalar", "qS")
            pb.emit("scalar", lambda E: E.activation(presq, presq, AF.Square),
                    inc="mscl")
            pb.wait("vector", "mscl")
            pb.emit("vector", lambda E: E.reduce_sum(invgw[:], presq, AX.X))
            pb.wait("scalar", "mvec")
            pb.emit("scalar", lambda E: E.activation(invgw[:], invgw[:], AF.Sqrt))
            pb.wait("vector", "mscl")
            pb.emit("vector", lambda E: E.tensor_scalar_max(invgw[:], invgw[:],
                                                            1e-12))
            pb.emit("vector", lambda E: E.reciprocal(invgw[:], invgw[:]))
            pb.emit("vector", lambda E: E.tensor_copy(gwT_hi[:], gwT_s[:]))
            pb.emit("vector", lambda E: E.tensor_sub(gwT_lo[:], gwT_s[:],
                                                     gwT_hi[:]), inc="mvec")
            pb.wait("sync", "mvec")

        for tt in range(TT):
            V, S, Tn, G = "vector", "scalar", "tensor", "gpsimd"
            xb = xbuf[:, 1024 * (tt % 2):1024 * (tt % 2 + 1)]
            pb.wait("sync", "mvec")
            pb.wait("sync", "mscl")
            pb.dma("sync", xb, src_x[s, 128 * tt:128 * (tt + 1), :], inc="qS")
            qs_tgt = pb.cnt["qS"]
            mean, sumsq = stat[:, 0:1], stat[:, 1:2]
            var, rstd = stat[:, 2:3], stat[:, 3:4]
            mxn2, rmsinv, invl2 = stat[:, 4:5], stat[:, 5:6], stat[:, 6:7]
            m2 = stat[:, 7:8]
            pb.wait(V, "qS", qs_tgt)
            pb.wait(S, "qS", qs_tgt)
            pb.emit(V, lambda E, xb=xb: E.reduce_sum(mean, xb, AX.X))
            pb.emit(V, lambda E: E.tensor_scalar_mul(mean, mean, 1.0 / D),
                    inc="mvec")
            pb.emit(S, lambda E, xb=xb: E.activation(xsq[:], xb, AF.Square),
                    inc="mscl")
            pb.wait(V, "mscl")
            pb.emit(V, lambda E: E.reduce_sum(sumsq, xsq[:], AX.X))
            pb.wait(S, "mvec")
            pb.emit(S, lambda E: E.activation(m2, mean, AF.Square), inc="mscl")
            pb.wait(V, "mscl")
            pb.emit(V, lambda E: E.tensor_scalar(var, sumsq, 1.0 / D, m2,
                                                 ALU.mult, ALU.subtract),
                    inc="mvec")
            pb.wait(S, "mvec")
            pb.emit(S, lambda E: E.activation(rstd, var, AF.Sqrt, bias=1e-5),
                    inc="mscl")
            pb.wait(V, "mscl")
            pb.emit(V, lambda E: E.reciprocal(rstd, rstd))
            pb.emit(V, lambda E: E.tensor_mul(mxn2, var, rstd))
            pb.emit(V, lambda E: E.tensor_mul(mxn2, mxn2, rstd), inc="mvec")
            pb.wait(S, "mvec")
            pb.emit(S, lambda E: E.activation(rmsinv, mxn2, AF.Sqrt, bias=1e-6))
            pb.emit(S, lambda E: E.activation(invl2, mxn2, AF.Sqrt,
                                              scale=float(D)), inc="mscl")
            pb.wait(V, "mscl")
            pb.emit(V, lambda E: E.reciprocal(rmsinv, rmsinv))
            pb.emit(V, lambda E: E.tensor_scalar_max(invl2, invl2, 1e-12))
            pb.emit(V, lambda E: E.reciprocal(invl2, invl2))
            pb.emit(V, lambda E, xb=xb: E.tensor_scalar(
                xntt[:], xb, mean, rstd, ALU.subtract, ALU.mult))
            pb.emit(V, lambda E: E.tensor_scalar_mul(xnrm[:], xntt[:], rmsinv),
                    inc="mvec")
            pb.wait(Tn, "mvec")
            for j in range(8):
                bank = pbank[j // 4]
                pb.emit(Tn, lambda E, j=j, b=bank: E.transpose(
                    b[:, 128 * (j % 4):128 * (j % 4 + 1)],
                    xntt[:, 128 * j:128 * (j + 1)], ident[:]))
            for j in range(8):
                bank = pbank[2 + j // 4]
                pb.emit(Tn, lambda E, j=j, b=bank: E.transpose(
                    b[:, 128 * (j % 4):128 * (j % 4 + 1)],
                    xnrm[:, 128 * j:128 * (j + 1)], ident[:]),
                    inc=("mten" if j == 7 else None))
            pb.wait(V, "mten")
            pb.wait(S, "mten")
            pb.emit(V, lambda E: E.tensor_copy(xnTf[:, 0:512], pbank[0][:]))
            pb.emit(V, lambda E: E.tensor_copy(xnTf[:, 512:1024], pbank[1][:]),
                    inc="mvec")
            v5 = pb.cnt["mvec"]
            for half in range(2):
                for cq in range(4):
                    c = 4 * half + cq
                    dst = hrmsT[:, T * c + 128 * tt:T * c + 128 * (tt + 1)]
                    srcq = pbank[2 + half][:, 128 * cq:128 * (cq + 1)]
                    pb.emit(S, lambda E, d=dst, s2=srcq, c=c: E.activation(
                        d, s2, AF.Copy, scale=pcol(COL_NORMW + c)),
                        inc=("mscl" if c == 7 else None))
            s5 = pb.cnt["mscl"]
            pb.emit(V, lambda E: E.tensor_copy(xh[:], xnTf[:]))
            pb.emit(V, lambda E: E.tensor_sub(xl[:], xnTf[:], xh[:]))
            v5b = pb.cnt["mvec"]
            pb.wait(G, "mvec", v5b)
            dstg = xnT[:].rearrange("p (c n) -> p c n", n=T)[
                :, :, 128 * tt:128 * (tt + 1)]
            pb.emit(G, lambda E, d=dstg: E.tensor_copy(
                d, xh[:].rearrange("p (c n) -> p c n", n=128)), inc="mgps")
            g1 = pb.cnt["mgps"]
            pb.wait(Tn, "mvec", v5b)
            for c in range(8):
                for t3, (lw, rx) in enumerate(((gwT_hi, xh), (gwT_hi, xl),
                                               (gwT_lo, xh))):
                    pb.emit(Tn, lambda E, c=c, t3=t3, lw=lw, rx=rx: E.matmul(
                        pbank[4][0:16, 0:128], lw[:, 16 * c:16 * (c + 1)],
                        rx[:, 128 * c:128 * (c + 1)],
                        start=(c == 0 and t3 == 0), stop=(c == 7 and t3 == 2)))
            pb.emit(Tn, lambda E: E.transpose(pbank[5][0:1, 0:128], invl2,
                                              ident[:]), inc="mten")
            pb.wait(V, "mten")
            pb.emit(V, lambda E: E.tensor_copy(rowt[:], pbank[5][0:1, 0:128]))
            pb.emit(V, lambda E: E.tensor_copy(rowt_hi[:], rowt[:]))
            pb.emit(V, lambda E: E.tensor_sub(rowt_lo[:], rowt[:], rowt_hi[:]),
                    inc="mvec")
            pb.wait(Tn, "mvec")
            pb.emit(Tn, lambda E: E.matmul(pbank[5][0:16, 128:256],
                                           ones1b[:], rowt_hi[:],
                                           start=True, stop=False))
            pb.emit(Tn, lambda E: E.matmul(pbank[5][0:16, 128:256],
                                           ones1b[:], rowt_lo[:],
                                           start=False, stop=True),
                    inc="mten")
            pb.wait(S, "mten")
            pb.emit(S, lambda E: E.copy(rbc[:], pbank[5][0:16, 128:256]),
                    inc="mscl")
            pb.wait(V, "mscl")
            if tt == 0 and "t0praw" in dbg_out and s == 0:
                pb.emit(V, lambda E: E.tensor_copy(
                    dbg_praw[:], pbank[4][0:16, 0:128]))
                pb.emit(V, lambda E: E.tensor_copy(dbg_rbc[:], rbc[:]))
            pb.emit(V, lambda E: E.tensor_scalar_mul(rbc[:], rbc[:],
                                                     invgw[:]))
            pb.emit(V, lambda E, tt=tt: E.tensor_mul(
                logitsT[:, 128 * tt:128 * (tt + 1)], pbank[4][0:16, 0:128],
                rbc[:]), inc="mvec")
            pb.wait(Tn, "mvec")
            pb.wait(Tn, "mscl", s5)
            pb.wait(Tn, "mgps", g1)
            pb.wait(V, "mgps", g1)
            if tt == 0 and "t0stat" in dbg_out and s == 0:
                pb.wait("gpsimd", "mvec")
                pb.wait("gpsimd", "mscl")
                pb.dma("gpsimd", dbg_out["t0stat"], stat[:], inc="qO")
                pb.dma("gpsimd", dbg_out["t0xn"], xntt[:], inc="qO")
                pb.dma("gpsimd", dbg_out["t0xnTf"], xnTf[:], inc="qO")
                pb.wait("gpsimd", "qO")
        scoresT = logitsT
        pb.barrier("p1_tiles")
        if "rawlog" in dbg_out and s == 0:
            pb.dma("sync", dbg_out["rawlog"], logitsT[:], inc="qO")
            pb.wait("scalar", "qO")
        pb.emit("scalar", lambda E: E.activation(scoresT[:], logitsT[:],
                                                 AF.Sigmoid), inc="mscl")
        sct = sbt(st, "p1_sct", [128, 16], F32)
        max8 = sbt(st, "p1_m8", [128, 8], F32)
        pk2 = sbt(st, "p1_pk", [128, 2], F32)
        thr_row = sbt(st, "p1_rt0", [1, T], F32)
        rec_row = sbt(st, "p1_rt1", [1, T], F32)
        thr_b = sbt(st, "p1_thb", [16, T], F32)
        rec_b = sbt(st, "p1_reb", [16, T], F32)
        pb.wait("tensor", "mscl", pb.cnt["mscl"])
        for tt in range(TT):
            V, S, Tn = "vector", "scalar", "tensor"
            pb.emit(Tn, lambda E, tt=tt: E.transpose(
                pbank[0][:, 0:16], scoresT[:, 128 * tt:128 * (tt + 1)],
                ident[0:16, 0:16]), inc="mten")
            pb.wait(V, "mten", pb.cnt["mten"])
            pb.emit(V, lambda E: E.tensor_copy(sct[:], pbank[0][:, 0:16]))
            pb.emit(V, lambda E: E.max(max8[:], sct[:]))
            pb.emit(V, lambda E: E.reduce_sum(pk2[:, 1:2], max8[:], AX.X))
            pb.emit(V, lambda E: E.reciprocal(pk2[:, 1:2], pk2[:, 1:2]))
            pb.emit(V, lambda E: E.tensor_copy(pk2[:, 0:1], max8[:, 7:8]),
                    inc="mvec")
            pb.wait(Tn, "mvec", pb.cnt["mvec"])
            pb.emit(Tn, lambda E: E.transpose(pbank[1][0:1, 0:128],
                                              pk2[:, 0:1], ident[:]))
            pb.emit(Tn, lambda E: E.transpose(pbank[1][0:1, 128:256],
                                              pk2[:, 1:2], ident[:]),
                    inc="mten")
            pb.wait(V, "mten", pb.cnt["mten"])
            pb.emit(V, lambda E, tt=tt: E.tensor_copy(
                thr_row[:, 128 * tt:128 * (tt + 1)], pbank[1][0:1, 0:128]))
            pb.emit(V, lambda E, tt=tt: E.tensor_copy(
                rec_row[:, 128 * tt:128 * (tt + 1)], pbank[1][0:1, 128:256]),
                inc="mvec")
            pb.wait(Tn, "mvec", pb.cnt["mvec"])
        for k in range(4):
            pb.emit("tensor", lambda E, k=k: E.matmul(
                pbank[2][0:16, 0:512], ones1[:, 0:16],
                thr_row[:, 512 * k:512 * (k + 1)], start=True, stop=True))
            pb.emit("tensor", lambda E, k=k: E.matmul(
                pbank[3][0:16, 0:512], ones1[:, 0:16],
                rec_row[:, 512 * k:512 * (k + 1)], start=True, stop=True),
                inc="mten")
            pb.wait("scalar", "mten", pb.cnt["mten"])
            pb.emit("scalar", lambda E, k=k: E.copy(
                thr_b[:, 512 * k:512 * (k + 1)], pbank[2][0:16, 0:512]))
            pb.emit("scalar", lambda E, k=k: E.copy(
                rec_b[:, 512 * k:512 * (k + 1)], pbank[3][0:16, 0:512]),
                inc="mscl")
            pb.wait("tensor", "mscl", pb.cnt["mscl"])
        pb.wait("vector", "mscl", pb.cnt["mscl"])
        pb.emit("vector", lambda E: E.tensor_tensor(thr_b[:], scoresT[:],
                                                    thr_b[:], ALU.is_ge))
        pb.emit("vector", lambda E: E.tensor_mul(combT[:], scoresT[:], thr_b[:]))
        pb.emit("vector", lambda E: E.tensor_mul(combT[:], combT[:], rec_b[:]))
        pb.barrier("p1_end")

        if "t0praw" in dbg_out and s == 0:
            pb.dma("sync", dbg_out["t0praw"], dbg_praw[:], inc="qO")
            pb.dma("sync", dbg_out["t0rbc"], dbg_rbc[:], inc="qO")
        if "xnT" in dbg_out and s == 0:
            pb.dma("gpsimd", dbg_out["xnT"], xnT[:], inc="qO")
        if "hrmsT" in dbg_out and s == 0:
            pb.dma("gpsimd", dbg_out["hrmsT"], hrmsT[:], inc="qO")
        if "combT" in dbg_out and s == 0:
            pb.dma("gpsimd", dbg_out["combT"], combT[:], inc="qO")
        if "logitsT" in dbg_out and s == 0:
            pb.dma("sync", dbg_out["logitsT"], scoresT[:], inc="qO")
        pb.barrier("p1_dbg")
        st.close()

    def emit_P2(s, src_x, dst_x):
        st = ExitStack()
        hrmsT = res_seq["hrmsT"]
        QT = sbt(st, "p2_QT", [128, NC * T], BF16)
        KT = sbt(st, "p2_KT", [128, NC * T], BF16)
        V65 = sbt(st, "p2_V65", [128, TT * 1040], BF16)
        ctxT = hrmsT
        PTs = sbt(st, "p2_PT", [128, 4 * 512], BF16)
        wbf = sbt(st, "p2_wbf", [128, 2048], BF16)
        wpb = QT[:, 0:8192]
        dts = QT[:, 8192:16384].bitcast(F32)
        xtt = KT[:, 0:2048].bitcast(F32)
        nm2 = sbt(st, "p2_nm2", [128, 8192], BF16)
        oball = sbt(st, "p2_obl", [128, 128], BF16)
        pb.emit("gpsimd", lambda E: E.memset(oball[:], 1.0))
        sqt = sbt(st, "p2_sqt", [128, 512], BF16)
        qrow = sbt(st, "p2_qr", [1, 1024], BF16)
        kcells = sbt(st, "p2_kc", [1, 8], F32)
        kcell = sbt(st, "p2_kc1", [1, 1], F32)
        knegs = sbt(st, "p2_kn", [1, 16], F32)
        rrow = sbt(st, "p2_rr", [1, 1024], F32)
        rr_hi = sbt(st, "p2_rrh", [1, 1024], BF16)
        rr_lo = sbt(st, "p2_rrl", [1, 1024], BF16)
        bcs = sbt(st, "p2_bcs", [64, 1024], F32)
        o1b = sbt(st, "p2_o1b", [1, 128], BF16)
        onescb = sbt(st, "p2_o1c", [128, 1], BF16)
        pb.emit("gpsimd", lambda E: E.memset(onescb[:], 1.0))
        sT_marks = {}
        ctx_marks = {}
        pb.emit("gpsimd", lambda E: E.memset(o1b[:], 1.0))
        pb.emit("gpsimd", lambda E: E.memset(V65[:], 1.0))
        V, S, Tn, G = "vector", "scalar", "tensor", "gpsimd"

        def hQK(m, h, col, n):
            base = 64 * (h % 2)
            return m[base:base + 64, T * (h // 2) + col:T * (h // 2) + col + n]

        qk_marks = {}
        wb_marks = {}
        for oc in range(16):
            par = oc % 2
            wb = wbf[:, 1024 * par:1024 * (par + 1)]
            if par in wb_marks:
                pb.wait("gpsimd", "mten", wb_marks[par])
            pb.dma("gpsimd", wb.rearrange("p (c n) -> p c n", n=128),
                   wqkv_oc[oc], inc="qG")
            gv = pb.cnt["qG"]
            dst = QT if oc < 8 else KT
            ocl = oc % 8
            pb.wait(Tn, "qG", gv)
            for ts in range(TS):
                bk = (oc * TS + ts) % 2
                if bk in qk_marks:
                    pb.wait(Tn, *qk_marks[bk])
                for dc in range(NC):
                    pb.emit(Tn, lambda E, bk=bk, wb=wb, dc=dc, ts=ts: E.matmul(
                        pbank[bk], wb[:, 128 * dc:128 * (dc + 1)],
                        hrmsT[:, T * dc + 512 * ts:T * dc + 512 * (ts + 1)],
                        start=(dc == 0), stop=(dc == 7)), inc="mten")
                eng = S if bk == 0 else V
                pb.wait(eng, "mten")
                dsl2 = dst[:, T * ocl + 512 * ts:T * ocl + 512 * (ts + 1)]
                if bk == 0:
                    pb.emit(S, lambda E, d=dsl2, bk=bk: E.copy(d, pbank[bk]),
                            inc="mscl")
                    qk_marks[bk] = ("mscl", pb.cnt["mscl"])
                else:
                    pb.emit(V, lambda E, d=dsl2, bk=bk: E.tensor_copy(
                        d, pbank[bk]), inc="mvec")
                    qk_marks[bk] = ("mvec", pb.cnt["mvec"])
            wb_marks[par] = pb.cnt["mten"]
        qk_marks = {}
        pb.wait(Tn, "mscl")
        pb.wait(Tn, "mvec")
        wb_marks = {}
        pb.wait("gpsimd", "mten")
        for vs in range(8):
            par = vs % 2
            wvb = wbf[:, 1024 * par:1024 * (par + 1)].rearrange(
                "p (c n) -> p c n", n=128)
            if par in wb_marks:
                pb.wait("gpsimd", "mten", wb_marks[par])
            pb.dma("gpsimd", wvb, wqkv_oc[16 + vs], inc="qG")
            gv = pb.cnt["qG"]
            pb.wait(Tn, "qG", gv)
            for tt in range(TT):
                bk = tt % 2
                if bk in qk_marks:
                    pb.wait(Tn, *qk_marks[bk])
                for dc in range(NC):
                    pb.emit(Tn, lambda E, bk=bk, dc=dc, tt=tt, wvb=wvb:
                            E.matmul(
                        pbank[bk][:, 0:128],
                        hrmsT[:, T * dc + 128 * tt:T * dc + 128 * (tt + 1)],
                        wvb[:, dc], start=(dc == 0), stop=(dc == 7)),
                        inc="mten")
                pb.wait(V, "mten")
                dstv = V65[:, 1040 * tt:1040 * (tt + 1)].rearrange(
                    "p (h c) -> p h c", c=65)[:, 2 * vs:2 * (vs + 1), 0:64]
                pb.emit(V, lambda E, d=dstv, bk=bk: E.tensor_copy(
                    d, pbank[bk][:, 0:128].rearrange("p (h c) -> p h c", c=64)),
                    inc="mvec")
                qk_marks[bk] = ("mvec", pb.cnt["mvec"])
            wb_marks[par] = pb.cnt["mten"]
        pb.barrier("p2_qkv")
        if s == 0 and "QTd" in dbg_out:
            pb.dma("gpsimd", dbg_out["QTd"], QT[:], inc="qO")
            pb.dma("gpsimd", dbg_out["KTd"], KT[:], inc="qO")
            pb.dma("gpsimd", dbg_out["V65d"], V65[:], inc="qO")
            pb.barrier("p2_qkvd")

        pb.wait(S, "mvec")
        for hp in range(NC):
            for ks in range(4):
                pb.emit(S, lambda E, hp=hp, ks=ks: E.activation(
                    sqt[:], KT[:, T * hp + 512 * ks:T * hp + 512 * (ks + 1)],
                    AF.Square), inc="mscl")
                pb.wait(Tn, "mscl")
                pb.wait(Tn, "mvec")
                for hl in range(2):
                    pb.emit(Tn, lambda E, hl=hl: E.matmul(
                        pbank[hl][0:1, 0:512],
                        onescb[64 * hl:64 * (hl + 1), :],
                        sqt[64 * hl:64 * (hl + 1), :],
                        start=True, stop=True), inc="mten")
                pb.wait(V, "mten")
                for hl in range(2):
                    pb.emit(V, lambda E, hl=hl, ks=ks: E.reduce_max(
                        kcells[:, 4 * hl + ks:4 * hl + ks + 1],
                        pbank[hl][0:1, 0:512], AX.X), inc="mvec")
                pb.wait(S, "mvec")
            for hl in range(2):
                h = 2 * hp + hl
                pb.emit(V, lambda E, hl=hl: E.reduce_max(
                    kcell[:, 0:1], kcells[0:1, 4 * hl:4 * (hl + 1)], AX.X))
                pb.emit(V, lambda E: E.tensor_scalar_max(kcell[:, 0:1],
                                                         kcell[:, 0:1], 0.0),
                        inc="mvec")
                pb.wait(S, "mvec")
                pb.emit(S, lambda E, h=h: E.activation(
                    knegs[:, h:h + 1], kcell[:, 0:1], AF.Sqrt), inc="mscl")
                pb.wait(V, "mscl")
                pb.emit(V, lambda E, h=h: E.tensor_scalar_mul(
                    knegs[:, h:h + 1], knegs[:, h:h + 1], -1.0), inc="mvec")
        pb.barrier("p2_kneg")

        for qs in range(TS):
            pb.wait(Tn, "mscl")
            pb.wait(Tn, "mvec")
            for hp in range(NC):
                pb.emit(S, lambda E, hp=hp, qs=qs: E.activation(
                    sqt[:], QT[:, T * hp + 512 * qs:T * hp + 512 * (qs + 1)],
                    AF.Square), inc="mscl")
                pb.wait(Tn, "mscl")
                for hl in range(2):
                    pb.emit(Tn, lambda E, hp=hp, hl=hl: E.matmul(
                        pbank[2 + hl][0:1, 0:512],
                        onescb[64 * hl:64 * (hl + 1), :],
                        sqt[64 * hl:64 * (hl + 1), :],
                        start=True, stop=True), inc="mten")
                pb.wait(S, "mten")
                for hl in range(2):
                    pb.emit(S, lambda E, hl=hl: E.activation(
                        qrow[:, 512 * hl:512 * (hl + 1)],
                        pbank[2 + hl][0:1, 0:512], AF.Sqrt), inc="mscl")
                pb.wait(V, "mscl")
                for hl in range(2):
                    h = 2 * hp + hl
                    pb.emit(V, lambda E, h=h, hl=hl: E.tensor_scalar(
                        nm2[0:1, 512 * h:512 * (h + 1)],
                        qrow[:, 512 * hl:512 * (hl + 1)], knegs[:, h:h + 1],
                        None, ALU.mult), inc="mvec")
            pb.wait("gpsimd", "mvec")
            pb.dma("gpsimd", nm2[64:65, :], nm2[0:1, :], inc="qG")
            pb.wait(Tn, "mvec")
            pb.wait(Tn, "qG")
            pb.wait(S, "mvec")

            for hp in range(NC):
                hA, hB = 2 * hp, 2 * hp + 1
                cbA = pgrp[1][0:65, 1024 * (hp % 2):1024 * (hp % 2) + 512]
                cbB = pgrp[1][0:65, 1024 * (hp % 2) + 512:1024 * (hp % 2) + 1024]
                if (hp % 2) in ctx_marks:
                    pb.wait(Tn, "mvec", ctx_marks[hp % 2])
                def ctx_pair(kb, evmark):
                    pb.wait(Tn, "mscl", evmark)
                    slA = PTs[:, 512 * (kb % 2):512 * (kb % 2 + 1)]
                    slB = PTs[:, 1024 + 512 * (kb % 2):1024 + 512 * (kb % 2 + 1)]
                    pb.emit(Tn, lambda E, hA=hA, kb=kb, slA=slA, cbA=cbA:
                            E.matmul(
                        cbA, V65[:, 1040 * kb + 65 * hA:1040 * kb + 65 * (hA + 1)],
                        slA, start=(kb == 0), stop=(kb == 15)))
                    pb.emit(Tn, lambda E, hB=hB, kb=kb, slB=slB, cbB=cbB:
                            E.matmul(
                        cbB, V65[:, 1040 * kb + 65 * hB:1040 * kb + 65 * (hB + 1)],
                        slB, start=(kb == 0), stop=(kb == 15)), inc="mten")

                evm = [0] * 16
                for kb in range(16):
                    bkA = kb % 2
                    bkB = 2 + kb % 2
                    if bkA in sT_marks:
                        pb.wait(Tn, *sT_marks.pop(bkA))
                    if bkB in sT_marks:
                        pb.wait(Tn, *sT_marks.pop(bkB))
                    pb.emit(Tn, lambda E, hA=hA, kb=kb, qs=qs, bkA=bkA:
                            E.matmul(
                        pbank[bkA], hQK(KT, hA, 128 * kb, 128),
                        hQK(QT, hA, 512 * qs, 512), start=True, stop=False))
                    pb.emit(Tn, lambda E, hB=hB, kb=kb, qs=qs, bkB=bkB:
                            E.matmul(
                        pbank[bkB], hQK(KT, hB, 128 * kb, 128),
                        hQK(QT, hB, 512 * qs, 512), start=True, stop=False))
                    pb.emit(Tn, lambda E, hA=hA, bkA=bkA: E.matmul(
                        pbank[bkA], oball[0:1, :],
                        nm2[0:1, 512 * hA:512 * (hA + 1)],
                        start=False, stop=True))
                    pb.emit(Tn, lambda E, hB=hB, bkB=bkB: E.matmul(
                        pbank[bkB], oball[64:65, :],
                        nm2[64:65, 512 * hB:512 * (hB + 1)],
                        start=False, stop=True), inc="mten")
                    if kb > 0:
                        ctx_pair(kb - 1, evm[kb - 1])
                    slA = PTs[:, 512 * (kb % 2):512 * (kb % 2 + 1)]
                    slB = PTs[:, 1024 + 512 * (kb % 2):1024 + 512 * (kb % 2 + 1)]
                    pb.wait(S, "mten")
                    pb.emit(S, lambda E, slA=slA, bkA=bkA: E.activation(
                        slA, pbank[bkA], AF.Exp, scale=0.125))
                    pb.emit(S, lambda E, slB=slB, bkB=bkB: E.activation(
                        slB, pbank[bkB], AF.Exp, scale=0.125), inc="mscl")
                    evm[kb] = pb.cnt["mscl"]
                    sT_marks[bkA] = ("mscl", evm[kb])
                    sT_marks[bkB] = ("mscl", evm[kb])
                ctx_pair(15, evm[15])
                ctx_done = pb.cnt["mten"]
                pb.wait(S, "mten")
                pb.emit(S, lambda E, cbA=cbA: E.activation(
                    rrow[:, 0:512], cbA[64:65, :], AF.Ln))
                pb.emit(S, lambda E, cbB=cbB: E.activation(
                    rrow[:, 512:1024], cbB[64:65, :], AF.Ln))
                pb.emit(S, lambda E: E.activation(rrow[:], rrow[:], AF.Exp,
                                                  scale=-1.0), inc="mscl")
                pb.wait(V, "mscl")
                pb.emit(V, lambda E: E.tensor_copy(rr_hi[:], rrow[:]))
                pb.emit(V, lambda E: E.tensor_sub(rr_lo[:], rrow[:], rr_hi[:]),
                        inc="mvec")
                rdy = pb.cnt["mvec"]
                for i, h in enumerate((hA, hB)):
                    bk = i
                    bk = 0 if i == 0 else 2
                    if bk in sT_marks:
                        pb.wait(Tn, *sT_marks.pop(bk))
                    pb.wait(Tn, "mvec", rdy)
                    pb.emit(Tn, lambda E, bk=bk, i=i: E.matmul(
                        pbank[bk][0:64, :], o1b[:, 0:64],
                        rr_hi[:, 512 * i:512 * (i + 1)],
                        start=True, stop=False))
                    pb.emit(Tn, lambda E, bk=bk, i=i: E.matmul(
                        pbank[bk][0:64, :], o1b[:, 0:64],
                        rr_lo[:, 512 * i:512 * (i + 1)],
                        start=False, stop=True), inc="mten")
                    eng = S if i == 0 else V
                    pb.wait(eng, "mten")
                    if i == 0:
                        pb.emit(S, lambda E, bk=bk, i=i: E.copy(
                            bcs[:, 512 * i:512 * (i + 1)], pbank[bk][0:64, :]),
                            inc="mscl")
                        pb.wait(V, "mscl")
                    else:
                        pb.emit(V, lambda E, bk=bk, i=i: E.tensor_copy(
                            bcs[:, 512 * i:512 * (i + 1)], pbank[bk][0:64, :]),
                            inc="mvec")
                    cb = cbA if i == 0 else cbB
                    pb.emit(V, lambda E, h=h, qs=qs, cb=cb, i=i: E.tensor_mul(
                        hQK(ctxT, h, 512 * qs, 512), cb[0:64, :],
                        bcs[:, 512 * i:512 * (i + 1)]), inc="mvec")
                    sT_marks[bk] = ("mvec", pb.cnt["mvec"])
                ctx_marks[hp % 2] = pb.cnt["mvec"]
        pb.barrier("p2_attn")

        pb.dma("gpsimd", wpb, wproj, inc="qG")
        pb.wait(Tn, "qG")
        for ts in range(TS):
            for oc in range(NC):
                bk = oc % 2
                pb.wait(Tn, "mvec")
                for dc in range(NC):
                    pb.emit(Tn, lambda E, bk=bk, oc=oc, dc=dc, ts=ts: E.matmul(
                        pbank[bk], wpb[:, 1024 * dc + 128 * oc:1024 * dc + 128 * (oc + 1)],
                        ctxT[:, T * dc + 512 * ts:T * dc + 512 * (ts + 1)],
                        start=(dc == 0), stop=(dc == 7)), inc="mten")
                pb.wait(V, "mten")
                pb.emit(V, lambda E, oc=oc, ts=ts, bk=bk: E.tensor_add(
                    dts[:, 512 * oc:512 * (oc + 1)], pbank[bk],
                    xnT[:, T * oc + 512 * ts:T * oc + 512 * (ts + 1)]),
                    inc="mvec")
            pb.wait(Tn, "mvec")
            for j in range(4):
                tt = 4 * ts + j
                pb.wait("sync", "mvec")
                pb.wait("sync", "qG")
                pb.dma("sync", xtt[:], src_x[s, 128 * tt:128 * (tt + 1), :],
                       inc="qS")
                pb.wait(Tn, "mvec")
                for oc in range(NC):
                    pb.emit(Tn, lambda E, j=j, oc=oc: E.transpose(
                        pgrp[1][:, 128 * oc:128 * (oc + 1)],
                        dts[:, 512 * oc + 128 * j:512 * oc + 128 * (j + 1)],
                        ident[:]), inc=("mten" if oc == 7 else None))
                qs_t = pb.cnt["qS"]
                pb.wait(V, "qS", qs_t)
                pb.wait(V, "mten")
                pb.emit(V, lambda E: E.tensor_add(
                    xtt[:], xtt[:], pgrp[1][:, 0:1024]), inc="mvec")
                pb.wait(G, "mvec")
                pb.dma("gpsimd", dst_x[s, 128 * tt:128 * (tt + 1), :], xtt[:],
                       inc="qG")
        pb.barrier("p2_end")
        if "x1" in dbg_out and s == 0:
            pb.dma("gpsimd", dbg_out["x1"], dst_x[0], inc="qO")
        if "ctxT" in dbg_out and s == 0:
            pb.dma("gpsimd", dbg_out["ctxT"], ctxT[:], inc="qO")

        pb.barrier("p2_dbg")
        st.close()

    def emit_P3(s, src_x, dst_x):
        st = ExitStack()
        combT = res_seq["combT"]
        dlo3 = res_seq["hrmsT"][:].bitcast(F32)
        dhi3 = sbt(st, "p3_dl", [128, 4 * T], F32)

        def dsl3(oc, a, b):
            return (dlo3[:, T * oc + a:T * oc + b] if oc < 4 else
                    dhi3[:, T * (oc - 4) + a:T * (oc - 4) + b])
        wgb2 = sbt(st, "p3_wg", [128, 2 * 4096], BF16)
        wub2 = sbt(st, "p3_wu", [128, 2 * 4096], BF16)
        wdb2 = sbt(st, "p3_wd", [128, 2 * 4096], BF16)
        ehot_s = sbt(st, "p3_eh", [16, 2048], BF16)
        cbc = sbt(st, "p3_cbc", [128, 1024], F32)
        sgb = sbt(st, "p3_sg", [128, 4 * 512], BF16)
        hcb = sbt(st, "p3_hc", [128, 4 * 512], BF16)
        xtt = sbt(st, "p3_xtt", [128, 1024], F32)
        V, S, Tn, G = "vector", "scalar", "tensor", "gpsimd"
        pb.dma("sync", ehot_s[:], ehot, inc="qS")
        for oc in range(NC):
            pb.emit(G, lambda E, oc=oc: E.memset(dsl3(oc, 0, T), 0.0))
        pb.barrier("p3_init")

        bmk = {}
        wset_mark = {}
        p3_prev_hc = [0]
        p3_prev_dn = [0]

        def bwait(bk):
            if bk in bmk:
                pb.wait(Tn, *bmk[bk])
        for e in range(17):
            wset = e % 2
            wgb = wgb2[:, 4096 * wset:4096 * (wset + 1)]
            wub = wub2[:, 4096 * wset:4096 * (wset + 1)]
            wdb = wdb2[:, 4096 * wset:4096 * (wset + 1)]
            if e < 16:
                srcs = ((wg_e[e], wgb), (wu_e[e], wub), (wd_e[e], wdb))
            else:
                srcs = ((sw_g, wgb), (sw_u, wub), (sw_d, wdb))
            if wset in wset_mark:
                pb.wait("gpsimd", "mten", wset_mark[wset])
            for (mat, dst3) in srcs:
                pb.dma("gpsimd", dst3, mat, inc="qG")
            pb.wait(Tn, "qG")
            for ts in range(TS):
                pb.wait(S, "mvec", p3_prev_hc[0])
                pb.wait(V, "mten", p3_prev_dn[0])
                if e < 16:
                    bwait(7)
                    pb.emit(Tn, lambda E, e=e, ts=ts: E.matmul(
                        pbank[7], ehot_s[:, 128 * e:128 * (e + 1)],
                        combT[0:16, 512 * ts:512 * (ts + 1)],
                        start=True, stop=True), inc="mten")
                    pb.wait(S, "mten")
                    pb.emit(S, lambda E, e=e, ts=ts: E.copy(
                        cbc[:, 512 * ((e + ts) % 2):512 * ((e + ts) % 2 + 1)],
                        pbank[7]), inc="mscl")
                    bmk[7] = ("mscl", pb.cnt["mscl"])
                cslc = cbc[:, 512 * ((e + ts) % 2):512 * ((e + ts) % 2 + 1)]
                g_mark, u_mark = {}, {}
                for hc in range(4):
                    bwait(hc)
                    for dc in range(NC):
                        pb.emit(Tn, lambda E, hc=hc, dc=dc, ts=ts, wgb=wgb:
                                E.matmul(
                            pbank[hc],
                            wgb[:, 512 * dc + 128 * hc:512 * dc + 128 * (hc + 1)],
                            xnT[:, T * dc + 512 * ts:T * dc + 512 * (ts + 1)],
                            start=(dc == 0), stop=(dc == 7)),
                            inc=("mten" if dc == 7 else None))
                    g_mark[hc] = pb.cnt["mten"]
                for hc in range(4):
                    bwait(4 + hc)
                    for dc in range(NC):
                        pb.emit(Tn, lambda E, hc=hc, dc=dc, ts=ts, wub=wub:
                                E.matmul(
                            pbank[4 + hc],
                            wub[:, 512 * dc + 128 * hc:512 * dc + 128 * (hc + 1)],
                            xnT[:, T * dc + 512 * ts:T * dc + 512 * (ts + 1)],
                            start=(dc == 0), stop=(dc == 7)),
                            inc=("mten" if dc == 7 else None))
                    u_mark[hc] = pb.cnt["mten"]
                h_marks = []
                for hc in range(4):
                    pb.wait(S, "mten", g_mark[hc])
                    pb.emit(S, lambda E, hc=hc: E.activation(
                        sgb[:, 512 * hc:512 * (hc + 1)], pbank[hc], AF.Silu),
                        inc="mscl")
                    bmk[hc] = ("mscl", pb.cnt["mscl"])
                    pb.wait(V, "mscl")
                    pb.wait(V, "mten", u_mark[hc])
                    pb.emit(V, lambda E, hc=hc: E.tensor_mul(
                        hcb[:, 512 * hc:512 * (hc + 1)], pbank[4 + hc],
                        sgb[:, 512 * hc:512 * (hc + 1)]))
                    if e < 16:
                        pb.emit(V, lambda E, hc=hc, cslc=cslc: E.tensor_mul(
                            hcb[:, 512 * hc:512 * (hc + 1)],
                            hcb[:, 512 * hc:512 * (hc + 1)], cslc),
                            inc="mvec")
                    else:
                        pb.emit(V, lambda E: E.engine_nop(), inc="mvec")
                    bmk[4 + hc] = ("mvec", pb.cnt["mvec"])
                    h_marks.append(pb.cnt["mvec"])
                p3_prev_hc[0] = h_marks[-1]
                dn_mark = {}
                for hc in range(4):
                    pb.wait(Tn, "mvec", h_marks[hc])
                    for oc in range(NC):
                        if hc == 0:
                            bwait(oc)
                        pb.emit(Tn, lambda E, oc=oc, hc=hc, wdb=wdb: E.matmul(
                            pbank[oc],
                            wdb[:, 1024 * hc + 128 * oc:1024 * hc + 128 * (oc + 1)],
                            hcb[:, 512 * hc:512 * (hc + 1)],
                            start=(hc == 0), stop=(hc == 3)), inc="mten")
                        if hc == 3:
                            dn_mark[oc] = pb.cnt["mten"]
                for oc in range(NC):
                    pb.wait(V, "mten", dn_mark[oc])
                    dd3 = dsl3(oc, 512 * ts, 512 * (ts + 1))
                    pb.emit(V, lambda E, dd3=dd3, oc=oc: E.tensor_add(
                        dd3, dd3, pbank[oc]), inc="mvec")
                    bmk[oc] = ("mvec", pb.cnt["mvec"])
                wset_mark[wset] = pb.cnt["mten"]
                p3_prev_dn[0] = pb.cnt["mten"]
        pb.barrier("p3_expert_done")

        for tt in range(TT):
            pb.wait("sync", "mvec")
            pb.wait("sync", "qG")
            pb.dma("sync", xtt[:], src_x[s, 128 * tt:128 * (tt + 1), :],
                   inc="qS")
            pb.wait(Tn, "mvec")
            for oc in range(NC):
                pb.emit(Tn, lambda E, tt=tt, oc=oc: E.transpose(
                    pgrp[1][:, 128 * oc:128 * (oc + 1)],
                    dsl3(oc, 128 * tt, 128 * (tt + 1)),
                    ident[:]), inc=("mten" if oc == 7 else None))
            qs_t = pb.cnt["qS"]
            pb.wait(V, "qS", qs_t)
            pb.wait(V, "mten")
            pb.emit(V, lambda E: E.tensor_add(xtt[:], xtt[:], pgrp[1][:, 0:1024]),
                    inc="mvec")
            pb.wait(G, "mvec")
            pb.dma("gpsimd", dst_x[s, 128 * tt:128 * (tt + 1), :], xtt[:],
                   inc="qG")
        pb.barrier("p3_end")
        if "x1f" in dbg_out and s == 0:
            pb.dma("gpsimd", dbg_out["x1f"], dst_x[0], inc="qO")
        pb.barrier("p3_dbg")
        st.close()

    def emit_P4(s, src_x):
        st = ExitStack()
        dxst = res_seq["dxT_st"] = ExitStack()
        dxT = res_seq["dxT"] = sbt(dxst, "dxT", [128, NC * T], BF16)
        xbuf = sbt(st, "p4_x", [128, 2 * 1024], F32)
        xsq = sbt(st, "p4_sq", [128, 1024], F32)
        stat = sbt(st, "p4_st", [128, 8], F32)
        xntt = sbt(st, "p4_xn", [128, 1024], F32)
        V, S, Tn, G = "vector", "scalar", "tensor", "gpsimd"
        for tt in range(TT):
            xb = xbuf[:, 1024 * (tt % 2):1024 * (tt % 2 + 1)]
            pb.wait("sync", "mvec")
            pb.wait("sync", "mscl")
            pb.dma("sync", xb, src_x[s, 128 * tt:128 * (tt + 1), :], inc="qS")
            qs_tgt = pb.cnt["qS"]
            mean, sumsq = stat[:, 0:1], stat[:, 1:2]
            var, rstd, m2 = stat[:, 2:3], stat[:, 3:4], stat[:, 7:8]
            pb.wait(V, "qS", qs_tgt)
            pb.wait(S, "qS", qs_tgt)
            pb.emit(V, lambda E, xb=xb: E.reduce_sum(mean, xb, AX.X))
            pb.emit(V, lambda E: E.tensor_scalar_mul(mean, mean, 1.0 / D),
                    inc="mvec")
            pb.emit(S, lambda E, xb=xb: E.activation(xsq[:], xb, AF.Square),
                    inc="mscl")
            pb.wait(V, "mscl")
            pb.emit(V, lambda E: E.reduce_sum(sumsq, xsq[:], AX.X))
            pb.wait(S, "mvec")
            pb.emit(S, lambda E: E.activation(m2, mean, AF.Square), inc="mscl")
            pb.wait(V, "mscl")
            pb.emit(V, lambda E: E.tensor_scalar(var, sumsq, 1.0 / D, m2,
                                                 ALU.mult, ALU.subtract),
                    inc="mvec")
            pb.wait(S, "mvec")
            pb.emit(S, lambda E: E.activation(rstd, var, AF.Sqrt, bias=1e-5),
                    inc="mscl")
            pb.wait(V, "mscl")
            pb.emit(V, lambda E: E.reciprocal(rstd, rstd))
            pb.emit(V, lambda E, xb=xb: E.tensor_scalar(
                xntt[:], xb, mean, rstd, ALU.subtract, ALU.mult), inc="mvec")
            pb.wait(Tn, "mvec")
            for j in range(8):
                bank = pbank[j // 4]
                pb.emit(Tn, lambda E, j=j, b=bank: E.transpose(
                    b[:, 128 * (j % 4):128 * (j % 4 + 1)],
                    xntt[:, 128 * j:128 * (j + 1)], ident[:]),
                    inc=("mten" if j == 7 else None))
            pb.wait(V, "mten")
            for half in range(2):
                dst = xnT[:].rearrange("p (c n) -> p c n", n=T)[
                    :, 4 * half:4 * (half + 1), 128 * tt:128 * (tt + 1)]
                pb.emit(V, lambda E, d=dst, h=half: E.tensor_copy(
                    d, pbank[h][:].rearrange("p (c n) -> p c n", n=128)),
                    inc="mvec")
            pb.wait(Tn, "mvec")
        pb.barrier("p4_ln")
        for c in range(NC):
            xc = xnT[:, T * c:T * (c + 1)]
            dc_ = dxT[:, T * c:T * (c + 1)]
            pb.emit(V, lambda E, xc=xc, dc_=dc_: E.tensor_scalar(
                dc_[:, 1:T], xc[:, 0:T - 1], 0.5, None, ALU.mult))
            pb.emit(V, lambda E, dc_=dc_: E.memset(dc_[:, 0:1], 0.0))
            for lo, hi in ((0, 1024), (1024, 2047)):
                n = hi - lo
                pb.emit(V, lambda E, xc=xc, lo=lo, n=n: E.tensor_scalar(
                    xsq[:, 0:n], xc[:, lo + 1:lo + 1 + n], 0.5, None, ALU.mult))
                pb.emit(V, lambda E, dc_=dc_, lo=lo, n=n: E.tensor_add(
                    dc_[:, lo:lo + n], dc_[:, lo:lo + n], xsq[:, 0:n]))
            pb.emit(V, lambda E, xc=xc, dc_=dc_: E.tensor_sub(dc_, dc_, xc),
                    inc="mvec")
        pb.barrier("p4_end")
        if "dxT" in dbg_out and s == 0:
            pb.dma("gpsimd", dbg_out["dxT"], dxT[:], inc="qO")
            pb.barrier("p4_dbg")
        st.close()

    def emit_P5(s, src_x, dst_x):
        st = ExitStack()
        dlo = res_seq["hrmsT"][:].bitcast(F32)
        dhi = sbt(st, "p5_dl", [128, 4 * T], F32)

        def dsl(oc, a, b):
            return (dlo[:, T * oc + a:T * oc + b] if oc < 4 else
                    dhi[:, T * (oc - 4) + a:T * (oc - 4) + b])
        stg5 = sbt(st, "p5_stg", [128, 2048], F32)
        sw1 = sbt(st, "p5_sw1", [128, 4096], BF16)
        sw2 = sbt(st, "p5_sw2", [128, 4096], BF16)
        hbuf = sbt(st, "p5_h", [128, 4 * T], BF16)
        xtt = sbt(st, "p5_xtt", [128, 1024], F32)
        tmp5 = sbt(st, "p5_tmp", [128, 512], F32)
        V, S, Tn, G = "vector", "scalar", "tensor", "gpsimd"
        if s == 0:
            for c in range(32):
                pb.wait("sync", "mvec")
                pb.dma("sync", stg5[:, 0:1024], w1[128 * c:128 * (c + 1), :],
                       inc="qS")
                pb.wait(V, "qS")
                pb.emit(V, lambda E, c=c: E.reduce_sum(
                    alpha1[:, c:c + 1], stg5[:, 0:1024], AX.X,
                    apply_absolute_value=True), inc="mvec")
            for c in range(8):
                for half in range(2):
                    pb.wait("sync", "mvec")
                    pb.dma("sync", stg5[:], w2[128 * c:128 * (c + 1),
                                               2048 * half:2048 * (half + 1)],
                           inc="qS")
                    pb.wait(V, "qS")
                    pb.emit(V, lambda E, c=c, half=half: E.reduce_sum(
                        alpha2[:, c:c + 1] if half == 0 else tmp5[:, 0:1],
                        stg5[:], AX.X, apply_absolute_value=True), inc="mvec")
                pb.emit(V, lambda E, c=c: E.tensor_add(
                    alpha2[:, c:c + 1], alpha2[:, c:c + 1], tmp5[:, 0:1]))
            pb.emit(V, lambda E: E.tensor_scalar_mul(alpha1[:], alpha1[:],
                                                     1.0 / 1024))
            pb.emit(V, lambda E: E.tensor_scalar_mul(alpha2[:], alpha2[:],
                                                     1.0 / 4096), inc="mvec")
            pb.barrier("p5_alpha")

        p5_marks = {}
        for g in range(8):
            for half in range(2):
                pb.wait("sync", "mscl")
                pb.dma("sync", stg5[:], w1t_g[g, half], inc="qS")
                pb.wait(S, "qS")
                for c in range(8):
                    pb.emit(S, lambda E, c=c, half=half: E.sign(
                        sw1[:, 512 * c + 256 * half:512 * c + 256 * (half + 1)],
                        stg5[:, 256 * c:256 * (c + 1)]), inc="mscl")
            pb.wait(S, "mten")
            pb.wait(Tn, "mscl")
            for hc in range(4):
                for ts in range(TS):
                    bk = (hc * TS + ts) % 2
                    if bk in p5_marks:
                        pb.wait(Tn, "mscl", p5_marks[bk])
                    for dc in range(NC):
                        pb.emit(Tn, lambda E, bk=bk, hc=hc, dc=dc, ts=ts: E.matmul(
                            pbank[bk],
                            sw1[:, 512 * dc + 128 * hc:512 * dc + 128 * (hc + 1)],
                            xnT[:, T * dc + 512 * ts:T * dc + 512 * (ts + 1)],
                            start=(dc == 0), stop=(dc == 7)), inc="mten")
                    pb.wait(S, "mten")
                    pb.emit(S, lambda E, hc=hc, ts=ts, bk=bk, g=g: E.activation(
                        hbuf[:, T * hc + 512 * ts:T * hc + 512 * (ts + 1)],
                        pbank[bk], AF.Gelu, bias=pcol(COL_B1 + 4 * g + hc),
                        scale=alpha1[:, 4 * g + hc:4 * g + hc + 1]),
                        inc="mscl")
                    p5_marks[bk] = pb.cnt["mscl"]
            for half in range(2):
                pb.wait("sync", "mscl")
                pb.dma("sync", stg5[:], w2t_g[g, half], inc="qS")
                pb.wait(S, "qS")
                for c in range(2):
                    pb.emit(S, lambda E, c=c, half=half: E.sign(
                        sw2[:, 1024 * (2 * half + c):1024 * (2 * half + c + 1)],
                        stg5[:, 1024 * c:1024 * (c + 1)]), inc="mscl")
            pb.wait(Tn, "mscl")
            for ts in range(TS):
                for oc in range(NC):
                    bk = 2 + (ts * NC + oc) % 2
                    if bk in p5_marks:
                        pb.wait(Tn, "mvec", p5_marks[bk])
                    for hc in range(4):
                        pb.emit(Tn, lambda E, bk=bk, oc=oc, hc=hc, ts=ts: E.matmul(
                            pbank[bk],
                            sw2[:, 1024 * hc + 128 * oc:1024 * hc + 128 * (oc + 1)],
                            hbuf[:, T * hc + 512 * ts:T * hc + 512 * (ts + 1)],
                            start=(hc == 0), stop=(hc == 3)), inc="mten")
                    pb.wait(V, "mten")
                    dd = dsl(oc, 512 * ts, 512 * (ts + 1))
                    if g == 0:
                        pb.emit(V, lambda E, dd=dd, oc=oc, bk=bk: E.tensor_scalar(
                            dd, pbank[bk], alpha2[:, oc:oc + 1],
                            pcol(COL_B2 + oc), ALU.mult, ALU.add), inc="mvec")
                    else:
                        pb.emit(V, lambda E, oc=oc, bk=bk: E.tensor_scalar(
                            tmp5[:], pbank[bk], alpha2[:, oc:oc + 1], None,
                            ALU.mult))
                        pb.emit(V, lambda E, dd=dd: E.tensor_add(
                            dd, dd, tmp5[:]), inc="mvec")
                    p5_marks[bk] = pb.cnt["mvec"]
        pb.barrier("p5_groups")
        for tt in range(TT):
            pb.wait("sync", "mvec")
            pb.wait("sync", "qG")
            pb.dma("sync", xtt[:], src_x[s, 128 * tt:128 * (tt + 1), :],
                   inc="qS")
            pb.wait(Tn, "mvec")
            for oc in range(NC):
                pb.emit(Tn, lambda E, tt=tt, oc=oc: E.transpose(
                    pgrp[1][:, 128 * oc:128 * (oc + 1)],
                    dsl(oc, 128 * tt, 128 * (tt + 1)),
                    ident[:]), inc=("mten" if oc == 7 else None))
            qs_t = pb.cnt["qS"]
            pb.wait(V, "qS", qs_t)
            pb.wait(V, "mten")
            pb.emit(V, lambda E: E.tensor_add(xtt[:], xtt[:], pgrp[1][:, 0:1024]),
                    inc="mvec")
            pb.wait(G, "mvec")
            pb.dma("gpsimd", dst_x[s, 128 * tt:128 * (tt + 1), :], xtt[:],
                   inc="qG")
        pb.barrier("p5_end")
        if "x2f" in dbg_out and s == 0:
            pb.dma("gpsimd", dbg_out["x2f"], dst_x[0], inc="qO")
        pb.barrier("p5_dbg")
        st.close()

    def emit_P6(s, src_x, dst_x):
        V, S, Tn, G = "vector", "scalar", "tensor", "gpsimd"
        dxT = res_seq["dxT"]
        rT = res_seq["hrmsT"]
        stP = ExitStack()
        kT = sbt(stP, "p6_kT", [128, NC * T], BF16)
        t1b = sbt(stP, "p6_t1", [64, T], BF16)
        masksb = sbt(stP, "p6_mk", [128, 4 * 512], mybir.dt.uint8)
        dw2b = sbt(stP, "p6_dw2", [64, 1024], BF16)
        pb.dma("gpsimd", masksb[:].rearrange("p (j n) -> p j n", j=4),
               lion_masks.rearrange("j p n -> p j n"), inc="qG")

        stA = ExitStack()
        wtb = sbt(stA, "p6_wtb", [128, NC * 1024], BF16)
        xcon2 = sbt(stA, "p6_xc", [128, 2 * NC * 512], BF16)
        xcon = xcon2[:, 0:NC * 512]
        af32 = sbt(stA, "p6_af", [128, 512], F32)
        xxa = sbt(stA, "p6_xxa", [64, T], BF16)
        xxb = sbt(stA, "p6_xxb", [64, T], BF16)
        xxc = sbt(stA, "p6_xxc", [32, T], BF16)
        maw1b = sbt(stA, "p6_m1", [128, 8 * 160], BF16)
        maw2b = sbt(stA, "p6_m2", [64, 5 * 1024], BF16)
        dw1b = sbt(stA, "p6_dw1", [128, 8 * 64], BF16)
        vtmp = sbt(stA, "p6_vt", [128, 512], BF16)
        gtmp = sbt(stA, "p6_gt", [128, 512], BF16)

        pb.dma("gpsimd", maw1b[:], maa_w1, inc="qG")
        pb.dma("gpsimd", maw2b[0:32, :].rearrange("m (f d) -> m f d", f=5),
               maa_w2.rearrange("f m d -> m f d"), inc="qG")
        pb.dma("gpsimd", maw2b[32:64, :].rearrange("m (f d) -> m f d", f=5),
               maa_w2.rearrange("f m d -> m f d"), inc="qG")
        pb.dma("gpsimd", dw1b[:], dec_w1, inc="qG")
        pb.dma("gpsimd", dw2b[:], dec_w2, inc="qG")
        pb.wait(Tn, "qG")
        pb.wait("vector", "qG")

        MF = {1: 1, 2: 2, 3: 3, 4: 4}
        MAAC = {1: 2, 2: 3, 3: 4, 4: 5}
        XXROW = {1: (None, 32), 2: (None, 0), 3: (None, 32), 4: (None, 0)}

        for ts in range(TS):
            for dc in range(NC):
                pb.emit(V, lambda E, dc=dc, ts=ts: E.tensor_scalar(
                    af32[:], dxT[:, T * dc + 512 * ts:T * dc + 512 * (ts + 1)],
                    pcol(COL_MAA + 0 + dc), None, ALU.mult))
                pb.emit(V, lambda E, dc=dc, ts=ts: E.tensor_add(
                    xcon[:, 512 * dc:512 * (dc + 1)], af32[:],
                    xnT[:, T * dc + 512 * ts:T * dc + 512 * (ts + 1)]),
                    inc="mvec")
            pb.wait(Tn, "mvec")
            pb.wait(Tn, "mscl")
            for dc in range(NC):
                pb.emit(Tn, lambda E, dc=dc: E.matmul(
                    pbank[0], maw1b[:, 160 * dc:160 * dc + 128],
                    xcon[:, 512 * dc:512 * (dc + 1)],
                    start=(dc == 0), stop=(dc == 7)))
            for dc in range(NC):
                pb.emit(Tn, lambda E, dc=dc: E.matmul(
                    pbank[1][0:32, :], maw1b[:, 160 * dc + 128:160 * (dc + 1)],
                    xcon[:, 512 * dc:512 * (dc + 1)],
                    start=(dc == 0), stop=(dc == 7)), inc="mten")
            pb.wait(S, "mten")
            pb.emit(S, lambda E, ts=ts: E.activation(
                xxa[:, 512 * ts:512 * (ts + 1)], pbank[0][0:64, :], AF.Tanh))
            pb.emit(S, lambda E, ts=ts: E.activation(
                xxb[:, 512 * ts:512 * (ts + 1)], pbank[0][64:128, :], AF.Tanh))
            pb.emit(S, lambda E, ts=ts: E.activation(
                xxc[:, 512 * ts:512 * (ts + 1)], pbank[1][0:32, :], AF.Tanh),
                inc="mscl")
        pb.barrier("p6_xxx")

        for ts in range(TS):
            for dc in range(NC):
                pb.wait(Tn, "mvec")
                pb.emit(Tn, lambda E, dc=dc, ts=ts: E.matmul(
                    pgrp[1][:, 1536:2048],
                    maw2b[0:32, 128 * dc:128 * (dc + 1)],
                    xxa[0:32, 512 * ts:512 * (ts + 1)],
                    start=True, stop=True), inc="mten")
                pb.wait(V, "mten")
                pb.emit(V, lambda E, dc=dc: E.tensor_scalar(
                    af32[:], pgrp[1][:, 1536:2048],
                    pcol(COL_MAA + 8 + dc), None, ALU.add))
                pb.emit(V, lambda E, dc=dc, ts=ts: E.tensor_mul(
                    af32[:], af32[:],
                    dxT[:, T * dc + 512 * ts:T * dc + 512 * (ts + 1)]))
                pb.emit(V, lambda E, dc=dc, ts=ts: E.tensor_add(
                    xcon[:, 512 * dc:512 * (dc + 1)], af32[:],
                    xnT[:, T * dc + 512 * ts:T * dc + 512 * (ts + 1)]),
                    inc="mvec")
            pb.wait(Tn, "mvec")
            for dc in range(NC):
                pb.emit(Tn, lambda E, dc=dc: E.matmul(
                    pbank[2][0:64, :], dw1b[:, 64 * dc:64 * (dc + 1)],
                    xcon[:, 512 * dc:512 * (dc + 1)],
                    start=(dc == 0), stop=(dc == 7)), inc="mten")
            pb.wait(S, "mten")
            pb.emit(S, lambda E, ts=ts: E.activation(
                t1b[:, 512 * ts:512 * (ts + 1)], pbank[2][0:64, :], AF.Tanh),
                inc="mscl")
            pb.wait(Tn, "mscl")
        pb.barrier("p6_t1")
        if "t1d" in dbg_out and s == 0:
            pb.dma("gpsimd", dbg_out["t1d"], t1b[:], inc="qO")
            pb.barrier("p6_t1d")

        def load_wt(mat):
            pb.wait("gpsimd", "mten")
            pb.dma("gpsimd", wtb[:], mat, inc="qG")
            pb.wait(Tn, "qG")

        ep_marks = {}

        def build_half(vec_idx, xx_src, xx_row, ts, xcon, half):
            for dc in range(4 * half, 4 * half + 4):
                slot = pgrp[1][:, 512 * (dc % 4):512 * (dc % 4 + 1)]
                if (dc % 4) in ep_marks:
                    pb.wait(Tn, "mvec", ep_marks[dc % 4])
                pb.emit(Tn, lambda E, dc=dc, ts=ts, vec_idx=vec_idx,
                        xx_src=xx_src, xx_row=xx_row, slot=slot: E.matmul(
                    slot,
                    maw2b[xx_row:xx_row + 32,
                          1024 * MF[vec_idx] + 128 * dc:
                          1024 * MF[vec_idx] + 128 * (dc + 1)],
                    xx_src[xx_row:xx_row + 32, 512 * ts:512 * (ts + 1)],
                    start=True, stop=True), inc="mten")
                pb.wait(V, "mten")
                pb.emit(V, lambda E, dc=dc, vec_idx=vec_idx, slot=slot:
                        E.tensor_scalar(
                    af32[:], slot, pcol(COL_MAA + 8 * MAAC[vec_idx] + dc),
                    None, ALU.add))
                pb.emit(V, lambda E, dc=dc, ts=ts: E.tensor_mul(
                    af32[:], af32[:],
                    dxT[:, T * dc + 512 * ts:T * dc + 512 * (ts + 1)]))
                pb.emit(V, lambda E, dc=dc, ts=ts, xcon=xcon: E.tensor_add(
                    xcon[:, 512 * dc:512 * (dc + 1)], af32[:],
                    xnT[:, T * dc + 512 * ts:T * dc + 512 * (ts + 1)]),
                    inc="mvec")
                ep_marks[dc % 4] = pb.cnt["mvec"]

        def proj_part(kind, dstT, xcon, ts, part):
            if kind in ("r", "k", "g"):
                for oc in range(4 * part, 4 * part + 4):
                    bk = oc % 2
                    if bk in p6_marks:
                        pb.wait(Tn, *p6_marks[bk])
                    for dc in range(NC):
                        pb.emit(Tn, lambda E, bk=bk, oc=oc, dc=dc, xcon=xcon:
                                E.matmul(
                            pbank[bk],
                            wtb[:, 1024 * dc + 128 * oc:1024 * dc + 128 * (oc + 1)],
                            xcon[:, 512 * dc:512 * (dc + 1)],
                            start=(dc == 0), stop=(dc == 7)), inc="mten")
                    if kind == "g":
                        pb.wait(S, "mten")
                        pb.wait(S, "qG")
                        pb.emit(S, lambda E, oc=oc, bk=bk: E.activation(
                            gtmp[:], pbank[bk], AF.Silu), inc="mscl")
                        p6_marks[bk] = ("mscl", pb.cnt["mscl"])
                        pb.wait(G, "mscl")
                        pb.dma("gpsimd", gdr[oc, :, 512 * ts:512 * (ts + 1)],
                               gtmp[:], inc="qG")
                    else:
                        eng = S if bk == 0 else V
                        pb.wait(eng, "mten")
                        dsl6 = dstT[:, T * oc + 512 * ts:T * oc + 512 * (ts + 1)]
                        if bk == 0:
                            pb.emit(S, lambda E, d=dsl6, bk=bk: E.copy(
                                d, pbank[bk]), inc="mscl")
                            p6_marks[bk] = ("mscl", pb.cnt["mscl"])
                        else:
                            pb.emit(V, lambda E, d=dsl6, bk=bk: E.tensor_copy(
                                d, pbank[bk]), inc="mvec")
                            p6_marks[bk] = ("mvec", pb.cnt["mvec"])
            else:
                for jv in range(4 * part, 4 * part + 4):
                    j, vs = jv // 2, jv % 2
                    tt = 4 * ts + j
                    bk = jv % 2
                    if bk in p6_marks:
                        pb.wait(Tn, *p6_marks[bk])
                    for dc in range(NC):
                        pb.emit(Tn, lambda E, bk=bk, j=j, vs=vs, dc=dc,
                                xcon=xcon: E.matmul(
                            pbank[bk],
                            xcon[:, 512 * dc + 128 * j:512 * dc + 128 * (j + 1)],
                            wtb[:, 1024 * dc + 512 * vs:1024 * dc + 512 * (vs + 1)],
                            start=(dc == 0), stop=(dc == 7)), inc="mten")
                    pb.wait(V, "mten")
                    pb.wait(V, "qG")
                    pb.emit(V, lambda E, bk=bk: E.tensor_copy(
                        vtmp[:], pbank[bk]), inc="mvec")
                    p6_marks[bk] = ("mvec", pb.cnt["mvec"])
                    pb.wait(G, "mvec")
                    pb.dma("gpsimd",
                           vdr[4 * vs:4 * vs + 4, 128 * tt:128 * (tt + 1),
                               :].rearrange("c p n -> p c n"),
                           vtmp[:].rearrange("p (c n) -> p c n", n=128),
                           inc="qG")

        p6_marks = {}
        for (mat, vec_idx, kind) in ((WrT, 3, "r"), (WkT, 1, "k"),
                                     (WvT, 2, "v"), (WgT, 4, "g")):
            p6_marks = {}
            load_wt(mat)
            pb.wait(Tn, "mscl")
            pb.wait(Tn, "mvec")
            xx_src = {1: xxa, 2: xxb, 3: xxb, 4: xxc}[vec_idx]
            xx_row = {1: 32, 2: 0, 3: 32, 4: 0}[vec_idx]
            dstT = rT if kind == "r" else kT
            for half in range(2):
                build_half(vec_idx, xx_src, xx_row, 0, xcon2[:, 0:4096], half)
            for ts in range(TS):
                xcon = xcon2[:, 4096 * (ts % 2):4096 * (ts % 2 + 1)]
                nxt = xcon2[:, 4096 * ((ts + 1) % 2):4096 * ((ts + 1) % 2 + 1)]
                pb.wait(Tn, "mvec", ep_marks.get(3, 0))
                if ts + 1 < TS:
                    build_half(vec_idx, xx_src, xx_row, ts + 1, nxt, 0)
                proj_part(kind, dstT, xcon, ts, 0)
                if ts + 1 < TS:
                    build_half(vec_idx, xx_src, xx_row, ts + 1, nxt, 1)
                proj_part(kind, dstT, xcon, ts, 1)
        pb.barrier("p6_proj")
        if s == 0:
            if "rTd" in dbg_out:
                pb.dma("gpsimd", dbg_out["rTd"], rT[:], inc="qO")
            if "kTd" in dbg_out:
                pb.dma("gpsimd", dbg_out["kTd"], kT[:], inc="qO")
            if "gdrd" in dbg_out:
                pb.dma("gpsimd", dbg_out["gdrd"], gdr, inc="qO")
            if "vdrd" in dbg_out:
                pb.dma("gpsimd", dbg_out["vdrd"], vdr, inc="qO")
            pb.barrier("p6_projd")
        stA.close()

        stB = ExitStack()
        A_ = sbt(stB, "p6_A", [128, T], F32)
        B_ = sbt(stB, "p6_B", [128, T], F32)
        C_ = sbt(stB, "p6_C", [128, T], F32)
        zerosb = sbt(stB, "p6_z0", [128, T], BF16)
        pb.emit(G, lambda E: E.memset(zerosb[:], 0.0))
        vhp = sbt(stB, "p6_vhp", [128, TT * 128], BF16)
        ghp = sbt(stB, "p6_ghp", [128, T], BF16)
        ysq1 = sbt(stB, "p6_ysq", [128, 512], F32)
        afmp = sbt(stB, "p6_af2", [128, 512], BF16)
        afmp1 = sbt(stB, "p6_af3", [128, 512], BF16)
        o1b2 = sbt(stB, "p6_o1b", [128, 64], BF16)
        O2f = sbt(stB, "p6_O2", [128, 2], F32)
        pb.emit(G, lambda E: E.memset(o1b2[:], 1.0))
        pb.emit(G, lambda E: E.memset(O2f[:], 0.0))
        pb.emit(G, lambda E: E.memset(O2f[0:64, 0:1], 1.0))
        pb.emit(G, lambda E: E.memset(O2f[64:128, 1:2], 1.0))
        rfb = dxT[:, 0:T]
        kfb = dxT[:, T:2 * T]
        rbb = dxT[:, 2 * T:3 * T]
        kbb = dxT[:, 3 * T:4 * T]
        gnb2 = dxT[:, 4 * T:4 * T + 4096]
        ysbf = dxT[:, 6 * T:8 * T].bitcast(F32)
        AT = [xnT[:, 0:16 * 512], xnT[:, 16 * 512:32 * 512]]

        for hp in range(NC):
            pb.wait("gpsimd", "mten")
            pb.dma("gpsimd", vhp[:].rearrange("p (tt n) -> p tt n", n=128),
                   vdr[hp].rearrange("(tt p) n -> p tt n", p=128), inc="qG")
            pb.dma("gpsimd", ghp[:], gdr[hp], inc="qG")
            pb.wait(Tn, "mvec")
            pb.wait(Tn, "mscl")
            for ts in range(TS):
                pb.emit(Tn, lambda E, hp=hp, ts=ts: E.matmul(
                    pbank[0], dw2b[:, 128 * hp:128 * (hp + 1)],
                    t1b[:, 512 * ts:512 * (ts + 1)], start=True, stop=True),
                    inc="mten")
                pb.wait(S, "mten")
                pb.emit(S, lambda E, hp=hp, ts=ts: E.activation(
                    A_[:, 512 * ts:512 * (ts + 1)], pbank[0], AF.Exp,
                    bias=pcol(COL_TDEC + hp)), inc="mscl")
            pb.wait(V, "mscl")
            pb.emit(V, lambda E: E.tensor_tensor_scan(
                B_[:], A_[:], zerosb[:], 0.0, ALU.add, ALU.add))
            pb.emit(V, lambda E: E.tensor_sub(C_[:], B_[:], A_[:]))
            pb.emit(V, lambda E: E.tensor_scalar(
                A_[:], B_[:], B_[:, 1024:1025], 60.0, ALU.subtract, ALU.min))
            pb.emit(V, lambda E: E.tensor_scalar_max(A_[:], A_[:], -60.0))
            pb.emit(V, lambda E: E.tensor_scalar(
                B_[:], C_[:], C_[:, 1024:1025], 60.0, ALU.subtract, ALU.min))
            pb.emit(V, lambda E: E.tensor_scalar_max(B_[:], B_[:], -60.0),
                    inc="mvec")
            pb.wait(S, "mvec")
            for (dst, srcC, sc, base) in ((rfb, A_, -1.0, rT), (kfb, A_, 1.0, kT),
                                          (rbb, B_, 1.0, rT), (kbb, B_, -1.0, kT)):
                pb.emit(S, lambda E, srcC=srcC, sc=sc: E.activation(
                    C_[:], srcC[:], AF.Exp, scale=sc), inc="mscl")
                pb.wait(V, "mscl")
                pb.emit(V, lambda E, dst=dst, base=base, hp=hp: E.tensor_mul(
                    dst, C_[:], base[:, T * hp:T * (hp + 1)]), inc="mvec")
                pb.wait(S, "mvec")
            if hp == 0 and s == 0 and "cfpd" in dbg_out:
                pb.wait("gpsimd", "mvec")
                pb.dma("gpsimd", dbg_out["cfpd"], A_[:], inc="qO")
                pb.dma("gpsimd", dbg_out["cbpd"], B_[:], inc="qO")
                pb.dma("gpsimd", dbg_out["rfbd"], rfb, inc="qO")
                pb.dma("gpsimd", dbg_out["kfbd"], kfb, inc="qO")
                pb.wait("gpsimd", "qO")
            pb.wait(Tn, "mvec")
            pb.wait(Tn, "qG")
            bkfree = {}
            sel_prev = {}
            yev_mark = [0]
            sq_guard = [0]
            apply_mark = [0]

            def emit_y(cy, atrdy):
                for mk in atrdy[cy]:
                    pb.wait(Tn, *mk)
                if cy == 0 and yev_mark[0]:
                    pb.wait(Tn, "mvec", yev_mark[0])
                for hl in (0, 1):
                    yb = (pgrp[1][0:64, 1024:1536] if hl == 0
                          else pgrp[1][64:128, 1536:2048])
                    pb.emit(Tn, lambda E, yb=yb, hl=hl, cy=cy: E.matmul(
                        yb, vhp[:, 128 * cy + 64 * hl:128 * cy + 64 * (hl + 1)],
                        AT[hl][:, 512 * cy:512 * (cy + 1)],
                        start=(cy == 0), stop=(cy == 15)), inc="mten")

            for qs in range(TS):
                pb.wait(S, "mten")
                pb.wait(V, "mten")
                atrdy = {}
                for c in range(16):
                    diag = 4 * qs <= c <= 4 * qs + 3
                    bkA, bkB = c % 2, 2 + c % 2
                    for bk in (bkA, bkB):
                        if bk in bkfree:
                            pb.wait(Tn, *bkfree[bk])
                    f_c = c < 4 * qs or diag
                    b_c = c > 4 * qs + 3 or diag
                    if f_c:
                        for hl, bk in ((0, bkA), (1, bkB)):
                            r0 = 64 * hl
                            pb.emit(Tn, lambda E, bk=bk, c=c, qs=qs, r0=r0:
                                    E.matmul(
                                pbank[bk], kfb[r0:r0 + 64, 128 * c:128 * (c + 1)],
                                rfb[r0:r0 + 64, 512 * qs:512 * (qs + 1)],
                                start=True, stop=True), inc="mten")
                    if b_c:
                        for hl, bk in ((0, bkA), (1, bkB)):
                            r0 = 64 * hl
                            if diag:
                                if hl in sel_prev:
                                    pb.wait(Tn, "mvec", sel_prev[hl])
                                ob = pgrp[1][:, 512 * hl:512 * (hl + 1)]
                            else:
                                ob = pbank[bk]
                            pb.emit(Tn, lambda E, ob=ob, c=c, qs=qs, r0=r0:
                                    E.matmul(
                                ob, kbb[r0:r0 + 64, 128 * c:128 * (c + 1)],
                                rbb[r0:r0 + 64, 512 * qs:512 * (qs + 1)],
                                start=True, stop=True), inc="mten")
                    if diag:
                        j = c - 4 * qs
                        if sel_prev:
                            pb.wait(S, "mvec", max(sel_prev.values()))
                        pb.wait(S, "mten")
                        pb.emit(S, lambda E, bkA=bkA: E.copy(afmp[:], pbank[bkA]),
                                inc="mscl")
                        bkfree[bkA] = ("mscl", pb.cnt["mscl"])
                        pb.emit(S, lambda E, bkB=bkB: E.copy(afmp1[:], pbank[bkB]),
                                inc="mscl")
                        bkfree[bkB] = ("mscl", pb.cnt["mscl"])
                        pb.wait(V, "mscl")
                        pb.wait(V, "mten")
                        pb.emit(V, lambda E, c=c, j=j: E.select(
                            AT[0][:, 512 * c:512 * (c + 1)],
                            masksb[:, 512 * j:512 * (j + 1)],
                            afmp[:], pgrp[1][:, 0:512]), inc="mvec")
                        sel_prev[0] = pb.cnt["mvec"]
                        pb.emit(V, lambda E, c=c, j=j: E.select(
                            AT[1][:, 512 * c:512 * (c + 1)],
                            masksb[:, 512 * j:512 * (j + 1)],
                            afmp1[:], pgrp[1][:, 512:1024]), inc="mvec")
                        sel_prev[1] = pb.cnt["mvec"]
                        atrdy[c] = [("mvec", sel_prev[1])]
                    else:
                        pb.wait(S, "mten")
                        pb.emit(S, lambda E, c=c, bkA=bkA: E.copy(
                            AT[0][:, 512 * c:512 * (c + 1)], pbank[bkA]),
                            inc="mscl")
                        bkfree[bkA] = ("mscl", pb.cnt["mscl"])
                        pb.wait(V, "mten")
                        pb.emit(V, lambda E, c=c, bkB=bkB: E.tensor_copy(
                            AT[1][:, 512 * c:512 * (c + 1)], pbank[bkB]),
                            inc="mvec")
                        bkfree[bkB] = ("mvec", pb.cnt["mvec"])
                        atrdy[c] = [bkfree[bkA], bkfree[bkB]]
                    if c >= 2:
                        emit_y(c - 2, atrdy)
                if (hp == 0 and qs == 0 and s == 0 and "ATd" in dbg_out):
                    pb.wait("gpsimd", "mscl")
                    pb.wait("gpsimd", "mvec")
                    pb.dma("gpsimd", dbg_out["ATd"], AT[0], inc="qO")
                    pb.wait("gpsimd", "qO")
                emit_y(14, atrdy)
                emit_y(15, atrdy)
                pb.wait(V, "mten")
                pb.emit(V, lambda E, qs=qs: E.tensor_copy(
                    ysbf[0:64, 512 * qs:512 * (qs + 1)],
                    pgrp[1][0:64, 1024:1536]))
                pb.emit(V, lambda E, qs=qs: E.tensor_copy(
                    ysbf[64:128, 512 * qs:512 * (qs + 1)],
                    pgrp[1][64:128, 1536:2048]), inc="mvec")
                if (hp == 0 and qs == 0 and s == 0 and "yd" in dbg_out):
                    pb.wait("gpsimd", "mvec")
                    pb.dma("gpsimd", dbg_out["yd"], ysbf[0:64, 0:512], inc="qO")
                    pb.wait("gpsimd", "qO")
                if sq_guard[0]:
                    pb.wait(S, "mten", sq_guard[0])
                pb.wait(S, "mvec")
                pb.emit(S, lambda E, qs=qs: E.activation(
                    ysq1[:], ysbf[:, 512 * qs:512 * (qs + 1)], AF.Square),
                    inc="mscl")
                pb.wait(Tn, "mvec")
                pb.wait(Tn, "mscl")
                pb.emit(Tn, lambda E, qs=qs: E.matmul(
                    pgrp[1][64:66, 1024:1536], O2f[:],
                    ysbf[:, 512 * qs:512 * (qs + 1)],
                    start=True, stop=True), inc="mten")
                pb.emit(Tn, lambda E: E.matmul(
                    pgrp[1][64:66, 1536:2048], O2f[:], ysq1[:],
                    start=True, stop=True), inc="mten")
                sq_guard[0] = pb.cnt["mten"]
                pb.wait(V, "mten")
                pb.emit(V, lambda E, qs=qs: E.tensor_copy(
                    A_[64:66, 512 * qs:512 * (qs + 1)],
                    pgrp[1][64:66, 1024:1536]))
                pb.emit(V, lambda E, qs=qs: E.tensor_copy(
                    B_[64:66, 512 * qs:512 * (qs + 1)],
                    pgrp[1][64:66, 1536:2048]), inc="mvec")
                yev_mark[0] = pb.cnt["mvec"]
            pb.emit(V, lambda E: E.tensor_scalar(
                C_[64:66, 0:T], A_[64:66, 0:T], 1.0 / 64, None, ALU.mult),
                inc="mvec")
            pb.wait(S, "mvec")
            pb.emit(S, lambda E: E.activation(
                A_[64:66, 0:T], C_[64:66, 0:T], AF.Square), inc="mscl")
            pb.emit(V, lambda E: E.tensor_scalar(
                B_[64:66, 0:T], B_[64:66, 0:T], 1.0 / 64, None, ALU.mult))
            pb.wait(V, "mscl")
            pb.emit(V, lambda E: E.tensor_sub(
                B_[64:66, 0:T], B_[64:66, 0:T], A_[64:66, 0:T]), inc="mvec")
            pb.wait(S, "mvec")
            pb.emit(S, lambda E: E.activation(
                A_[64:66, 0:T], B_[64:66, 0:T], AF.Sqrt, bias=6.4e-4),
                inc="mscl")
            pb.wait(V, "mscl")
            pb.emit(V, lambda E: E.reciprocal(A_[64:66, 0:T], A_[64:66, 0:T]))
            pb.emit(V, lambda E: E.tensor_copy(gnb2[64:66, 0:2048],
                                               C_[64:66, 0:T]))
            pb.emit(V, lambda E: E.tensor_copy(gnb2[64:66, 2048:4096],
                                               A_[64:66, 0:T]), inc="mvec")
            pb.wait(G, "mvec")
            pb.dma("gpsimd", gnb2[0:1, 0:4096], gnb2[64:65, 0:4096], inc="qG")
            pb.dma("gpsimd", gnb2[32:33, 0:4096], gnb2[65:66, 0:4096], inc="qG")
            pb.wait(Tn, "qG")
            for qs in range(TS):
                if apply_mark[0]:
                    pb.wait(Tn, "mvec", apply_mark[0])
                for hl, rb in ((0, 0), (1, 32)):
                    ro = slice(64 * hl, 64 * (hl + 1))
                    pb.emit(Tn, lambda E, ro=ro, rb=rb, qs=qs: E.matmul(
                        pgrp[1][ro, 1024:1536], o1b2[rb:rb + 1, :],
                        gnb2[rb:rb + 1, 512 * qs:512 * (qs + 1)],
                        start=True, stop=True))
                    pb.emit(Tn, lambda E, ro=ro, rb=rb, qs=qs: E.matmul(
                        pgrp[1][ro, 1536:2048], o1b2[rb:rb + 1, :],
                        gnb2[rb:rb + 1, 2048 + 512 * qs:2048 + 512 * (qs + 1)],
                        start=True, stop=True), inc="mten")
                pb.wait(V, "mten")
                for hl in (0, 1):
                    ro = slice(64 * hl, 64 * (hl + 1))
                    ysl = ysbf[64 * hl:64 * (hl + 1), 512 * qs:512 * (qs + 1)]
                    pb.emit(V, lambda E, ysl=ysl, ro=ro: E.tensor_sub(
                        ysl, ysl, pgrp[1][ro, 1024:1536]))
                    pb.emit(V, lambda E, ysl=ysl, ro=ro: E.tensor_mul(
                        ysl, ysl, pgrp[1][ro, 1536:2048]))
                    pb.emit(V, lambda E, ysl=ysl, hl=hl, hp=hp: E.tensor_scalar(
                        ysl, ysl,
                        packA_s[64 * hl:64 * (hl + 1),
                                COL_LNW + hp:COL_LNW + hp + 1],
                        packA_s[64 * hl:64 * (hl + 1),
                                COL_LNB + hp:COL_LNB + hp + 1],
                        ALU.mult, ALU.add))
                    pb.emit(V, lambda E, ysl=ysl, hl=hl, qs=qs: E.tensor_mul(
                        ghp[64 * hl:64 * (hl + 1), 512 * qs:512 * (qs + 1)],
                        ghp[64 * hl:64 * (hl + 1), 512 * qs:512 * (qs + 1)],
                        ysl), inc="mvec")
                apply_mark[0] = pb.cnt["mvec"]
            pb.wait(G, "mvec")
            pb.dma("gpsimd", gdr[hp], ghp[:], inc="qG")
        pb.barrier("p6_lion")
        if "zd" in dbg_out and s == 0:
            pb.dma("gpsimd", dbg_out["zd"], gdr, inc="qO")
            pb.barrier("p6_zd")
        stB.close()

        stC = ExitStack()
        WoTb = kT[:, 0:8192]
        dts = kT[:, 8192:16384].bitcast(F32)
        zts = rT[:, 0:4096]
        xtt = rT[:, 4096:6144].bitcast(F32)
        pb.dma("gpsimd", WoTb, WoT, inc="qG")
        pb.wait(Tn, "qG")
        op_marks = {}
        for ts in range(TS):
            pb.wait("sync", "mten")
            pb.dma("sync", zts[:].rearrange("p (c n) -> p c n", n=512),
                   gdr.rearrange("c p n -> p c n")[:, :, 512 * ts:512 * (ts + 1)],
                   inc="qS")
            pb.wait(Tn, "qS")
            for oc in range(NC):
                bk = oc % 2
                if bk in op_marks:
                    pb.wait(Tn, "mvec", op_marks[bk])
                for dc in range(NC):
                    pb.emit(Tn, lambda E, bk=bk, oc=oc, dc=dc: E.matmul(
                        pbank[bk],
                        WoTb[:, 1024 * dc + 128 * oc:1024 * dc + 128 * (oc + 1)],
                        zts[:, 512 * dc:512 * (dc + 1)],
                        start=(dc == 0), stop=(dc == 7)), inc="mten")
                pb.wait(V, "mten")
                pb.emit(V, lambda E, oc=oc, bk=bk: E.tensor_copy(
                    dts[:, 512 * oc:512 * (oc + 1)], pbank[bk]), inc="mvec")
                op_marks[bk] = pb.cnt["mvec"]
            pb.wait(Tn, "mvec")
            for j in range(4):
                tt = 4 * ts + j
                pb.wait("sync", "mvec")
                pb.wait("sync", "qO")
                pb.dma("sync", xtt[:], src_x[s, 128 * tt:128 * (tt + 1), :],
                       inc="qS")
                pb.wait(Tn, "mvec")
                for oc in range(NC):
                    pb.emit(Tn, lambda E, j=j, oc=oc: E.transpose(
                        pgrp[1][:, 128 * oc:128 * (oc + 1)],
                        dts[:, 512 * oc + 128 * j:512 * oc + 128 * (j + 1)],
                        ident[:]), inc=("mten" if oc == 7 else None))
                qs_t = pb.cnt["qS"]
                pb.wait(V, "qS", qs_t)
                pb.wait(V, "mten")
                pb.emit(V, lambda E: E.tensor_add(xtt[:], xtt[:],
                                                  pgrp[1][:, 0:1024]),
                        inc="mvec")
                pb.wait(G, "mvec")
                pb.dma("gpsimd", dst_x[s, 128 * tt:128 * (tt + 1), :], xtt[:],
                       inc="qO")
        pb.barrier("p6_end")
        stC.close()
        stP.close()
        res_seq.pop("dxT_st").close()
        res_seq.pop("dxT")

    for s in range(SEQS):
        if "P1" in phases:
            emit_P1(s, x_in)
        if "P2" in phases:
            emit_P2(s, x_in, xdr)
        if "P3" in phases:
            emit_P3(s, xdr, xdr)
        if "P4" in phases:
            emit_P4(s, xdr)
        if "P5" in phases:
            emit_P5(s, xdr, xdr)
        if "P6" in phases:
            emit_P6(s, xdr, out)

    if "P6" not in phases:
        with nc.sbuf_tensor("cp_buf", [128, 1024], F32) as cpb:
            for s in range(SEQS):
                for tt in range(TT):
                    pb.dma("sync", cpb[:], x_in[s, 128 * tt:128 * (tt + 1), :],
                           inc="qS")
                    pb.wait("sync", "qS")
                    pb.dma("sync", out[s, 128 * tt:128 * (tt + 1), :], cpb[:],
                           inc="qO")
                    pb.wait("sync", "qO")
            pb.barrier("out_done")
    pb.wait("gpsimd", "qO", pb.cnt["qO"])
    pb.flush()
    ctx.close()
    return nc




def _pack_cols(inp):
    pk = np.zeros((128, 120), np.float32)

    def put(base, vec, n):
        v = np.asarray(vec, np.float32).reshape(-1)
        pk[:, base:base + n] = v.reshape(n, 128).T
    put(0, inp["L0_attn_norm_w"], 8)
    put(8, inp["L1_attn_time_maa_x"], 8)
    put(16, inp["L1_attn_time_maa_w"], 8)
    put(24, inp["L1_attn_time_maa_k"], 8)
    put(32, inp["L1_attn_time_maa_v"], 8)
    put(40, inp["L1_attn_time_maa_r"], 8)
    put(48, inp["L1_attn_time_maa_g"], 8)
    put(56, inp["L1_attn_time_decay"], 8)
    put(64, inp["L1_attn_ln_w"], 8)
    put(72, inp["L1_attn_ln_b"], 8)
    put(80, inp["L1_ffn_b1"], 32)
    put(112, inp["L1_ffn_b2"], 8)
    return pk


def _consts():
    ident = np.eye(128, dtype=np.float32)
    masks = np.zeros((4, 128, 512), np.uint8)
    kp = np.arange(128)[:, None]
    qf = np.arange(512)[None, :]
    for j in range(4):
        masks[j] = (qf >= kp + 128 * j).astype(np.uint8)
    return ident, masks


def _pm(a):
    a = np.asarray(a, np.float32)
    C = a.shape[0] // 128
    return np.ascontiguousarray(
        a.reshape(C, 128, a.shape[1]).transpose(1, 0, 2).reshape(128, -1))


def _core_inputs(inp, core):
    ident, masks = _consts()
    f32c = np.ascontiguousarray
    wqkv = np.asarray(inp["L0_attn_w_qkv"], np.float32)
    wqkv_oc = np.ascontiguousarray(
        wqkv.reshape(8, 128, 24, 128).transpose(2, 1, 0, 3).reshape(
            24, 128, 1024))
    w1T = np.asarray(inp["L1_ffn_w1"], np.float32).T
    w1t_g = np.ascontiguousarray(
        w1T.reshape(8, 128, 8, 2, 256).transpose(2, 3, 1, 0, 4).reshape(
            8, 2, 128, 2048))
    w2T = np.asarray(inp["L1_ffn_w2"], np.float32).T
    w2t_g = np.ascontiguousarray(
        w2T.reshape(8, 2, 2, 128, 1024).transpose(0, 1, 3, 2, 4).reshape(
            8, 2, 128, 2048))
    gwT = np.asarray(inp["L0_ffn_gate_w"], np.float32).T
    gate_wT = np.ascontiguousarray(
        gwT.reshape(8, 128, 16).transpose(1, 0, 2).reshape(128, 128))
    m = {
        "x": f32c(inp["L0_attn_x"][2 * core:2 * core + 2]),
        "wqkv_oc": wqkv_oc,
        "wproj": _pm(inp["L0_attn_w_proj"]),
        "gate_w": inp["L0_ffn_gate_w"],
        "gate_wT": gate_wT,
        "wg_e": np.stack([_pm(w) for w in inp["L0_ffn_w_gate"]]),
        "wu_e": np.stack([_pm(w) for w in inp["L0_ffn_w_up"]]),
        "wd_e": np.stack([_pm(w) for w in inp["L0_ffn_w_down"]]),
        "sw_g": _pm(inp["L0_ffn_sw_gate"]),
        "sw_u": _pm(inp["L0_ffn_sw_up"]),
        "sw_d": _pm(inp["L0_ffn_sw_down"]),
        "maa_w1": _pm(inp["L1_attn_time_maa_w1"]),
        "maa_w2": inp["L1_attn_time_maa_w2"],
        "dec_w1": _pm(inp["L1_attn_time_decay_w1"]),
        "dec_w2": inp["L1_attn_time_decay_w2"],
        "WrT": _pm(np.asarray(inp["L1_attn_W_r"], np.float32).T),
        "WkT": _pm(np.asarray(inp["L1_attn_W_k"], np.float32).T),
        "WvT": _pm(np.asarray(inp["L1_attn_W_v"], np.float32).T),
        "WgT": _pm(np.asarray(inp["L1_attn_W_g"], np.float32).T),
        "WoT": _pm(np.asarray(inp["L1_attn_W_o"], np.float32).T),
        "w1": inp["L1_ffn_w1"],
        "w1t_g": w1t_g,
        "w2": inp["L1_ffn_w2"],
        "w2t_g": w2t_g,
        "packA": _pack_cols(inp),
        "ehot": np.kron(np.eye(16), np.ones((1, 128))).astype(ml_dtypes.bfloat16),
        "ident_f32": ident,
        "lion_masks": masks,
    }
    return {k: np.ascontiguousarray(v) if k in ("ehot", "lion_masks") else
            np.ascontiguousarray(np.asarray(v, np.float32)) for k, v in m.items()}


def kernel(**inputs):
    nc = build_program()
    in_maps = [_core_inputs(inputs, c) for c in range(NCORES)]
    res = run_bass_kernel_spmd(nc, in_maps, core_ids=list(range(NCORES)))
    outs = [r["out"] for r in res.results]
    return np.concatenate(outs, axis=0).astype(np.float32)


# version 25
# speedup vs baseline: 1.1878x; 1.1878x over previous
import numpy as np
import ml_dtypes
from contextlib import ExitStack

import concourse.bass as bass
import concourse.mybir as mybir
from concourse.bass_utils import run_bass_kernel_spmd

F32 = mybir.dt.float32
BF16 = mybir.dt.bfloat16
AF = mybir.ActivationFunctionType
ALU = mybir.AluOpType
AX = mybir.AxisListType

NCORES = 8
SEQS = 2
T = 2048
D = 1024
NC = 8
TT = 16
TS = 4
NH = 16
HD = 64



class PB:
    ENGS = ("tensor", "vector", "scalar", "gpsimd", "sync")

    def __init__(self, nc, block, ctx):
        self.nc = nc
        self.block = block
        self.ctx = ctx
        self.buf = {e: [] for e in self.ENGS}
        self.sems = {}
        self.cnt = {}
        self.dirty = {e: False for e in self.ENGS}

    def sem(self, name):
        if name not in self.sems:
            self.sems[name] = self.ctx.enter_context(self.nc.semaphore(name))
            self.cnt[name] = 0
        return self.sems[name]

    CHAIN = {"vector": "mvec", "scalar": "mscl", "gpsimd": "mgps"}

    def emit(self, eng, fn, inc=None, n=1):
        ch = self.CHAIN.get(eng)
        if ch is not None and inc is None:
            inc, n = ch, 1
        if ch is not None:
            h = self.sem(ch)
            v = self.cnt[ch]
            if v > 0:
                self.buf[eng].append(lambda E, h=h, v=v: E.wait_ge(h, v))
        if inc is not None:
            h2 = self.sem(inc)
            self.cnt[inc] += n
            self.buf[eng].append(lambda E, fn=fn, h=h2, n=n: fn(E).then_inc(h2, n))
        else:
            self.buf[eng].append(fn)
        if eng != "sync":
            self.dirty[eng] = True

    def wait(self, eng, sem, val=None):
        if val is None:
            val = self.cnt.get(sem, 0)
        if val <= 0:
            return
        h = self.sem(sem)
        self.buf[eng].append(lambda E, h=h, v=val: E.wait_ge(h, v))

    def dma(self, queue, out, in_, inc, **kw):
        self.emit(queue, lambda E, o=out, i=in_, kw=kw: E.dma_start(out=o, in_=i, **kw),
                  inc=inc, n=16)

    def flush(self):
        for eng in self.ENGS:
            ops = self.buf[eng]
            if not ops:
                continue
            self.buf[eng] = []

            def section(E, ops=ops):
                for op in ops:
                    op(E)
            getattr(self.block, eng)(section)

    def barrier(self, name=""):
        active = [e for e in ("tensor", "vector", "scalar", "gpsimd")
                  if self.dirty[e]]
        for e in active:
            self.emit(e, lambda E: E.nop(), inc="m" + e)
        targets = {e: self.cnt["m" + e] for e in active}
        dma_sems = [s for s in self.cnt if s.startswith("q")]
        for e in self.ENGS:
            for o in active:
                if o != e:
                    self.wait(e, "m" + o, targets[o])
            for s in dma_sems:
                if self.cnt[s] > 0:
                    self.wait(e, s, self.cnt[s])
        for e in active:
            self.dirty[e] = False
        self.flush()




def build_program(dbg=(), phases=("P1", "P2", "P3", "P4", "P5", "P6")):
    nc = bass.Bass("TRN2", target_bir_lowering=False, debug=False)

    def din(name, shape, dt=F32):
        return nc.dram_tensor(name, shape, dt, kind="ExternalInput").ap()

    x_in = din("x", [SEQS, T, D])
    wqkv_oc = din("wqkv_oc", [24, 128, 1024])
    wproj = din("wproj", [128, NC * 1024])
    gate_w = din("gate_w", [16, D])
    gate_wT = din("gate_wT", [128, 128])
    wg_e = din("wg_e", [16, 128, NC * 512])
    wu_e = din("wu_e", [16, 128, NC * 512])
    wd_e = din("wd_e", [16, 128, 4 * 1024])
    sw_g = din("sw_g", [128, NC * 512])
    sw_u = din("sw_u", [128, NC * 512])
    sw_d = din("sw_d", [128, 4 * 1024])
    maa_w1 = din("maa_w1", [128, NC * 160])
    maa_w2 = din("maa_w2", [5, 32, D])
    dec_w1 = din("dec_w1", [128, NC * 64])
    dec_w2 = din("dec_w2", [64, D])
    WrT = din("WrT", [128, NC * 1024])
    WkT = din("WkT", [128, NC * 1024])
    WvT = din("WvT", [128, NC * 1024])
    WgT = din("WgT", [128, NC * 1024])
    WoT = din("WoT", [128, NC * 1024])
    w1 = din("w1", [4096, D])
    w1t_g = din("w1t_g", [8, 2, 128, NC * 256])
    w2 = din("w2", [D, 4096])
    w2t_g = din("w2t_g", [8, 2, 128, 2 * 1024])
    packA = din("packA", [128, 120])
    ehot = din("ehot", [16, 2048], BF16)
    ident_f32 = din("ident_f32", [128, 128])
    lion_masks = din("lion_masks", [4, 128, 512], mybir.dt.uint8)

    out = nc.dram_tensor("out", [SEQS, T, D], F32, kind="ExternalOutput").ap()
    xdr = nc.dram_tensor("xdr", [SEQS, T, D], F32, kind="Internal").ap()
    vdr = nc.dram_tensor("vdr", [NC, T, 128], BF16, kind="Internal").ap()
    gdr = nc.dram_tensor("gdr", [NC, 128, T], BF16, kind="Internal").ap()

    dbg_out = {}
    for name, shape in dbg:
        dbg_out[name] = nc.dram_tensor("dbg_" + name, shape, F32,
                                       kind="ExternalOutput").ap()

    ctx = ExitStack()

    def sb(name, shape, dt):
        return ctx.enter_context(nc.sbuf_tensor(name, shape, dt))

    packA_s = sb("packA_s", [128, 120], F32)
    cvals = sb("cvals", [128, 8], F32)
    ident = sb("ident", [128, 128], F32)
    gwT_s = sb("gwT_s", [128, 128], F32)
    gwT_hi = sb("gwT_hi", [128, 128], BF16)
    gwT_lo = sb("gwT_lo", [128, 128], BF16)
    invgw = sb("invgw", [16, 1], F32)
    alpha1 = sb("alpha1", [128, 32], F32)
    alpha2 = sb("alpha2", [128, 8], F32)
    ones1 = sb("ones1", [1, 128], F32)
    onescol = sb("onescol", [128, 1], F32)

    COL_NORMW = 0
    COL_MAA = 8
    COL_TDEC = 56
    COL_LNW = 64
    COL_LNB = 72
    COL_B1 = 80
    COL_B2 = 112

    def pcol(c):
        return packA_s[:, c:c + 1]

    pgrp = [ctx.enter_context(nc.psum_tensor(f"pg{i}", [128, 2048], F32))
            for i in range(2)]
    pbank = [pgrp[i // 4][:, 512 * (i % 4):512 * (i % 4 + 1)] for i in range(8)]

    block = ctx.enter_context(nc.Block())
    pb = PB(nc, block, ctx)

    uid = [0]

    def usem(tag):
        uid[0] += 1
        return f"u{uid[0]}_{tag}"

    def sbt(stack, name, shape, dt):
        uid[0] += 1
        return stack.enter_context(nc.sbuf_tensor(f"{name}_{uid[0]}", shape, dt))

    pb.dma("sync", packA_s[:], packA, inc="qS")
    pb.dma("sync", ident[:], ident_f32, inc="qS")
    pb.dma("sync", gwT_s[:], gate_wT, inc="qS")

    for i, v in enumerate([1e-5, 1e-6, 6.4e-4, 1024.0, 0.125, -1.0, -0.5]):
        pb.emit("gpsimd", lambda E, i=i, v=v: E.memset(cvals[:, i:i + 1], v))
        nc.const_aps.register(F32, v, cvals[:, i:i + 1])

    pb.emit("gpsimd", lambda E: E.memset(ones1[:], 1.0))
    pb.emit("gpsimd", lambda E: E.memset(onescol[:], 1.0))

    pb.barrier("preamble")

    xnT = sb("xnT", [128, NC * T], BF16)
    res_seq = {}

    def emit_P1(s, src_x):
        st = ExitStack()
        hrmsT = res_seq["hrmsT"] = ctx.enter_context(
            nc.sbuf_tensor(f"hrmsT{s}", [128, NC * T], BF16)) \
            if "hrmsT" not in res_seq else res_seq["hrmsT"]
        combT = res_seq["combT"] = ctx.enter_context(
            nc.sbuf_tensor(f"combT{s}", [16, T], BF16)) \
            if "combT" not in res_seq else res_seq["combT"]

        xbuf = sbt(st, "p1_x", [128, 2 * 1024], F32)
        xsq = sbt(st, "p1_sq", [128, 1024], F32)
        stat = sbt(st, "p1_st", [128, 8], F32)
        xntt = sbt(st, "p1_xn", [128, 1024], F32)
        xnrm = sbt(st, "p1_xr", [128, 1024], F32)
        xnTf = sbt(st, "p1_xnTf", [128, 1024], F32)
        xh = sbt(st, "p1_xh", [128, 1024], BF16)
        xl = sbt(st, "p1_xl", [128, 1024], BF16)
        logitsT = sbt(st, "p1_lg", [16, T], F32)
        rowt = sbt(st, "p1_rt", [1, 128], F32)
        dbg_praw = sbt(st, "p1_dpr", [16, 128], F32)
        dbg_rbc = sbt(st, "p1_drb", [16, 128], F32)
        rowt_hi = sbt(st, "p1_rth", [1, 128], BF16)
        rowt_lo = sbt(st, "p1_rtl", [1, 128], BF16)
        ones1b = sbt(st, "p1_o1b", [1, 16], BF16)
        pb.emit("gpsimd", lambda E: E.memset(ones1b[:], 1.0))
        rbc = sbt(st, "p1_rbc", [16, 128], F32)

        if s == 0:
            presq = xbuf[0:16, 0:1024]
            pb.dma("sync", presq, gate_w, inc="qS")
            pb.wait("vector", "qS")
            pb.wait("scalar", "qS")
            pb.emit("scalar", lambda E: E.activation(presq, presq, AF.Square),
                    inc="mscl")
            pb.wait("vector", "mscl")
            pb.emit("vector", lambda E: E.reduce_sum(invgw[:], presq, AX.X))
            pb.wait("scalar", "mvec")
            pb.emit("scalar", lambda E: E.activation(invgw[:], invgw[:], AF.Sqrt))
            pb.wait("vector", "mscl")
            pb.emit("vector", lambda E: E.tensor_scalar_max(invgw[:], invgw[:],
                                                            1e-12))
            pb.emit("vector", lambda E: E.reciprocal(invgw[:], invgw[:]))
            pb.emit("vector", lambda E: E.tensor_copy(gwT_hi[:], gwT_s[:]))
            pb.emit("vector", lambda E: E.tensor_sub(gwT_lo[:], gwT_s[:],
                                                     gwT_hi[:]), inc="mvec")
            pb.wait("sync", "mvec")

        for tt in range(TT):
            V, S, Tn, G = "vector", "scalar", "tensor", "gpsimd"
            xb = xbuf[:, 1024 * (tt % 2):1024 * (tt % 2 + 1)]
            pb.wait("sync", "mvec")
            pb.wait("sync", "mscl")
            pb.dma("sync", xb, src_x[s, 128 * tt:128 * (tt + 1), :], inc="qS")
            qs_tgt = pb.cnt["qS"]
            mean, sumsq = stat[:, 0:1], stat[:, 1:2]
            var, rstd = stat[:, 2:3], stat[:, 3:4]
            mxn2, rmsinv, invl2 = stat[:, 4:5], stat[:, 5:6], stat[:, 6:7]
            m2 = stat[:, 7:8]
            pb.wait(V, "qS", qs_tgt)
            pb.wait(S, "qS", qs_tgt)
            pb.emit(V, lambda E, xb=xb: E.reduce_sum(mean, xb, AX.X))
            pb.emit(V, lambda E: E.tensor_scalar_mul(mean, mean, 1.0 / D),
                    inc="mvec")
            pb.emit(S, lambda E, xb=xb: E.activation(xsq[:], xb, AF.Square),
                    inc="mscl")
            pb.wait(V, "mscl")
            pb.emit(V, lambda E: E.reduce_sum(sumsq, xsq[:], AX.X))
            pb.wait(S, "mvec")
            pb.emit(S, lambda E: E.activation(m2, mean, AF.Square), inc="mscl")
            pb.wait(V, "mscl")
            pb.emit(V, lambda E: E.tensor_scalar(var, sumsq, 1.0 / D, m2,
                                                 ALU.mult, ALU.subtract),
                    inc="mvec")
            pb.wait(S, "mvec")
            pb.emit(S, lambda E: E.activation(rstd, var, AF.Sqrt, bias=1e-5),
                    inc="mscl")
            pb.wait(V, "mscl")
            pb.emit(V, lambda E: E.reciprocal(rstd, rstd))
            pb.emit(V, lambda E: E.tensor_mul(mxn2, var, rstd))
            pb.emit(V, lambda E: E.tensor_mul(mxn2, mxn2, rstd), inc="mvec")
            pb.wait(S, "mvec")
            pb.emit(S, lambda E: E.activation(rmsinv, mxn2, AF.Sqrt, bias=1e-6))
            pb.emit(S, lambda E: E.activation(invl2, mxn2, AF.Sqrt,
                                              scale=float(D)), inc="mscl")
            pb.wait(V, "mscl")
            pb.emit(V, lambda E: E.reciprocal(rmsinv, rmsinv))
            pb.emit(V, lambda E: E.tensor_scalar_max(invl2, invl2, 1e-12))
            pb.emit(V, lambda E: E.reciprocal(invl2, invl2))
            pb.emit(V, lambda E, xb=xb: E.tensor_scalar(
                xntt[:], xb, mean, rstd, ALU.subtract, ALU.mult))
            pb.emit(V, lambda E: E.tensor_scalar_mul(xnrm[:], xntt[:], rmsinv),
                    inc="mvec")
            pb.wait(Tn, "mvec")
            for j in range(8):
                bank = pbank[j // 4]
                pb.emit(Tn, lambda E, j=j, b=bank: E.transpose(
                    b[:, 128 * (j % 4):128 * (j % 4 + 1)],
                    xntt[:, 128 * j:128 * (j + 1)], ident[:]))
            for j in range(8):
                bank = pbank[2 + j // 4]
                pb.emit(Tn, lambda E, j=j, b=bank: E.transpose(
                    b[:, 128 * (j % 4):128 * (j % 4 + 1)],
                    xnrm[:, 128 * j:128 * (j + 1)], ident[:]),
                    inc=("mten" if j == 7 else None))
            pb.wait(V, "mten")
            pb.wait(S, "mten")
            pb.emit(V, lambda E: E.tensor_copy(xnTf[:, 0:512], pbank[0][:]))
            pb.emit(V, lambda E: E.tensor_copy(xnTf[:, 512:1024], pbank[1][:]),
                    inc="mvec")
            v5 = pb.cnt["mvec"]
            for half in range(2):
                for cq in range(4):
                    c = 4 * half + cq
                    dst = hrmsT[:, T * c + 128 * tt:T * c + 128 * (tt + 1)]
                    srcq = pbank[2 + half][:, 128 * cq:128 * (cq + 1)]
                    pb.emit(S, lambda E, d=dst, s2=srcq, c=c: E.activation(
                        d, s2, AF.Copy, scale=pcol(COL_NORMW + c)),
                        inc=("mscl" if c == 7 else None))
            s5 = pb.cnt["mscl"]
            pb.emit(V, lambda E: E.tensor_copy(xh[:], xnTf[:]))
            pb.emit(V, lambda E: E.tensor_sub(xl[:], xnTf[:], xh[:]))
            v5b = pb.cnt["mvec"]
            pb.wait(G, "mvec", v5b)
            dstg = xnT[:].rearrange("p (c n) -> p c n", n=T)[
                :, :, 128 * tt:128 * (tt + 1)]
            pb.emit(G, lambda E, d=dstg: E.tensor_copy(
                d, xh[:].rearrange("p (c n) -> p c n", n=128)), inc="mgps")
            g1 = pb.cnt["mgps"]
            pb.wait(Tn, "mvec", v5b)
            for c in range(8):
                for t3, (lw, rx) in enumerate(((gwT_hi, xh), (gwT_hi, xl),
                                               (gwT_lo, xh))):
                    pb.emit(Tn, lambda E, c=c, t3=t3, lw=lw, rx=rx: E.matmul(
                        pbank[4][0:16, 0:128], lw[:, 16 * c:16 * (c + 1)],
                        rx[:, 128 * c:128 * (c + 1)],
                        start=(c == 0 and t3 == 0), stop=(c == 7 and t3 == 2)))
            pb.emit(Tn, lambda E: E.transpose(pbank[5][0:1, 0:128], invl2,
                                              ident[:]), inc="mten")
            pb.wait(V, "mten")
            pb.emit(V, lambda E: E.tensor_copy(rowt[:], pbank[5][0:1, 0:128]))
            pb.emit(V, lambda E: E.tensor_copy(rowt_hi[:], rowt[:]))
            pb.emit(V, lambda E: E.tensor_sub(rowt_lo[:], rowt[:], rowt_hi[:]),
                    inc="mvec")
            pb.wait(Tn, "mvec")
            pb.emit(Tn, lambda E: E.matmul(pbank[5][0:16, 128:256],
                                           ones1b[:], rowt_hi[:],
                                           start=True, stop=False))
            pb.emit(Tn, lambda E: E.matmul(pbank[5][0:16, 128:256],
                                           ones1b[:], rowt_lo[:],
                                           start=False, stop=True),
                    inc="mten")
            pb.wait(S, "mten")
            pb.emit(S, lambda E: E.copy(rbc[:], pbank[5][0:16, 128:256]),
                    inc="mscl")
            pb.wait(V, "mscl")
            if tt == 0 and "t0praw" in dbg_out and s == 0:
                pb.emit(V, lambda E: E.tensor_copy(
                    dbg_praw[:], pbank[4][0:16, 0:128]))
                pb.emit(V, lambda E: E.tensor_copy(dbg_rbc[:], rbc[:]))
            pb.emit(V, lambda E: E.tensor_scalar_mul(rbc[:], rbc[:],
                                                     invgw[:]))
            pb.emit(V, lambda E, tt=tt: E.tensor_mul(
                logitsT[:, 128 * tt:128 * (tt + 1)], pbank[4][0:16, 0:128],
                rbc[:]), inc="mvec")
            pb.wait(Tn, "mvec")
            pb.wait(Tn, "mscl", s5)
            pb.wait(Tn, "mgps", g1)
            pb.wait(V, "mgps", g1)
            if tt == 0 and "t0stat" in dbg_out and s == 0:
                pb.wait("gpsimd", "mvec")
                pb.wait("gpsimd", "mscl")
                pb.dma("gpsimd", dbg_out["t0stat"], stat[:], inc="qO")
                pb.dma("gpsimd", dbg_out["t0xn"], xntt[:], inc="qO")
                pb.dma("gpsimd", dbg_out["t0xnTf"], xnTf[:], inc="qO")
                pb.wait("gpsimd", "qO")
        scoresT = logitsT
        pb.barrier("p1_tiles")
        if "rawlog" in dbg_out and s == 0:
            pb.dma("sync", dbg_out["rawlog"], logitsT[:], inc="qO")
            pb.wait("scalar", "qO")
        pb.emit("scalar", lambda E: E.activation(scoresT[:], logitsT[:],
                                                 AF.Sigmoid), inc="mscl")
        sct = sbt(st, "p1_sct", [128, 16], F32)
        max8 = sbt(st, "p1_m8", [128, 8], F32)
        pk2 = sbt(st, "p1_pk", [128, 2], F32)
        thr_row = sbt(st, "p1_rt0", [1, T], F32)
        rec_row = sbt(st, "p1_rt1", [1, T], F32)
        thr_b = sbt(st, "p1_thb", [16, T], F32)
        rec_b = sbt(st, "p1_reb", [16, T], F32)
        pb.wait("tensor", "mscl", pb.cnt["mscl"])
        for tt in range(TT):
            V, S, Tn = "vector", "scalar", "tensor"
            pb.emit(Tn, lambda E, tt=tt: E.transpose(
                pbank[0][:, 0:16], scoresT[:, 128 * tt:128 * (tt + 1)],
                ident[0:16, 0:16]), inc="mten")
            pb.wait(V, "mten", pb.cnt["mten"])
            pb.emit(V, lambda E: E.tensor_copy(sct[:], pbank[0][:, 0:16]))
            pb.emit(V, lambda E: E.max(max8[:], sct[:]))
            pb.emit(V, lambda E: E.reduce_sum(pk2[:, 1:2], max8[:], AX.X))
            pb.emit(V, lambda E: E.reciprocal(pk2[:, 1:2], pk2[:, 1:2]))
            pb.emit(V, lambda E: E.tensor_copy(pk2[:, 0:1], max8[:, 7:8]),
                    inc="mvec")
            pb.wait(Tn, "mvec", pb.cnt["mvec"])
            pb.emit(Tn, lambda E: E.transpose(pbank[1][0:1, 0:128],
                                              pk2[:, 0:1], ident[:]))
            pb.emit(Tn, lambda E: E.transpose(pbank[1][0:1, 128:256],
                                              pk2[:, 1:2], ident[:]),
                    inc="mten")
            pb.wait(V, "mten", pb.cnt["mten"])
            pb.emit(V, lambda E, tt=tt: E.tensor_copy(
                thr_row[:, 128 * tt:128 * (tt + 1)], pbank[1][0:1, 0:128]))
            pb.emit(V, lambda E, tt=tt: E.tensor_copy(
                rec_row[:, 128 * tt:128 * (tt + 1)], pbank[1][0:1, 128:256]),
                inc="mvec")
            pb.wait(Tn, "mvec", pb.cnt["mvec"])
        for k in range(4):
            pb.emit("tensor", lambda E, k=k: E.matmul(
                pbank[2][0:16, 0:512], ones1[:, 0:16],
                thr_row[:, 512 * k:512 * (k + 1)], start=True, stop=True))
            pb.emit("tensor", lambda E, k=k: E.matmul(
                pbank[3][0:16, 0:512], ones1[:, 0:16],
                rec_row[:, 512 * k:512 * (k + 1)], start=True, stop=True),
                inc="mten")
            pb.wait("scalar", "mten", pb.cnt["mten"])
            pb.emit("scalar", lambda E, k=k: E.copy(
                thr_b[:, 512 * k:512 * (k + 1)], pbank[2][0:16, 0:512]))
            pb.emit("scalar", lambda E, k=k: E.copy(
                rec_b[:, 512 * k:512 * (k + 1)], pbank[3][0:16, 0:512]),
                inc="mscl")
            pb.wait("tensor", "mscl", pb.cnt["mscl"])
        pb.wait("vector", "mscl", pb.cnt["mscl"])
        pb.emit("vector", lambda E: E.tensor_tensor(thr_b[:], scoresT[:],
                                                    thr_b[:], ALU.is_ge))
        pb.emit("vector", lambda E: E.tensor_mul(combT[:], scoresT[:], thr_b[:]))
        pb.emit("vector", lambda E: E.tensor_mul(combT[:], combT[:], rec_b[:]))
        pb.barrier("p1_end")

        if "t0praw" in dbg_out and s == 0:
            pb.dma("sync", dbg_out["t0praw"], dbg_praw[:], inc="qO")
            pb.dma("sync", dbg_out["t0rbc"], dbg_rbc[:], inc="qO")
        if "xnT" in dbg_out and s == 0:
            pb.dma("gpsimd", dbg_out["xnT"], xnT[:], inc="qO")
        if "hrmsT" in dbg_out and s == 0:
            pb.dma("gpsimd", dbg_out["hrmsT"], hrmsT[:], inc="qO")
        if "combT" in dbg_out and s == 0:
            pb.dma("gpsimd", dbg_out["combT"], combT[:], inc="qO")
        if "logitsT" in dbg_out and s == 0:
            pb.dma("sync", dbg_out["logitsT"], scoresT[:], inc="qO")
        pb.barrier("p1_dbg")
        st.close()

    def emit_P2(s, src_x, dst_x):
        st = ExitStack()
        hrmsT = res_seq["hrmsT"]
        QT = sbt(st, "p2_QT", [128, NC * T], BF16)
        KT = sbt(st, "p2_KT", [128, NC * T], BF16)
        V65 = sbt(st, "p2_V65", [128, TT * 1040], BF16)
        ctxT = hrmsT
        PTs = sbt(st, "p2_PT", [128, 4 * 512], BF16)
        wbf = sbt(st, "p2_wbf", [128, 2048], BF16)
        wpb = QT[:, 0:8192]
        dts = QT[:, 8192:16384].bitcast(F32)
        xtt = KT[:, 0:2048].bitcast(F32)
        nm2 = sbt(st, "p2_nm2", [128, 8192], BF16)
        oball = sbt(st, "p2_obl", [128, 128], BF16)
        pb.emit("gpsimd", lambda E: E.memset(oball[:], 1.0))
        sqt = sbt(st, "p2_sqt", [128, 512], BF16)
        qrow = sbt(st, "p2_qr", [1, 1024], BF16)
        kcells = sbt(st, "p2_kc", [1, 8], F32)
        kcell = sbt(st, "p2_kc1", [1, 1], F32)
        knegs = sbt(st, "p2_kn", [1, 16], F32)
        rrow = sbt(st, "p2_rr", [1, 1024], F32)
        rr_hi = sbt(st, "p2_rrh", [1, 1024], BF16)
        rr_lo = sbt(st, "p2_rrl", [1, 1024], BF16)
        bcs = sbt(st, "p2_bcs", [64, 1024], F32)
        o1b = sbt(st, "p2_o1b", [1, 128], BF16)
        onescb = sbt(st, "p2_o1c", [128, 1], BF16)
        pb.emit("gpsimd", lambda E: E.memset(onescb[:], 1.0))
        sT_marks = {}
        ctx_marks = {}
        pb.emit("gpsimd", lambda E: E.memset(o1b[:], 1.0))
        pb.emit("gpsimd", lambda E: E.memset(V65[:], 1.0))
        V, S, Tn, G = "vector", "scalar", "tensor", "gpsimd"

        def hQK(m, h, col, n):
            base = 64 * (h % 2)
            return m[base:base + 64, T * (h // 2) + col:T * (h // 2) + col + n]

        qk_marks = {}
        wb_marks = {}
        for oc in range(16):
            par = oc % 2
            wb = wbf[:, 1024 * par:1024 * (par + 1)]
            if par in wb_marks:
                pb.wait("gpsimd", "mten", wb_marks[par])
            pb.dma("gpsimd", wb.rearrange("p (c n) -> p c n", n=128),
                   wqkv_oc[oc], inc="qG")
            gv = pb.cnt["qG"]
            dst = QT if oc < 8 else KT
            ocl = oc % 8
            pb.wait(Tn, "qG", gv)
            for ts in range(TS):
                bk = (oc * TS + ts) % 2
                if bk in qk_marks:
                    pb.wait(Tn, *qk_marks[bk])
                for dc in range(NC):
                    pb.emit(Tn, lambda E, bk=bk, wb=wb, dc=dc, ts=ts: E.matmul(
                        pbank[bk], wb[:, 128 * dc:128 * (dc + 1)],
                        hrmsT[:, T * dc + 512 * ts:T * dc + 512 * (ts + 1)],
                        start=(dc == 0), stop=(dc == 7)), inc="mten")
                eng = S if bk == 0 else V
                pb.wait(eng, "mten")
                dsl2 = dst[:, T * ocl + 512 * ts:T * ocl + 512 * (ts + 1)]
                if bk == 0:
                    pb.emit(S, lambda E, d=dsl2, bk=bk: E.copy(d, pbank[bk]),
                            inc="mscl")
                    qk_marks[bk] = ("mscl", pb.cnt["mscl"])
                else:
                    pb.emit(V, lambda E, d=dsl2, bk=bk: E.tensor_copy(
                        d, pbank[bk]), inc="mvec")
                    qk_marks[bk] = ("mvec", pb.cnt["mvec"])
            wb_marks[par] = pb.cnt["mten"]
        qk_marks = {}
        pb.wait(Tn, "mscl")
        pb.wait(Tn, "mvec")
        wb_marks = {}
        pb.wait("gpsimd", "mten")
        for vs in range(8):
            par = vs % 2
            wvb = wbf[:, 1024 * par:1024 * (par + 1)].rearrange(
                "p (c n) -> p c n", n=128)
            if par in wb_marks:
                pb.wait("gpsimd", "mten", wb_marks[par])
            pb.dma("gpsimd", wvb, wqkv_oc[16 + vs], inc="qG")
            gv = pb.cnt["qG"]
            pb.wait(Tn, "qG", gv)
            for tt in range(TT):
                bk = tt % 2
                if bk in qk_marks:
                    pb.wait(Tn, *qk_marks[bk])
                for dc in range(NC):
                    pb.emit(Tn, lambda E, bk=bk, dc=dc, tt=tt, wvb=wvb:
                            E.matmul(
                        pbank[bk][:, 0:128],
                        hrmsT[:, T * dc + 128 * tt:T * dc + 128 * (tt + 1)],
                        wvb[:, dc], start=(dc == 0), stop=(dc == 7)),
                        inc="mten")
                pb.wait(V, "mten")
                dstv = V65[:, 1040 * tt:1040 * (tt + 1)].rearrange(
                    "p (h c) -> p h c", c=65)[:, 2 * vs:2 * (vs + 1), 0:64]
                pb.emit(V, lambda E, d=dstv, bk=bk: E.tensor_copy(
                    d, pbank[bk][:, 0:128].rearrange("p (h c) -> p h c", c=64)),
                    inc="mvec")
                qk_marks[bk] = ("mvec", pb.cnt["mvec"])
            wb_marks[par] = pb.cnt["mten"]
        pb.barrier("p2_qkv")
        if s == 0 and "QTd" in dbg_out:
            pb.dma("gpsimd", dbg_out["QTd"], QT[:], inc="qO")
            pb.dma("gpsimd", dbg_out["KTd"], KT[:], inc="qO")
            pb.dma("gpsimd", dbg_out["V65d"], V65[:], inc="qO")
            pb.barrier("p2_qkvd")

        pb.wait(S, "mvec")
        for hp in range(NC):
            for ks in range(4):
                pb.emit(S, lambda E, hp=hp, ks=ks: E.activation(
                    sqt[:], KT[:, T * hp + 512 * ks:T * hp + 512 * (ks + 1)],
                    AF.Square), inc="mscl")
                pb.wait(Tn, "mscl")
                pb.wait(Tn, "mvec")
                for hl in range(2):
                    pb.emit(Tn, lambda E, hl=hl: E.matmul(
                        pbank[hl][0:1, 0:512],
                        onescb[64 * hl:64 * (hl + 1), :],
                        sqt[64 * hl:64 * (hl + 1), :],
                        start=True, stop=True), inc="mten")
                pb.wait(V, "mten")
                for hl in range(2):
                    pb.emit(V, lambda E, hl=hl, ks=ks: E.reduce_max(
                        kcells[:, 4 * hl + ks:4 * hl + ks + 1],
                        pbank[hl][0:1, 0:512], AX.X), inc="mvec")
                pb.wait(S, "mvec")
            for hl in range(2):
                h = 2 * hp + hl
                pb.emit(V, lambda E, hl=hl: E.reduce_max(
                    kcell[:, 0:1], kcells[0:1, 4 * hl:4 * (hl + 1)], AX.X))
                pb.emit(V, lambda E: E.tensor_scalar_max(kcell[:, 0:1],
                                                         kcell[:, 0:1], 0.0),
                        inc="mvec")
                pb.wait(S, "mvec")
                pb.emit(S, lambda E, h=h: E.activation(
                    knegs[:, h:h + 1], kcell[:, 0:1], AF.Sqrt), inc="mscl")
                pb.wait(V, "mscl")
                pb.emit(V, lambda E, h=h: E.tensor_scalar_mul(
                    knegs[:, h:h + 1], knegs[:, h:h + 1], -1.0), inc="mvec")
        pb.barrier("p2_kneg")

        for qs in range(TS):
            pb.wait(Tn, "mscl")
            pb.wait(Tn, "mvec")
            for hp in range(NC):
                pb.emit(S, lambda E, hp=hp, qs=qs: E.activation(
                    sqt[:], QT[:, T * hp + 512 * qs:T * hp + 512 * (qs + 1)],
                    AF.Square), inc="mscl")
                pb.wait(Tn, "mscl")
                for hl in range(2):
                    pb.emit(Tn, lambda E, hp=hp, hl=hl: E.matmul(
                        pbank[2 + hl][0:1, 0:512],
                        onescb[64 * hl:64 * (hl + 1), :],
                        sqt[64 * hl:64 * (hl + 1), :],
                        start=True, stop=True), inc="mten")
                pb.wait(S, "mten")
                for hl in range(2):
                    pb.emit(S, lambda E, hl=hl: E.activation(
                        qrow[:, 512 * hl:512 * (hl + 1)],
                        pbank[2 + hl][0:1, 0:512], AF.Sqrt), inc="mscl")
                pb.wait(V, "mscl")
                for hl in range(2):
                    h = 2 * hp + hl
                    pb.emit(V, lambda E, h=h, hl=hl: E.tensor_scalar(
                        nm2[0:1, 512 * h:512 * (h + 1)],
                        qrow[:, 512 * hl:512 * (hl + 1)], knegs[:, h:h + 1],
                        None, ALU.mult), inc="mvec")
            pb.wait("gpsimd", "mvec")
            pb.dma("gpsimd", nm2[64:65, :], nm2[0:1, :], inc="qG")
            pb.wait(Tn, "mvec")
            pb.wait(Tn, "qG")
            pb.wait(S, "mvec")

            for hp in range(NC):
                hA, hB = 2 * hp, 2 * hp + 1
                cbA = pgrp[1][0:65, 1024 * (hp % 2):1024 * (hp % 2) + 512]
                cbB = pgrp[1][0:65, 1024 * (hp % 2) + 512:1024 * (hp % 2) + 1024]
                if (hp % 2) in ctx_marks:
                    pb.wait(Tn, "mvec", ctx_marks[hp % 2])
                def ctx_pair(kb, evmark):
                    pb.wait(Tn, "mscl", evmark)
                    slA = PTs[:, 512 * (kb % 2):512 * (kb % 2 + 1)]
                    slB = PTs[:, 1024 + 512 * (kb % 2):1024 + 512 * (kb % 2 + 1)]
                    pb.emit(Tn, lambda E, hA=hA, kb=kb, slA=slA, cbA=cbA:
                            E.matmul(
                        cbA, V65[:, 1040 * kb + 65 * hA:1040 * kb + 65 * (hA + 1)],
                        slA, start=(kb == 0), stop=(kb == 15)))
                    pb.emit(Tn, lambda E, hB=hB, kb=kb, slB=slB, cbB=cbB:
                            E.matmul(
                        cbB, V65[:, 1040 * kb + 65 * hB:1040 * kb + 65 * (hB + 1)],
                        slB, start=(kb == 0), stop=(kb == 15)), inc="mten")

                evm = [0] * 16
                for kb in range(16):
                    bkA = kb % 2
                    bkB = 2 + kb % 2
                    if bkA in sT_marks:
                        pb.wait(Tn, *sT_marks.pop(bkA))
                    if bkB in sT_marks:
                        pb.wait(Tn, *sT_marks.pop(bkB))
                    pb.emit(Tn, lambda E, hA=hA, kb=kb, qs=qs, bkA=bkA:
                            E.matmul(
                        pbank[bkA], hQK(KT, hA, 128 * kb, 128),
                        hQK(QT, hA, 512 * qs, 512), start=True, stop=False))
                    pb.emit(Tn, lambda E, hB=hB, kb=kb, qs=qs, bkB=bkB:
                            E.matmul(
                        pbank[bkB], hQK(KT, hB, 128 * kb, 128),
                        hQK(QT, hB, 512 * qs, 512), start=True, stop=False))
                    pb.emit(Tn, lambda E, hA=hA, bkA=bkA: E.matmul(
                        pbank[bkA], oball[0:1, :],
                        nm2[0:1, 512 * hA:512 * (hA + 1)],
                        start=False, stop=True))
                    pb.emit(Tn, lambda E, hB=hB, bkB=bkB: E.matmul(
                        pbank[bkB], oball[64:65, :],
                        nm2[64:65, 512 * hB:512 * (hB + 1)],
                        start=False, stop=True), inc="mten")
                    if kb > 0:
                        ctx_pair(kb - 1, evm[kb - 1])
                    slA = PTs[:, 512 * (kb % 2):512 * (kb % 2 + 1)]
                    slB = PTs[:, 1024 + 512 * (kb % 2):1024 + 512 * (kb % 2 + 1)]
                    pb.wait(S, "mten")
                    pb.emit(S, lambda E, slA=slA, bkA=bkA: E.activation(
                        slA, pbank[bkA], AF.Exp, scale=0.125))
                    pb.emit(S, lambda E, slB=slB, bkB=bkB: E.activation(
                        slB, pbank[bkB], AF.Exp, scale=0.125), inc="mscl")
                    evm[kb] = pb.cnt["mscl"]
                    sT_marks[bkA] = ("mscl", evm[kb])
                    sT_marks[bkB] = ("mscl", evm[kb])
                ctx_pair(15, evm[15])
                ctx_done = pb.cnt["mten"]
                pb.wait(S, "mten")
                pb.emit(S, lambda E, cbA=cbA: E.activation(
                    rrow[:, 0:512], cbA[64:65, :], AF.Ln))
                pb.emit(S, lambda E, cbB=cbB: E.activation(
                    rrow[:, 512:1024], cbB[64:65, :], AF.Ln))
                pb.emit(S, lambda E: E.activation(rrow[:], rrow[:], AF.Exp,
                                                  scale=-1.0), inc="mscl")
                pb.wait(V, "mscl")
                pb.emit(V, lambda E: E.tensor_copy(rr_hi[:], rrow[:]))
                pb.emit(V, lambda E: E.tensor_sub(rr_lo[:], rrow[:], rr_hi[:]),
                        inc="mvec")
                rdy = pb.cnt["mvec"]
                for i, h in enumerate((hA, hB)):
                    bk = i
                    bk = 0 if i == 0 else 2
                    if bk in sT_marks:
                        pb.wait(Tn, *sT_marks.pop(bk))
                    pb.wait(Tn, "mvec", rdy)
                    pb.emit(Tn, lambda E, bk=bk, i=i: E.matmul(
                        pbank[bk][0:64, :], o1b[:, 0:64],
                        rr_hi[:, 512 * i:512 * (i + 1)],
                        start=True, stop=False))
                    pb.emit(Tn, lambda E, bk=bk, i=i: E.matmul(
                        pbank[bk][0:64, :], o1b[:, 0:64],
                        rr_lo[:, 512 * i:512 * (i + 1)],
                        start=False, stop=True), inc="mten")
                    eng = S if i == 0 else V
                    pb.wait(eng, "mten")
                    if i == 0:
                        pb.emit(S, lambda E, bk=bk, i=i: E.copy(
                            bcs[:, 512 * i:512 * (i + 1)], pbank[bk][0:64, :]),
                            inc="mscl")
                        pb.wait(V, "mscl")
                    else:
                        pb.emit(V, lambda E, bk=bk, i=i: E.tensor_copy(
                            bcs[:, 512 * i:512 * (i + 1)], pbank[bk][0:64, :]),
                            inc="mvec")
                    cb = cbA if i == 0 else cbB
                    pb.emit(V, lambda E, h=h, qs=qs, cb=cb, i=i: E.tensor_mul(
                        hQK(ctxT, h, 512 * qs, 512), cb[0:64, :],
                        bcs[:, 512 * i:512 * (i + 1)]), inc="mvec")
                    sT_marks[bk] = ("mvec", pb.cnt["mvec"])
                ctx_marks[hp % 2] = pb.cnt["mvec"]
        pb.barrier("p2_attn")

        pb.dma("gpsimd", wpb, wproj, inc="qG")
        pb.wait(Tn, "qG")
        for ts in range(TS):
            for oc in range(NC):
                bk = oc % 2
                pb.wait(Tn, "mvec")
                for dc in range(NC):
                    pb.emit(Tn, lambda E, bk=bk, oc=oc, dc=dc, ts=ts: E.matmul(
                        pbank[bk], wpb[:, 1024 * dc + 128 * oc:1024 * dc + 128 * (oc + 1)],
                        ctxT[:, T * dc + 512 * ts:T * dc + 512 * (ts + 1)],
                        start=(dc == 0), stop=(dc == 7)), inc="mten")
                pb.wait(V, "mten")
                pb.emit(V, lambda E, oc=oc, ts=ts, bk=bk: E.tensor_add(
                    dts[:, 512 * oc:512 * (oc + 1)], pbank[bk],
                    xnT[:, T * oc + 512 * ts:T * oc + 512 * (ts + 1)]),
                    inc="mvec")
            pb.wait(Tn, "mvec")
            for j in range(4):
                tt = 4 * ts + j
                pb.wait("sync", "mvec")
                pb.wait("sync", "qG")
                pb.dma("sync", xtt[:], src_x[s, 128 * tt:128 * (tt + 1), :],
                       inc="qS")
                pb.wait(Tn, "mvec")
                for oc in range(NC):
                    pb.emit(Tn, lambda E, j=j, oc=oc: E.transpose(
                        pgrp[1][:, 128 * oc:128 * (oc + 1)],
                        dts[:, 512 * oc + 128 * j:512 * oc + 128 * (j + 1)],
                        ident[:]), inc=("mten" if oc == 7 else None))
                qs_t = pb.cnt["qS"]
                pb.wait(V, "qS", qs_t)
                pb.wait(V, "mten")
                pb.emit(V, lambda E: E.tensor_add(
                    xtt[:], xtt[:], pgrp[1][:, 0:1024]), inc="mvec")
                pb.wait(G, "mvec")
                pb.dma("gpsimd", dst_x[s, 128 * tt:128 * (tt + 1), :], xtt[:],
                       inc="qG")
        pb.barrier("p2_end")
        if "x1" in dbg_out and s == 0:
            pb.dma("gpsimd", dbg_out["x1"], dst_x[0], inc="qO")
        if "ctxT" in dbg_out and s == 0:
            pb.dma("gpsimd", dbg_out["ctxT"], ctxT[:], inc="qO")

        pb.barrier("p2_dbg")
        st.close()

    def emit_P3(s, src_x, dst_x):
        st = ExitStack()
        combT = res_seq["combT"]
        dlo3 = res_seq["hrmsT"][:].bitcast(F32)
        dhi3 = sbt(st, "p3_dl", [128, 4 * T], F32)

        def dsl3(oc, a, b):
            return (dlo3[:, T * oc + a:T * oc + b] if oc < 4 else
                    dhi3[:, T * (oc - 4) + a:T * (oc - 4) + b])
        wgb2 = sbt(st, "p3_wg", [128, 2 * 4096], BF16)
        wub2 = sbt(st, "p3_wu", [128, 2 * 4096], BF16)
        wdb2 = sbt(st, "p3_wd", [128, 2 * 4096], BF16)
        ehot_s = sbt(st, "p3_eh", [16, 2048], BF16)
        cbc = sbt(st, "p3_cbc", [128, 1024], F32)
        sgb = sbt(st, "p3_sg", [128, 4 * 512], BF16)
        hcb = sbt(st, "p3_hc", [128, 4 * 512], BF16)
        xtt = sbt(st, "p3_xtt", [128, 2048], F32)
        V, S, Tn, G = "vector", "scalar", "tensor", "gpsimd"
        pb.dma("sync", ehot_s[:], ehot, inc="qS")
        for oc in range(NC):
            pb.emit(G, lambda E, oc=oc: E.memset(dsl3(oc, 0, T), 0.0))
        pb.barrier("p3_init")

        bmk = {}
        wset_mark = {}
        p3_prev_hc = [0]
        p3_prev_dn = [0]

        def bwait(bk):
            if bk in bmk:
                pb.wait(Tn, *bmk[bk])
        for e in range(17):
            wset = e % 2
            wgb = wgb2[:, 4096 * wset:4096 * (wset + 1)]
            wub = wub2[:, 4096 * wset:4096 * (wset + 1)]
            wdb = wdb2[:, 4096 * wset:4096 * (wset + 1)]
            if e < 16:
                srcs = ((wg_e[e], wgb), (wu_e[e], wub), (wd_e[e], wdb))
            else:
                srcs = ((sw_g, wgb), (sw_u, wub), (sw_d, wdb))
            if wset in wset_mark:
                pb.wait("gpsimd", "mten", wset_mark[wset])
            for (mat, dst3) in srcs:
                pb.dma("gpsimd", dst3, mat, inc="qG")
            pb.wait(Tn, "qG")
            for ts in range(TS):
                pb.wait(S, "mvec", p3_prev_hc[0])
                pb.wait(V, "mten", p3_prev_dn[0])
                if e < 16:
                    bwait(7)
                    pb.emit(Tn, lambda E, e=e, ts=ts: E.matmul(
                        pbank[7], ehot_s[:, 128 * e:128 * (e + 1)],
                        combT[0:16, 512 * ts:512 * (ts + 1)],
                        start=True, stop=True), inc="mten")
                    pb.wait(S, "mten")
                    pb.emit(S, lambda E, e=e, ts=ts: E.copy(
                        cbc[:, 512 * ((e + ts) % 2):512 * ((e + ts) % 2 + 1)],
                        pbank[7]), inc="mscl")
                    bmk[7] = ("mscl", pb.cnt["mscl"])
                cslc = cbc[:, 512 * ((e + ts) % 2):512 * ((e + ts) % 2 + 1)]
                g_mark, u_mark = {}, {}
                for hc in range(4):
                    bwait(hc)
                    for dc in range(NC):
                        pb.emit(Tn, lambda E, hc=hc, dc=dc, ts=ts, wgb=wgb:
                                E.matmul(
                            pbank[hc],
                            wgb[:, 512 * dc + 128 * hc:512 * dc + 128 * (hc + 1)],
                            xnT[:, T * dc + 512 * ts:T * dc + 512 * (ts + 1)],
                            start=(dc == 0), stop=(dc == 7)),
                            inc=("mten" if dc == 7 else None))
                    g_mark[hc] = pb.cnt["mten"]
                for hc in range(4):
                    bwait(4 + hc)
                    for dc in range(NC):
                        pb.emit(Tn, lambda E, hc=hc, dc=dc, ts=ts, wub=wub:
                                E.matmul(
                            pbank[4 + hc],
                            wub[:, 512 * dc + 128 * hc:512 * dc + 128 * (hc + 1)],
                            xnT[:, T * dc + 512 * ts:T * dc + 512 * (ts + 1)],
                            start=(dc == 0), stop=(dc == 7)),
                            inc=("mten" if dc == 7 else None))
                    u_mark[hc] = pb.cnt["mten"]
                h_marks = []
                for hc in range(4):
                    pb.wait(S, "mten", g_mark[hc])
                    pb.emit(S, lambda E, hc=hc: E.activation(
                        sgb[:, 512 * hc:512 * (hc + 1)], pbank[hc], AF.Silu),
                        inc="mscl")
                    bmk[hc] = ("mscl", pb.cnt["mscl"])
                    pb.wait(V, "mscl")
                    pb.wait(V, "mten", u_mark[hc])
                    pb.emit(V, lambda E, hc=hc: E.tensor_mul(
                        hcb[:, 512 * hc:512 * (hc + 1)], pbank[4 + hc],
                        sgb[:, 512 * hc:512 * (hc + 1)]))
                    if e < 16:
                        pb.emit(V, lambda E, hc=hc, cslc=cslc: E.tensor_mul(
                            hcb[:, 512 * hc:512 * (hc + 1)],
                            hcb[:, 512 * hc:512 * (hc + 1)], cslc),
                            inc="mvec")
                    else:
                        pb.emit(V, lambda E: E.engine_nop(), inc="mvec")
                    bmk[4 + hc] = ("mvec", pb.cnt["mvec"])
                    h_marks.append(pb.cnt["mvec"])
                p3_prev_hc[0] = h_marks[-1]
                dn_mark = {}
                for hc in range(4):
                    pb.wait(Tn, "mvec", h_marks[hc])
                    for oc in range(NC):
                        if hc == 0:
                            bwait(oc)
                        pb.emit(Tn, lambda E, oc=oc, hc=hc, wdb=wdb: E.matmul(
                            pbank[oc],
                            wdb[:, 1024 * hc + 128 * oc:1024 * hc + 128 * (oc + 1)],
                            hcb[:, 512 * hc:512 * (hc + 1)],
                            start=(hc == 0), stop=(hc == 3)), inc="mten")
                        if hc == 3:
                            dn_mark[oc] = pb.cnt["mten"]
                for oc in range(NC):
                    pb.wait(V, "mten", dn_mark[oc])
                    dd3 = dsl3(oc, 512 * ts, 512 * (ts + 1))
                    pb.emit(V, lambda E, dd3=dd3, oc=oc: E.tensor_add(
                        dd3, dd3, pbank[oc]), inc="mvec")
                    bmk[oc] = ("mvec", pb.cnt["mvec"])
                wset_mark[wset] = pb.cnt["mten"]
                p3_prev_dn[0] = pb.cnt["mten"]
        pb.barrier("p3_expert_done")

        qs0 = pb.cnt["qS"]
        qg0 = pb.cnt["qG"]
        add_mark = {}

        def wb_load(tt):
            par = tt % 2
            if tt >= 2:
                pb.wait("sync", "mvec", add_mark[tt - 2])
                pb.wait("sync", "qG", qg0 + 16 * (tt - 1))
            pb.dma("sync", xtt[:, 1024 * par:1024 * (par + 1)],
                   src_x[s, 128 * tt:128 * (tt + 1), :], inc="qS")
        wb_load(0)
        wb_load(1)
        for tt in range(TT):
            par = tt % 2
            xs = xtt[:, 1024 * par:1024 * (par + 1)]
            pg = pgrp[1][:, 1024 * par:1024 * (par + 1)]
            if tt >= 2:
                pb.wait(Tn, "mvec", add_mark[tt - 2])
            for oc in range(NC):
                pb.emit(Tn, lambda E, tt=tt, oc=oc, pg=pg: E.transpose(
                    pg[:, 128 * oc:128 * (oc + 1)],
                    dsl3(oc, 128 * tt, 128 * (tt + 1)),
                    ident[:]), inc=("mten" if oc == 7 else None))
            pb.wait(V, "qS", qs0 + 16 * (tt + 1))
            pb.wait(V, "mten")
            pb.emit(V, lambda E, xs=xs, pg=pg: E.tensor_add(xs, xs, pg),
                    inc="mvec")
            add_mark[tt] = pb.cnt["mvec"]
            pb.wait(G, "mvec", add_mark[tt])
            pb.dma("gpsimd", dst_x[s, 128 * tt:128 * (tt + 1), :], xs,
                   inc="qG")
            if tt + 2 < TT:
                wb_load(tt + 2)
        pb.barrier("p3_end")
        if "x1f" in dbg_out and s == 0:
            pb.dma("gpsimd", dbg_out["x1f"], dst_x[0], inc="qO")
        pb.barrier("p3_dbg")
        st.close()

    def emit_P4(s, src_x):
        st = ExitStack()
        dxst = res_seq["dxT_st"] = ExitStack()
        dxT = res_seq["dxT"] = sbt(dxst, "dxT", [128, NC * T], BF16)
        xbuf = sbt(st, "p4_x", [128, 2 * 1024], F32)
        xsq = sbt(st, "p4_sq", [128, 1024], F32)
        stat = sbt(st, "p4_st", [128, 8], F32)
        xntt = sbt(st, "p4_xn", [128, 1024], F32)
        V, S, Tn, G = "vector", "scalar", "tensor", "gpsimd"
        for tt in range(TT):
            xb = xbuf[:, 1024 * (tt % 2):1024 * (tt % 2 + 1)]
            pb.wait("sync", "mvec")
            pb.wait("sync", "mscl")
            pb.dma("sync", xb, src_x[s, 128 * tt:128 * (tt + 1), :], inc="qS")
            qs_tgt = pb.cnt["qS"]
            mean, sumsq = stat[:, 0:1], stat[:, 1:2]
            var, rstd, m2 = stat[:, 2:3], stat[:, 3:4], stat[:, 7:8]
            pb.wait(V, "qS", qs_tgt)
            pb.wait(S, "qS", qs_tgt)
            pb.emit(V, lambda E, xb=xb: E.reduce_sum(mean, xb, AX.X))
            pb.emit(V, lambda E: E.tensor_scalar_mul(mean, mean, 1.0 / D),
                    inc="mvec")
            pb.emit(S, lambda E, xb=xb: E.activation(xsq[:], xb, AF.Square),
                    inc="mscl")
            pb.wait(V, "mscl")
            pb.emit(V, lambda E: E.reduce_sum(sumsq, xsq[:], AX.X))
            pb.wait(S, "mvec")
            pb.emit(S, lambda E: E.activation(m2, mean, AF.Square), inc="mscl")
            pb.wait(V, "mscl")
            pb.emit(V, lambda E: E.tensor_scalar(var, sumsq, 1.0 / D, m2,
                                                 ALU.mult, ALU.subtract),
                    inc="mvec")
            pb.wait(S, "mvec")
            pb.emit(S, lambda E: E.activation(rstd, var, AF.Sqrt, bias=1e-5),
                    inc="mscl")
            pb.wait(V, "mscl")
            pb.emit(V, lambda E: E.reciprocal(rstd, rstd))
            pb.emit(V, lambda E, xb=xb: E.tensor_scalar(
                xntt[:], xb, mean, rstd, ALU.subtract, ALU.mult), inc="mvec")
            pb.wait(Tn, "mvec")
            for j in range(8):
                bank = pbank[j // 4]
                pb.emit(Tn, lambda E, j=j, b=bank: E.transpose(
                    b[:, 128 * (j % 4):128 * (j % 4 + 1)],
                    xntt[:, 128 * j:128 * (j + 1)], ident[:]),
                    inc=("mten" if j == 7 else None))
            pb.wait(V, "mten")
            for half in range(2):
                dst = xnT[:].rearrange("p (c n) -> p c n", n=T)[
                    :, 4 * half:4 * (half + 1), 128 * tt:128 * (tt + 1)]
                pb.emit(V, lambda E, d=dst, h=half: E.tensor_copy(
                    d, pbank[h][:].rearrange("p (c n) -> p c n", n=128)),
                    inc="mvec")
            pb.wait(Tn, "mvec")
        pb.barrier("p4_ln")
        for c in range(NC):
            xc = xnT[:, T * c:T * (c + 1)]
            dc_ = dxT[:, T * c:T * (c + 1)]
            pb.emit(V, lambda E, xc=xc, dc_=dc_: E.tensor_scalar(
                dc_[:, 1:T], xc[:, 0:T - 1], 0.5, None, ALU.mult))
            pb.emit(V, lambda E, dc_=dc_: E.memset(dc_[:, 0:1], 0.0))
            for lo, hi in ((0, 1024), (1024, 2047)):
                n = hi - lo
                pb.emit(V, lambda E, xc=xc, lo=lo, n=n: E.tensor_scalar(
                    xsq[:, 0:n], xc[:, lo + 1:lo + 1 + n], 0.5, None, ALU.mult))
                pb.emit(V, lambda E, dc_=dc_, lo=lo, n=n: E.tensor_add(
                    dc_[:, lo:lo + n], dc_[:, lo:lo + n], xsq[:, 0:n]))
            pb.emit(V, lambda E, xc=xc, dc_=dc_: E.tensor_sub(dc_, dc_, xc),
                    inc="mvec")
        pb.barrier("p4_end")
        if "dxT" in dbg_out and s == 0:
            pb.dma("gpsimd", dbg_out["dxT"], dxT[:], inc="qO")
            pb.barrier("p4_dbg")
        st.close()

    def emit_P5(s, src_x, dst_x):
        st = ExitStack()
        dlo = res_seq["hrmsT"][:].bitcast(F32)
        dhi = sbt(st, "p5_dl", [128, 4 * T], F32)

        def dsl(oc, a, b):
            return (dlo[:, T * oc + a:T * oc + b] if oc < 4 else
                    dhi[:, T * (oc - 4) + a:T * (oc - 4) + b])
        stg5 = sbt(st, "p5_stg", [128, 2048], F32)
        sw1 = sbt(st, "p5_sw1", [128, 4096], BF16)
        sw2 = sbt(st, "p5_sw2", [128, 4096], BF16)
        hbuf = sbt(st, "p5_h", [128, 4 * T], BF16)
        xtt = sbt(st, "p5_xtt", [128, 1024], F32)
        tmp5 = sbt(st, "p5_tmp", [128, 512], F32)
        V, S, Tn, G = "vector", "scalar", "tensor", "gpsimd"
        if s == 0:
            for c in range(32):
                pb.wait("sync", "mvec")
                pb.dma("sync", stg5[:, 0:1024], w1[128 * c:128 * (c + 1), :],
                       inc="qS")
                pb.wait(V, "qS")
                pb.emit(V, lambda E, c=c: E.reduce_sum(
                    alpha1[:, c:c + 1], stg5[:, 0:1024], AX.X,
                    apply_absolute_value=True), inc="mvec")
            for c in range(8):
                for half in range(2):
                    pb.wait("sync", "mvec")
                    pb.dma("sync", stg5[:], w2[128 * c:128 * (c + 1),
                                               2048 * half:2048 * (half + 1)],
                           inc="qS")
                    pb.wait(V, "qS")
                    pb.emit(V, lambda E, c=c, half=half: E.reduce_sum(
                        alpha2[:, c:c + 1] if half == 0 else tmp5[:, 0:1],
                        stg5[:], AX.X, apply_absolute_value=True), inc="mvec")
                pb.emit(V, lambda E, c=c: E.tensor_add(
                    alpha2[:, c:c + 1], alpha2[:, c:c + 1], tmp5[:, 0:1]))
            pb.emit(V, lambda E: E.tensor_scalar_mul(alpha1[:], alpha1[:],
                                                     1.0 / 1024))
            pb.emit(V, lambda E: E.tensor_scalar_mul(alpha2[:], alpha2[:],
                                                     1.0 / 4096), inc="mvec")
            pb.barrier("p5_alpha")

        p5_marks = {}
        for g in range(8):
            for half in range(2):
                pb.wait("sync", "mscl")
                pb.dma("sync", stg5[:], w1t_g[g, half], inc="qS")
                pb.wait(S, "qS")
                for c in range(8):
                    pb.emit(S, lambda E, c=c, half=half: E.sign(
                        sw1[:, 512 * c + 256 * half:512 * c + 256 * (half + 1)],
                        stg5[:, 256 * c:256 * (c + 1)]), inc="mscl")
            pb.wait(S, "mten")
            pb.wait(Tn, "mscl")
            for hc in range(4):
                for ts in range(TS):
                    bk = (hc * TS + ts) % 2
                    if bk in p5_marks:
                        pb.wait(Tn, "mscl", p5_marks[bk])
                    for dc in range(NC):
                        pb.emit(Tn, lambda E, bk=bk, hc=hc, dc=dc, ts=ts: E.matmul(
                            pbank[bk],
                            sw1[:, 512 * dc + 128 * hc:512 * dc + 128 * (hc + 1)],
                            xnT[:, T * dc + 512 * ts:T * dc + 512 * (ts + 1)],
                            start=(dc == 0), stop=(dc == 7)), inc="mten")
                    pb.wait(S, "mten")
                    pb.emit(S, lambda E, hc=hc, ts=ts, bk=bk, g=g: E.activation(
                        hbuf[:, T * hc + 512 * ts:T * hc + 512 * (ts + 1)],
                        pbank[bk], AF.Gelu, bias=pcol(COL_B1 + 4 * g + hc),
                        scale=alpha1[:, 4 * g + hc:4 * g + hc + 1]),
                        inc="mscl")
                    p5_marks[bk] = pb.cnt["mscl"]
            for half in range(2):
                pb.wait("sync", "mscl")
                pb.dma("sync", stg5[:], w2t_g[g, half], inc="qS")
                pb.wait(S, "qS")
                for c in range(2):
                    pb.emit(S, lambda E, c=c, half=half: E.sign(
                        sw2[:, 1024 * (2 * half + c):1024 * (2 * half + c + 1)],
                        stg5[:, 1024 * c:1024 * (c + 1)]), inc="mscl")
            pb.wait(Tn, "mscl")
            for ts in range(TS):
                for oc in range(NC):
                    bk = 2 + (ts * NC + oc) % 2
                    if bk in p5_marks:
                        pb.wait(Tn, "mvec", p5_marks[bk])
                    for hc in range(4):
                        pb.emit(Tn, lambda E, bk=bk, oc=oc, hc=hc, ts=ts: E.matmul(
                            pbank[bk],
                            sw2[:, 1024 * hc + 128 * oc:1024 * hc + 128 * (oc + 1)],
                            hbuf[:, T * hc + 512 * ts:T * hc + 512 * (ts + 1)],
                            start=(hc == 0), stop=(hc == 3)), inc="mten")
                    pb.wait(V, "mten")
                    dd = dsl(oc, 512 * ts, 512 * (ts + 1))
                    if g == 0:
                        pb.emit(V, lambda E, dd=dd, oc=oc, bk=bk: E.tensor_scalar(
                            dd, pbank[bk], alpha2[:, oc:oc + 1],
                            pcol(COL_B2 + oc), ALU.mult, ALU.add), inc="mvec")
                    else:
                        pb.emit(V, lambda E, oc=oc, bk=bk: E.tensor_scalar(
                            tmp5[:], pbank[bk], alpha2[:, oc:oc + 1], None,
                            ALU.mult))
                        pb.emit(V, lambda E, dd=dd: E.tensor_add(
                            dd, dd, tmp5[:]), inc="mvec")
                    p5_marks[bk] = pb.cnt["mvec"]
        pb.barrier("p5_groups")
        for tt in range(TT):
            pb.wait("sync", "mvec")
            pb.wait("sync", "qG")
            pb.dma("sync", xtt[:], src_x[s, 128 * tt:128 * (tt + 1), :],
                   inc="qS")
            pb.wait(Tn, "mvec")
            for oc in range(NC):
                pb.emit(Tn, lambda E, tt=tt, oc=oc: E.transpose(
                    pgrp[1][:, 128 * oc:128 * (oc + 1)],
                    dsl(oc, 128 * tt, 128 * (tt + 1)),
                    ident[:]), inc=("mten" if oc == 7 else None))
            qs_t = pb.cnt["qS"]
            pb.wait(V, "qS", qs_t)
            pb.wait(V, "mten")
            pb.emit(V, lambda E: E.tensor_add(xtt[:], xtt[:], pgrp[1][:, 0:1024]),
                    inc="mvec")
            pb.wait(G, "mvec")
            pb.dma("gpsimd", dst_x[s, 128 * tt:128 * (tt + 1), :], xtt[:],
                   inc="qG")
        pb.barrier("p5_end")
        if "x2f" in dbg_out and s == 0:
            pb.dma("gpsimd", dbg_out["x2f"], dst_x[0], inc="qO")
        pb.barrier("p5_dbg")
        st.close()

    def emit_P6(s, src_x, dst_x):
        V, S, Tn, G = "vector", "scalar", "tensor", "gpsimd"
        dxT = res_seq["dxT"]
        rT = res_seq["hrmsT"]
        stP = ExitStack()
        kT = sbt(stP, "p6_kT", [128, NC * T], BF16)
        t1b = sbt(stP, "p6_t1", [64, T], BF16)
        masksb = sbt(stP, "p6_mk", [128, 4 * 512], mybir.dt.uint8)
        dw2b = sbt(stP, "p6_dw2", [64, 1024], BF16)
        pb.dma("gpsimd", masksb[:].rearrange("p (j n) -> p j n", j=4),
               lion_masks.rearrange("j p n -> p j n"), inc="qG")

        stA = ExitStack()
        wtb = sbt(stA, "p6_wtb", [128, NC * 1024], BF16)
        xcon2 = sbt(stA, "p6_xc", [128, 2 * NC * 512], BF16)
        xcon = xcon2[:, 0:NC * 512]
        af32 = sbt(stA, "p6_af", [128, 512], F32)
        xxa = sbt(stA, "p6_xxa", [64, T], BF16)
        xxb = sbt(stA, "p6_xxb", [64, T], BF16)
        xxc = sbt(stA, "p6_xxc", [32, T], BF16)
        maw1b = sbt(stA, "p6_m1", [128, 8 * 160], BF16)
        maw2b = sbt(stA, "p6_m2", [64, 5 * 1024], BF16)
        dw1b = sbt(stA, "p6_dw1", [128, 8 * 64], BF16)
        vtmp = sbt(stA, "p6_vt", [128, 512], BF16)
        gtmp = sbt(stA, "p6_gt", [128, 512], BF16)

        pb.dma("gpsimd", maw1b[:], maa_w1, inc="qG")
        pb.dma("gpsimd", maw2b[0:32, :].rearrange("m (f d) -> m f d", f=5),
               maa_w2.rearrange("f m d -> m f d"), inc="qG")
        pb.dma("gpsimd", maw2b[32:64, :].rearrange("m (f d) -> m f d", f=5),
               maa_w2.rearrange("f m d -> m f d"), inc="qG")
        pb.dma("gpsimd", dw1b[:], dec_w1, inc="qG")
        pb.dma("gpsimd", dw2b[:], dec_w2, inc="qG")
        pb.wait(Tn, "qG")
        pb.wait("vector", "qG")

        MF = {1: 1, 2: 2, 3: 3, 4: 4}
        MAAC = {1: 2, 2: 3, 3: 4, 4: 5}
        XXROW = {1: (None, 32), 2: (None, 0), 3: (None, 32), 4: (None, 0)}

        for ts in range(TS):
            for dc in range(NC):
                pb.emit(V, lambda E, dc=dc, ts=ts: E.tensor_scalar(
                    af32[:], dxT[:, T * dc + 512 * ts:T * dc + 512 * (ts + 1)],
                    pcol(COL_MAA + 0 + dc), None, ALU.mult))
                pb.emit(V, lambda E, dc=dc, ts=ts: E.tensor_add(
                    xcon[:, 512 * dc:512 * (dc + 1)], af32[:],
                    xnT[:, T * dc + 512 * ts:T * dc + 512 * (ts + 1)]),
                    inc="mvec")
            pb.wait(Tn, "mvec")
            pb.wait(Tn, "mscl")
            for dc in range(NC):
                pb.emit(Tn, lambda E, dc=dc: E.matmul(
                    pbank[0], maw1b[:, 160 * dc:160 * dc + 128],
                    xcon[:, 512 * dc:512 * (dc + 1)],
                    start=(dc == 0), stop=(dc == 7)))
            for dc in range(NC):
                pb.emit(Tn, lambda E, dc=dc: E.matmul(
                    pbank[1][0:32, :], maw1b[:, 160 * dc + 128:160 * (dc + 1)],
                    xcon[:, 512 * dc:512 * (dc + 1)],
                    start=(dc == 0), stop=(dc == 7)), inc="mten")
            pb.wait(S, "mten")
            pb.emit(S, lambda E, ts=ts: E.activation(
                xxa[:, 512 * ts:512 * (ts + 1)], pbank[0][0:64, :], AF.Tanh))
            pb.emit(S, lambda E, ts=ts: E.activation(
                xxb[:, 512 * ts:512 * (ts + 1)], pbank[0][64:128, :], AF.Tanh))
            pb.emit(S, lambda E, ts=ts: E.activation(
                xxc[:, 512 * ts:512 * (ts + 1)], pbank[1][0:32, :], AF.Tanh),
                inc="mscl")
        pb.barrier("p6_xxx")

        for ts in range(TS):
            for dc in range(NC):
                pb.wait(Tn, "mvec")
                pb.emit(Tn, lambda E, dc=dc, ts=ts: E.matmul(
                    pgrp[1][:, 1536:2048],
                    maw2b[0:32, 128 * dc:128 * (dc + 1)],
                    xxa[0:32, 512 * ts:512 * (ts + 1)],
                    start=True, stop=True), inc="mten")
                pb.wait(V, "mten")
                pb.emit(V, lambda E, dc=dc: E.tensor_scalar(
                    af32[:], pgrp[1][:, 1536:2048],
                    pcol(COL_MAA + 8 + dc), None, ALU.add))
                pb.emit(V, lambda E, dc=dc, ts=ts: E.tensor_mul(
                    af32[:], af32[:],
                    dxT[:, T * dc + 512 * ts:T * dc + 512 * (ts + 1)]))
                pb.emit(V, lambda E, dc=dc, ts=ts: E.tensor_add(
                    xcon[:, 512 * dc:512 * (dc + 1)], af32[:],
                    xnT[:, T * dc + 512 * ts:T * dc + 512 * (ts + 1)]),
                    inc="mvec")
            pb.wait(Tn, "mvec")
            for dc in range(NC):
                pb.emit(Tn, lambda E, dc=dc: E.matmul(
                    pbank[2][0:64, :], dw1b[:, 64 * dc:64 * (dc + 1)],
                    xcon[:, 512 * dc:512 * (dc + 1)],
                    start=(dc == 0), stop=(dc == 7)), inc="mten")
            pb.wait(S, "mten")
            pb.emit(S, lambda E, ts=ts: E.activation(
                t1b[:, 512 * ts:512 * (ts + 1)], pbank[2][0:64, :], AF.Tanh),
                inc="mscl")
            pb.wait(Tn, "mscl")
        pb.barrier("p6_t1")
        if "t1d" in dbg_out and s == 0:
            pb.dma("gpsimd", dbg_out["t1d"], t1b[:], inc="qO")
            pb.barrier("p6_t1d")

        def load_wt(mat):
            pb.wait("gpsimd", "mten")
            pb.dma("gpsimd", wtb[:], mat, inc="qG")
            pb.wait(Tn, "qG")

        ep_marks = {}

        def build_half(vec_idx, xx_src, xx_row, ts, xcon, half):
            for dc in range(4 * half, 4 * half + 4):
                slot = pgrp[1][:, 512 * (dc % 4):512 * (dc % 4 + 1)]
                if (dc % 4) in ep_marks:
                    pb.wait(Tn, "mvec", ep_marks[dc % 4])
                pb.emit(Tn, lambda E, dc=dc, ts=ts, vec_idx=vec_idx,
                        xx_src=xx_src, xx_row=xx_row, slot=slot: E.matmul(
                    slot,
                    maw2b[xx_row:xx_row + 32,
                          1024 * MF[vec_idx] + 128 * dc:
                          1024 * MF[vec_idx] + 128 * (dc + 1)],
                    xx_src[xx_row:xx_row + 32, 512 * ts:512 * (ts + 1)],
                    start=True, stop=True), inc="mten")
                pb.wait(V, "mten")
                pb.emit(V, lambda E, dc=dc, vec_idx=vec_idx, slot=slot:
                        E.tensor_scalar(
                    af32[:], slot, pcol(COL_MAA + 8 * MAAC[vec_idx] + dc),
                    None, ALU.add))
                pb.emit(V, lambda E, dc=dc, ts=ts: E.tensor_mul(
                    af32[:], af32[:],
                    dxT[:, T * dc + 512 * ts:T * dc + 512 * (ts + 1)]))
                pb.emit(V, lambda E, dc=dc, ts=ts, xcon=xcon: E.tensor_add(
                    xcon[:, 512 * dc:512 * (dc + 1)], af32[:],
                    xnT[:, T * dc + 512 * ts:T * dc + 512 * (ts + 1)]),
                    inc="mvec")
                ep_marks[dc % 4] = pb.cnt["mvec"]

        def proj_part(kind, dstT, xcon, ts, part):
            if kind in ("r", "k", "g"):
                for oc in range(4 * part, 4 * part + 4):
                    bk = oc % 2
                    if bk in p6_marks:
                        pb.wait(Tn, *p6_marks[bk])
                    for dc in range(NC):
                        pb.emit(Tn, lambda E, bk=bk, oc=oc, dc=dc, xcon=xcon:
                                E.matmul(
                            pbank[bk],
                            wtb[:, 1024 * dc + 128 * oc:1024 * dc + 128 * (oc + 1)],
                            xcon[:, 512 * dc:512 * (dc + 1)],
                            start=(dc == 0), stop=(dc == 7)), inc="mten")
                    if kind == "g":
                        pb.wait(S, "mten")
                        pb.wait(S, "qG")
                        pb.emit(S, lambda E, oc=oc, bk=bk: E.activation(
                            gtmp[:], pbank[bk], AF.Silu), inc="mscl")
                        p6_marks[bk] = ("mscl", pb.cnt["mscl"])
                        pb.wait(G, "mscl")
                        pb.dma("gpsimd", gdr[oc, :, 512 * ts:512 * (ts + 1)],
                               gtmp[:], inc="qG")
                    else:
                        eng = S if bk == 0 else V
                        pb.wait(eng, "mten")
                        dsl6 = dstT[:, T * oc + 512 * ts:T * oc + 512 * (ts + 1)]
                        if bk == 0:
                            pb.emit(S, lambda E, d=dsl6, bk=bk: E.copy(
                                d, pbank[bk]), inc="mscl")
                            p6_marks[bk] = ("mscl", pb.cnt["mscl"])
                        else:
                            pb.emit(V, lambda E, d=dsl6, bk=bk: E.tensor_copy(
                                d, pbank[bk]), inc="mvec")
                            p6_marks[bk] = ("mvec", pb.cnt["mvec"])
            else:
                for jv in range(4 * part, 4 * part + 4):
                    j, vs = jv // 2, jv % 2
                    tt = 4 * ts + j
                    bk = jv % 2
                    if bk in p6_marks:
                        pb.wait(Tn, *p6_marks[bk])
                    for dc in range(NC):
                        pb.emit(Tn, lambda E, bk=bk, j=j, vs=vs, dc=dc,
                                xcon=xcon: E.matmul(
                            pbank[bk],
                            xcon[:, 512 * dc + 128 * j:512 * dc + 128 * (j + 1)],
                            wtb[:, 1024 * dc + 512 * vs:1024 * dc + 512 * (vs + 1)],
                            start=(dc == 0), stop=(dc == 7)), inc="mten")
                    pb.wait(V, "mten")
                    pb.wait(V, "qG")
                    pb.emit(V, lambda E, bk=bk: E.tensor_copy(
                        vtmp[:], pbank[bk]), inc="mvec")
                    p6_marks[bk] = ("mvec", pb.cnt["mvec"])
                    pb.wait(G, "mvec")
                    pb.dma("gpsimd",
                           vdr[4 * vs:4 * vs + 4, 128 * tt:128 * (tt + 1),
                               :].rearrange("c p n -> p c n"),
                           vtmp[:].rearrange("p (c n) -> p c n", n=128),
                           inc="qG")

        p6_marks = {}
        for (mat, vec_idx, kind) in ((WrT, 3, "r"), (WkT, 1, "k"),
                                     (WvT, 2, "v"), (WgT, 4, "g")):
            p6_marks = {}
            load_wt(mat)
            pb.wait(Tn, "mscl")
            pb.wait(Tn, "mvec")
            xx_src = {1: xxa, 2: xxb, 3: xxb, 4: xxc}[vec_idx]
            xx_row = {1: 32, 2: 0, 3: 32, 4: 0}[vec_idx]
            dstT = rT if kind == "r" else kT
            for half in range(2):
                build_half(vec_idx, xx_src, xx_row, 0, xcon2[:, 0:4096], half)
            for ts in range(TS):
                xcon = xcon2[:, 4096 * (ts % 2):4096 * (ts % 2 + 1)]
                nxt = xcon2[:, 4096 * ((ts + 1) % 2):4096 * ((ts + 1) % 2 + 1)]
                pb.wait(Tn, "mvec", ep_marks.get(3, 0))
                if ts + 1 < TS:
                    build_half(vec_idx, xx_src, xx_row, ts + 1, nxt, 0)
                proj_part(kind, dstT, xcon, ts, 0)
                if ts + 1 < TS:
                    build_half(vec_idx, xx_src, xx_row, ts + 1, nxt, 1)
                proj_part(kind, dstT, xcon, ts, 1)
        pb.barrier("p6_proj")
        if s == 0:
            if "rTd" in dbg_out:
                pb.dma("gpsimd", dbg_out["rTd"], rT[:], inc="qO")
            if "kTd" in dbg_out:
                pb.dma("gpsimd", dbg_out["kTd"], kT[:], inc="qO")
            if "gdrd" in dbg_out:
                pb.dma("gpsimd", dbg_out["gdrd"], gdr, inc="qO")
            if "vdrd" in dbg_out:
                pb.dma("gpsimd", dbg_out["vdrd"], vdr, inc="qO")
            pb.barrier("p6_projd")
        stA.close()

        stB = ExitStack()
        A_ = sbt(stB, "p6_A", [128, T], F32)
        B_ = sbt(stB, "p6_B", [128, T], F32)
        C_ = sbt(stB, "p6_C", [128, T], F32)
        zerosb = sbt(stB, "p6_z0", [128, T], BF16)
        pb.emit(G, lambda E: E.memset(zerosb[:], 0.0))
        vhp = sbt(stB, "p6_vhp", [128, TT * 128], BF16)
        ghp = sbt(stB, "p6_ghp", [128, T], BF16)
        ysq1 = sbt(stB, "p6_ysq", [128, 512], F32)
        afmp = sbt(stB, "p6_af2", [128, 512], BF16)
        afmp1 = sbt(stB, "p6_af3", [128, 512], BF16)
        o1b2 = sbt(stB, "p6_o1b", [128, 64], BF16)
        O2f = sbt(stB, "p6_O2", [128, 2], F32)
        pb.emit(G, lambda E: E.memset(o1b2[:], 1.0))
        pb.emit(G, lambda E: E.memset(O2f[:], 0.0))
        pb.emit(G, lambda E: E.memset(O2f[0:64, 0:1], 1.0))
        pb.emit(G, lambda E: E.memset(O2f[64:128, 1:2], 1.0))
        rfb = dxT[:, 0:T]
        kfb = dxT[:, T:2 * T]
        rbb = dxT[:, 2 * T:3 * T]
        kbb = dxT[:, 3 * T:4 * T]
        gnb2 = dxT[:, 4 * T:4 * T + 4096]
        ysbf = dxT[:, 6 * T:8 * T].bitcast(F32)
        AT = [xnT[:, 0:16 * 512], xnT[:, 16 * 512:32 * 512]]

        for hp in range(NC):
            pb.wait("gpsimd", "mten")
            pb.dma("gpsimd", vhp[:].rearrange("p (tt n) -> p tt n", n=128),
                   vdr[hp].rearrange("(tt p) n -> p tt n", p=128), inc="qG")
            pb.dma("gpsimd", ghp[:], gdr[hp], inc="qG")
            pb.wait(Tn, "mvec")
            pb.wait(Tn, "mscl")
            for ts in range(TS):
                pb.emit(Tn, lambda E, hp=hp, ts=ts: E.matmul(
                    pbank[0], dw2b[:, 128 * hp:128 * (hp + 1)],
                    t1b[:, 512 * ts:512 * (ts + 1)], start=True, stop=True),
                    inc="mten")
                pb.wait(S, "mten")
                pb.emit(S, lambda E, hp=hp, ts=ts: E.activation(
                    A_[:, 512 * ts:512 * (ts + 1)], pbank[0], AF.Exp,
                    bias=pcol(COL_TDEC + hp)), inc="mscl")
            pb.wait(V, "mscl")
            pb.emit(V, lambda E: E.tensor_tensor_scan(
                B_[:], A_[:], zerosb[:], 0.0, ALU.add, ALU.add))
            pb.emit(V, lambda E: E.tensor_sub(C_[:], B_[:], A_[:]))
            pb.emit(V, lambda E: E.tensor_scalar(
                A_[:], B_[:], B_[:, 1024:1025], 60.0, ALU.subtract, ALU.min))
            pb.emit(V, lambda E: E.tensor_scalar_max(A_[:], A_[:], -60.0))
            pb.emit(V, lambda E: E.tensor_scalar(
                B_[:], C_[:], C_[:, 1024:1025], 60.0, ALU.subtract, ALU.min))
            pb.emit(V, lambda E: E.tensor_scalar_max(B_[:], B_[:], -60.0),
                    inc="mvec")
            pb.wait(S, "mvec")
            for (dst, srcC, sc, base) in ((rfb, A_, -1.0, rT), (kfb, A_, 1.0, kT),
                                          (rbb, B_, 1.0, rT), (kbb, B_, -1.0, kT)):
                pb.emit(S, lambda E, srcC=srcC, sc=sc: E.activation(
                    C_[:], srcC[:], AF.Exp, scale=sc), inc="mscl")
                pb.wait(V, "mscl")
                pb.emit(V, lambda E, dst=dst, base=base, hp=hp: E.tensor_mul(
                    dst, C_[:], base[:, T * hp:T * (hp + 1)]), inc="mvec")
                pb.wait(S, "mvec")
            if hp == 0 and s == 0 and "cfpd" in dbg_out:
                pb.wait("gpsimd", "mvec")
                pb.dma("gpsimd", dbg_out["cfpd"], A_[:], inc="qO")
                pb.dma("gpsimd", dbg_out["cbpd"], B_[:], inc="qO")
                pb.dma("gpsimd", dbg_out["rfbd"], rfb, inc="qO")
                pb.dma("gpsimd", dbg_out["kfbd"], kfb, inc="qO")
                pb.wait("gpsimd", "qO")
            pb.wait(Tn, "mvec")
            pb.wait(Tn, "qG")
            bkfree = {}
            sel_prev = {}
            yev_mark = [0]
            sq_guard = [0]
            apply_mark = [0]

            def emit_y(cy, atrdy):
                for mk in atrdy[cy]:
                    pb.wait(Tn, *mk)
                if cy == 0 and yev_mark[0]:
                    pb.wait(Tn, "mvec", yev_mark[0])
                for hl in (0, 1):
                    yb = (pgrp[1][0:64, 1024:1536] if hl == 0
                          else pgrp[1][64:128, 1536:2048])
                    pb.emit(Tn, lambda E, yb=yb, hl=hl, cy=cy: E.matmul(
                        yb, vhp[:, 128 * cy + 64 * hl:128 * cy + 64 * (hl + 1)],
                        AT[hl][:, 512 * cy:512 * (cy + 1)],
                        start=(cy == 0), stop=(cy == 15)), inc="mten")

            for qs in range(TS):
                pb.wait(S, "mten")
                pb.wait(V, "mten")
                atrdy = {}
                for c in range(16):
                    diag = 4 * qs <= c <= 4 * qs + 3
                    bkA, bkB = c % 2, 2 + c % 2
                    for bk in (bkA, bkB):
                        if bk in bkfree:
                            pb.wait(Tn, *bkfree[bk])
                    f_c = c < 4 * qs or diag
                    b_c = c > 4 * qs + 3 or diag
                    if f_c:
                        for hl, bk in ((0, bkA), (1, bkB)):
                            r0 = 64 * hl
                            pb.emit(Tn, lambda E, bk=bk, c=c, qs=qs, r0=r0:
                                    E.matmul(
                                pbank[bk], kfb[r0:r0 + 64, 128 * c:128 * (c + 1)],
                                rfb[r0:r0 + 64, 512 * qs:512 * (qs + 1)],
                                start=True, stop=True), inc="mten")
                    if b_c:
                        for hl, bk in ((0, bkA), (1, bkB)):
                            r0 = 64 * hl
                            if diag:
                                if hl in sel_prev:
                                    pb.wait(Tn, "mvec", sel_prev[hl])
                                ob = pgrp[1][:, 512 * hl:512 * (hl + 1)]
                            else:
                                ob = pbank[bk]
                            pb.emit(Tn, lambda E, ob=ob, c=c, qs=qs, r0=r0:
                                    E.matmul(
                                ob, kbb[r0:r0 + 64, 128 * c:128 * (c + 1)],
                                rbb[r0:r0 + 64, 512 * qs:512 * (qs + 1)],
                                start=True, stop=True), inc="mten")
                    if diag:
                        j = c - 4 * qs
                        if sel_prev:
                            pb.wait(S, "mvec", max(sel_prev.values()))
                        pb.wait(S, "mten")
                        pb.emit(S, lambda E, bkA=bkA: E.copy(afmp[:], pbank[bkA]),
                                inc="mscl")
                        bkfree[bkA] = ("mscl", pb.cnt["mscl"])
                        pb.emit(S, lambda E, bkB=bkB: E.copy(afmp1[:], pbank[bkB]),
                                inc="mscl")
                        bkfree[bkB] = ("mscl", pb.cnt["mscl"])
                        pb.wait(V, "mscl")
                        pb.wait(V, "mten")
                        pb.emit(V, lambda E, c=c, j=j: E.select(
                            AT[0][:, 512 * c:512 * (c + 1)],
                            masksb[:, 512 * j:512 * (j + 1)],
                            afmp[:], pgrp[1][:, 0:512]), inc="mvec")
                        sel_prev[0] = pb.cnt["mvec"]
                        pb.emit(V, lambda E, c=c, j=j: E.select(
                            AT[1][:, 512 * c:512 * (c + 1)],
                            masksb[:, 512 * j:512 * (j + 1)],
                            afmp1[:], pgrp[1][:, 512:1024]), inc="mvec")
                        sel_prev[1] = pb.cnt["mvec"]
                        atrdy[c] = [("mvec", sel_prev[1])]
                    else:
                        pb.wait(S, "mten")
                        pb.emit(S, lambda E, c=c, bkA=bkA: E.copy(
                            AT[0][:, 512 * c:512 * (c + 1)], pbank[bkA]),
                            inc="mscl")
                        bkfree[bkA] = ("mscl", pb.cnt["mscl"])
                        pb.wait(V, "mten")
                        pb.emit(V, lambda E, c=c, bkB=bkB: E.tensor_copy(
                            AT[1][:, 512 * c:512 * (c + 1)], pbank[bkB]),
                            inc="mvec")
                        bkfree[bkB] = ("mvec", pb.cnt["mvec"])
                        atrdy[c] = [bkfree[bkA], bkfree[bkB]]
                    if c >= 2:
                        emit_y(c - 2, atrdy)
                if (hp == 0 and qs == 0 and s == 0 and "ATd" in dbg_out):
                    pb.wait("gpsimd", "mscl")
                    pb.wait("gpsimd", "mvec")
                    pb.dma("gpsimd", dbg_out["ATd"], AT[0], inc="qO")
                    pb.wait("gpsimd", "qO")
                emit_y(14, atrdy)
                emit_y(15, atrdy)
                pb.wait(V, "mten")
                pb.emit(V, lambda E, qs=qs: E.tensor_copy(
                    ysbf[0:64, 512 * qs:512 * (qs + 1)],
                    pgrp[1][0:64, 1024:1536]))
                pb.emit(V, lambda E, qs=qs: E.tensor_copy(
                    ysbf[64:128, 512 * qs:512 * (qs + 1)],
                    pgrp[1][64:128, 1536:2048]), inc="mvec")
                if (hp == 0 and qs == 0 and s == 0 and "yd" in dbg_out):
                    pb.wait("gpsimd", "mvec")
                    pb.dma("gpsimd", dbg_out["yd"], ysbf[0:64, 0:512], inc="qO")
                    pb.wait("gpsimd", "qO")
                if sq_guard[0]:
                    pb.wait(S, "mten", sq_guard[0])
                pb.wait(S, "mvec")
                pb.emit(S, lambda E, qs=qs: E.activation(
                    ysq1[:], ysbf[:, 512 * qs:512 * (qs + 1)], AF.Square),
                    inc="mscl")
                pb.wait(Tn, "mvec")
                pb.wait(Tn, "mscl")
                pb.emit(Tn, lambda E, qs=qs: E.matmul(
                    pgrp[1][64:66, 1024:1536], O2f[:],
                    ysbf[:, 512 * qs:512 * (qs + 1)],
                    start=True, stop=True), inc="mten")
                pb.emit(Tn, lambda E: E.matmul(
                    pgrp[1][64:66, 1536:2048], O2f[:], ysq1[:],
                    start=True, stop=True), inc="mten")
                sq_guard[0] = pb.cnt["mten"]
                pb.wait(V, "mten")
                pb.emit(V, lambda E, qs=qs: E.tensor_copy(
                    A_[64:66, 512 * qs:512 * (qs + 1)],
                    pgrp[1][64:66, 1024:1536]))
                pb.emit(V, lambda E, qs=qs: E.tensor_copy(
                    B_[64:66, 512 * qs:512 * (qs + 1)],
                    pgrp[1][64:66, 1536:2048]), inc="mvec")
                yev_mark[0] = pb.cnt["mvec"]
            pb.emit(V, lambda E: E.tensor_scalar(
                C_[64:66, 0:T], A_[64:66, 0:T], 1.0 / 64, None, ALU.mult),
                inc="mvec")
            pb.wait(S, "mvec")
            pb.emit(S, lambda E: E.activation(
                A_[64:66, 0:T], C_[64:66, 0:T], AF.Square), inc="mscl")
            pb.emit(V, lambda E: E.tensor_scalar(
                B_[64:66, 0:T], B_[64:66, 0:T], 1.0 / 64, None, ALU.mult))
            pb.wait(V, "mscl")
            pb.emit(V, lambda E: E.tensor_sub(
                B_[64:66, 0:T], B_[64:66, 0:T], A_[64:66, 0:T]), inc="mvec")
            pb.wait(S, "mvec")
            pb.emit(S, lambda E: E.activation(
                A_[64:66, 0:T], B_[64:66, 0:T], AF.Sqrt, bias=6.4e-4),
                inc="mscl")
            pb.wait(V, "mscl")
            pb.emit(V, lambda E: E.reciprocal(A_[64:66, 0:T], A_[64:66, 0:T]))
            pb.emit(V, lambda E: E.tensor_copy(gnb2[64:66, 0:2048],
                                               C_[64:66, 0:T]))
            pb.emit(V, lambda E: E.tensor_copy(gnb2[64:66, 2048:4096],
                                               A_[64:66, 0:T]), inc="mvec")
            pb.wait(G, "mvec")
            pb.dma("gpsimd", gnb2[0:1, 0:4096], gnb2[64:65, 0:4096], inc="qG")
            pb.dma("gpsimd", gnb2[32:33, 0:4096], gnb2[65:66, 0:4096], inc="qG")
            pb.wait(Tn, "qG")
            for qs in range(TS):
                if apply_mark[0]:
                    pb.wait(Tn, "mvec", apply_mark[0])
                for hl, rb in ((0, 0), (1, 32)):
                    ro = slice(64 * hl, 64 * (hl + 1))
                    pb.emit(Tn, lambda E, ro=ro, rb=rb, qs=qs: E.matmul(
                        pgrp[1][ro, 1024:1536], o1b2[rb:rb + 1, :],
                        gnb2[rb:rb + 1, 512 * qs:512 * (qs + 1)],
                        start=True, stop=True))
                    pb.emit(Tn, lambda E, ro=ro, rb=rb, qs=qs: E.matmul(
                        pgrp[1][ro, 1536:2048], o1b2[rb:rb + 1, :],
                        gnb2[rb:rb + 1, 2048 + 512 * qs:2048 + 512 * (qs + 1)],
                        start=True, stop=True), inc="mten")
                pb.wait(V, "mten")
                for hl in (0, 1):
                    ro = slice(64 * hl, 64 * (hl + 1))
                    ysl = ysbf[64 * hl:64 * (hl + 1), 512 * qs:512 * (qs + 1)]
                    pb.emit(V, lambda E, ysl=ysl, ro=ro: E.tensor_sub(
                        ysl, ysl, pgrp[1][ro, 1024:1536]))
                    pb.emit(V, lambda E, ysl=ysl, ro=ro: E.tensor_mul(
                        ysl, ysl, pgrp[1][ro, 1536:2048]))
                    pb.emit(V, lambda E, ysl=ysl, hl=hl, hp=hp: E.tensor_scalar(
                        ysl, ysl,
                        packA_s[64 * hl:64 * (hl + 1),
                                COL_LNW + hp:COL_LNW + hp + 1],
                        packA_s[64 * hl:64 * (hl + 1),
                                COL_LNB + hp:COL_LNB + hp + 1],
                        ALU.mult, ALU.add))
                    pb.emit(V, lambda E, ysl=ysl, hl=hl, qs=qs: E.tensor_mul(
                        ghp[64 * hl:64 * (hl + 1), 512 * qs:512 * (qs + 1)],
                        ghp[64 * hl:64 * (hl + 1), 512 * qs:512 * (qs + 1)],
                        ysl), inc="mvec")
                apply_mark[0] = pb.cnt["mvec"]
            pb.wait(G, "mvec")
            pb.dma("gpsimd", gdr[hp], ghp[:], inc="qG")
        pb.barrier("p6_lion")
        if "zd" in dbg_out and s == 0:
            pb.dma("gpsimd", dbg_out["zd"], gdr, inc="qO")
            pb.barrier("p6_zd")
        stB.close()

        stC = ExitStack()
        WoTb = kT[:, 0:8192]
        dts = kT[:, 8192:16384].bitcast(F32)
        zts = rT[:, 0:4096]
        xtt = rT[:, 4096:6144].bitcast(F32)
        pb.dma("gpsimd", WoTb, WoT, inc="qG")
        pb.wait(Tn, "qG")
        op_marks = {}
        for ts in range(TS):
            pb.wait("sync", "mten")
            pb.dma("sync", zts[:].rearrange("p (c n) -> p c n", n=512),
                   gdr.rearrange("c p n -> p c n")[:, :, 512 * ts:512 * (ts + 1)],
                   inc="qS")
            pb.wait(Tn, "qS")
            for oc in range(NC):
                bk = oc % 2
                if bk in op_marks:
                    pb.wait(Tn, "mvec", op_marks[bk])
                for dc in range(NC):
                    pb.emit(Tn, lambda E, bk=bk, oc=oc, dc=dc: E.matmul(
                        pbank[bk],
                        WoTb[:, 1024 * dc + 128 * oc:1024 * dc + 128 * (oc + 1)],
                        zts[:, 512 * dc:512 * (dc + 1)],
                        start=(dc == 0), stop=(dc == 7)), inc="mten")
                pb.wait(V, "mten")
                pb.emit(V, lambda E, oc=oc, bk=bk: E.tensor_copy(
                    dts[:, 512 * oc:512 * (oc + 1)], pbank[bk]), inc="mvec")
                op_marks[bk] = pb.cnt["mvec"]
            pb.wait(Tn, "mvec")
            for j in range(4):
                tt = 4 * ts + j
                pb.wait("sync", "mvec")
                pb.wait("sync", "qO")
                pb.dma("sync", xtt[:], src_x[s, 128 * tt:128 * (tt + 1), :],
                       inc="qS")
                pb.wait(Tn, "mvec")
                for oc in range(NC):
                    pb.emit(Tn, lambda E, j=j, oc=oc: E.transpose(
                        pgrp[1][:, 128 * oc:128 * (oc + 1)],
                        dts[:, 512 * oc + 128 * j:512 * oc + 128 * (j + 1)],
                        ident[:]), inc=("mten" if oc == 7 else None))
                qs_t = pb.cnt["qS"]
                pb.wait(V, "qS", qs_t)
                pb.wait(V, "mten")
                pb.emit(V, lambda E: E.tensor_add(xtt[:], xtt[:],
                                                  pgrp[1][:, 0:1024]),
                        inc="mvec")
                pb.wait(G, "mvec")
                pb.dma("gpsimd", dst_x[s, 128 * tt:128 * (tt + 1), :], xtt[:],
                       inc="qO")
        pb.barrier("p6_end")
        stC.close()
        stP.close()
        res_seq.pop("dxT_st").close()
        res_seq.pop("dxT")

    for s in range(SEQS):
        if "P1" in phases:
            emit_P1(s, x_in)
        if "P2" in phases:
            emit_P2(s, x_in, xdr)
        if "P3" in phases:
            emit_P3(s, xdr, xdr)
        if "P4" in phases:
            emit_P4(s, xdr)
        if "P5" in phases:
            emit_P5(s, xdr, xdr)
        if "P6" in phases:
            emit_P6(s, xdr, out)

    if "P6" not in phases:
        with nc.sbuf_tensor("cp_buf", [128, 1024], F32) as cpb:
            for s in range(SEQS):
                for tt in range(TT):
                    pb.dma("sync", cpb[:], x_in[s, 128 * tt:128 * (tt + 1), :],
                           inc="qS")
                    pb.wait("sync", "qS")
                    pb.dma("sync", out[s, 128 * tt:128 * (tt + 1), :], cpb[:],
                           inc="qO")
                    pb.wait("sync", "qO")
            pb.barrier("out_done")
    pb.wait("gpsimd", "qO", pb.cnt["qO"])
    pb.flush()
    ctx.close()
    return nc




def _pack_cols(inp):
    pk = np.zeros((128, 120), np.float32)

    def put(base, vec, n):
        v = np.asarray(vec, np.float32).reshape(-1)
        pk[:, base:base + n] = v.reshape(n, 128).T
    put(0, inp["L0_attn_norm_w"], 8)
    put(8, inp["L1_attn_time_maa_x"], 8)
    put(16, inp["L1_attn_time_maa_w"], 8)
    put(24, inp["L1_attn_time_maa_k"], 8)
    put(32, inp["L1_attn_time_maa_v"], 8)
    put(40, inp["L1_attn_time_maa_r"], 8)
    put(48, inp["L1_attn_time_maa_g"], 8)
    put(56, inp["L1_attn_time_decay"], 8)
    put(64, inp["L1_attn_ln_w"], 8)
    put(72, inp["L1_attn_ln_b"], 8)
    put(80, inp["L1_ffn_b1"], 32)
    put(112, inp["L1_ffn_b2"], 8)
    return pk


def _consts():
    ident = np.eye(128, dtype=np.float32)
    masks = np.zeros((4, 128, 512), np.uint8)
    kp = np.arange(128)[:, None]
    qf = np.arange(512)[None, :]
    for j in range(4):
        masks[j] = (qf >= kp + 128 * j).astype(np.uint8)
    return ident, masks


def _pm(a):
    a = np.asarray(a, np.float32)
    C = a.shape[0] // 128
    return np.ascontiguousarray(
        a.reshape(C, 128, a.shape[1]).transpose(1, 0, 2).reshape(128, -1))


def _core_inputs(inp, core):
    ident, masks = _consts()
    f32c = np.ascontiguousarray
    wqkv = np.asarray(inp["L0_attn_w_qkv"], np.float32)
    wqkv_oc = np.ascontiguousarray(
        wqkv.reshape(8, 128, 24, 128).transpose(2, 1, 0, 3).reshape(
            24, 128, 1024))
    w1T = np.asarray(inp["L1_ffn_w1"], np.float32).T
    w1t_g = np.ascontiguousarray(
        w1T.reshape(8, 128, 8, 2, 256).transpose(2, 3, 1, 0, 4).reshape(
            8, 2, 128, 2048))
    w2T = np.asarray(inp["L1_ffn_w2"], np.float32).T
    w2t_g = np.ascontiguousarray(
        w2T.reshape(8, 2, 2, 128, 1024).transpose(0, 1, 3, 2, 4).reshape(
            8, 2, 128, 2048))
    gwT = np.asarray(inp["L0_ffn_gate_w"], np.float32).T
    gate_wT = np.ascontiguousarray(
        gwT.reshape(8, 128, 16).transpose(1, 0, 2).reshape(128, 128))
    m = {
        "x": f32c(inp["L0_attn_x"][2 * core:2 * core + 2]),
        "wqkv_oc": wqkv_oc,
        "wproj": _pm(inp["L0_attn_w_proj"]),
        "gate_w": inp["L0_ffn_gate_w"],
        "gate_wT": gate_wT,
        "wg_e": np.stack([_pm(w) for w in inp["L0_ffn_w_gate"]]),
        "wu_e": np.stack([_pm(w) for w in inp["L0_ffn_w_up"]]),
        "wd_e": np.stack([_pm(w) for w in inp["L0_ffn_w_down"]]),
        "sw_g": _pm(inp["L0_ffn_sw_gate"]),
        "sw_u": _pm(inp["L0_ffn_sw_up"]),
        "sw_d": _pm(inp["L0_ffn_sw_down"]),
        "maa_w1": _pm(inp["L1_attn_time_maa_w1"]),
        "maa_w2": inp["L1_attn_time_maa_w2"],
        "dec_w1": _pm(inp["L1_attn_time_decay_w1"]),
        "dec_w2": inp["L1_attn_time_decay_w2"],
        "WrT": _pm(np.asarray(inp["L1_attn_W_r"], np.float32).T),
        "WkT": _pm(np.asarray(inp["L1_attn_W_k"], np.float32).T),
        "WvT": _pm(np.asarray(inp["L1_attn_W_v"], np.float32).T),
        "WgT": _pm(np.asarray(inp["L1_attn_W_g"], np.float32).T),
        "WoT": _pm(np.asarray(inp["L1_attn_W_o"], np.float32).T),
        "w1": inp["L1_ffn_w1"],
        "w1t_g": w1t_g,
        "w2": inp["L1_ffn_w2"],
        "w2t_g": w2t_g,
        "packA": _pack_cols(inp),
        "ehot": np.kron(np.eye(16), np.ones((1, 128))).astype(ml_dtypes.bfloat16),
        "ident_f32": ident,
        "lion_masks": masks,
    }
    return {k: np.ascontiguousarray(v) if k in ("ehot", "lion_masks") else
            np.ascontiguousarray(np.asarray(v, np.float32)) for k, v in m.items()}


def kernel(**inputs):
    nc = build_program()
    in_maps = [_core_inputs(inputs, c) for c in range(NCORES)]
    res = run_bass_kernel_spmd(nc, in_maps, core_ids=list(range(NCORES)))
    outs = [r["out"] for r in res.results]
    return np.concatenate(outs, axis=0).astype(np.float32)


# version 26
# speedup vs baseline: 1.2016x; 1.0116x over previous
import numpy as np
import ml_dtypes
from contextlib import ExitStack

import concourse.bass as bass
import concourse.mybir as mybir
from concourse.bass_utils import run_bass_kernel_spmd

F32 = mybir.dt.float32
BF16 = mybir.dt.bfloat16
AF = mybir.ActivationFunctionType
ALU = mybir.AluOpType
AX = mybir.AxisListType

NCORES = 8
SEQS = 2
T = 2048
D = 1024
NC = 8
TT = 16
TS = 4
NH = 16
HD = 64



class PB:
    ENGS = ("tensor", "vector", "scalar", "gpsimd", "sync")

    def __init__(self, nc, block, ctx):
        self.nc = nc
        self.block = block
        self.ctx = ctx
        self.buf = {e: [] for e in self.ENGS}
        self.sems = {}
        self.cnt = {}
        self.dirty = {e: False for e in self.ENGS}

    def sem(self, name):
        if name not in self.sems:
            self.sems[name] = self.ctx.enter_context(self.nc.semaphore(name))
            self.cnt[name] = 0
        return self.sems[name]

    CHAIN = {"vector": "mvec", "scalar": "mscl", "gpsimd": "mgps"}

    def emit(self, eng, fn, inc=None, n=1):
        ch = self.CHAIN.get(eng)
        if ch is not None and inc is None:
            inc, n = ch, 1
        if ch is not None:
            h = self.sem(ch)
            v = self.cnt[ch]
            if v > 0:
                self.buf[eng].append(lambda E, h=h, v=v: E.wait_ge(h, v))
        if inc is not None:
            h2 = self.sem(inc)
            self.cnt[inc] += n
            self.buf[eng].append(lambda E, fn=fn, h=h2, n=n: fn(E).then_inc(h2, n))
        else:
            self.buf[eng].append(fn)
        if eng != "sync":
            self.dirty[eng] = True

    def wait(self, eng, sem, val=None):
        if val is None:
            val = self.cnt.get(sem, 0)
        if val <= 0:
            return
        h = self.sem(sem)
        self.buf[eng].append(lambda E, h=h, v=val: E.wait_ge(h, v))

    def dma(self, queue, out, in_, inc, **kw):
        self.emit(queue, lambda E, o=out, i=in_, kw=kw: E.dma_start(out=o, in_=i, **kw),
                  inc=inc, n=16)

    def flush(self):
        for eng in self.ENGS:
            ops = self.buf[eng]
            if not ops:
                continue
            self.buf[eng] = []

            def section(E, ops=ops):
                for op in ops:
                    op(E)
            getattr(self.block, eng)(section)

    def barrier(self, name=""):
        active = [e for e in ("tensor", "vector", "scalar", "gpsimd")
                  if self.dirty[e]]
        for e in active:
            self.emit(e, lambda E: E.nop(), inc="m" + e)
        targets = {e: self.cnt["m" + e] for e in active}
        dma_sems = [s for s in self.cnt if s.startswith("q")]
        for e in self.ENGS:
            for o in active:
                if o != e:
                    self.wait(e, "m" + o, targets[o])
            for s in dma_sems:
                if self.cnt[s] > 0:
                    self.wait(e, s, self.cnt[s])
        for e in active:
            self.dirty[e] = False
        self.flush()




def build_program(dbg=(), phases=("P1", "P2", "P3", "P4", "P5", "P6")):
    nc = bass.Bass("TRN2", target_bir_lowering=False, debug=False)

    def din(name, shape, dt=F32):
        return nc.dram_tensor(name, shape, dt, kind="ExternalInput").ap()

    x_in = din("x", [SEQS, T, D])
    wqkv_oc = din("wqkv_oc", [24, 128, 1024])
    wproj = din("wproj", [128, NC * 1024])
    gate_w = din("gate_w", [16, D])
    gate_wT = din("gate_wT", [128, 128])
    wg_e = din("wg_e", [16, 128, NC * 512])
    wu_e = din("wu_e", [16, 128, NC * 512])
    wd_e = din("wd_e", [16, 128, 4 * 1024])
    sw_g = din("sw_g", [128, NC * 512])
    sw_u = din("sw_u", [128, NC * 512])
    sw_d = din("sw_d", [128, 4 * 1024])
    maa_w1 = din("maa_w1", [128, NC * 160])
    maa_w2 = din("maa_w2", [5, 32, D])
    dec_w1 = din("dec_w1", [128, NC * 64])
    dec_w2 = din("dec_w2", [64, D])
    WrT = din("WrT", [128, NC * 1024])
    WkT = din("WkT", [128, NC * 1024])
    WvT = din("WvT", [128, NC * 1024])
    WgT = din("WgT", [128, NC * 1024])
    WoT = din("WoT", [128, NC * 1024])
    w1 = din("w1", [4096, D])
    w1t_g = din("w1t_g", [8, 2, 128, NC * 256])
    w2 = din("w2", [D, 4096])
    w2t_g = din("w2t_g", [8, 2, 128, 2 * 1024])
    packA = din("packA", [128, 120])
    ehot = din("ehot", [16, 2048], BF16)
    ident_f32 = din("ident_f32", [128, 128])
    lion_masks = din("lion_masks", [4, 128, 512], mybir.dt.uint8)

    out = nc.dram_tensor("out", [SEQS, T, D], F32, kind="ExternalOutput").ap()
    xdr = nc.dram_tensor("xdr", [SEQS, T, D], F32, kind="Internal").ap()
    vdr = nc.dram_tensor("vdr", [NC, T, 128], BF16, kind="Internal").ap()
    gdr = nc.dram_tensor("gdr", [NC, 128, T], BF16, kind="Internal").ap()

    dbg_out = {}
    for name, shape in dbg:
        dbg_out[name] = nc.dram_tensor("dbg_" + name, shape, F32,
                                       kind="ExternalOutput").ap()

    ctx = ExitStack()

    def sb(name, shape, dt):
        return ctx.enter_context(nc.sbuf_tensor(name, shape, dt))

    packA_s = sb("packA_s", [128, 120], F32)
    cvals = sb("cvals", [128, 8], F32)
    ident = sb("ident", [128, 128], F32)
    gwT_s = sb("gwT_s", [128, 128], F32)
    gwT_hi = sb("gwT_hi", [128, 128], BF16)
    gwT_lo = sb("gwT_lo", [128, 128], BF16)
    invgw = sb("invgw", [16, 1], F32)
    alpha1 = sb("alpha1", [128, 32], F32)
    alpha2 = sb("alpha2", [128, 8], F32)
    ones1 = sb("ones1", [1, 128], F32)
    onescol = sb("onescol", [128, 1], F32)

    COL_NORMW = 0
    COL_MAA = 8
    COL_TDEC = 56
    COL_LNW = 64
    COL_LNB = 72
    COL_B1 = 80
    COL_B2 = 112

    def pcol(c):
        return packA_s[:, c:c + 1]

    pgrp = [ctx.enter_context(nc.psum_tensor(f"pg{i}", [128, 2048], F32))
            for i in range(2)]
    pbank = [pgrp[i // 4][:, 512 * (i % 4):512 * (i % 4 + 1)] for i in range(8)]

    block = ctx.enter_context(nc.Block())
    pb = PB(nc, block, ctx)

    uid = [0]

    def usem(tag):
        uid[0] += 1
        return f"u{uid[0]}_{tag}"

    def sbt(stack, name, shape, dt):
        uid[0] += 1
        return stack.enter_context(nc.sbuf_tensor(f"{name}_{uid[0]}", shape, dt))

    pb.dma("sync", packA_s[:], packA, inc="qS")
    pb.dma("sync", ident[:], ident_f32, inc="qS")
    pb.dma("sync", gwT_s[:], gate_wT, inc="qS")

    for i, v in enumerate([1e-5, 1e-6, 6.4e-4, 1024.0, 0.125, -1.0, -0.5]):
        pb.emit("gpsimd", lambda E, i=i, v=v: E.memset(cvals[:, i:i + 1], v))
        nc.const_aps.register(F32, v, cvals[:, i:i + 1])

    pb.emit("gpsimd", lambda E: E.memset(ones1[:], 1.0))
    pb.emit("gpsimd", lambda E: E.memset(onescol[:], 1.0))

    pb.barrier("preamble")

    xnT = sb("xnT", [128, NC * T], BF16)
    res_seq = {}

    def emit_P1(s, src_x):
        st = ExitStack()
        hrmsT = res_seq["hrmsT"] = ctx.enter_context(
            nc.sbuf_tensor(f"hrmsT{s}", [128, NC * T], BF16)) \
            if "hrmsT" not in res_seq else res_seq["hrmsT"]
        combT = res_seq["combT"] = ctx.enter_context(
            nc.sbuf_tensor(f"combT{s}", [16, T], BF16)) \
            if "combT" not in res_seq else res_seq["combT"]

        xbuf = sbt(st, "p1_x", [128, 2 * 1024], F32)
        xsq = sbt(st, "p1_sq", [128, 1024], F32)
        stat = sbt(st, "p1_st", [128, 8], F32)
        xntt = sbt(st, "p1_xn", [128, 1024], F32)
        xnrm = sbt(st, "p1_xr", [128, 1024], F32)
        xnTf = sbt(st, "p1_xnTf", [128, 1024], F32)
        xh = sbt(st, "p1_xh", [128, 1024], BF16)
        xl = sbt(st, "p1_xl", [128, 1024], BF16)
        logitsT = sbt(st, "p1_lg", [16, T], F32)
        rowt = sbt(st, "p1_rt", [1, 128], F32)
        dbg_praw = sbt(st, "p1_dpr", [16, 128], F32)
        dbg_rbc = sbt(st, "p1_drb", [16, 128], F32)
        rowt_hi = sbt(st, "p1_rth", [1, 128], BF16)
        rowt_lo = sbt(st, "p1_rtl", [1, 128], BF16)
        ones1b = sbt(st, "p1_o1b", [1, 16], BF16)
        pb.emit("gpsimd", lambda E: E.memset(ones1b[:], 1.0))
        rbc = sbt(st, "p1_rbc", [16, 128], F32)

        if s == 0:
            presq = xbuf[0:16, 0:1024]
            pb.dma("sync", presq, gate_w, inc="qS")
            pb.wait("vector", "qS")
            pb.wait("scalar", "qS")
            pb.emit("scalar", lambda E: E.activation(presq, presq, AF.Square),
                    inc="mscl")
            pb.wait("vector", "mscl")
            pb.emit("vector", lambda E: E.reduce_sum(invgw[:], presq, AX.X))
            pb.wait("scalar", "mvec")
            pb.emit("scalar", lambda E: E.activation(invgw[:], invgw[:], AF.Sqrt))
            pb.wait("vector", "mscl")
            pb.emit("vector", lambda E: E.tensor_scalar_max(invgw[:], invgw[:],
                                                            1e-12))
            pb.emit("vector", lambda E: E.reciprocal(invgw[:], invgw[:]))
            pb.emit("vector", lambda E: E.tensor_copy(gwT_hi[:], gwT_s[:]))
            pb.emit("vector", lambda E: E.tensor_sub(gwT_lo[:], gwT_s[:],
                                                     gwT_hi[:]), inc="mvec")
            pb.wait("sync", "mvec")

        for tt in range(TT):
            V, S, Tn, G = "vector", "scalar", "tensor", "gpsimd"
            xb = xbuf[:, 1024 * (tt % 2):1024 * (tt % 2 + 1)]
            pb.wait("sync", "mvec")
            pb.wait("sync", "mscl")
            pb.dma("sync", xb, src_x[s, 128 * tt:128 * (tt + 1), :], inc="qS")
            qs_tgt = pb.cnt["qS"]
            mean, sumsq = stat[:, 0:1], stat[:, 1:2]
            var, rstd = stat[:, 2:3], stat[:, 3:4]
            mxn2, rmsinv, invl2 = stat[:, 4:5], stat[:, 5:6], stat[:, 6:7]
            m2 = stat[:, 7:8]
            pb.wait(V, "qS", qs_tgt)
            pb.wait(S, "qS", qs_tgt)
            pb.emit(V, lambda E, xb=xb: E.reduce_sum(mean, xb, AX.X))
            pb.emit(V, lambda E: E.tensor_scalar_mul(mean, mean, 1.0 / D),
                    inc="mvec")
            pb.emit(S, lambda E, xb=xb: E.activation(xsq[:], xb, AF.Square),
                    inc="mscl")
            pb.wait(V, "mscl")
            pb.emit(V, lambda E: E.reduce_sum(sumsq, xsq[:], AX.X))
            pb.wait(S, "mvec")
            pb.emit(S, lambda E: E.activation(m2, mean, AF.Square), inc="mscl")
            pb.wait(V, "mscl")
            pb.emit(V, lambda E: E.tensor_scalar(var, sumsq, 1.0 / D, m2,
                                                 ALU.mult, ALU.subtract),
                    inc="mvec")
            pb.wait(S, "mvec")
            pb.emit(S, lambda E: E.activation(rstd, var, AF.Sqrt, bias=1e-5),
                    inc="mscl")
            pb.wait(V, "mscl")
            pb.emit(V, lambda E: E.reciprocal(rstd, rstd))
            pb.emit(V, lambda E: E.tensor_mul(mxn2, var, rstd))
            pb.emit(V, lambda E: E.tensor_mul(mxn2, mxn2, rstd), inc="mvec")
            pb.wait(S, "mvec")
            pb.emit(S, lambda E: E.activation(rmsinv, mxn2, AF.Sqrt, bias=1e-6))
            pb.emit(S, lambda E: E.activation(invl2, mxn2, AF.Sqrt,
                                              scale=float(D)), inc="mscl")
            pb.wait(V, "mscl")
            pb.emit(V, lambda E: E.reciprocal(rmsinv, rmsinv))
            pb.emit(V, lambda E: E.tensor_scalar_max(invl2, invl2, 1e-12))
            pb.emit(V, lambda E: E.reciprocal(invl2, invl2))
            pb.emit(V, lambda E, xb=xb: E.tensor_scalar(
                xntt[:], xb, mean, rstd, ALU.subtract, ALU.mult))
            pb.emit(V, lambda E: E.tensor_scalar_mul(xnrm[:], xntt[:], rmsinv),
                    inc="mvec")
            pb.wait(Tn, "mvec")
            for j in range(8):
                bank = pbank[j // 4]
                pb.emit(Tn, lambda E, j=j, b=bank: E.transpose(
                    b[:, 128 * (j % 4):128 * (j % 4 + 1)],
                    xntt[:, 128 * j:128 * (j + 1)], ident[:]))
            for j in range(8):
                bank = pbank[2 + j // 4]
                pb.emit(Tn, lambda E, j=j, b=bank: E.transpose(
                    b[:, 128 * (j % 4):128 * (j % 4 + 1)],
                    xnrm[:, 128 * j:128 * (j + 1)], ident[:]),
                    inc=("mten" if j == 7 else None))
            pb.wait(V, "mten")
            pb.wait(S, "mten")
            pb.emit(V, lambda E: E.tensor_copy(xnTf[:, 0:512], pbank[0][:]))
            pb.emit(V, lambda E: E.tensor_copy(xnTf[:, 512:1024], pbank[1][:]),
                    inc="mvec")
            v5 = pb.cnt["mvec"]
            for half in range(2):
                for cq in range(4):
                    c = 4 * half + cq
                    dst = hrmsT[:, T * c + 128 * tt:T * c + 128 * (tt + 1)]
                    srcq = pbank[2 + half][:, 128 * cq:128 * (cq + 1)]
                    pb.emit(S, lambda E, d=dst, s2=srcq, c=c: E.activation(
                        d, s2, AF.Copy, scale=pcol(COL_NORMW + c)),
                        inc=("mscl" if c == 7 else None))
            s5 = pb.cnt["mscl"]
            pb.emit(V, lambda E: E.tensor_copy(xh[:], xnTf[:]))
            pb.emit(V, lambda E: E.tensor_sub(xl[:], xnTf[:], xh[:]))
            v5b = pb.cnt["mvec"]
            pb.wait(G, "mvec", v5b)
            dstg = xnT[:].rearrange("p (c n) -> p c n", n=T)[
                :, :, 128 * tt:128 * (tt + 1)]
            pb.emit(G, lambda E, d=dstg: E.tensor_copy(
                d, xh[:].rearrange("p (c n) -> p c n", n=128)), inc="mgps")
            g1 = pb.cnt["mgps"]
            pb.wait(Tn, "mvec", v5b)
            for c in range(8):
                for t3, (lw, rx) in enumerate(((gwT_hi, xh), (gwT_hi, xl),
                                               (gwT_lo, xh))):
                    pb.emit(Tn, lambda E, c=c, t3=t3, lw=lw, rx=rx: E.matmul(
                        pbank[4][0:16, 0:128], lw[:, 16 * c:16 * (c + 1)],
                        rx[:, 128 * c:128 * (c + 1)],
                        start=(c == 0 and t3 == 0), stop=(c == 7 and t3 == 2)))
            pb.emit(Tn, lambda E: E.transpose(pbank[5][0:1, 0:128], invl2,
                                              ident[:]), inc="mten")
            pb.wait(V, "mten")
            pb.emit(V, lambda E: E.tensor_copy(rowt[:], pbank[5][0:1, 0:128]))
            pb.emit(V, lambda E: E.tensor_copy(rowt_hi[:], rowt[:]))
            pb.emit(V, lambda E: E.tensor_sub(rowt_lo[:], rowt[:], rowt_hi[:]),
                    inc="mvec")
            pb.wait(Tn, "mvec")
            pb.emit(Tn, lambda E: E.matmul(pbank[5][0:16, 128:256],
                                           ones1b[:], rowt_hi[:],
                                           start=True, stop=False))
            pb.emit(Tn, lambda E: E.matmul(pbank[5][0:16, 128:256],
                                           ones1b[:], rowt_lo[:],
                                           start=False, stop=True),
                    inc="mten")
            pb.wait(S, "mten")
            pb.emit(S, lambda E: E.copy(rbc[:], pbank[5][0:16, 128:256]),
                    inc="mscl")
            pb.wait(V, "mscl")
            if tt == 0 and "t0praw" in dbg_out and s == 0:
                pb.emit(V, lambda E: E.tensor_copy(
                    dbg_praw[:], pbank[4][0:16, 0:128]))
                pb.emit(V, lambda E: E.tensor_copy(dbg_rbc[:], rbc[:]))
            pb.emit(V, lambda E: E.tensor_scalar_mul(rbc[:], rbc[:],
                                                     invgw[:]))
            pb.emit(V, lambda E, tt=tt: E.tensor_mul(
                logitsT[:, 128 * tt:128 * (tt + 1)], pbank[4][0:16, 0:128],
                rbc[:]), inc="mvec")
            pb.wait(Tn, "mvec")
            pb.wait(Tn, "mscl", s5)
            pb.wait(Tn, "mgps", g1)
            pb.wait(V, "mgps", g1)
            if tt == 0 and "t0stat" in dbg_out and s == 0:
                pb.wait("gpsimd", "mvec")
                pb.wait("gpsimd", "mscl")
                pb.dma("gpsimd", dbg_out["t0stat"], stat[:], inc="qO")
                pb.dma("gpsimd", dbg_out["t0xn"], xntt[:], inc="qO")
                pb.dma("gpsimd", dbg_out["t0xnTf"], xnTf[:], inc="qO")
                pb.wait("gpsimd", "qO")
        scoresT = logitsT
        pb.barrier("p1_tiles")
        if "rawlog" in dbg_out and s == 0:
            pb.dma("sync", dbg_out["rawlog"], logitsT[:], inc="qO")
            pb.wait("scalar", "qO")
        pb.emit("scalar", lambda E: E.activation(scoresT[:], logitsT[:],
                                                 AF.Sigmoid), inc="mscl")
        sct = sbt(st, "p1_sct", [128, 16], F32)
        max8 = sbt(st, "p1_m8", [128, 8], F32)
        pk2 = sbt(st, "p1_pk", [128, 2], F32)
        thr_row = sbt(st, "p1_rt0", [1, T], F32)
        rec_row = sbt(st, "p1_rt1", [1, T], F32)
        thr_b = sbt(st, "p1_thb", [16, T], F32)
        rec_b = sbt(st, "p1_reb", [16, T], F32)
        pb.wait("tensor", "mscl", pb.cnt["mscl"])
        for tt in range(TT):
            V, S, Tn = "vector", "scalar", "tensor"
            pb.emit(Tn, lambda E, tt=tt: E.transpose(
                pbank[0][:, 0:16], scoresT[:, 128 * tt:128 * (tt + 1)],
                ident[0:16, 0:16]), inc="mten")
            pb.wait(V, "mten", pb.cnt["mten"])
            pb.emit(V, lambda E: E.tensor_copy(sct[:], pbank[0][:, 0:16]))
            pb.emit(V, lambda E: E.max(max8[:], sct[:]))
            pb.emit(V, lambda E: E.reduce_sum(pk2[:, 1:2], max8[:], AX.X))
            pb.emit(V, lambda E: E.reciprocal(pk2[:, 1:2], pk2[:, 1:2]))
            pb.emit(V, lambda E: E.tensor_copy(pk2[:, 0:1], max8[:, 7:8]),
                    inc="mvec")
            pb.wait(Tn, "mvec", pb.cnt["mvec"])
            pb.emit(Tn, lambda E: E.transpose(pbank[1][0:1, 0:128],
                                              pk2[:, 0:1], ident[:]))
            pb.emit(Tn, lambda E: E.transpose(pbank[1][0:1, 128:256],
                                              pk2[:, 1:2], ident[:]),
                    inc="mten")
            pb.wait(V, "mten", pb.cnt["mten"])
            pb.emit(V, lambda E, tt=tt: E.tensor_copy(
                thr_row[:, 128 * tt:128 * (tt + 1)], pbank[1][0:1, 0:128]))
            pb.emit(V, lambda E, tt=tt: E.tensor_copy(
                rec_row[:, 128 * tt:128 * (tt + 1)], pbank[1][0:1, 128:256]),
                inc="mvec")
            pb.wait(Tn, "mvec", pb.cnt["mvec"])
        for k in range(4):
            pb.emit("tensor", lambda E, k=k: E.matmul(
                pbank[2][0:16, 0:512], ones1[:, 0:16],
                thr_row[:, 512 * k:512 * (k + 1)], start=True, stop=True))
            pb.emit("tensor", lambda E, k=k: E.matmul(
                pbank[3][0:16, 0:512], ones1[:, 0:16],
                rec_row[:, 512 * k:512 * (k + 1)], start=True, stop=True),
                inc="mten")
            pb.wait("scalar", "mten", pb.cnt["mten"])
            pb.emit("scalar", lambda E, k=k: E.copy(
                thr_b[:, 512 * k:512 * (k + 1)], pbank[2][0:16, 0:512]))
            pb.emit("scalar", lambda E, k=k: E.copy(
                rec_b[:, 512 * k:512 * (k + 1)], pbank[3][0:16, 0:512]),
                inc="mscl")
            pb.wait("tensor", "mscl", pb.cnt["mscl"])
        pb.wait("vector", "mscl", pb.cnt["mscl"])
        pb.emit("vector", lambda E: E.tensor_tensor(thr_b[:], scoresT[:],
                                                    thr_b[:], ALU.is_ge))
        pb.emit("vector", lambda E: E.tensor_mul(combT[:], scoresT[:], thr_b[:]))
        pb.emit("vector", lambda E: E.tensor_mul(combT[:], combT[:], rec_b[:]))
        pb.barrier("p1_end")

        if "t0praw" in dbg_out and s == 0:
            pb.dma("sync", dbg_out["t0praw"], dbg_praw[:], inc="qO")
            pb.dma("sync", dbg_out["t0rbc"], dbg_rbc[:], inc="qO")
        if "xnT" in dbg_out and s == 0:
            pb.dma("gpsimd", dbg_out["xnT"], xnT[:], inc="qO")
        if "hrmsT" in dbg_out and s == 0:
            pb.dma("gpsimd", dbg_out["hrmsT"], hrmsT[:], inc="qO")
        if "combT" in dbg_out and s == 0:
            pb.dma("gpsimd", dbg_out["combT"], combT[:], inc="qO")
        if "logitsT" in dbg_out and s == 0:
            pb.dma("sync", dbg_out["logitsT"], scoresT[:], inc="qO")
        pb.barrier("p1_dbg")
        st.close()

    def emit_P2(s, src_x, dst_x):
        st = ExitStack()
        hrmsT = res_seq["hrmsT"]
        QT = sbt(st, "p2_QT", [128, NC * T], BF16)
        KT = sbt(st, "p2_KT", [128, NC * T], BF16)
        V65 = sbt(st, "p2_V65", [128, TT * 1040], BF16)
        ctxT = hrmsT
        PTs = sbt(st, "p2_PT", [128, 4 * 512], BF16)
        wbf = sbt(st, "p2_wbf", [128, 2048], BF16)
        wpb = QT[:, 0:8192]
        dts = QT[:, 8192:16384].bitcast(F32)
        xtt = KT[:, 0:2048].bitcast(F32)
        nm2 = sbt(st, "p2_nm2", [128, 8192], BF16)
        oball = sbt(st, "p2_obl", [128, 128], BF16)
        pb.emit("gpsimd", lambda E: E.memset(oball[:], 1.0))
        sqt = sbt(st, "p2_sqt", [128, 512], BF16)
        qrow = sbt(st, "p2_qr", [1, 1024], BF16)
        kcells = sbt(st, "p2_kc", [1, 8], F32)
        kcell = sbt(st, "p2_kc1", [1, 1], F32)
        knegs = sbt(st, "p2_kn", [1, 16], F32)
        rrow = sbt(st, "p2_rr", [1, 1024], F32)
        rr_hi = sbt(st, "p2_rrh", [1, 1024], BF16)
        rr_lo = sbt(st, "p2_rrl", [1, 1024], BF16)
        bcs = sbt(st, "p2_bcs", [64, 1024], F32)
        o1b = sbt(st, "p2_o1b", [1, 128], BF16)
        onescb = sbt(st, "p2_o1c", [128, 1], BF16)
        pb.emit("gpsimd", lambda E: E.memset(onescb[:], 1.0))
        sT_marks = {}
        ctx_marks = {}
        pb.emit("gpsimd", lambda E: E.memset(o1b[:], 1.0))
        pb.emit("gpsimd", lambda E: E.memset(V65[:], 1.0))
        V, S, Tn, G = "vector", "scalar", "tensor", "gpsimd"

        def hQK(m, h, col, n):
            base = 64 * (h % 2)
            return m[base:base + 64, T * (h // 2) + col:T * (h // 2) + col + n]

        qk_marks = {}
        wb_marks = {}
        for oc in range(16):
            par = oc % 2
            wb = wbf[:, 1024 * par:1024 * (par + 1)]
            if par in wb_marks:
                pb.wait("gpsimd", "mten", wb_marks[par])
            pb.dma("gpsimd", wb.rearrange("p (c n) -> p c n", n=128),
                   wqkv_oc[oc], inc="qG")
            gv = pb.cnt["qG"]
            dst = QT if oc < 8 else KT
            ocl = oc % 8
            pb.wait(Tn, "qG", gv)
            for ts in range(TS):
                bk = (oc * TS + ts) % 2
                if bk in qk_marks:
                    pb.wait(Tn, *qk_marks[bk])
                for dc in range(NC):
                    pb.emit(Tn, lambda E, bk=bk, wb=wb, dc=dc, ts=ts: E.matmul(
                        pbank[bk], wb[:, 128 * dc:128 * (dc + 1)],
                        hrmsT[:, T * dc + 512 * ts:T * dc + 512 * (ts + 1)],
                        start=(dc == 0), stop=(dc == 7)), inc="mten")
                eng = S if bk == 0 else V
                pb.wait(eng, "mten")
                dsl2 = dst[:, T * ocl + 512 * ts:T * ocl + 512 * (ts + 1)]
                if bk == 0:
                    pb.emit(S, lambda E, d=dsl2, bk=bk: E.copy(d, pbank[bk]),
                            inc="mscl")
                    qk_marks[bk] = ("mscl", pb.cnt["mscl"])
                else:
                    pb.emit(V, lambda E, d=dsl2, bk=bk: E.tensor_copy(
                        d, pbank[bk]), inc="mvec")
                    qk_marks[bk] = ("mvec", pb.cnt["mvec"])
            wb_marks[par] = pb.cnt["mten"]
        qk_marks = {}
        pb.wait(Tn, "mscl")
        pb.wait(Tn, "mvec")
        wb_marks = {}
        pb.wait("gpsimd", "mten")
        for vs in range(8):
            par = vs % 2
            wvb = wbf[:, 1024 * par:1024 * (par + 1)].rearrange(
                "p (c n) -> p c n", n=128)
            if par in wb_marks:
                pb.wait("gpsimd", "mten", wb_marks[par])
            pb.dma("gpsimd", wvb, wqkv_oc[16 + vs], inc="qG")
            gv = pb.cnt["qG"]
            pb.wait(Tn, "qG", gv)
            for tt in range(TT):
                bk = tt % 2
                if bk in qk_marks:
                    pb.wait(Tn, *qk_marks[bk])
                for dc in range(NC):
                    pb.emit(Tn, lambda E, bk=bk, dc=dc, tt=tt, wvb=wvb:
                            E.matmul(
                        pbank[bk][:, 0:128],
                        hrmsT[:, T * dc + 128 * tt:T * dc + 128 * (tt + 1)],
                        wvb[:, dc], start=(dc == 0), stop=(dc == 7)),
                        inc="mten")
                pb.wait(V, "mten")
                dstv = V65[:, 1040 * tt:1040 * (tt + 1)].rearrange(
                    "p (h c) -> p h c", c=65)[:, 2 * vs:2 * (vs + 1), 0:64]
                pb.emit(V, lambda E, d=dstv, bk=bk: E.tensor_copy(
                    d, pbank[bk][:, 0:128].rearrange("p (h c) -> p h c", c=64)),
                    inc="mvec")
                qk_marks[bk] = ("mvec", pb.cnt["mvec"])
            wb_marks[par] = pb.cnt["mten"]
        pb.barrier("p2_qkv")
        if s == 0 and "QTd" in dbg_out:
            pb.dma("gpsimd", dbg_out["QTd"], QT[:], inc="qO")
            pb.dma("gpsimd", dbg_out["KTd"], KT[:], inc="qO")
            pb.dma("gpsimd", dbg_out["V65d"], V65[:], inc="qO")
            pb.barrier("p2_qkvd")

        pb.wait(S, "mvec")
        for hp in range(NC):
            for ks in range(4):
                pb.emit(S, lambda E, hp=hp, ks=ks: E.activation(
                    sqt[:], KT[:, T * hp + 512 * ks:T * hp + 512 * (ks + 1)],
                    AF.Square), inc="mscl")
                pb.wait(Tn, "mscl")
                pb.wait(Tn, "mvec")
                for hl in range(2):
                    pb.emit(Tn, lambda E, hl=hl: E.matmul(
                        pbank[hl][0:1, 0:512],
                        onescb[64 * hl:64 * (hl + 1), :],
                        sqt[64 * hl:64 * (hl + 1), :],
                        start=True, stop=True), inc="mten")
                pb.wait(V, "mten")
                for hl in range(2):
                    pb.emit(V, lambda E, hl=hl, ks=ks: E.reduce_max(
                        kcells[:, 4 * hl + ks:4 * hl + ks + 1],
                        pbank[hl][0:1, 0:512], AX.X), inc="mvec")
                pb.wait(S, "mvec")
            for hl in range(2):
                h = 2 * hp + hl
                pb.emit(V, lambda E, hl=hl: E.reduce_max(
                    kcell[:, 0:1], kcells[0:1, 4 * hl:4 * (hl + 1)], AX.X))
                pb.emit(V, lambda E: E.tensor_scalar_max(kcell[:, 0:1],
                                                         kcell[:, 0:1], 0.0),
                        inc="mvec")
                pb.wait(S, "mvec")
                pb.emit(S, lambda E, h=h: E.activation(
                    knegs[:, h:h + 1], kcell[:, 0:1], AF.Sqrt), inc="mscl")
                pb.wait(V, "mscl")
                pb.emit(V, lambda E, h=h: E.tensor_scalar_mul(
                    knegs[:, h:h + 1], knegs[:, h:h + 1], -1.0), inc="mvec")
        pb.barrier("p2_kneg")

        for qs in range(TS):
            pb.wait(Tn, "mscl")
            pb.wait(Tn, "mvec")
            for hp in range(NC):
                pb.emit(S, lambda E, hp=hp, qs=qs: E.activation(
                    sqt[:], QT[:, T * hp + 512 * qs:T * hp + 512 * (qs + 1)],
                    AF.Square), inc="mscl")
                pb.wait(Tn, "mscl")
                for hl in range(2):
                    pb.emit(Tn, lambda E, hp=hp, hl=hl: E.matmul(
                        pbank[2 + hl][0:1, 0:512],
                        onescb[64 * hl:64 * (hl + 1), :],
                        sqt[64 * hl:64 * (hl + 1), :],
                        start=True, stop=True), inc="mten")
                pb.wait(S, "mten")
                for hl in range(2):
                    pb.emit(S, lambda E, hl=hl: E.activation(
                        qrow[:, 512 * hl:512 * (hl + 1)],
                        pbank[2 + hl][0:1, 0:512], AF.Sqrt), inc="mscl")
                pb.wait(V, "mscl")
                for hl in range(2):
                    h = 2 * hp + hl
                    pb.emit(V, lambda E, h=h, hl=hl: E.tensor_scalar(
                        nm2[0:1, 512 * h:512 * (h + 1)],
                        qrow[:, 512 * hl:512 * (hl + 1)], knegs[:, h:h + 1],
                        None, ALU.mult), inc="mvec")
            pb.wait("gpsimd", "mvec")
            pb.dma("gpsimd", nm2[64:65, :], nm2[0:1, :], inc="qG")
            pb.wait(Tn, "mvec")
            pb.wait(Tn, "qG")
            pb.wait(S, "mvec")

            for hp in range(NC):
                hA, hB = 2 * hp, 2 * hp + 1
                cbA = pgrp[1][0:65, 1024 * (hp % 2):1024 * (hp % 2) + 512]
                cbB = pgrp[1][0:65, 1024 * (hp % 2) + 512:1024 * (hp % 2) + 1024]
                if (hp % 2) in ctx_marks:
                    pb.wait(Tn, "mvec", ctx_marks[hp % 2])
                def ctx_pair(kb, evmark):
                    pb.wait(Tn, "mscl", evmark)
                    slA = PTs[:, 512 * (kb % 2):512 * (kb % 2 + 1)]
                    slB = PTs[:, 1024 + 512 * (kb % 2):1024 + 512 * (kb % 2 + 1)]
                    pb.emit(Tn, lambda E, hA=hA, kb=kb, slA=slA, cbA=cbA:
                            E.matmul(
                        cbA, V65[:, 1040 * kb + 65 * hA:1040 * kb + 65 * (hA + 1)],
                        slA, start=(kb == 0), stop=(kb == 15)))
                    pb.emit(Tn, lambda E, hB=hB, kb=kb, slB=slB, cbB=cbB:
                            E.matmul(
                        cbB, V65[:, 1040 * kb + 65 * hB:1040 * kb + 65 * (hB + 1)],
                        slB, start=(kb == 0), stop=(kb == 15)), inc="mten")

                evm = [0] * 16
                for kb in range(16):
                    bkA = kb % 2
                    bkB = 2 + kb % 2
                    if bkA in sT_marks:
                        pb.wait(Tn, *sT_marks.pop(bkA))
                    if bkB in sT_marks:
                        pb.wait(Tn, *sT_marks.pop(bkB))
                    pb.emit(Tn, lambda E, hA=hA, kb=kb, qs=qs, bkA=bkA:
                            E.matmul(
                        pbank[bkA], hQK(KT, hA, 128 * kb, 128),
                        hQK(QT, hA, 512 * qs, 512), start=True, stop=False))
                    pb.emit(Tn, lambda E, hB=hB, kb=kb, qs=qs, bkB=bkB:
                            E.matmul(
                        pbank[bkB], hQK(KT, hB, 128 * kb, 128),
                        hQK(QT, hB, 512 * qs, 512), start=True, stop=False))
                    pb.emit(Tn, lambda E, hA=hA, bkA=bkA: E.matmul(
                        pbank[bkA], oball[0:1, :],
                        nm2[0:1, 512 * hA:512 * (hA + 1)],
                        start=False, stop=True))
                    pb.emit(Tn, lambda E, hB=hB, bkB=bkB: E.matmul(
                        pbank[bkB], oball[64:65, :],
                        nm2[64:65, 512 * hB:512 * (hB + 1)],
                        start=False, stop=True), inc="mten")
                    if kb > 0:
                        ctx_pair(kb - 1, evm[kb - 1])
                    slA = PTs[:, 512 * (kb % 2):512 * (kb % 2 + 1)]
                    slB = PTs[:, 1024 + 512 * (kb % 2):1024 + 512 * (kb % 2 + 1)]
                    pb.wait(S, "mten")
                    pb.emit(S, lambda E, slA=slA, bkA=bkA: E.activation(
                        slA, pbank[bkA], AF.Exp, scale=0.125))
                    pb.emit(S, lambda E, slB=slB, bkB=bkB: E.activation(
                        slB, pbank[bkB], AF.Exp, scale=0.125), inc="mscl")
                    evm[kb] = pb.cnt["mscl"]
                    sT_marks[bkA] = ("mscl", evm[kb])
                    sT_marks[bkB] = ("mscl", evm[kb])
                ctx_pair(15, evm[15])
                ctx_done = pb.cnt["mten"]
                pb.wait(S, "mten")
                pb.emit(S, lambda E, cbA=cbA: E.activation(
                    rrow[:, 0:512], cbA[64:65, :], AF.Ln))
                pb.emit(S, lambda E, cbB=cbB: E.activation(
                    rrow[:, 512:1024], cbB[64:65, :], AF.Ln))
                pb.emit(S, lambda E: E.activation(rrow[:], rrow[:], AF.Exp,
                                                  scale=-1.0), inc="mscl")
                pb.wait(V, "mscl")
                pb.emit(V, lambda E: E.tensor_copy(rr_hi[:], rrow[:]))
                pb.emit(V, lambda E: E.tensor_sub(rr_lo[:], rrow[:], rr_hi[:]),
                        inc="mvec")
                rdy = pb.cnt["mvec"]
                for i, h in enumerate((hA, hB)):
                    bk = i
                    bk = 0 if i == 0 else 2
                    if bk in sT_marks:
                        pb.wait(Tn, *sT_marks.pop(bk))
                    pb.wait(Tn, "mvec", rdy)
                    pb.emit(Tn, lambda E, bk=bk, i=i: E.matmul(
                        pbank[bk][0:64, :], o1b[:, 0:64],
                        rr_hi[:, 512 * i:512 * (i + 1)],
                        start=True, stop=False))
                    pb.emit(Tn, lambda E, bk=bk, i=i: E.matmul(
                        pbank[bk][0:64, :], o1b[:, 0:64],
                        rr_lo[:, 512 * i:512 * (i + 1)],
                        start=False, stop=True), inc="mten")
                    eng = S if i == 0 else V
                    pb.wait(eng, "mten")
                    if i == 0:
                        pb.emit(S, lambda E, bk=bk, i=i: E.copy(
                            bcs[:, 512 * i:512 * (i + 1)], pbank[bk][0:64, :]),
                            inc="mscl")
                        pb.wait(V, "mscl")
                    else:
                        pb.emit(V, lambda E, bk=bk, i=i: E.tensor_copy(
                            bcs[:, 512 * i:512 * (i + 1)], pbank[bk][0:64, :]),
                            inc="mvec")
                    cb = cbA if i == 0 else cbB
                    pb.emit(V, lambda E, h=h, qs=qs, cb=cb, i=i: E.tensor_mul(
                        hQK(ctxT, h, 512 * qs, 512), cb[0:64, :],
                        bcs[:, 512 * i:512 * (i + 1)]), inc="mvec")
                    sT_marks[bk] = ("mvec", pb.cnt["mvec"])
                ctx_marks[hp % 2] = pb.cnt["mvec"]
        pb.barrier("p2_attn")

        pb.dma("gpsimd", wpb, wproj, inc="qG")
        pb.wait(Tn, "qG")
        for ts in range(TS):
            for oc in range(NC):
                bk = oc % 2
                pb.wait(Tn, "mvec")
                for dc in range(NC):
                    pb.emit(Tn, lambda E, bk=bk, oc=oc, dc=dc, ts=ts: E.matmul(
                        pbank[bk], wpb[:, 1024 * dc + 128 * oc:1024 * dc + 128 * (oc + 1)],
                        ctxT[:, T * dc + 512 * ts:T * dc + 512 * (ts + 1)],
                        start=(dc == 0), stop=(dc == 7)), inc="mten")
                pb.wait(V, "mten")
                pb.emit(V, lambda E, oc=oc, ts=ts, bk=bk: E.tensor_add(
                    dts[:, 512 * oc:512 * (oc + 1)], pbank[bk],
                    xnT[:, T * oc + 512 * ts:T * oc + 512 * (ts + 1)]),
                    inc="mvec")
            pb.wait(Tn, "mvec")
            for j in range(4):
                tt = 4 * ts + j
                pb.wait("sync", "mvec")
                pb.wait("sync", "qG")
                pb.dma("sync", xtt[:], src_x[s, 128 * tt:128 * (tt + 1), :],
                       inc="qS")
                pb.wait(Tn, "mvec")
                for oc in range(NC):
                    pb.emit(Tn, lambda E, j=j, oc=oc: E.transpose(
                        pgrp[1][:, 128 * oc:128 * (oc + 1)],
                        dts[:, 512 * oc + 128 * j:512 * oc + 128 * (j + 1)],
                        ident[:]), inc=("mten" if oc == 7 else None))
                qs_t = pb.cnt["qS"]
                pb.wait(V, "qS", qs_t)
                pb.wait(V, "mten")
                pb.emit(V, lambda E: E.tensor_add(
                    xtt[:], xtt[:], pgrp[1][:, 0:1024]), inc="mvec")
                pb.wait(G, "mvec")
                pb.dma("gpsimd", dst_x[s, 128 * tt:128 * (tt + 1), :], xtt[:],
                       inc="qG")
        pb.barrier("p2_end")
        if "x1" in dbg_out and s == 0:
            pb.dma("gpsimd", dbg_out["x1"], dst_x[0], inc="qO")
        if "ctxT" in dbg_out and s == 0:
            pb.dma("gpsimd", dbg_out["ctxT"], ctxT[:], inc="qO")

        pb.barrier("p2_dbg")
        st.close()

    def emit_P3(s, src_x, dst_x):
        st = ExitStack()
        combT = res_seq["combT"]
        dlo3 = res_seq["hrmsT"][:].bitcast(F32)
        dhi3 = sbt(st, "p3_dl", [128, 4 * T], F32)

        def dsl3(oc, a, b):
            return (dlo3[:, T * oc + a:T * oc + b] if oc < 4 else
                    dhi3[:, T * (oc - 4) + a:T * (oc - 4) + b])
        wgb2 = sbt(st, "p3_wg", [128, 2 * 4096], BF16)
        wub2 = sbt(st, "p3_wu", [128, 2 * 4096], BF16)
        wdb2 = sbt(st, "p3_wd", [128, 2 * 4096], BF16)
        ehot_s = sbt(st, "p3_eh", [16, 2048], BF16)
        cbc = sbt(st, "p3_cbc", [128, 1024], F32)
        sgb = sbt(st, "p3_sg", [128, 4 * 512], BF16)
        hcb = sbt(st, "p3_hc", [128, 4 * 512], BF16)
        xtt = sbt(st, "p3_xtt", [128, 2048], F32)
        V, S, Tn, G = "vector", "scalar", "tensor", "gpsimd"
        pb.dma("sync", ehot_s[:], ehot, inc="qS")
        for oc in range(NC):
            pb.emit(G, lambda E, oc=oc: E.memset(dsl3(oc, 0, T), 0.0))
        pb.barrier("p3_init")

        bmk = {}
        wset_mark = {}
        p3_prev_hc = [0]
        p3_prev_dn = [0]

        def bwait(bk):
            if bk in bmk:
                pb.wait(Tn, *bmk[bk])
        for e in range(17):
            wset = e % 2
            wgb = wgb2[:, 4096 * wset:4096 * (wset + 1)]
            wub = wub2[:, 4096 * wset:4096 * (wset + 1)]
            wdb = wdb2[:, 4096 * wset:4096 * (wset + 1)]
            if e < 16:
                srcs = ((wg_e[e], wgb), (wu_e[e], wub), (wd_e[e], wdb))
            else:
                srcs = ((sw_g, wgb), (sw_u, wub), (sw_d, wdb))
            if wset in wset_mark:
                pb.wait("gpsimd", "mten", wset_mark[wset])
            for (mat, dst3) in srcs:
                pb.dma("gpsimd", dst3, mat, inc="qG")
            pb.wait(Tn, "qG")
            for ts in range(TS):
                pb.wait(S, "mvec", p3_prev_hc[0])
                pb.wait(V, "mten", p3_prev_dn[0])
                if e < 16:
                    bwait(7)
                    pb.emit(Tn, lambda E, e=e, ts=ts: E.matmul(
                        pbank[7], ehot_s[:, 128 * e:128 * (e + 1)],
                        combT[0:16, 512 * ts:512 * (ts + 1)],
                        start=True, stop=True), inc="mten")
                    pb.wait(S, "mten")
                    pb.emit(S, lambda E, e=e, ts=ts: E.copy(
                        cbc[:, 512 * ((e + ts) % 2):512 * ((e + ts) % 2 + 1)],
                        pbank[7]), inc="mscl")
                    bmk[7] = ("mscl", pb.cnt["mscl"])
                cslc = cbc[:, 512 * ((e + ts) % 2):512 * ((e + ts) % 2 + 1)]
                g_mark, u_mark = {}, {}
                for hc in range(4):
                    bwait(hc)
                    for dc in range(NC):
                        pb.emit(Tn, lambda E, hc=hc, dc=dc, ts=ts, wgb=wgb:
                                E.matmul(
                            pbank[hc],
                            wgb[:, 512 * dc + 128 * hc:512 * dc + 128 * (hc + 1)],
                            xnT[:, T * dc + 512 * ts:T * dc + 512 * (ts + 1)],
                            start=(dc == 0), stop=(dc == 7)),
                            inc=("mten" if dc == 7 else None))
                    g_mark[hc] = pb.cnt["mten"]
                for hc in range(4):
                    bwait(4 + hc)
                    for dc in range(NC):
                        pb.emit(Tn, lambda E, hc=hc, dc=dc, ts=ts, wub=wub:
                                E.matmul(
                            pbank[4 + hc],
                            wub[:, 512 * dc + 128 * hc:512 * dc + 128 * (hc + 1)],
                            xnT[:, T * dc + 512 * ts:T * dc + 512 * (ts + 1)],
                            start=(dc == 0), stop=(dc == 7)),
                            inc=("mten" if dc == 7 else None))
                    u_mark[hc] = pb.cnt["mten"]
                h_marks = []
                for hc in range(4):
                    pb.wait(S, "mten", g_mark[hc])
                    pb.emit(S, lambda E, hc=hc: E.activation(
                        sgb[:, 512 * hc:512 * (hc + 1)], pbank[hc], AF.Silu),
                        inc="mscl")
                    bmk[hc] = ("mscl", pb.cnt["mscl"])
                    pb.wait(V, "mscl")
                    pb.wait(V, "mten", u_mark[hc])
                    pb.emit(V, lambda E, hc=hc: E.tensor_mul(
                        hcb[:, 512 * hc:512 * (hc + 1)], pbank[4 + hc],
                        sgb[:, 512 * hc:512 * (hc + 1)]))
                    if e < 16:
                        pb.emit(V, lambda E, hc=hc, cslc=cslc: E.tensor_mul(
                            hcb[:, 512 * hc:512 * (hc + 1)],
                            hcb[:, 512 * hc:512 * (hc + 1)], cslc),
                            inc="mvec")
                    else:
                        pb.emit(V, lambda E: E.engine_nop(), inc="mvec")
                    bmk[4 + hc] = ("mvec", pb.cnt["mvec"])
                    h_marks.append(pb.cnt["mvec"])
                p3_prev_hc[0] = h_marks[-1]
                dn_mark = {}
                for hc in range(4):
                    pb.wait(Tn, "mvec", h_marks[hc])
                    for oc in range(NC):
                        if hc == 0:
                            bwait(oc)
                        pb.emit(Tn, lambda E, oc=oc, hc=hc, wdb=wdb: E.matmul(
                            pbank[oc],
                            wdb[:, 1024 * hc + 128 * oc:1024 * hc + 128 * (oc + 1)],
                            hcb[:, 512 * hc:512 * (hc + 1)],
                            start=(hc == 0), stop=(hc == 3)), inc="mten")
                        if hc == 3:
                            dn_mark[oc] = pb.cnt["mten"]
                for oc in range(NC):
                    pb.wait(V, "mten", dn_mark[oc])
                    dd3 = dsl3(oc, 512 * ts, 512 * (ts + 1))
                    pb.emit(V, lambda E, dd3=dd3, oc=oc: E.tensor_add(
                        dd3, dd3, pbank[oc]), inc="mvec")
                    bmk[oc] = ("mvec", pb.cnt["mvec"])
                wset_mark[wset] = pb.cnt["mten"]
                p3_prev_dn[0] = pb.cnt["mten"]
        pb.barrier("p3_expert_done")

        qs0 = pb.cnt["qS"]
        qg0 = pb.cnt["qG"]
        add_mark = {}

        def wb_load(tt):
            par = tt % 2
            if tt >= 2:
                pb.wait("sync", "mvec", add_mark[tt - 2])
                pb.wait("sync", "qG", qg0 + 16 * (tt - 1))
            pb.dma("sync", xtt[:, 1024 * par:1024 * (par + 1)],
                   src_x[s, 128 * tt:128 * (tt + 1), :], inc="qS")
        wb_load(0)
        wb_load(1)
        for tt in range(TT):
            par = tt % 2
            xs = xtt[:, 1024 * par:1024 * (par + 1)]
            pg = pgrp[1][:, 1024 * par:1024 * (par + 1)]
            if tt >= 2:
                pb.wait(Tn, "mvec", add_mark[tt - 2])
            for oc in range(NC):
                pb.emit(Tn, lambda E, tt=tt, oc=oc, pg=pg: E.transpose(
                    pg[:, 128 * oc:128 * (oc + 1)],
                    dsl3(oc, 128 * tt, 128 * (tt + 1)),
                    ident[:]), inc=("mten" if oc == 7 else None))
            pb.wait(V, "qS", qs0 + 16 * (tt + 1))
            pb.wait(V, "mten")
            pb.emit(V, lambda E, xs=xs, pg=pg: E.tensor_add(xs, xs, pg),
                    inc="mvec")
            add_mark[tt] = pb.cnt["mvec"]
            pb.wait(G, "mvec", add_mark[tt])
            pb.dma("gpsimd", dst_x[s, 128 * tt:128 * (tt + 1), :], xs,
                   inc="qG")
            if tt + 2 < TT:
                wb_load(tt + 2)
        pb.barrier("p3_end")
        if "x1f" in dbg_out and s == 0:
            pb.dma("gpsimd", dbg_out["x1f"], dst_x[0], inc="qO")
        pb.barrier("p3_dbg")
        st.close()

    def emit_P4(s, src_x):
        st = ExitStack()
        dxst = res_seq["dxT_st"] = ExitStack()
        dxT = res_seq["dxT"] = sbt(dxst, "dxT", [128, NC * T], BF16)
        xbuf = sbt(st, "p4_x", [128, 2 * 1024], F32)
        xsq = sbt(st, "p4_sq", [128, 1024], F32)
        stat = sbt(st, "p4_st", [128, 8], F32)
        xntt = sbt(st, "p4_xn", [128, 1024], F32)
        V, S, Tn, G = "vector", "scalar", "tensor", "gpsimd"
        for tt in range(TT):
            xb = xbuf[:, 1024 * (tt % 2):1024 * (tt % 2 + 1)]
            pb.wait("sync", "mvec")
            pb.wait("sync", "mscl")
            pb.dma("sync", xb, src_x[s, 128 * tt:128 * (tt + 1), :], inc="qS")
            qs_tgt = pb.cnt["qS"]
            mean, sumsq = stat[:, 0:1], stat[:, 1:2]
            var, rstd, m2 = stat[:, 2:3], stat[:, 3:4], stat[:, 7:8]
            pb.wait(V, "qS", qs_tgt)
            pb.wait(S, "qS", qs_tgt)
            pb.emit(V, lambda E, xb=xb: E.reduce_sum(mean, xb, AX.X))
            pb.emit(V, lambda E: E.tensor_scalar_mul(mean, mean, 1.0 / D),
                    inc="mvec")
            pb.emit(S, lambda E, xb=xb: E.activation(xsq[:], xb, AF.Square),
                    inc="mscl")
            pb.wait(V, "mscl")
            pb.emit(V, lambda E: E.reduce_sum(sumsq, xsq[:], AX.X))
            pb.wait(S, "mvec")
            pb.emit(S, lambda E: E.activation(m2, mean, AF.Square), inc="mscl")
            pb.wait(V, "mscl")
            pb.emit(V, lambda E: E.tensor_scalar(var, sumsq, 1.0 / D, m2,
                                                 ALU.mult, ALU.subtract),
                    inc="mvec")
            pb.wait(S, "mvec")
            pb.emit(S, lambda E: E.activation(rstd, var, AF.Sqrt, bias=1e-5),
                    inc="mscl")
            pb.wait(V, "mscl")
            pb.emit(V, lambda E: E.reciprocal(rstd, rstd))
            pb.emit(V, lambda E, xb=xb: E.tensor_scalar(
                xntt[:], xb, mean, rstd, ALU.subtract, ALU.mult), inc="mvec")
            pb.wait(Tn, "mvec")
            for j in range(8):
                bank = pbank[j // 4]
                pb.emit(Tn, lambda E, j=j, b=bank: E.transpose(
                    b[:, 128 * (j % 4):128 * (j % 4 + 1)],
                    xntt[:, 128 * j:128 * (j + 1)], ident[:]),
                    inc=("mten" if j == 7 else None))
            pb.wait(V, "mten")
            for half in range(2):
                dst = xnT[:].rearrange("p (c n) -> p c n", n=T)[
                    :, 4 * half:4 * (half + 1), 128 * tt:128 * (tt + 1)]
                pb.emit(V, lambda E, d=dst, h=half: E.tensor_copy(
                    d, pbank[h][:].rearrange("p (c n) -> p c n", n=128)),
                    inc="mvec")
            pb.wait(Tn, "mvec")
        pb.barrier("p4_ln")
        for c in range(NC):
            xc = xnT[:, T * c:T * (c + 1)]
            dc_ = dxT[:, T * c:T * (c + 1)]
            pb.emit(V, lambda E, xc=xc, dc_=dc_: E.tensor_scalar(
                dc_[:, 1:T], xc[:, 0:T - 1], 0.5, None, ALU.mult))
            pb.emit(V, lambda E, dc_=dc_: E.memset(dc_[:, 0:1], 0.0))
            for lo, hi in ((0, 1024), (1024, 2047)):
                n = hi - lo
                pb.emit(V, lambda E, xc=xc, lo=lo, n=n: E.tensor_scalar(
                    xsq[:, 0:n], xc[:, lo + 1:lo + 1 + n], 0.5, None, ALU.mult))
                pb.emit(V, lambda E, dc_=dc_, lo=lo, n=n: E.tensor_add(
                    dc_[:, lo:lo + n], dc_[:, lo:lo + n], xsq[:, 0:n]))
            pb.emit(V, lambda E, xc=xc, dc_=dc_: E.tensor_sub(dc_, dc_, xc),
                    inc="mvec")
        pb.barrier("p4_end")
        if "dxT" in dbg_out and s == 0:
            pb.dma("gpsimd", dbg_out["dxT"], dxT[:], inc="qO")
            pb.barrier("p4_dbg")
        st.close()

    def emit_P5(s, src_x, dst_x):
        st = ExitStack()
        dlo = res_seq["hrmsT"][:].bitcast(F32)
        dhi = sbt(st, "p5_dl", [128, 4 * T], F32)

        def dsl(oc, a, b):
            return (dlo[:, T * oc + a:T * oc + b] if oc < 4 else
                    dhi[:, T * (oc - 4) + a:T * (oc - 4) + b])
        stg5 = sbt(st, "p5_stg", [128, 2048], F32)
        sb5 = sbt(st, "p5_sb", [128, 8192], BF16)
        sw1 = sbt(st, "p5_sw1", [128, 4096], BF16)
        sw2 = sbt(st, "p5_sw2", [128, 4096], BF16)
        hbuf = sbt(st, "p5_h", [128, 4 * T], BF16)
        xtt = sbt(st, "p5_xtt", [128, 1024], F32)
        tmp5 = sbt(st, "p5_tmp", [128, 512], F32)
        V, S, Tn, G = "vector", "scalar", "tensor", "gpsimd"
        if s == 0:
            for c in range(32):
                pb.wait("sync", "mvec")
                pb.dma("sync", stg5[:, 0:1024], w1[128 * c:128 * (c + 1), :],
                       inc="qS")
                pb.wait(V, "qS")
                pb.emit(V, lambda E, c=c: E.reduce_sum(
                    alpha1[:, c:c + 1], stg5[:, 0:1024], AX.X,
                    apply_absolute_value=True), inc="mvec")
            for c in range(8):
                for half in range(2):
                    pb.wait("sync", "mvec")
                    pb.dma("sync", stg5[:], w2[128 * c:128 * (c + 1),
                                               2048 * half:2048 * (half + 1)],
                           inc="qS")
                    pb.wait(V, "qS")
                    pb.emit(V, lambda E, c=c, half=half: E.reduce_sum(
                        alpha2[:, c:c + 1] if half == 0 else tmp5[:, 0:1],
                        stg5[:], AX.X, apply_absolute_value=True), inc="mvec")
                pb.emit(V, lambda E, c=c: E.tensor_add(
                    alpha2[:, c:c + 1], alpha2[:, c:c + 1], tmp5[:, 0:1]))
            pb.emit(V, lambda E: E.tensor_scalar_mul(alpha1[:], alpha1[:],
                                                     1.0 / 1024))
            pb.emit(V, lambda E: E.tensor_scalar_mul(alpha2[:], alpha2[:],
                                                     1.0 / 4096), inc="mvec")
            pb.barrier("p5_alpha")

        p5_marks = {}
        w1mk, w2mk, sgmk, hmk, dnmk = {}, {}, {}, {}, {}

        def ld_w1(g):
            if "w1" in sgmk:
                pb.wait("gpsimd", "mscl", sgmk["w1"])
            for half in range(2):
                pb.dma("gpsimd", sb5[:, 2048 * half:2048 * (half + 1)],
                       w1t_g[g, half], inc="qG")
            w1mk[g] = pb.cnt["qG"]

        def ld_w2(g):
            if "w2" in sgmk:
                pb.wait("gpsimd", "mscl", sgmk["w2"])
            for half in range(2):
                pb.dma("gpsimd", sb5[:, 4096 + 2048 * half:4096 + 2048 * (half + 1)],
                       w2t_g[g, half], inc="qG")
            w2mk[g] = pb.cnt["qG"]
        ld_w1(0)
        ld_w2(0)
        for g in range(8):
            pb.wait(S, "qG", w1mk[g])
            if g:
                pb.wait(S, "mten", hmk[g - 1])
            for half in range(2):
                for c in range(8):
                    pb.emit(S, lambda E, c=c, half=half: E.sign(
                        sw1[:, 512 * c + 256 * half:512 * c + 256 * (half + 1)],
                        sb5[:, 2048 * half + 256 * c:2048 * half + 256 * (c + 1)]),
                        inc="mscl")
            sgmk["w1"] = pb.cnt["mscl"]
            if g:
                pb.wait(S, "mten", dnmk[g - 1])
            pb.wait(Tn, "mscl")
            for hc in range(4):
                for ts in range(TS):
                    bk = (hc * TS + ts) % 2
                    if bk in p5_marks:
                        pb.wait(Tn, "mscl", p5_marks[bk])
                    for dc in range(NC):
                        pb.emit(Tn, lambda E, bk=bk, hc=hc, dc=dc, ts=ts: E.matmul(
                            pbank[bk],
                            sw1[:, 512 * dc + 128 * hc:512 * dc + 128 * (hc + 1)],
                            xnT[:, T * dc + 512 * ts:T * dc + 512 * (ts + 1)],
                            start=(dc == 0), stop=(dc == 7)), inc="mten")
                    pb.wait(S, "mten")
                    pb.emit(S, lambda E, hc=hc, ts=ts, bk=bk, g=g: E.activation(
                        hbuf[:, T * hc + 512 * ts:T * hc + 512 * (ts + 1)],
                        pbank[bk], AF.Gelu, bias=pcol(COL_B1 + 4 * g + hc),
                        scale=alpha1[:, 4 * g + hc:4 * g + hc + 1]),
                        inc="mscl")
                    p5_marks[bk] = pb.cnt["mscl"]
            hmk[g] = pb.cnt["mten"]
            if g + 1 < 8:
                ld_w1(g + 1)
            pb.wait(S, "qG", w2mk[g])
            if g:
                pb.wait(S, "mten", dnmk[g - 1])
            for half in range(2):
                for c in range(2):
                    pb.emit(S, lambda E, c=c, half=half: E.sign(
                        sw2[:, 1024 * (2 * half + c):1024 * (2 * half + c + 1)],
                        sb5[:, 4096 + 2048 * half + 1024 * c:
                             4096 + 2048 * half + 1024 * (c + 1)]), inc="mscl")
            sgmk["w2"] = pb.cnt["mscl"]
            pb.wait(Tn, "mscl")
            for ts in range(TS):
                for oc in range(NC):
                    bk = 2 + (ts * NC + oc) % 2
                    if bk in p5_marks:
                        pb.wait(Tn, "mvec", p5_marks[bk])
                    for hc in range(4):
                        pb.emit(Tn, lambda E, bk=bk, oc=oc, hc=hc, ts=ts: E.matmul(
                            pbank[bk],
                            sw2[:, 1024 * hc + 128 * oc:1024 * hc + 128 * (oc + 1)],
                            hbuf[:, T * hc + 512 * ts:T * hc + 512 * (ts + 1)],
                            start=(hc == 0), stop=(hc == 3)), inc="mten")
                    pb.wait(V, "mten")
                    dd = dsl(oc, 512 * ts, 512 * (ts + 1))
                    if g == 0:
                        pb.emit(V, lambda E, dd=dd, oc=oc, bk=bk: E.tensor_scalar(
                            dd, pbank[bk], alpha2[:, oc:oc + 1],
                            pcol(COL_B2 + oc), ALU.mult, ALU.add), inc="mvec")
                    else:
                        pb.emit(V, lambda E, oc=oc, bk=bk: E.tensor_scalar(
                            tmp5[:], pbank[bk], alpha2[:, oc:oc + 1], None,
                            ALU.mult))
                        pb.emit(V, lambda E, dd=dd: E.tensor_add(
                            dd, dd, tmp5[:]), inc="mvec")
                    p5_marks[bk] = pb.cnt["mvec"]
            dnmk[g] = pb.cnt["mten"]
            if g + 1 < 8:
                ld_w2(g + 1)
        pb.barrier("p5_groups")
        for tt in range(TT):
            pb.wait("sync", "mvec")
            pb.wait("sync", "qG")
            pb.dma("sync", xtt[:], src_x[s, 128 * tt:128 * (tt + 1), :],
                   inc="qS")
            pb.wait(Tn, "mvec")
            for oc in range(NC):
                pb.emit(Tn, lambda E, tt=tt, oc=oc: E.transpose(
                    pgrp[1][:, 128 * oc:128 * (oc + 1)],
                    dsl(oc, 128 * tt, 128 * (tt + 1)),
                    ident[:]), inc=("mten" if oc == 7 else None))
            qs_t = pb.cnt["qS"]
            pb.wait(V, "qS", qs_t)
            pb.wait(V, "mten")
            pb.emit(V, lambda E: E.tensor_add(xtt[:], xtt[:], pgrp[1][:, 0:1024]),
                    inc="mvec")
            pb.wait(G, "mvec")
            pb.dma("gpsimd", dst_x[s, 128 * tt:128 * (tt + 1), :], xtt[:],
                   inc="qG")
        pb.barrier("p5_end")
        if "x2f" in dbg_out and s == 0:
            pb.dma("gpsimd", dbg_out["x2f"], dst_x[0], inc="qO")
        pb.barrier("p5_dbg")
        st.close()

    def emit_P6(s, src_x, dst_x):
        V, S, Tn, G = "vector", "scalar", "tensor", "gpsimd"
        dxT = res_seq["dxT"]
        rT = res_seq["hrmsT"]
        stP = ExitStack()
        kT = sbt(stP, "p6_kT", [128, NC * T], BF16)
        t1b = sbt(stP, "p6_t1", [64, T], BF16)
        masksb = sbt(stP, "p6_mk", [128, 4 * 512], mybir.dt.uint8)
        dw2b = sbt(stP, "p6_dw2", [64, 1024], BF16)
        pb.dma("gpsimd", masksb[:].rearrange("p (j n) -> p j n", j=4),
               lion_masks.rearrange("j p n -> p j n"), inc="qG")

        stA = ExitStack()
        wtb = sbt(stA, "p6_wtb", [128, NC * 1024], BF16)
        xcon2 = sbt(stA, "p6_xc", [128, 2 * NC * 512], BF16)
        xcon = xcon2[:, 0:NC * 512]
        af32 = sbt(stA, "p6_af", [128, 512], F32)
        xxa = sbt(stA, "p6_xxa", [64, T], BF16)
        xxb = sbt(stA, "p6_xxb", [64, T], BF16)
        xxc = sbt(stA, "p6_xxc", [32, T], BF16)
        maw1b = sbt(stA, "p6_m1", [128, 8 * 160], BF16)
        maw2b = sbt(stA, "p6_m2", [64, 5 * 1024], BF16)
        dw1b = sbt(stA, "p6_dw1", [128, 8 * 64], BF16)
        vtmp = sbt(stA, "p6_vt", [128, 512], BF16)
        gtmp = sbt(stA, "p6_gt", [128, 512], BF16)

        pb.dma("gpsimd", maw1b[:], maa_w1, inc="qG")
        pb.dma("gpsimd", maw2b[0:32, :].rearrange("m (f d) -> m f d", f=5),
               maa_w2.rearrange("f m d -> m f d"), inc="qG")
        pb.dma("gpsimd", maw2b[32:64, :].rearrange("m (f d) -> m f d", f=5),
               maa_w2.rearrange("f m d -> m f d"), inc="qG")
        pb.dma("gpsimd", dw1b[:], dec_w1, inc="qG")
        pb.dma("gpsimd", dw2b[:], dec_w2, inc="qG")
        pb.wait(Tn, "qG")
        pb.wait("vector", "qG")

        MF = {1: 1, 2: 2, 3: 3, 4: 4}
        MAAC = {1: 2, 2: 3, 3: 4, 4: 5}
        XXROW = {1: (None, 32), 2: (None, 0), 3: (None, 32), 4: (None, 0)}

        for ts in range(TS):
            for dc in range(NC):
                pb.emit(V, lambda E, dc=dc, ts=ts: E.tensor_scalar(
                    af32[:], dxT[:, T * dc + 512 * ts:T * dc + 512 * (ts + 1)],
                    pcol(COL_MAA + 0 + dc), None, ALU.mult))
                pb.emit(V, lambda E, dc=dc, ts=ts: E.tensor_add(
                    xcon[:, 512 * dc:512 * (dc + 1)], af32[:],
                    xnT[:, T * dc + 512 * ts:T * dc + 512 * (ts + 1)]),
                    inc="mvec")
            pb.wait(Tn, "mvec")
            pb.wait(Tn, "mscl")
            for dc in range(NC):
                pb.emit(Tn, lambda E, dc=dc: E.matmul(
                    pbank[0], maw1b[:, 160 * dc:160 * dc + 128],
                    xcon[:, 512 * dc:512 * (dc + 1)],
                    start=(dc == 0), stop=(dc == 7)))
            for dc in range(NC):
                pb.emit(Tn, lambda E, dc=dc: E.matmul(
                    pbank[1][0:32, :], maw1b[:, 160 * dc + 128:160 * (dc + 1)],
                    xcon[:, 512 * dc:512 * (dc + 1)],
                    start=(dc == 0), stop=(dc == 7)), inc="mten")
            pb.wait(S, "mten")
            pb.emit(S, lambda E, ts=ts: E.activation(
                xxa[:, 512 * ts:512 * (ts + 1)], pbank[0][0:64, :], AF.Tanh))
            pb.emit(S, lambda E, ts=ts: E.activation(
                xxb[:, 512 * ts:512 * (ts + 1)], pbank[0][64:128, :], AF.Tanh))
            pb.emit(S, lambda E, ts=ts: E.activation(
                xxc[:, 512 * ts:512 * (ts + 1)], pbank[1][0:32, :], AF.Tanh),
                inc="mscl")
        pb.barrier("p6_xxx")

        for ts in range(TS):
            for dc in range(NC):
                pb.wait(Tn, "mvec")
                pb.emit(Tn, lambda E, dc=dc, ts=ts: E.matmul(
                    pgrp[1][:, 1536:2048],
                    maw2b[0:32, 128 * dc:128 * (dc + 1)],
                    xxa[0:32, 512 * ts:512 * (ts + 1)],
                    start=True, stop=True), inc="mten")
                pb.wait(V, "mten")
                pb.emit(V, lambda E, dc=dc: E.tensor_scalar(
                    af32[:], pgrp[1][:, 1536:2048],
                    pcol(COL_MAA + 8 + dc), None, ALU.add))
                pb.emit(V, lambda E, dc=dc, ts=ts: E.tensor_mul(
                    af32[:], af32[:],
                    dxT[:, T * dc + 512 * ts:T * dc + 512 * (ts + 1)]))
                pb.emit(V, lambda E, dc=dc, ts=ts: E.tensor_add(
                    xcon[:, 512 * dc:512 * (dc + 1)], af32[:],
                    xnT[:, T * dc + 512 * ts:T * dc + 512 * (ts + 1)]),
                    inc="mvec")
            pb.wait(Tn, "mvec")
            for dc in range(NC):
                pb.emit(Tn, lambda E, dc=dc: E.matmul(
                    pbank[2][0:64, :], dw1b[:, 64 * dc:64 * (dc + 1)],
                    xcon[:, 512 * dc:512 * (dc + 1)],
                    start=(dc == 0), stop=(dc == 7)), inc="mten")
            pb.wait(S, "mten")
            pb.emit(S, lambda E, ts=ts: E.activation(
                t1b[:, 512 * ts:512 * (ts + 1)], pbank[2][0:64, :], AF.Tanh),
                inc="mscl")
            pb.wait(Tn, "mscl")
        pb.barrier("p6_t1")
        if "t1d" in dbg_out and s == 0:
            pb.dma("gpsimd", dbg_out["t1d"], t1b[:], inc="qO")
            pb.barrier("p6_t1d")

        def load_wt(mat):
            pb.wait("gpsimd", "mten")
            pb.dma("gpsimd", wtb[:], mat, inc="qG")
            pb.wait(Tn, "qG")

        ep_marks = {}

        def build_half(vec_idx, xx_src, xx_row, ts, xcon, half):
            for dc in range(4 * half, 4 * half + 4):
                slot = pgrp[1][:, 512 * (dc % 4):512 * (dc % 4 + 1)]
                if (dc % 4) in ep_marks:
                    pb.wait(Tn, "mvec", ep_marks[dc % 4])
                pb.emit(Tn, lambda E, dc=dc, ts=ts, vec_idx=vec_idx,
                        xx_src=xx_src, xx_row=xx_row, slot=slot: E.matmul(
                    slot,
                    maw2b[xx_row:xx_row + 32,
                          1024 * MF[vec_idx] + 128 * dc:
                          1024 * MF[vec_idx] + 128 * (dc + 1)],
                    xx_src[xx_row:xx_row + 32, 512 * ts:512 * (ts + 1)],
                    start=True, stop=True), inc="mten")
                pb.wait(V, "mten")
                pb.emit(V, lambda E, dc=dc, vec_idx=vec_idx, slot=slot:
                        E.tensor_scalar(
                    af32[:], slot, pcol(COL_MAA + 8 * MAAC[vec_idx] + dc),
                    None, ALU.add))
                pb.emit(V, lambda E, dc=dc, ts=ts: E.tensor_mul(
                    af32[:], af32[:],
                    dxT[:, T * dc + 512 * ts:T * dc + 512 * (ts + 1)]))
                pb.emit(V, lambda E, dc=dc, ts=ts, xcon=xcon: E.tensor_add(
                    xcon[:, 512 * dc:512 * (dc + 1)], af32[:],
                    xnT[:, T * dc + 512 * ts:T * dc + 512 * (ts + 1)]),
                    inc="mvec")
                ep_marks[dc % 4] = pb.cnt["mvec"]

        def proj_part(kind, dstT, xcon, ts, part):
            if kind in ("r", "k", "g"):
                for oc in range(4 * part, 4 * part + 4):
                    bk = oc % 2
                    if bk in p6_marks:
                        pb.wait(Tn, *p6_marks[bk])
                    for dc in range(NC):
                        pb.emit(Tn, lambda E, bk=bk, oc=oc, dc=dc, xcon=xcon:
                                E.matmul(
                            pbank[bk],
                            wtb[:, 1024 * dc + 128 * oc:1024 * dc + 128 * (oc + 1)],
                            xcon[:, 512 * dc:512 * (dc + 1)],
                            start=(dc == 0), stop=(dc == 7)), inc="mten")
                    if kind == "g":
                        pb.wait(S, "mten")
                        pb.wait(S, "qG")
                        pb.emit(S, lambda E, oc=oc, bk=bk: E.activation(
                            gtmp[:], pbank[bk], AF.Silu), inc="mscl")
                        p6_marks[bk] = ("mscl", pb.cnt["mscl"])
                        pb.wait(G, "mscl")
                        pb.dma("gpsimd", gdr[oc, :, 512 * ts:512 * (ts + 1)],
                               gtmp[:], inc="qG")
                    else:
                        eng = S if bk == 0 else V
                        pb.wait(eng, "mten")
                        dsl6 = dstT[:, T * oc + 512 * ts:T * oc + 512 * (ts + 1)]
                        if bk == 0:
                            pb.emit(S, lambda E, d=dsl6, bk=bk: E.copy(
                                d, pbank[bk]), inc="mscl")
                            p6_marks[bk] = ("mscl", pb.cnt["mscl"])
                        else:
                            pb.emit(V, lambda E, d=dsl6, bk=bk: E.tensor_copy(
                                d, pbank[bk]), inc="mvec")
                            p6_marks[bk] = ("mvec", pb.cnt["mvec"])
            else:
                for jv in range(4 * part, 4 * part + 4):
                    j, vs = jv // 2, jv % 2
                    tt = 4 * ts + j
                    bk = jv % 2
                    if bk in p6_marks:
                        pb.wait(Tn, *p6_marks[bk])
                    for dc in range(NC):
                        pb.emit(Tn, lambda E, bk=bk, j=j, vs=vs, dc=dc,
                                xcon=xcon: E.matmul(
                            pbank[bk],
                            xcon[:, 512 * dc + 128 * j:512 * dc + 128 * (j + 1)],
                            wtb[:, 1024 * dc + 512 * vs:1024 * dc + 512 * (vs + 1)],
                            start=(dc == 0), stop=(dc == 7)), inc="mten")
                    pb.wait(V, "mten")
                    pb.wait(V, "qG")
                    pb.emit(V, lambda E, bk=bk: E.tensor_copy(
                        vtmp[:], pbank[bk]), inc="mvec")
                    p6_marks[bk] = ("mvec", pb.cnt["mvec"])
                    pb.wait(G, "mvec")
                    pb.dma("gpsimd",
                           vdr[4 * vs:4 * vs + 4, 128 * tt:128 * (tt + 1),
                               :].rearrange("c p n -> p c n"),
                           vtmp[:].rearrange("p (c n) -> p c n", n=128),
                           inc="qG")

        p6_marks = {}
        for (mat, vec_idx, kind) in ((WrT, 3, "r"), (WkT, 1, "k"),
                                     (WvT, 2, "v"), (WgT, 4, "g")):
            p6_marks = {}
            load_wt(mat)
            pb.wait(Tn, "mscl")
            pb.wait(Tn, "mvec")
            xx_src = {1: xxa, 2: xxb, 3: xxb, 4: xxc}[vec_idx]
            xx_row = {1: 32, 2: 0, 3: 32, 4: 0}[vec_idx]
            dstT = rT if kind == "r" else kT
            for half in range(2):
                build_half(vec_idx, xx_src, xx_row, 0, xcon2[:, 0:4096], half)
            for ts in range(TS):
                xcon = xcon2[:, 4096 * (ts % 2):4096 * (ts % 2 + 1)]
                nxt = xcon2[:, 4096 * ((ts + 1) % 2):4096 * ((ts + 1) % 2 + 1)]
                pb.wait(Tn, "mvec", ep_marks.get(3, 0))
                if ts + 1 < TS:
                    build_half(vec_idx, xx_src, xx_row, ts + 1, nxt, 0)
                proj_part(kind, dstT, xcon, ts, 0)
                if ts + 1 < TS:
                    build_half(vec_idx, xx_src, xx_row, ts + 1, nxt, 1)
                proj_part(kind, dstT, xcon, ts, 1)
        pb.barrier("p6_proj")
        if s == 0:
            if "rTd" in dbg_out:
                pb.dma("gpsimd", dbg_out["rTd"], rT[:], inc="qO")
            if "kTd" in dbg_out:
                pb.dma("gpsimd", dbg_out["kTd"], kT[:], inc="qO")
            if "gdrd" in dbg_out:
                pb.dma("gpsimd", dbg_out["gdrd"], gdr, inc="qO")
            if "vdrd" in dbg_out:
                pb.dma("gpsimd", dbg_out["vdrd"], vdr, inc="qO")
            pb.barrier("p6_projd")
        stA.close()

        stB = ExitStack()
        A_ = sbt(stB, "p6_A", [128, T], F32)
        B_ = sbt(stB, "p6_B", [128, T], F32)
        C_ = sbt(stB, "p6_C", [128, T], F32)
        zerosb = sbt(stB, "p6_z0", [128, T], BF16)
        pb.emit(G, lambda E: E.memset(zerosb[:], 0.0))
        vhp = sbt(stB, "p6_vhp", [128, TT * 128], BF16)
        ghp = sbt(stB, "p6_ghp", [128, T], BF16)
        ysq1 = sbt(stB, "p6_ysq", [128, 512], F32)
        afmp = sbt(stB, "p6_af2", [128, 512], BF16)
        afmp1 = sbt(stB, "p6_af3", [128, 512], BF16)
        o1b2 = sbt(stB, "p6_o1b", [128, 64], BF16)
        O2f = sbt(stB, "p6_O2", [128, 2], F32)
        pb.emit(G, lambda E: E.memset(o1b2[:], 1.0))
        pb.emit(G, lambda E: E.memset(O2f[:], 0.0))
        pb.emit(G, lambda E: E.memset(O2f[0:64, 0:1], 1.0))
        pb.emit(G, lambda E: E.memset(O2f[64:128, 1:2], 1.0))
        rfb = dxT[:, 0:T]
        kfb = dxT[:, T:2 * T]
        rbb = dxT[:, 2 * T:3 * T]
        kbb = dxT[:, 3 * T:4 * T]
        gnb2 = dxT[:, 4 * T:4 * T + 4096]
        ysbf = dxT[:, 6 * T:8 * T].bitcast(F32)
        AT = [xnT[:, 0:16 * 512], xnT[:, 16 * 512:32 * 512]]

        for hp in range(NC):
            pb.wait("gpsimd", "mten")
            pb.dma("gpsimd", vhp[:].rearrange("p (tt n) -> p tt n", n=128),
                   vdr[hp].rearrange("(tt p) n -> p tt n", p=128), inc="qG")
            pb.dma("gpsimd", ghp[:], gdr[hp], inc="qG")
            pb.wait(Tn, "mvec")
            pb.wait(Tn, "mscl")
            for ts in range(TS):
                pb.emit(Tn, lambda E, hp=hp, ts=ts: E.matmul(
                    pbank[0], dw2b[:, 128 * hp:128 * (hp + 1)],
                    t1b[:, 512 * ts:512 * (ts + 1)], start=True, stop=True),
                    inc="mten")
                pb.wait(S, "mten")
                pb.emit(S, lambda E, hp=hp, ts=ts: E.activation(
                    A_[:, 512 * ts:512 * (ts + 1)], pbank[0], AF.Exp,
                    bias=pcol(COL_TDEC + hp)), inc="mscl")
            pb.wait(V, "mscl")
            pb.emit(V, lambda E: E.tensor_tensor_scan(
                B_[:], A_[:], zerosb[:], 0.0, ALU.add, ALU.add))
            pb.emit(V, lambda E: E.tensor_sub(C_[:], B_[:], A_[:]))
            pb.emit(V, lambda E: E.tensor_scalar(
                A_[:], B_[:], B_[:, 1024:1025], 60.0, ALU.subtract, ALU.min))
            pb.emit(V, lambda E: E.tensor_scalar_max(A_[:], A_[:], -60.0))
            pb.emit(V, lambda E: E.tensor_scalar(
                B_[:], C_[:], C_[:, 1024:1025], 60.0, ALU.subtract, ALU.min))
            pb.emit(V, lambda E: E.tensor_scalar_max(B_[:], B_[:], -60.0),
                    inc="mvec")
            pb.wait(S, "mvec")
            for (dst, srcC, sc, base) in ((rfb, A_, -1.0, rT), (kfb, A_, 1.0, kT),
                                          (rbb, B_, 1.0, rT), (kbb, B_, -1.0, kT)):
                pb.emit(S, lambda E, srcC=srcC, sc=sc: E.activation(
                    C_[:], srcC[:], AF.Exp, scale=sc), inc="mscl")
                pb.wait(V, "mscl")
                pb.emit(V, lambda E, dst=dst, base=base, hp=hp: E.tensor_mul(
                    dst, C_[:], base[:, T * hp:T * (hp + 1)]), inc="mvec")
                pb.wait(S, "mvec")
            if hp == 0 and s == 0 and "cfpd" in dbg_out:
                pb.wait("gpsimd", "mvec")
                pb.dma("gpsimd", dbg_out["cfpd"], A_[:], inc="qO")
                pb.dma("gpsimd", dbg_out["cbpd"], B_[:], inc="qO")
                pb.dma("gpsimd", dbg_out["rfbd"], rfb, inc="qO")
                pb.dma("gpsimd", dbg_out["kfbd"], kfb, inc="qO")
                pb.wait("gpsimd", "qO")
            pb.wait(Tn, "mvec")
            pb.wait(Tn, "qG")
            bkfree = {}
            sel_prev = {}
            yev_mark = [0]
            sq_guard = [0]
            apply_mark = [0]

            def emit_y(cy, atrdy):
                for mk in atrdy[cy]:
                    pb.wait(Tn, *mk)
                if cy == 0 and yev_mark[0]:
                    pb.wait(Tn, "mvec", yev_mark[0])
                for hl in (0, 1):
                    yb = (pgrp[1][0:64, 1024:1536] if hl == 0
                          else pgrp[1][64:128, 1536:2048])
                    pb.emit(Tn, lambda E, yb=yb, hl=hl, cy=cy: E.matmul(
                        yb, vhp[:, 128 * cy + 64 * hl:128 * cy + 64 * (hl + 1)],
                        AT[hl][:, 512 * cy:512 * (cy + 1)],
                        start=(cy == 0), stop=(cy == 15)), inc="mten")

            for qs in range(TS):
                pb.wait(S, "mten")
                pb.wait(V, "mten")
                atrdy = {}
                for c in range(16):
                    diag = 4 * qs <= c <= 4 * qs + 3
                    bkA, bkB = c % 2, 2 + c % 2
                    for bk in (bkA, bkB):
                        if bk in bkfree:
                            pb.wait(Tn, *bkfree[bk])
                    f_c = c < 4 * qs or diag
                    b_c = c > 4 * qs + 3 or diag
                    if f_c:
                        for hl, bk in ((0, bkA), (1, bkB)):
                            r0 = 64 * hl
                            pb.emit(Tn, lambda E, bk=bk, c=c, qs=qs, r0=r0:
                                    E.matmul(
                                pbank[bk], kfb[r0:r0 + 64, 128 * c:128 * (c + 1)],
                                rfb[r0:r0 + 64, 512 * qs:512 * (qs + 1)],
                                start=True, stop=True), inc="mten")
                    if b_c:
                        for hl, bk in ((0, bkA), (1, bkB)):
                            r0 = 64 * hl
                            if diag:
                                if hl in sel_prev:
                                    pb.wait(Tn, "mvec", sel_prev[hl])
                                ob = pgrp[1][:, 512 * hl:512 * (hl + 1)]
                            else:
                                ob = pbank[bk]
                            pb.emit(Tn, lambda E, ob=ob, c=c, qs=qs, r0=r0:
                                    E.matmul(
                                ob, kbb[r0:r0 + 64, 128 * c:128 * (c + 1)],
                                rbb[r0:r0 + 64, 512 * qs:512 * (qs + 1)],
                                start=True, stop=True), inc="mten")
                    if diag:
                        j = c - 4 * qs
                        if sel_prev:
                            pb.wait(S, "mvec", max(sel_prev.values()))
                        pb.wait(S, "mten")
                        pb.emit(S, lambda E, bkA=bkA: E.copy(afmp[:], pbank[bkA]),
                                inc="mscl")
                        bkfree[bkA] = ("mscl", pb.cnt["mscl"])
                        pb.emit(S, lambda E, bkB=bkB: E.copy(afmp1[:], pbank[bkB]),
                                inc="mscl")
                        bkfree[bkB] = ("mscl", pb.cnt["mscl"])
                        pb.wait(V, "mscl")
                        pb.wait(V, "mten")
                        pb.emit(V, lambda E, c=c, j=j: E.select(
                            AT[0][:, 512 * c:512 * (c + 1)],
                            masksb[:, 512 * j:512 * (j + 1)],
                            afmp[:], pgrp[1][:, 0:512]), inc="mvec")
                        sel_prev[0] = pb.cnt["mvec"]
                        pb.emit(V, lambda E, c=c, j=j: E.select(
                            AT[1][:, 512 * c:512 * (c + 1)],
                            masksb[:, 512 * j:512 * (j + 1)],
                            afmp1[:], pgrp[1][:, 512:1024]), inc="mvec")
                        sel_prev[1] = pb.cnt["mvec"]
                        atrdy[c] = [("mvec", sel_prev[1])]
                    else:
                        pb.wait(S, "mten")
                        pb.emit(S, lambda E, c=c, bkA=bkA: E.copy(
                            AT[0][:, 512 * c:512 * (c + 1)], pbank[bkA]),
                            inc="mscl")
                        bkfree[bkA] = ("mscl", pb.cnt["mscl"])
                        pb.wait(V, "mten")
                        pb.emit(V, lambda E, c=c, bkB=bkB: E.tensor_copy(
                            AT[1][:, 512 * c:512 * (c + 1)], pbank[bkB]),
                            inc="mvec")
                        bkfree[bkB] = ("mvec", pb.cnt["mvec"])
                        atrdy[c] = [bkfree[bkA], bkfree[bkB]]
                    if c >= 2:
                        emit_y(c - 2, atrdy)
                if (hp == 0 and qs == 0 and s == 0 and "ATd" in dbg_out):
                    pb.wait("gpsimd", "mscl")
                    pb.wait("gpsimd", "mvec")
                    pb.dma("gpsimd", dbg_out["ATd"], AT[0], inc="qO")
                    pb.wait("gpsimd", "qO")
                emit_y(14, atrdy)
                emit_y(15, atrdy)
                pb.wait(V, "mten")
                pb.emit(V, lambda E, qs=qs: E.tensor_copy(
                    ysbf[0:64, 512 * qs:512 * (qs + 1)],
                    pgrp[1][0:64, 1024:1536]))
                pb.emit(V, lambda E, qs=qs: E.tensor_copy(
                    ysbf[64:128, 512 * qs:512 * (qs + 1)],
                    pgrp[1][64:128, 1536:2048]), inc="mvec")
                if (hp == 0 and qs == 0 and s == 0 and "yd" in dbg_out):
                    pb.wait("gpsimd", "mvec")
                    pb.dma("gpsimd", dbg_out["yd"], ysbf[0:64, 0:512], inc="qO")
                    pb.wait("gpsimd", "qO")
                if sq_guard[0]:
                    pb.wait(S, "mten", sq_guard[0])
                pb.wait(S, "mvec")
                pb.emit(S, lambda E, qs=qs: E.activation(
                    ysq1[:], ysbf[:, 512 * qs:512 * (qs + 1)], AF.Square),
                    inc="mscl")
                pb.wait(Tn, "mvec")
                pb.wait(Tn, "mscl")
                pb.emit(Tn, lambda E, qs=qs: E.matmul(
                    pgrp[1][64:66, 1024:1536], O2f[:],
                    ysbf[:, 512 * qs:512 * (qs + 1)],
                    start=True, stop=True), inc="mten")
                pb.emit(Tn, lambda E: E.matmul(
                    pgrp[1][64:66, 1536:2048], O2f[:], ysq1[:],
                    start=True, stop=True), inc="mten")
                sq_guard[0] = pb.cnt["mten"]
                pb.wait(V, "mten")
                pb.emit(V, lambda E, qs=qs: E.tensor_copy(
                    A_[64:66, 512 * qs:512 * (qs + 1)],
                    pgrp[1][64:66, 1024:1536]))
                pb.emit(V, lambda E, qs=qs: E.tensor_copy(
                    B_[64:66, 512 * qs:512 * (qs + 1)],
                    pgrp[1][64:66, 1536:2048]), inc="mvec")
                yev_mark[0] = pb.cnt["mvec"]
            pb.emit(V, lambda E: E.tensor_scalar(
                C_[64:66, 0:T], A_[64:66, 0:T], 1.0 / 64, None, ALU.mult),
                inc="mvec")
            pb.wait(S, "mvec")
            pb.emit(S, lambda E: E.activation(
                A_[64:66, 0:T], C_[64:66, 0:T], AF.Square), inc="mscl")
            pb.emit(V, lambda E: E.tensor_scalar(
                B_[64:66, 0:T], B_[64:66, 0:T], 1.0 / 64, None, ALU.mult))
            pb.wait(V, "mscl")
            pb.emit(V, lambda E: E.tensor_sub(
                B_[64:66, 0:T], B_[64:66, 0:T], A_[64:66, 0:T]), inc="mvec")
            pb.wait(S, "mvec")
            pb.emit(S, lambda E: E.activation(
                A_[64:66, 0:T], B_[64:66, 0:T], AF.Sqrt, bias=6.4e-4),
                inc="mscl")
            pb.wait(V, "mscl")
            pb.emit(V, lambda E: E.reciprocal(A_[64:66, 0:T], A_[64:66, 0:T]))
            pb.emit(V, lambda E: E.tensor_copy(gnb2[64:66, 0:2048],
                                               C_[64:66, 0:T]))
            pb.emit(V, lambda E: E.tensor_copy(gnb2[64:66, 2048:4096],
                                               A_[64:66, 0:T]), inc="mvec")
            pb.wait(G, "mvec")
            pb.dma("gpsimd", gnb2[0:1, 0:4096], gnb2[64:65, 0:4096], inc="qG")
            pb.dma("gpsimd", gnb2[32:33, 0:4096], gnb2[65:66, 0:4096], inc="qG")
            pb.wait(Tn, "qG")
            for qs in range(TS):
                if apply_mark[0]:
                    pb.wait(Tn, "mvec", apply_mark[0])
                for hl, rb in ((0, 0), (1, 32)):
                    ro = slice(64 * hl, 64 * (hl + 1))
                    pb.emit(Tn, lambda E, ro=ro, rb=rb, qs=qs: E.matmul(
                        pgrp[1][ro, 1024:1536], o1b2[rb:rb + 1, :],
                        gnb2[rb:rb + 1, 512 * qs:512 * (qs + 1)],
                        start=True, stop=True))
                    pb.emit(Tn, lambda E, ro=ro, rb=rb, qs=qs: E.matmul(
                        pgrp[1][ro, 1536:2048], o1b2[rb:rb + 1, :],
                        gnb2[rb:rb + 1, 2048 + 512 * qs:2048 + 512 * (qs + 1)],
                        start=True, stop=True), inc="mten")
                pb.wait(V, "mten")
                for hl in (0, 1):
                    ro = slice(64 * hl, 64 * (hl + 1))
                    ysl = ysbf[64 * hl:64 * (hl + 1), 512 * qs:512 * (qs + 1)]
                    pb.emit(V, lambda E, ysl=ysl, ro=ro: E.tensor_sub(
                        ysl, ysl, pgrp[1][ro, 1024:1536]))
                    pb.emit(V, lambda E, ysl=ysl, ro=ro: E.tensor_mul(
                        ysl, ysl, pgrp[1][ro, 1536:2048]))
                    pb.emit(V, lambda E, ysl=ysl, hl=hl, hp=hp: E.tensor_scalar(
                        ysl, ysl,
                        packA_s[64 * hl:64 * (hl + 1),
                                COL_LNW + hp:COL_LNW + hp + 1],
                        packA_s[64 * hl:64 * (hl + 1),
                                COL_LNB + hp:COL_LNB + hp + 1],
                        ALU.mult, ALU.add))
                    pb.emit(V, lambda E, ysl=ysl, hl=hl, qs=qs: E.tensor_mul(
                        ghp[64 * hl:64 * (hl + 1), 512 * qs:512 * (qs + 1)],
                        ghp[64 * hl:64 * (hl + 1), 512 * qs:512 * (qs + 1)],
                        ysl), inc="mvec")
                apply_mark[0] = pb.cnt["mvec"]
            pb.wait(G, "mvec")
            pb.dma("gpsimd", gdr[hp], ghp[:], inc="qG")
        pb.barrier("p6_lion")
        if "zd" in dbg_out and s == 0:
            pb.dma("gpsimd", dbg_out["zd"], gdr, inc="qO")
            pb.barrier("p6_zd")
        stB.close()

        stC = ExitStack()
        WoTb = kT[:, 0:8192]
        dts = kT[:, 8192:16384].bitcast(F32)
        zts = rT[:, 0:4096]
        xtt = rT[:, 4096:6144].bitcast(F32)
        pb.dma("gpsimd", WoTb, WoT, inc="qG")
        pb.wait(Tn, "qG")
        op_marks = {}
        for ts in range(TS):
            pb.wait("sync", "mten")
            pb.dma("sync", zts[:].rearrange("p (c n) -> p c n", n=512),
                   gdr.rearrange("c p n -> p c n")[:, :, 512 * ts:512 * (ts + 1)],
                   inc="qS")
            pb.wait(Tn, "qS")
            for oc in range(NC):
                bk = oc % 2
                if bk in op_marks:
                    pb.wait(Tn, "mvec", op_marks[bk])
                for dc in range(NC):
                    pb.emit(Tn, lambda E, bk=bk, oc=oc, dc=dc: E.matmul(
                        pbank[bk],
                        WoTb[:, 1024 * dc + 128 * oc:1024 * dc + 128 * (oc + 1)],
                        zts[:, 512 * dc:512 * (dc + 1)],
                        start=(dc == 0), stop=(dc == 7)), inc="mten")
                pb.wait(V, "mten")
                pb.emit(V, lambda E, oc=oc, bk=bk: E.tensor_copy(
                    dts[:, 512 * oc:512 * (oc + 1)], pbank[bk]), inc="mvec")
                op_marks[bk] = pb.cnt["mvec"]
            pb.wait(Tn, "mvec")
            for j in range(4):
                tt = 4 * ts + j
                pb.wait("sync", "mvec")
                pb.wait("sync", "qO")
                pb.dma("sync", xtt[:], src_x[s, 128 * tt:128 * (tt + 1), :],
                       inc="qS")
                pb.wait(Tn, "mvec")
                for oc in range(NC):
                    pb.emit(Tn, lambda E, j=j, oc=oc: E.transpose(
                        pgrp[1][:, 128 * oc:128 * (oc + 1)],
                        dts[:, 512 * oc + 128 * j:512 * oc + 128 * (j + 1)],
                        ident[:]), inc=("mten" if oc == 7 else None))
                qs_t = pb.cnt["qS"]
                pb.wait(V, "qS", qs_t)
                pb.wait(V, "mten")
                pb.emit(V, lambda E: E.tensor_add(xtt[:], xtt[:],
                                                  pgrp[1][:, 0:1024]),
                        inc="mvec")
                pb.wait(G, "mvec")
                pb.dma("gpsimd", dst_x[s, 128 * tt:128 * (tt + 1), :], xtt[:],
                       inc="qO")
        pb.barrier("p6_end")
        stC.close()
        stP.close()
        res_seq.pop("dxT_st").close()
        res_seq.pop("dxT")

    for s in range(SEQS):
        if "P1" in phases:
            emit_P1(s, x_in)
        if "P2" in phases:
            emit_P2(s, x_in, xdr)
        if "P3" in phases:
            emit_P3(s, xdr, xdr)
        if "P4" in phases:
            emit_P4(s, xdr)
        if "P5" in phases:
            emit_P5(s, xdr, xdr)
        if "P6" in phases:
            emit_P6(s, xdr, out)

    if "P6" not in phases:
        with nc.sbuf_tensor("cp_buf", [128, 1024], F32) as cpb:
            for s in range(SEQS):
                for tt in range(TT):
                    pb.dma("sync", cpb[:], x_in[s, 128 * tt:128 * (tt + 1), :],
                           inc="qS")
                    pb.wait("sync", "qS")
                    pb.dma("sync", out[s, 128 * tt:128 * (tt + 1), :], cpb[:],
                           inc="qO")
                    pb.wait("sync", "qO")
            pb.barrier("out_done")
    pb.wait("gpsimd", "qO", pb.cnt["qO"])
    pb.flush()
    ctx.close()
    return nc




def _pack_cols(inp):
    pk = np.zeros((128, 120), np.float32)

    def put(base, vec, n):
        v = np.asarray(vec, np.float32).reshape(-1)
        pk[:, base:base + n] = v.reshape(n, 128).T
    put(0, inp["L0_attn_norm_w"], 8)
    put(8, inp["L1_attn_time_maa_x"], 8)
    put(16, inp["L1_attn_time_maa_w"], 8)
    put(24, inp["L1_attn_time_maa_k"], 8)
    put(32, inp["L1_attn_time_maa_v"], 8)
    put(40, inp["L1_attn_time_maa_r"], 8)
    put(48, inp["L1_attn_time_maa_g"], 8)
    put(56, inp["L1_attn_time_decay"], 8)
    put(64, inp["L1_attn_ln_w"], 8)
    put(72, inp["L1_attn_ln_b"], 8)
    put(80, inp["L1_ffn_b1"], 32)
    put(112, inp["L1_ffn_b2"], 8)
    return pk


def _consts():
    ident = np.eye(128, dtype=np.float32)
    masks = np.zeros((4, 128, 512), np.uint8)
    kp = np.arange(128)[:, None]
    qf = np.arange(512)[None, :]
    for j in range(4):
        masks[j] = (qf >= kp + 128 * j).astype(np.uint8)
    return ident, masks


def _pm(a):
    a = np.asarray(a, np.float32)
    C = a.shape[0] // 128
    return np.ascontiguousarray(
        a.reshape(C, 128, a.shape[1]).transpose(1, 0, 2).reshape(128, -1))


def _core_inputs(inp, core):
    ident, masks = _consts()
    f32c = np.ascontiguousarray
    wqkv = np.asarray(inp["L0_attn_w_qkv"], np.float32)
    wqkv_oc = np.ascontiguousarray(
        wqkv.reshape(8, 128, 24, 128).transpose(2, 1, 0, 3).reshape(
            24, 128, 1024))
    w1T = np.asarray(inp["L1_ffn_w1"], np.float32).T
    w1t_g = np.ascontiguousarray(
        w1T.reshape(8, 128, 8, 2, 256).transpose(2, 3, 1, 0, 4).reshape(
            8, 2, 128, 2048))
    w2T = np.asarray(inp["L1_ffn_w2"], np.float32).T
    w2t_g = np.ascontiguousarray(
        w2T.reshape(8, 2, 2, 128, 1024).transpose(0, 1, 3, 2, 4).reshape(
            8, 2, 128, 2048))
    gwT = np.asarray(inp["L0_ffn_gate_w"], np.float32).T
    gate_wT = np.ascontiguousarray(
        gwT.reshape(8, 128, 16).transpose(1, 0, 2).reshape(128, 128))
    m = {
        "x": f32c(inp["L0_attn_x"][2 * core:2 * core + 2]),
        "wqkv_oc": wqkv_oc,
        "wproj": _pm(inp["L0_attn_w_proj"]),
        "gate_w": inp["L0_ffn_gate_w"],
        "gate_wT": gate_wT,
        "wg_e": np.stack([_pm(w) for w in inp["L0_ffn_w_gate"]]),
        "wu_e": np.stack([_pm(w) for w in inp["L0_ffn_w_up"]]),
        "wd_e": np.stack([_pm(w) for w in inp["L0_ffn_w_down"]]),
        "sw_g": _pm(inp["L0_ffn_sw_gate"]),
        "sw_u": _pm(inp["L0_ffn_sw_up"]),
        "sw_d": _pm(inp["L0_ffn_sw_down"]),
        "maa_w1": _pm(inp["L1_attn_time_maa_w1"]),
        "maa_w2": inp["L1_attn_time_maa_w2"],
        "dec_w1": _pm(inp["L1_attn_time_decay_w1"]),
        "dec_w2": inp["L1_attn_time_decay_w2"],
        "WrT": _pm(np.asarray(inp["L1_attn_W_r"], np.float32).T),
        "WkT": _pm(np.asarray(inp["L1_attn_W_k"], np.float32).T),
        "WvT": _pm(np.asarray(inp["L1_attn_W_v"], np.float32).T),
        "WgT": _pm(np.asarray(inp["L1_attn_W_g"], np.float32).T),
        "WoT": _pm(np.asarray(inp["L1_attn_W_o"], np.float32).T),
        "w1": inp["L1_ffn_w1"],
        "w1t_g": w1t_g,
        "w2": inp["L1_ffn_w2"],
        "w2t_g": w2t_g,
        "packA": _pack_cols(inp),
        "ehot": np.kron(np.eye(16), np.ones((1, 128))).astype(ml_dtypes.bfloat16),
        "ident_f32": ident,
        "lion_masks": masks,
    }
    return {k: np.ascontiguousarray(v) if k in ("ehot", "lion_masks") else
            np.ascontiguousarray(np.asarray(v, np.float32)) for k, v in m.items()}


def kernel(**inputs):
    nc = build_program()
    in_maps = [_core_inputs(inputs, c) for c in range(NCORES)]
    res = run_bass_kernel_spmd(nc, in_maps, core_ids=list(range(NCORES)))
    outs = [r["out"] for r in res.results]
    return np.concatenate(outs, axis=0).astype(np.float32)


# version 27
# speedup vs baseline: 1.2092x; 1.0063x over previous
import numpy as np
import ml_dtypes
from contextlib import ExitStack

import concourse.bass as bass
import concourse.mybir as mybir
from concourse.bass_utils import run_bass_kernel_spmd

F32 = mybir.dt.float32
BF16 = mybir.dt.bfloat16
AF = mybir.ActivationFunctionType
ALU = mybir.AluOpType
AX = mybir.AxisListType

NCORES = 8
SEQS = 2
T = 2048
D = 1024
NC = 8
TT = 16
TS = 4
NH = 16
HD = 64



class PB:
    ENGS = ("tensor", "vector", "scalar", "gpsimd", "sync")

    def __init__(self, nc, block, ctx):
        self.nc = nc
        self.block = block
        self.ctx = ctx
        self.buf = {e: [] for e in self.ENGS}
        self.sems = {}
        self.cnt = {}
        self.dirty = {e: False for e in self.ENGS}

    def sem(self, name):
        if name not in self.sems:
            self.sems[name] = self.ctx.enter_context(self.nc.semaphore(name))
            self.cnt[name] = 0
        return self.sems[name]

    CHAIN = {"vector": "mvec", "scalar": "mscl", "gpsimd": "mgps"}

    def emit(self, eng, fn, inc=None, n=1):
        ch = self.CHAIN.get(eng)
        if ch is not None and inc is None:
            inc, n = ch, 1
        if ch is not None:
            h = self.sem(ch)
            v = self.cnt[ch]
            if v > 0:
                self.buf[eng].append(lambda E, h=h, v=v: E.wait_ge(h, v))
        if inc is not None:
            h2 = self.sem(inc)
            self.cnt[inc] += n
            self.buf[eng].append(lambda E, fn=fn, h=h2, n=n: fn(E).then_inc(h2, n))
        else:
            self.buf[eng].append(fn)
        if eng != "sync":
            self.dirty[eng] = True

    def wait(self, eng, sem, val=None):
        if val is None:
            val = self.cnt.get(sem, 0)
        if val <= 0:
            return
        h = self.sem(sem)
        self.buf[eng].append(lambda E, h=h, v=val: E.wait_ge(h, v))

    def dma(self, queue, out, in_, inc, **kw):
        self.emit(queue, lambda E, o=out, i=in_, kw=kw: E.dma_start(out=o, in_=i, **kw),
                  inc=inc, n=16)

    def flush(self):
        for eng in self.ENGS:
            ops = self.buf[eng]
            if not ops:
                continue
            self.buf[eng] = []

            def section(E, ops=ops):
                for op in ops:
                    op(E)
            getattr(self.block, eng)(section)

    def barrier(self, name=""):
        active = [e for e in ("tensor", "vector", "scalar", "gpsimd")
                  if self.dirty[e]]
        for e in active:
            self.emit(e, lambda E: E.nop(), inc="m" + e)
        targets = {e: self.cnt["m" + e] for e in active}
        dma_sems = [s for s in self.cnt if s.startswith("q")]
        for e in self.ENGS:
            for o in active:
                if o != e:
                    self.wait(e, "m" + o, targets[o])
            for s in dma_sems:
                if self.cnt[s] > 0:
                    self.wait(e, s, self.cnt[s])
        for e in active:
            self.dirty[e] = False
        self.flush()




def build_program(dbg=(), phases=("P1", "P2", "P3", "P4", "P5", "P6")):
    nc = bass.Bass("TRN2", target_bir_lowering=False, debug=False)

    def din(name, shape, dt=F32):
        return nc.dram_tensor(name, shape, dt, kind="ExternalInput").ap()

    x_in = din("x", [SEQS, T, D])
    wqkv_oc = din("wqkv_oc", [24, 128, 1024])
    wproj = din("wproj", [128, NC * 1024])
    gate_w = din("gate_w", [16, D])
    gate_wT = din("gate_wT", [128, 128])
    wg_e = din("wg_e", [16, 128, NC * 512])
    wu_e = din("wu_e", [16, 128, NC * 512])
    wd_e = din("wd_e", [16, 128, 4 * 1024])
    sw_g = din("sw_g", [128, NC * 512])
    sw_u = din("sw_u", [128, NC * 512])
    sw_d = din("sw_d", [128, 4 * 1024])
    maa_w1 = din("maa_w1", [128, NC * 160])
    maa_w2 = din("maa_w2", [5, 32, D])
    dec_w1 = din("dec_w1", [128, NC * 64])
    dec_w2 = din("dec_w2", [64, D])
    WrT = din("WrT", [128, NC * 1024])
    WkT = din("WkT", [128, NC * 1024])
    WvT = din("WvT", [128, NC * 1024])
    WgT = din("WgT", [128, NC * 1024])
    WoT = din("WoT", [128, NC * 1024])
    w1 = din("w1", [4096, D])
    w1t_g = din("w1t_g", [8, 2, 128, NC * 256])
    w2 = din("w2", [D, 4096])
    w2t_g = din("w2t_g", [8, 2, 128, 2 * 1024])
    packA = din("packA", [128, 120])
    ehot = din("ehot", [16, 2048], BF16)
    ident_f32 = din("ident_f32", [128, 128])
    lion_masks = din("lion_masks", [4, 128, 512], mybir.dt.uint8)

    out = nc.dram_tensor("out", [SEQS, T, D], F32, kind="ExternalOutput").ap()
    xdr = nc.dram_tensor("xdr", [SEQS, T, D], F32, kind="Internal").ap()
    vdr = nc.dram_tensor("vdr", [NC, T, 128], BF16, kind="Internal").ap()
    gdr = nc.dram_tensor("gdr", [NC, 128, T], BF16, kind="Internal").ap()

    dbg_out = {}
    for name, shape in dbg:
        dbg_out[name] = nc.dram_tensor("dbg_" + name, shape, F32,
                                       kind="ExternalOutput").ap()

    ctx = ExitStack()

    def sb(name, shape, dt):
        return ctx.enter_context(nc.sbuf_tensor(name, shape, dt))

    packA_s = sb("packA_s", [128, 120], F32)
    cvals = sb("cvals", [128, 8], F32)
    ident = sb("ident", [128, 128], F32)
    gwT_s = sb("gwT_s", [128, 128], F32)
    gwT_hi = sb("gwT_hi", [128, 128], BF16)
    gwT_lo = sb("gwT_lo", [128, 128], BF16)
    invgw = sb("invgw", [16, 1], F32)
    alpha1 = sb("alpha1", [128, 32], F32)
    alpha2 = sb("alpha2", [128, 8], F32)
    ones1 = sb("ones1", [1, 128], F32)
    onescol = sb("onescol", [128, 1], F32)

    COL_NORMW = 0
    COL_MAA = 8
    COL_TDEC = 56
    COL_LNW = 64
    COL_LNB = 72
    COL_B1 = 80
    COL_B2 = 112

    def pcol(c):
        return packA_s[:, c:c + 1]

    pgrp = [ctx.enter_context(nc.psum_tensor(f"pg{i}", [128, 2048], F32))
            for i in range(2)]
    pbank = [pgrp[i // 4][:, 512 * (i % 4):512 * (i % 4 + 1)] for i in range(8)]

    block = ctx.enter_context(nc.Block())
    pb = PB(nc, block, ctx)

    uid = [0]

    def usem(tag):
        uid[0] += 1
        return f"u{uid[0]}_{tag}"

    def sbt(stack, name, shape, dt):
        uid[0] += 1
        return stack.enter_context(nc.sbuf_tensor(f"{name}_{uid[0]}", shape, dt))

    pb.dma("sync", packA_s[:], packA, inc="qS")
    pb.dma("sync", ident[:], ident_f32, inc="qS")
    pb.dma("sync", gwT_s[:], gate_wT, inc="qS")

    for i, v in enumerate([1e-5, 1e-6, 6.4e-4, 1024.0, 0.125, -1.0, -0.5]):
        pb.emit("gpsimd", lambda E, i=i, v=v: E.memset(cvals[:, i:i + 1], v))
        nc.const_aps.register(F32, v, cvals[:, i:i + 1])

    pb.emit("gpsimd", lambda E: E.memset(ones1[:], 1.0))
    pb.emit("gpsimd", lambda E: E.memset(onescol[:], 1.0))

    pb.barrier("preamble")

    xnT = sb("xnT", [128, NC * T], BF16)
    res_seq = {}

    def emit_P1(s, src_x):
        st = ExitStack()
        hrmsT = res_seq["hrmsT"] = ctx.enter_context(
            nc.sbuf_tensor(f"hrmsT{s}", [128, NC * T], BF16)) \
            if "hrmsT" not in res_seq else res_seq["hrmsT"]
        combT = res_seq["combT"] = ctx.enter_context(
            nc.sbuf_tensor(f"combT{s}", [16, T], BF16)) \
            if "combT" not in res_seq else res_seq["combT"]

        xbuf = sbt(st, "p1_x", [128, 2 * 1024], F32)
        xsq = sbt(st, "p1_sq", [128, 1024], F32)
        stat = sbt(st, "p1_st", [128, 8], F32)
        xntt = sbt(st, "p1_xn", [128, 1024], F32)
        xnrm = sbt(st, "p1_xr", [128, 1024], F32)
        xnTf = sbt(st, "p1_xnTf", [128, 1024], F32)
        xh = sbt(st, "p1_xh", [128, 1024], BF16)
        xl = sbt(st, "p1_xl", [128, 1024], BF16)
        logitsT = sbt(st, "p1_lg", [16, T], F32)
        rowt = sbt(st, "p1_rt", [1, 128], F32)
        dbg_praw = sbt(st, "p1_dpr", [16, 128], F32)
        dbg_rbc = sbt(st, "p1_drb", [16, 128], F32)
        rowt_hi = sbt(st, "p1_rth", [1, 128], BF16)
        rowt_lo = sbt(st, "p1_rtl", [1, 128], BF16)
        ones1b = sbt(st, "p1_o1b", [1, 16], BF16)
        pb.emit("gpsimd", lambda E: E.memset(ones1b[:], 1.0))
        rbc = sbt(st, "p1_rbc", [16, 128], F32)

        if s == 0:
            presq = xbuf[0:16, 0:1024]
            pb.dma("sync", presq, gate_w, inc="qS")
            pb.wait("vector", "qS")
            pb.wait("scalar", "qS")
            pb.emit("scalar", lambda E: E.activation(presq, presq, AF.Square),
                    inc="mscl")
            pb.wait("vector", "mscl")
            pb.emit("vector", lambda E: E.reduce_sum(invgw[:], presq, AX.X))
            pb.wait("scalar", "mvec")
            pb.emit("scalar", lambda E: E.activation(invgw[:], invgw[:], AF.Sqrt))
            pb.wait("vector", "mscl")
            pb.emit("vector", lambda E: E.tensor_scalar_max(invgw[:], invgw[:],
                                                            1e-12))
            pb.emit("vector", lambda E: E.reciprocal(invgw[:], invgw[:]))
            pb.emit("vector", lambda E: E.tensor_copy(gwT_hi[:], gwT_s[:]))
            pb.emit("vector", lambda E: E.tensor_sub(gwT_lo[:], gwT_s[:],
                                                     gwT_hi[:]), inc="mvec")
            pb.wait("sync", "mvec")

        for tt in range(TT):
            V, S, Tn, G = "vector", "scalar", "tensor", "gpsimd"
            xb = xbuf[:, 1024 * (tt % 2):1024 * (tt % 2 + 1)]
            pb.wait("sync", "mvec")
            pb.wait("sync", "mscl")
            pb.dma("sync", xb, src_x[s, 128 * tt:128 * (tt + 1), :], inc="qS")
            qs_tgt = pb.cnt["qS"]
            mean, sumsq = stat[:, 0:1], stat[:, 1:2]
            var, rstd = stat[:, 2:3], stat[:, 3:4]
            mxn2, rmsinv, invl2 = stat[:, 4:5], stat[:, 5:6], stat[:, 6:7]
            m2 = stat[:, 7:8]
            pb.wait(V, "qS", qs_tgt)
            pb.wait(S, "qS", qs_tgt)
            pb.emit(V, lambda E, xb=xb: E.reduce_sum(mean, xb, AX.X))
            pb.emit(V, lambda E: E.tensor_scalar_mul(mean, mean, 1.0 / D),
                    inc="mvec")
            pb.emit(S, lambda E, xb=xb: E.activation(xsq[:], xb, AF.Square),
                    inc="mscl")
            pb.wait(V, "mscl")
            pb.emit(V, lambda E: E.reduce_sum(sumsq, xsq[:], AX.X))
            pb.wait(S, "mvec")
            pb.emit(S, lambda E: E.activation(m2, mean, AF.Square), inc="mscl")
            pb.wait(V, "mscl")
            pb.emit(V, lambda E: E.tensor_scalar(var, sumsq, 1.0 / D, m2,
                                                 ALU.mult, ALU.subtract),
                    inc="mvec")
            pb.wait(S, "mvec")
            pb.emit(S, lambda E: E.activation(rstd, var, AF.Sqrt, bias=1e-5),
                    inc="mscl")
            pb.wait(V, "mscl")
            pb.emit(V, lambda E: E.reciprocal(rstd, rstd))
            pb.emit(V, lambda E: E.tensor_mul(mxn2, var, rstd))
            pb.emit(V, lambda E: E.tensor_mul(mxn2, mxn2, rstd), inc="mvec")
            pb.wait(S, "mvec")
            pb.emit(S, lambda E: E.activation(rmsinv, mxn2, AF.Sqrt, bias=1e-6))
            pb.emit(S, lambda E: E.activation(invl2, mxn2, AF.Sqrt,
                                              scale=float(D)), inc="mscl")
            pb.wait(V, "mscl")
            pb.emit(V, lambda E: E.reciprocal(rmsinv, rmsinv))
            pb.emit(V, lambda E: E.tensor_scalar_max(invl2, invl2, 1e-12))
            pb.emit(V, lambda E: E.reciprocal(invl2, invl2))
            pb.emit(V, lambda E, xb=xb: E.tensor_scalar(
                xntt[:], xb, mean, rstd, ALU.subtract, ALU.mult))
            pb.emit(V, lambda E: E.tensor_scalar_mul(xnrm[:], xntt[:], rmsinv),
                    inc="mvec")
            pb.wait(Tn, "mvec")
            for j in range(8):
                bank = pbank[j // 4]
                pb.emit(Tn, lambda E, j=j, b=bank: E.transpose(
                    b[:, 128 * (j % 4):128 * (j % 4 + 1)],
                    xntt[:, 128 * j:128 * (j + 1)], ident[:]))
            for j in range(8):
                bank = pbank[2 + j // 4]
                pb.emit(Tn, lambda E, j=j, b=bank: E.transpose(
                    b[:, 128 * (j % 4):128 * (j % 4 + 1)],
                    xnrm[:, 128 * j:128 * (j + 1)], ident[:]),
                    inc=("mten" if j == 7 else None))
            pb.wait(V, "mten")
            pb.wait(S, "mten")
            pb.emit(V, lambda E: E.tensor_copy(xnTf[:, 0:512], pbank[0][:]))
            pb.emit(V, lambda E: E.tensor_copy(xnTf[:, 512:1024], pbank[1][:]),
                    inc="mvec")
            v5 = pb.cnt["mvec"]
            for half in range(2):
                for cq in range(4):
                    c = 4 * half + cq
                    dst = hrmsT[:, T * c + 128 * tt:T * c + 128 * (tt + 1)]
                    srcq = pbank[2 + half][:, 128 * cq:128 * (cq + 1)]
                    pb.emit(S, lambda E, d=dst, s2=srcq, c=c: E.activation(
                        d, s2, AF.Copy, scale=pcol(COL_NORMW + c)),
                        inc=("mscl" if c == 7 else None))
            s5 = pb.cnt["mscl"]
            pb.emit(V, lambda E: E.tensor_copy(xh[:], xnTf[:]))
            pb.emit(V, lambda E: E.tensor_sub(xl[:], xnTf[:], xh[:]))
            v5b = pb.cnt["mvec"]
            pb.wait(G, "mvec", v5b)
            dstg = xnT[:].rearrange("p (c n) -> p c n", n=T)[
                :, :, 128 * tt:128 * (tt + 1)]
            pb.emit(G, lambda E, d=dstg: E.tensor_copy(
                d, xh[:].rearrange("p (c n) -> p c n", n=128)), inc="mgps")
            g1 = pb.cnt["mgps"]
            pb.wait(Tn, "mvec", v5b)
            for c in range(8):
                for t3, (lw, rx) in enumerate(((gwT_hi, xh), (gwT_hi, xl),
                                               (gwT_lo, xh))):
                    pb.emit(Tn, lambda E, c=c, t3=t3, lw=lw, rx=rx: E.matmul(
                        pbank[4][0:16, 0:128], lw[:, 16 * c:16 * (c + 1)],
                        rx[:, 128 * c:128 * (c + 1)],
                        start=(c == 0 and t3 == 0), stop=(c == 7 and t3 == 2)))
            pb.emit(Tn, lambda E: E.transpose(pbank[5][0:1, 0:128], invl2,
                                              ident[:]), inc="mten")
            pb.wait(V, "mten")
            pb.emit(V, lambda E: E.tensor_copy(rowt[:], pbank[5][0:1, 0:128]))
            pb.emit(V, lambda E: E.tensor_copy(rowt_hi[:], rowt[:]))
            pb.emit(V, lambda E: E.tensor_sub(rowt_lo[:], rowt[:], rowt_hi[:]),
                    inc="mvec")
            pb.wait(Tn, "mvec")
            pb.emit(Tn, lambda E: E.matmul(pbank[5][0:16, 128:256],
                                           ones1b[:], rowt_hi[:],
                                           start=True, stop=False))
            pb.emit(Tn, lambda E: E.matmul(pbank[5][0:16, 128:256],
                                           ones1b[:], rowt_lo[:],
                                           start=False, stop=True),
                    inc="mten")
            pb.wait(S, "mten")
            pb.emit(S, lambda E: E.copy(rbc[:], pbank[5][0:16, 128:256]),
                    inc="mscl")
            pb.wait(V, "mscl")
            if tt == 0 and "t0praw" in dbg_out and s == 0:
                pb.emit(V, lambda E: E.tensor_copy(
                    dbg_praw[:], pbank[4][0:16, 0:128]))
                pb.emit(V, lambda E: E.tensor_copy(dbg_rbc[:], rbc[:]))
            pb.emit(V, lambda E: E.tensor_scalar_mul(rbc[:], rbc[:],
                                                     invgw[:]))
            pb.emit(V, lambda E, tt=tt: E.tensor_mul(
                logitsT[:, 128 * tt:128 * (tt + 1)], pbank[4][0:16, 0:128],
                rbc[:]), inc="mvec")
            pb.wait(Tn, "mvec")
            pb.wait(Tn, "mscl", s5)
            pb.wait(Tn, "mgps", g1)
            pb.wait(V, "mgps", g1)
            if tt == 0 and "t0stat" in dbg_out and s == 0:
                pb.wait("gpsimd", "mvec")
                pb.wait("gpsimd", "mscl")
                pb.dma("gpsimd", dbg_out["t0stat"], stat[:], inc="qO")
                pb.dma("gpsimd", dbg_out["t0xn"], xntt[:], inc="qO")
                pb.dma("gpsimd", dbg_out["t0xnTf"], xnTf[:], inc="qO")
                pb.wait("gpsimd", "qO")
        scoresT = logitsT
        pb.barrier("p1_tiles")
        if "rawlog" in dbg_out and s == 0:
            pb.dma("sync", dbg_out["rawlog"], logitsT[:], inc="qO")
            pb.wait("scalar", "qO")
        pb.emit("scalar", lambda E: E.activation(scoresT[:], logitsT[:],
                                                 AF.Sigmoid), inc="mscl")
        sct = sbt(st, "p1_sct", [128, 16], F32)
        max8 = sbt(st, "p1_m8", [128, 8], F32)
        pk2 = sbt(st, "p1_pk", [128, 2], F32)
        thr_row = sbt(st, "p1_rt0", [1, T], F32)
        rec_row = sbt(st, "p1_rt1", [1, T], F32)
        thr_b = sbt(st, "p1_thb", [16, T], F32)
        rec_b = sbt(st, "p1_reb", [16, T], F32)
        pb.wait("tensor", "mscl", pb.cnt["mscl"])
        for tt in range(TT):
            V, S, Tn = "vector", "scalar", "tensor"
            pb.emit(Tn, lambda E, tt=tt: E.transpose(
                pbank[0][:, 0:16], scoresT[:, 128 * tt:128 * (tt + 1)],
                ident[0:16, 0:16]), inc="mten")
            pb.wait(V, "mten", pb.cnt["mten"])
            pb.emit(V, lambda E: E.tensor_copy(sct[:], pbank[0][:, 0:16]))
            pb.emit(V, lambda E: E.max(max8[:], sct[:]))
            pb.emit(V, lambda E: E.reduce_sum(pk2[:, 1:2], max8[:], AX.X))
            pb.emit(V, lambda E: E.reciprocal(pk2[:, 1:2], pk2[:, 1:2]))
            pb.emit(V, lambda E: E.tensor_copy(pk2[:, 0:1], max8[:, 7:8]),
                    inc="mvec")
            pb.wait(Tn, "mvec", pb.cnt["mvec"])
            pb.emit(Tn, lambda E: E.transpose(pbank[1][0:1, 0:128],
                                              pk2[:, 0:1], ident[:]))
            pb.emit(Tn, lambda E: E.transpose(pbank[1][0:1, 128:256],
                                              pk2[:, 1:2], ident[:]),
                    inc="mten")
            pb.wait(V, "mten", pb.cnt["mten"])
            pb.emit(V, lambda E, tt=tt: E.tensor_copy(
                thr_row[:, 128 * tt:128 * (tt + 1)], pbank[1][0:1, 0:128]))
            pb.emit(V, lambda E, tt=tt: E.tensor_copy(
                rec_row[:, 128 * tt:128 * (tt + 1)], pbank[1][0:1, 128:256]),
                inc="mvec")
            pb.wait(Tn, "mvec", pb.cnt["mvec"])
        for k in range(4):
            pb.emit("tensor", lambda E, k=k: E.matmul(
                pbank[2][0:16, 0:512], ones1[:, 0:16],
                thr_row[:, 512 * k:512 * (k + 1)], start=True, stop=True))
            pb.emit("tensor", lambda E, k=k: E.matmul(
                pbank[3][0:16, 0:512], ones1[:, 0:16],
                rec_row[:, 512 * k:512 * (k + 1)], start=True, stop=True),
                inc="mten")
            pb.wait("scalar", "mten", pb.cnt["mten"])
            pb.emit("scalar", lambda E, k=k: E.copy(
                thr_b[:, 512 * k:512 * (k + 1)], pbank[2][0:16, 0:512]))
            pb.emit("scalar", lambda E, k=k: E.copy(
                rec_b[:, 512 * k:512 * (k + 1)], pbank[3][0:16, 0:512]),
                inc="mscl")
            pb.wait("tensor", "mscl", pb.cnt["mscl"])
        pb.wait("vector", "mscl", pb.cnt["mscl"])
        pb.emit("vector", lambda E: E.tensor_tensor(thr_b[:], scoresT[:],
                                                    thr_b[:], ALU.is_ge))
        pb.emit("vector", lambda E: E.tensor_mul(combT[:], scoresT[:], thr_b[:]))
        pb.emit("vector", lambda E: E.tensor_mul(combT[:], combT[:], rec_b[:]))
        pb.barrier("p1_end")

        if "t0praw" in dbg_out and s == 0:
            pb.dma("sync", dbg_out["t0praw"], dbg_praw[:], inc="qO")
            pb.dma("sync", dbg_out["t0rbc"], dbg_rbc[:], inc="qO")
        if "xnT" in dbg_out and s == 0:
            pb.dma("gpsimd", dbg_out["xnT"], xnT[:], inc="qO")
        if "hrmsT" in dbg_out and s == 0:
            pb.dma("gpsimd", dbg_out["hrmsT"], hrmsT[:], inc="qO")
        if "combT" in dbg_out and s == 0:
            pb.dma("gpsimd", dbg_out["combT"], combT[:], inc="qO")
        if "logitsT" in dbg_out and s == 0:
            pb.dma("sync", dbg_out["logitsT"], scoresT[:], inc="qO")
        pb.barrier("p1_dbg")
        st.close()

    def emit_P2(s, src_x, dst_x):
        st = ExitStack()
        hrmsT = res_seq["hrmsT"]
        QT = sbt(st, "p2_QT", [128, NC * T], BF16)
        KT = sbt(st, "p2_KT", [128, NC * T], BF16)
        V65 = sbt(st, "p2_V65", [128, TT * 1040], BF16)
        ctxT = hrmsT
        PTs = sbt(st, "p2_PT", [128, 4 * 512], BF16)
        wbf = sbt(st, "p2_wbf", [128, 2048], BF16)
        wpb = QT[:, 0:8192]
        dts = QT[:, 8192:16384].bitcast(F32)
        xtt = KT[:, 0:2048].bitcast(F32)
        nm2 = sbt(st, "p2_nm2", [128, 8192], BF16)
        oball = sbt(st, "p2_obl", [128, 128], BF16)
        pb.emit("gpsimd", lambda E: E.memset(oball[:], 1.0))
        sqt = sbt(st, "p2_sqt", [128, 512], BF16)
        qrow = sbt(st, "p2_qr", [1, 1024], BF16)
        kcells = sbt(st, "p2_kc", [1, 8], F32)
        kcell = sbt(st, "p2_kc1", [1, 1], F32)
        knegs = sbt(st, "p2_kn", [1, 16], F32)
        rrow = sbt(st, "p2_rr", [1, 1024], F32)
        rr_hi = sbt(st, "p2_rrh", [1, 1024], BF16)
        rr_lo = sbt(st, "p2_rrl", [1, 1024], BF16)
        bcs = sbt(st, "p2_bcs", [64, 1024], F32)
        o1b = sbt(st, "p2_o1b", [1, 128], BF16)
        onescb = sbt(st, "p2_o1c", [128, 1], BF16)
        pb.emit("gpsimd", lambda E: E.memset(onescb[:], 1.0))
        sT_marks = {}
        ctx_marks = {}
        pb.emit("gpsimd", lambda E: E.memset(o1b[:], 1.0))
        pb.emit("gpsimd", lambda E: E.memset(V65[:], 1.0))
        V, S, Tn, G = "vector", "scalar", "tensor", "gpsimd"

        def hQK(m, h, col, n):
            base = 64 * (h % 2)
            return m[base:base + 64, T * (h // 2) + col:T * (h // 2) + col + n]

        qk_marks = {}
        wb_marks = {}
        for oc in range(16):
            par = oc % 2
            wb = wbf[:, 1024 * par:1024 * (par + 1)]
            if par in wb_marks:
                pb.wait("gpsimd", "mten", wb_marks[par])
            pb.dma("gpsimd", wb.rearrange("p (c n) -> p c n", n=128),
                   wqkv_oc[oc], inc="qG")
            gv = pb.cnt["qG"]
            dst = QT if oc < 8 else KT
            ocl = oc % 8
            pb.wait(Tn, "qG", gv)
            for ts in range(TS):
                bk = (oc * TS + ts) % 2
                if bk in qk_marks:
                    pb.wait(Tn, *qk_marks[bk])
                for dc in range(NC):
                    pb.emit(Tn, lambda E, bk=bk, wb=wb, dc=dc, ts=ts: E.matmul(
                        pbank[bk], wb[:, 128 * dc:128 * (dc + 1)],
                        hrmsT[:, T * dc + 512 * ts:T * dc + 512 * (ts + 1)],
                        start=(dc == 0), stop=(dc == 7)), inc="mten")
                eng = S if bk == 0 else V
                pb.wait(eng, "mten")
                dsl2 = dst[:, T * ocl + 512 * ts:T * ocl + 512 * (ts + 1)]
                if bk == 0:
                    pb.emit(S, lambda E, d=dsl2, bk=bk: E.copy(d, pbank[bk]),
                            inc="mscl")
                    qk_marks[bk] = ("mscl", pb.cnt["mscl"])
                else:
                    pb.emit(V, lambda E, d=dsl2, bk=bk: E.tensor_copy(
                        d, pbank[bk]), inc="mvec")
                    qk_marks[bk] = ("mvec", pb.cnt["mvec"])
            wb_marks[par] = pb.cnt["mten"]
        qk_marks = {}
        pb.wait(Tn, "mscl")
        pb.wait(Tn, "mvec")
        wb_marks = {}
        pb.wait("gpsimd", "mten")
        for vs in range(8):
            par = vs % 2
            wvb = wbf[:, 1024 * par:1024 * (par + 1)].rearrange(
                "p (c n) -> p c n", n=128)
            if par in wb_marks:
                pb.wait("gpsimd", "mten", wb_marks[par])
            pb.dma("gpsimd", wvb, wqkv_oc[16 + vs], inc="qG")
            gv = pb.cnt["qG"]
            pb.wait(Tn, "qG", gv)
            for tt in range(TT):
                bk = tt % 2
                if bk in qk_marks:
                    pb.wait(Tn, *qk_marks[bk])
                for dc in range(NC):
                    pb.emit(Tn, lambda E, bk=bk, dc=dc, tt=tt, wvb=wvb:
                            E.matmul(
                        pbank[bk][:, 0:128],
                        hrmsT[:, T * dc + 128 * tt:T * dc + 128 * (tt + 1)],
                        wvb[:, dc], start=(dc == 0), stop=(dc == 7)),
                        inc="mten")
                pb.wait(V, "mten")
                dstv = V65[:, 1040 * tt:1040 * (tt + 1)].rearrange(
                    "p (h c) -> p h c", c=65)[:, 2 * vs:2 * (vs + 1), 0:64]
                pb.emit(V, lambda E, d=dstv, bk=bk: E.tensor_copy(
                    d, pbank[bk][:, 0:128].rearrange("p (h c) -> p h c", c=64)),
                    inc="mvec")
                qk_marks[bk] = ("mvec", pb.cnt["mvec"])
            wb_marks[par] = pb.cnt["mten"]
        pb.barrier("p2_qkv")
        if s == 0 and "QTd" in dbg_out:
            pb.dma("gpsimd", dbg_out["QTd"], QT[:], inc="qO")
            pb.dma("gpsimd", dbg_out["KTd"], KT[:], inc="qO")
            pb.dma("gpsimd", dbg_out["V65d"], V65[:], inc="qO")
            pb.barrier("p2_qkvd")

        pb.wait(S, "mvec")
        for hp in range(NC):
            for ks in range(4):
                pb.emit(S, lambda E, hp=hp, ks=ks: E.activation(
                    sqt[:], KT[:, T * hp + 512 * ks:T * hp + 512 * (ks + 1)],
                    AF.Square), inc="mscl")
                pb.wait(Tn, "mscl")
                pb.wait(Tn, "mvec")
                for hl in range(2):
                    pb.emit(Tn, lambda E, hl=hl: E.matmul(
                        pbank[hl][0:1, 0:512],
                        onescb[64 * hl:64 * (hl + 1), :],
                        sqt[64 * hl:64 * (hl + 1), :],
                        start=True, stop=True), inc="mten")
                pb.wait(V, "mten")
                for hl in range(2):
                    pb.emit(V, lambda E, hl=hl, ks=ks: E.reduce_max(
                        kcells[:, 4 * hl + ks:4 * hl + ks + 1],
                        pbank[hl][0:1, 0:512], AX.X), inc="mvec")
                pb.wait(S, "mvec")
            for hl in range(2):
                h = 2 * hp + hl
                pb.emit(V, lambda E, hl=hl: E.reduce_max(
                    kcell[:, 0:1], kcells[0:1, 4 * hl:4 * (hl + 1)], AX.X))
                pb.emit(V, lambda E: E.tensor_scalar_max(kcell[:, 0:1],
                                                         kcell[:, 0:1], 0.0),
                        inc="mvec")
                pb.wait(S, "mvec")
                pb.emit(S, lambda E, h=h: E.activation(
                    knegs[:, h:h + 1], kcell[:, 0:1], AF.Sqrt), inc="mscl")
                pb.wait(V, "mscl")
                pb.emit(V, lambda E, h=h: E.tensor_scalar_mul(
                    knegs[:, h:h + 1], knegs[:, h:h + 1], -1.0), inc="mvec")
        pb.barrier("p2_kneg")

        for qs in range(TS):
            pb.wait(Tn, "mscl")
            pb.wait(Tn, "mvec")
            for hp in range(NC):
                pb.emit(S, lambda E, hp=hp, qs=qs: E.activation(
                    sqt[:], QT[:, T * hp + 512 * qs:T * hp + 512 * (qs + 1)],
                    AF.Square), inc="mscl")
                pb.wait(Tn, "mscl")
                for hl in range(2):
                    pb.emit(Tn, lambda E, hp=hp, hl=hl: E.matmul(
                        pbank[2 + hl][0:1, 0:512],
                        onescb[64 * hl:64 * (hl + 1), :],
                        sqt[64 * hl:64 * (hl + 1), :],
                        start=True, stop=True), inc="mten")
                pb.wait(S, "mten")
                for hl in range(2):
                    pb.emit(S, lambda E, hl=hl: E.activation(
                        qrow[:, 512 * hl:512 * (hl + 1)],
                        pbank[2 + hl][0:1, 0:512], AF.Sqrt), inc="mscl")
                pb.wait(V, "mscl")
                for hl in range(2):
                    h = 2 * hp + hl
                    pb.emit(V, lambda E, h=h, hl=hl: E.tensor_scalar(
                        nm2[0:1, 512 * h:512 * (h + 1)],
                        qrow[:, 512 * hl:512 * (hl + 1)], knegs[:, h:h + 1],
                        None, ALU.mult), inc="mvec")
            pb.wait("gpsimd", "mvec")
            pb.dma("gpsimd", nm2[64:65, :], nm2[0:1, :], inc="qG")
            pb.wait(Tn, "mvec")
            pb.wait(Tn, "qG")
            pb.wait(S, "mvec")

            for hp in range(NC):
                hA, hB = 2 * hp, 2 * hp + 1
                cbA = pgrp[1][0:65, 1024 * (hp % 2):1024 * (hp % 2) + 512]
                cbB = pgrp[1][0:65, 1024 * (hp % 2) + 512:1024 * (hp % 2) + 1024]
                if (hp % 2) in ctx_marks:
                    pb.wait(Tn, "mvec", ctx_marks[hp % 2])
                def ctx_pair(kb, evmark):
                    pb.wait(Tn, "mscl", evmark)
                    slA = PTs[:, 512 * (kb % 2):512 * (kb % 2 + 1)]
                    slB = PTs[:, 1024 + 512 * (kb % 2):1024 + 512 * (kb % 2 + 1)]
                    pb.emit(Tn, lambda E, hA=hA, kb=kb, slA=slA, cbA=cbA:
                            E.matmul(
                        cbA, V65[:, 1040 * kb + 65 * hA:1040 * kb + 65 * (hA + 1)],
                        slA, start=(kb == 0), stop=(kb == 15)))
                    pb.emit(Tn, lambda E, hB=hB, kb=kb, slB=slB, cbB=cbB:
                            E.matmul(
                        cbB, V65[:, 1040 * kb + 65 * hB:1040 * kb + 65 * (hB + 1)],
                        slB, start=(kb == 0), stop=(kb == 15)), inc="mten")

                evm = [0] * 16
                for kb in range(16):
                    bkA = kb % 2
                    bkB = 2 + kb % 2
                    if bkA in sT_marks:
                        pb.wait(Tn, *sT_marks.pop(bkA))
                    if bkB in sT_marks:
                        pb.wait(Tn, *sT_marks.pop(bkB))
                    pb.emit(Tn, lambda E, hA=hA, kb=kb, qs=qs, bkA=bkA:
                            E.matmul(
                        pbank[bkA], hQK(KT, hA, 128 * kb, 128),
                        hQK(QT, hA, 512 * qs, 512), start=True, stop=False))
                    pb.emit(Tn, lambda E, hB=hB, kb=kb, qs=qs, bkB=bkB:
                            E.matmul(
                        pbank[bkB], hQK(KT, hB, 128 * kb, 128),
                        hQK(QT, hB, 512 * qs, 512), start=True, stop=False))
                    pb.emit(Tn, lambda E, hA=hA, bkA=bkA: E.matmul(
                        pbank[bkA], oball[0:1, :],
                        nm2[0:1, 512 * hA:512 * (hA + 1)],
                        start=False, stop=True))
                    pb.emit(Tn, lambda E, hB=hB, bkB=bkB: E.matmul(
                        pbank[bkB], oball[64:65, :],
                        nm2[64:65, 512 * hB:512 * (hB + 1)],
                        start=False, stop=True), inc="mten")
                    if kb > 0:
                        ctx_pair(kb - 1, evm[kb - 1])
                    slA = PTs[:, 512 * (kb % 2):512 * (kb % 2 + 1)]
                    slB = PTs[:, 1024 + 512 * (kb % 2):1024 + 512 * (kb % 2 + 1)]
                    pb.wait(S, "mten")
                    pb.emit(S, lambda E, slA=slA, bkA=bkA: E.activation(
                        slA, pbank[bkA], AF.Exp, scale=0.125))
                    pb.emit(S, lambda E, slB=slB, bkB=bkB: E.activation(
                        slB, pbank[bkB], AF.Exp, scale=0.125), inc="mscl")
                    evm[kb] = pb.cnt["mscl"]
                    sT_marks[bkA] = ("mscl", evm[kb])
                    sT_marks[bkB] = ("mscl", evm[kb])
                ctx_pair(15, evm[15])
                ctx_done = pb.cnt["mten"]
                pb.wait(S, "mten")
                pb.emit(S, lambda E, cbA=cbA: E.activation(
                    rrow[:, 0:512], cbA[64:65, :], AF.Ln))
                pb.emit(S, lambda E, cbB=cbB: E.activation(
                    rrow[:, 512:1024], cbB[64:65, :], AF.Ln))
                pb.emit(S, lambda E: E.activation(rrow[:], rrow[:], AF.Exp,
                                                  scale=-1.0), inc="mscl")
                pb.wait(V, "mscl")
                pb.emit(V, lambda E: E.tensor_copy(rr_hi[:], rrow[:]))
                pb.emit(V, lambda E: E.tensor_sub(rr_lo[:], rrow[:], rr_hi[:]),
                        inc="mvec")
                rdy = pb.cnt["mvec"]
                for i, h in enumerate((hA, hB)):
                    bk = i
                    bk = 0 if i == 0 else 2
                    if bk in sT_marks:
                        pb.wait(Tn, *sT_marks.pop(bk))
                    pb.wait(Tn, "mvec", rdy)
                    pb.emit(Tn, lambda E, bk=bk, i=i: E.matmul(
                        pbank[bk][0:64, :], o1b[:, 0:64],
                        rr_hi[:, 512 * i:512 * (i + 1)],
                        start=True, stop=False))
                    pb.emit(Tn, lambda E, bk=bk, i=i: E.matmul(
                        pbank[bk][0:64, :], o1b[:, 0:64],
                        rr_lo[:, 512 * i:512 * (i + 1)],
                        start=False, stop=True), inc="mten")
                    eng = S if i == 0 else V
                    pb.wait(eng, "mten")
                    if i == 0:
                        pb.emit(S, lambda E, bk=bk, i=i: E.copy(
                            bcs[:, 512 * i:512 * (i + 1)], pbank[bk][0:64, :]),
                            inc="mscl")
                        pb.wait(V, "mscl")
                    else:
                        pb.emit(V, lambda E, bk=bk, i=i: E.tensor_copy(
                            bcs[:, 512 * i:512 * (i + 1)], pbank[bk][0:64, :]),
                            inc="mvec")
                    cb = cbA if i == 0 else cbB
                    pb.emit(V, lambda E, h=h, qs=qs, cb=cb, i=i: E.tensor_mul(
                        hQK(ctxT, h, 512 * qs, 512), cb[0:64, :],
                        bcs[:, 512 * i:512 * (i + 1)]), inc="mvec")
                    sT_marks[bk] = ("mvec", pb.cnt["mvec"])
                ctx_marks[hp % 2] = pb.cnt["mvec"]
        pb.barrier("p2_attn")

        pb.dma("gpsimd", wpb, wproj, inc="qG")
        pb.wait(Tn, "qG")
        for ts in range(TS):
            for oc in range(NC):
                bk = oc % 2
                pb.wait(Tn, "mvec")
                for dc in range(NC):
                    pb.emit(Tn, lambda E, bk=bk, oc=oc, dc=dc, ts=ts: E.matmul(
                        pbank[bk], wpb[:, 1024 * dc + 128 * oc:1024 * dc + 128 * (oc + 1)],
                        ctxT[:, T * dc + 512 * ts:T * dc + 512 * (ts + 1)],
                        start=(dc == 0), stop=(dc == 7)), inc="mten")
                pb.wait(V, "mten")
                pb.emit(V, lambda E, oc=oc, ts=ts, bk=bk: E.tensor_add(
                    dts[:, 512 * oc:512 * (oc + 1)], pbank[bk],
                    xnT[:, T * oc + 512 * ts:T * oc + 512 * (ts + 1)]),
                    inc="mvec")
            pb.wait(Tn, "mvec")
            for j in range(4):
                tt = 4 * ts + j
                pb.wait("sync", "mvec")
                pb.wait("sync", "qG")
                pb.dma("sync", xtt[:], src_x[s, 128 * tt:128 * (tt + 1), :],
                       inc="qS")
                pb.wait(Tn, "mvec")
                for oc in range(NC):
                    pb.emit(Tn, lambda E, j=j, oc=oc: E.transpose(
                        pgrp[1][:, 128 * oc:128 * (oc + 1)],
                        dts[:, 512 * oc + 128 * j:512 * oc + 128 * (j + 1)],
                        ident[:]), inc=("mten" if oc == 7 else None))
                qs_t = pb.cnt["qS"]
                pb.wait(V, "qS", qs_t)
                pb.wait(V, "mten")
                pb.emit(V, lambda E: E.tensor_add(
                    xtt[:], xtt[:], pgrp[1][:, 0:1024]), inc="mvec")
                pb.wait(G, "mvec")
                pb.dma("gpsimd", dst_x[s, 128 * tt:128 * (tt + 1), :], xtt[:],
                       inc="qG")
        pb.barrier("p2_end")
        if "x1" in dbg_out and s == 0:
            pb.dma("gpsimd", dbg_out["x1"], dst_x[0], inc="qO")
        if "ctxT" in dbg_out and s == 0:
            pb.dma("gpsimd", dbg_out["ctxT"], ctxT[:], inc="qO")

        pb.barrier("p2_dbg")
        st.close()

    def emit_P3(s, src_x, dst_x):
        st = ExitStack()
        combT = res_seq["combT"]
        dlo3 = res_seq["hrmsT"][:].bitcast(F32)
        dhi3 = sbt(st, "p3_dl", [128, 4 * T], F32)

        def dsl3(oc, a, b):
            return (dlo3[:, T * oc + a:T * oc + b] if oc < 4 else
                    dhi3[:, T * (oc - 4) + a:T * (oc - 4) + b])
        wgb2 = sbt(st, "p3_wg", [128, 2 * 4096], BF16)
        wub2 = sbt(st, "p3_wu", [128, 2 * 4096], BF16)
        wdb2 = sbt(st, "p3_wd", [128, 2 * 4096], BF16)
        ehot_s = sbt(st, "p3_eh", [16, 2048], BF16)
        cbc = sbt(st, "p3_cbc", [128, 1024], F32)
        sgb = sbt(st, "p3_sg", [128, 4 * 512], BF16)
        hcb = sbt(st, "p3_hc", [128, 4 * 512], BF16)
        xtt = sbt(st, "p3_xtt", [128, 2048], F32)
        V, S, Tn, G = "vector", "scalar", "tensor", "gpsimd"
        pb.dma("sync", ehot_s[:], ehot, inc="qS")
        for oc in range(NC):
            pb.emit(G, lambda E, oc=oc: E.memset(dsl3(oc, 0, T), 0.0))
        pb.barrier("p3_init")

        bmk = {}
        wset_mark = {}
        p3_prev_hc = [0]
        p3_prev_dn = [0]

        def bwait(bk):
            if bk in bmk:
                pb.wait(Tn, *bmk[bk])
        for e in range(17):
            wset = e % 2
            wgb = wgb2[:, 4096 * wset:4096 * (wset + 1)]
            wub = wub2[:, 4096 * wset:4096 * (wset + 1)]
            wdb = wdb2[:, 4096 * wset:4096 * (wset + 1)]
            if e < 16:
                srcs = ((wg_e[e], wgb), (wu_e[e], wub), (wd_e[e], wdb))
            else:
                srcs = ((sw_g, wgb), (sw_u, wub), (sw_d, wdb))
            if wset in wset_mark:
                pb.wait("gpsimd", "mten", wset_mark[wset])
            for (mat, dst3) in srcs:
                pb.dma("gpsimd", dst3, mat, inc="qG")
            pb.wait(Tn, "qG")
            for ts in range(TS):
                pb.wait(S, "mvec", p3_prev_hc[0])
                pb.wait(V, "mten", p3_prev_dn[0])
                if e < 16:
                    bwait(7)
                    pb.emit(Tn, lambda E, e=e, ts=ts: E.matmul(
                        pbank[7], ehot_s[:, 128 * e:128 * (e + 1)],
                        combT[0:16, 512 * ts:512 * (ts + 1)],
                        start=True, stop=True), inc="mten")
                    pb.wait(S, "mten")
                    pb.emit(S, lambda E, e=e, ts=ts: E.copy(
                        cbc[:, 512 * ((e + ts) % 2):512 * ((e + ts) % 2 + 1)],
                        pbank[7]), inc="mscl")
                    bmk[7] = ("mscl", pb.cnt["mscl"])
                cslc = cbc[:, 512 * ((e + ts) % 2):512 * ((e + ts) % 2 + 1)]
                g_mark, u_mark = {}, {}
                for hc in range(4):
                    bwait(hc)
                    for dc in range(NC):
                        pb.emit(Tn, lambda E, hc=hc, dc=dc, ts=ts, wgb=wgb:
                                E.matmul(
                            pbank[hc],
                            wgb[:, 512 * dc + 128 * hc:512 * dc + 128 * (hc + 1)],
                            xnT[:, T * dc + 512 * ts:T * dc + 512 * (ts + 1)],
                            start=(dc == 0), stop=(dc == 7)),
                            inc=("mten" if dc == 7 else None))
                    g_mark[hc] = pb.cnt["mten"]
                for hc in range(4):
                    bwait(4 + hc)
                    for dc in range(NC):
                        pb.emit(Tn, lambda E, hc=hc, dc=dc, ts=ts, wub=wub:
                                E.matmul(
                            pbank[4 + hc],
                            wub[:, 512 * dc + 128 * hc:512 * dc + 128 * (hc + 1)],
                            xnT[:, T * dc + 512 * ts:T * dc + 512 * (ts + 1)],
                            start=(dc == 0), stop=(dc == 7)),
                            inc=("mten" if dc == 7 else None))
                    u_mark[hc] = pb.cnt["mten"]
                h_marks = []
                for hc in range(4):
                    pb.wait(S, "mten", g_mark[hc])
                    pb.emit(S, lambda E, hc=hc: E.activation(
                        sgb[:, 512 * hc:512 * (hc + 1)], pbank[hc], AF.Silu),
                        inc="mscl")
                    bmk[hc] = ("mscl", pb.cnt["mscl"])
                    pb.wait(V, "mscl")
                    pb.wait(V, "mten", u_mark[hc])
                    pb.emit(V, lambda E, hc=hc: E.tensor_mul(
                        hcb[:, 512 * hc:512 * (hc + 1)], pbank[4 + hc],
                        sgb[:, 512 * hc:512 * (hc + 1)]))
                    if e < 16:
                        pb.emit(V, lambda E, hc=hc, cslc=cslc: E.tensor_mul(
                            hcb[:, 512 * hc:512 * (hc + 1)],
                            hcb[:, 512 * hc:512 * (hc + 1)], cslc),
                            inc="mvec")
                    else:
                        pb.emit(V, lambda E: E.engine_nop(), inc="mvec")
                    bmk[4 + hc] = ("mvec", pb.cnt["mvec"])
                    h_marks.append(pb.cnt["mvec"])
                p3_prev_hc[0] = h_marks[-1]
                dn_mark = {}
                for hc in range(4):
                    pb.wait(Tn, "mvec", h_marks[hc])
                    for oc in range(NC):
                        if hc == 0:
                            bwait(oc)
                        pb.emit(Tn, lambda E, oc=oc, hc=hc, wdb=wdb: E.matmul(
                            pbank[oc],
                            wdb[:, 1024 * hc + 128 * oc:1024 * hc + 128 * (oc + 1)],
                            hcb[:, 512 * hc:512 * (hc + 1)],
                            start=(hc == 0), stop=(hc == 3)), inc="mten")
                        if hc == 3:
                            dn_mark[oc] = pb.cnt["mten"]
                for oc in range(NC):
                    pb.wait(V, "mten", dn_mark[oc])
                    dd3 = dsl3(oc, 512 * ts, 512 * (ts + 1))
                    pb.emit(V, lambda E, dd3=dd3, oc=oc: E.tensor_add(
                        dd3, dd3, pbank[oc]), inc="mvec")
                    bmk[oc] = ("mvec", pb.cnt["mvec"])
                wset_mark[wset] = pb.cnt["mten"]
                p3_prev_dn[0] = pb.cnt["mten"]
        pb.barrier("p3_expert_done")

        qs0 = pb.cnt["qS"]
        qg0 = pb.cnt["qG"]
        add_mark = {}

        def wb_load(tt):
            par = tt % 2
            if tt >= 2:
                pb.wait("sync", "mvec", add_mark[tt - 2])
                pb.wait("sync", "qG", qg0 + 16 * (tt - 1))
            pb.dma("sync", xtt[:, 1024 * par:1024 * (par + 1)],
                   src_x[s, 128 * tt:128 * (tt + 1), :], inc="qS")
        wb_load(0)
        wb_load(1)
        for tt in range(TT):
            par = tt % 2
            xs = xtt[:, 1024 * par:1024 * (par + 1)]
            pg = pgrp[1][:, 1024 * par:1024 * (par + 1)]
            if tt >= 2:
                pb.wait(Tn, "mvec", add_mark[tt - 2])
            for oc in range(NC):
                pb.emit(Tn, lambda E, tt=tt, oc=oc, pg=pg: E.transpose(
                    pg[:, 128 * oc:128 * (oc + 1)],
                    dsl3(oc, 128 * tt, 128 * (tt + 1)),
                    ident[:]), inc=("mten" if oc == 7 else None))
            pb.wait(V, "qS", qs0 + 16 * (tt + 1))
            pb.wait(V, "mten")
            pb.emit(V, lambda E, xs=xs, pg=pg: E.tensor_add(xs, xs, pg),
                    inc="mvec")
            add_mark[tt] = pb.cnt["mvec"]
            pb.wait(G, "mvec", add_mark[tt])
            pb.dma("gpsimd", dst_x[s, 128 * tt:128 * (tt + 1), :], xs,
                   inc="qG")
            if tt + 2 < TT:
                wb_load(tt + 2)
        pb.barrier("p3_end")
        if "x1f" in dbg_out and s == 0:
            pb.dma("gpsimd", dbg_out["x1f"], dst_x[0], inc="qO")
        pb.barrier("p3_dbg")
        st.close()

    def emit_P4(s, src_x):
        st = ExitStack()
        dxst = res_seq["dxT_st"] = ExitStack()
        dxT = res_seq["dxT"] = sbt(dxst, "dxT", [128, NC * T], BF16)
        xbuf = sbt(st, "p4_x", [128, 2 * 1024], F32)
        xsq = sbt(st, "p4_sq", [128, 1024], F32)
        stat = sbt(st, "p4_st", [128, 8], F32)
        xntt = sbt(st, "p4_xn", [128, 1024], F32)
        V, S, Tn, G = "vector", "scalar", "tensor", "gpsimd"
        for tt in range(TT):
            xb = xbuf[:, 1024 * (tt % 2):1024 * (tt % 2 + 1)]
            pb.wait("sync", "mvec")
            pb.wait("sync", "mscl")
            pb.dma("sync", xb, src_x[s, 128 * tt:128 * (tt + 1), :], inc="qS")
            qs_tgt = pb.cnt["qS"]
            mean, sumsq = stat[:, 0:1], stat[:, 1:2]
            var, rstd, m2 = stat[:, 2:3], stat[:, 3:4], stat[:, 7:8]
            pb.wait(V, "qS", qs_tgt)
            pb.wait(S, "qS", qs_tgt)
            pb.emit(V, lambda E, xb=xb: E.reduce_sum(mean, xb, AX.X))
            pb.emit(V, lambda E: E.tensor_scalar_mul(mean, mean, 1.0 / D),
                    inc="mvec")
            pb.emit(S, lambda E, xb=xb: E.activation(xsq[:], xb, AF.Square),
                    inc="mscl")
            pb.wait(V, "mscl")
            pb.emit(V, lambda E: E.reduce_sum(sumsq, xsq[:], AX.X))
            pb.wait(S, "mvec")
            pb.emit(S, lambda E: E.activation(m2, mean, AF.Square), inc="mscl")
            pb.wait(V, "mscl")
            pb.emit(V, lambda E: E.tensor_scalar(var, sumsq, 1.0 / D, m2,
                                                 ALU.mult, ALU.subtract),
                    inc="mvec")
            pb.wait(S, "mvec")
            pb.emit(S, lambda E: E.activation(rstd, var, AF.Sqrt, bias=1e-5),
                    inc="mscl")
            pb.wait(V, "mscl")
            pb.emit(V, lambda E: E.reciprocal(rstd, rstd))
            pb.emit(V, lambda E, xb=xb: E.tensor_scalar(
                xntt[:], xb, mean, rstd, ALU.subtract, ALU.mult), inc="mvec")
            pb.wait(Tn, "mvec")
            for j in range(8):
                bank = pbank[j // 4]
                pb.emit(Tn, lambda E, j=j, b=bank: E.transpose(
                    b[:, 128 * (j % 4):128 * (j % 4 + 1)],
                    xntt[:, 128 * j:128 * (j + 1)], ident[:]),
                    inc=("mten" if j == 7 else None))
            pb.wait(V, "mten")
            for half in range(2):
                dst = xnT[:].rearrange("p (c n) -> p c n", n=T)[
                    :, 4 * half:4 * (half + 1), 128 * tt:128 * (tt + 1)]
                pb.emit(V, lambda E, d=dst, h=half: E.tensor_copy(
                    d, pbank[h][:].rearrange("p (c n) -> p c n", n=128)),
                    inc="mvec")
            pb.wait(Tn, "mvec")
        pb.barrier("p4_ln")
        for c in range(NC):
            xc = xnT[:, T * c:T * (c + 1)]
            dc_ = dxT[:, T * c:T * (c + 1)]
            pb.emit(V, lambda E, xc=xc, dc_=dc_: E.tensor_scalar(
                dc_[:, 1:T], xc[:, 0:T - 1], 0.5, None, ALU.mult))
            pb.emit(V, lambda E, dc_=dc_: E.memset(dc_[:, 0:1], 0.0))
            for lo, hi in ((0, 1024), (1024, 2047)):
                n = hi - lo
                pb.emit(V, lambda E, xc=xc, lo=lo, n=n: E.tensor_scalar(
                    xsq[:, 0:n], xc[:, lo + 1:lo + 1 + n], 0.5, None, ALU.mult))
                pb.emit(V, lambda E, dc_=dc_, lo=lo, n=n: E.tensor_add(
                    dc_[:, lo:lo + n], dc_[:, lo:lo + n], xsq[:, 0:n]))
            pb.emit(V, lambda E, xc=xc, dc_=dc_: E.tensor_sub(dc_, dc_, xc),
                    inc="mvec")
        pb.barrier("p4_end")
        if "dxT" in dbg_out and s == 0:
            pb.dma("gpsimd", dbg_out["dxT"], dxT[:], inc="qO")
            pb.barrier("p4_dbg")
        st.close()

    def emit_P5(s, src_x, dst_x):
        st = ExitStack()
        dlo = res_seq["hrmsT"][:].bitcast(F32)
        dhi = sbt(st, "p5_dl", [128, 4 * T], F32)

        def dsl(oc, a, b):
            return (dlo[:, T * oc + a:T * oc + b] if oc < 4 else
                    dhi[:, T * (oc - 4) + a:T * (oc - 4) + b])
        stg5 = sbt(st, "p5_stg", [128, 2048], F32)
        sb5 = sbt(st, "p5_sb", [128, 8192], BF16)
        sw1 = sbt(st, "p5_sw1", [128, 4096], BF16)
        sw2 = sbt(st, "p5_sw2", [128, 4096], BF16)
        hbuf = sbt(st, "p5_h", [128, 4 * T], BF16)
        xtt = sbt(st, "p5_xtt", [128, 1024], F32)
        tmp5 = sbt(st, "p5_tmp", [128, 512], F32)
        V, S, Tn, G = "vector", "scalar", "tensor", "gpsimd"
        if s == 0:
            items = ([("w1", c, None) for c in range(32)] +
                     [("w2", c, half) for c in range(8) for half in range(2)])
            slots = [stg5[:, 0:2048], dhi[:, 0:2048]]
            queues = [("sync", "qS"), ("gpsimd", "qG")]
            ldmk, rdmk = {}, {}

            def a_issue(i):
                kind, c, half = items[i]
                par = i % 2
                qe, qs_ = queues[par]
                if i >= 2:
                    pb.wait(qe, "mvec", rdmk[i - 2])
                if kind == "w1":
                    pb.dma(qe, slots[par][:, 0:1024],
                           w1[128 * c:128 * (c + 1), :], inc=qs_)
                else:
                    pb.dma(qe, slots[par][:, 0:2048],
                           w2[128 * c:128 * (c + 1),
                              2048 * half:2048 * (half + 1)], inc=qs_)
                ldmk[i] = (qs_, pb.cnt[qs_])
            a_issue(0)
            a_issue(1)
            for i, (kind, c, half) in enumerate(items):
                par = i % 2
                pb.wait(V, *ldmk[i])
                if kind == "w1":
                    pb.emit(V, lambda E, c=c, par=par: E.reduce_sum(
                        alpha1[:, c:c + 1], slots[par][:, 0:1024], AX.X,
                        apply_absolute_value=True), inc="mvec")
                else:
                    pb.emit(V, lambda E, c=c, half=half, par=par: E.reduce_sum(
                        alpha2[:, c:c + 1] if half == 0 else tmp5[:, 0:1],
                        slots[par][:, 0:2048], AX.X,
                        apply_absolute_value=True), inc="mvec")
                    if half == 1:
                        pb.emit(V, lambda E, c=c: E.tensor_add(
                            alpha2[:, c:c + 1], alpha2[:, c:c + 1],
                            tmp5[:, 0:1]), inc="mvec")
                rdmk[i] = pb.cnt["mvec"]
                if i + 2 < len(items):
                    a_issue(i + 2)
            pb.emit(V, lambda E: E.tensor_scalar_mul(alpha1[:], alpha1[:],
                                                     1.0 / 1024))
            pb.emit(V, lambda E: E.tensor_scalar_mul(alpha2[:], alpha2[:],
                                                     1.0 / 4096), inc="mvec")
            pb.barrier("p5_alpha")

        p5_marks = {}
        w1mk, w2mk, sgmk, hmk, dnmk = {}, {}, {}, {}, {}

        def ld_w1(g):
            if "w1" in sgmk:
                pb.wait("gpsimd", "mscl", sgmk["w1"])
            for half in range(2):
                pb.dma("gpsimd", sb5[:, 2048 * half:2048 * (half + 1)],
                       w1t_g[g, half], inc="qG")
            w1mk[g] = pb.cnt["qG"]

        def ld_w2(g):
            if "w2" in sgmk:
                pb.wait("gpsimd", "mscl", sgmk["w2"])
            for half in range(2):
                pb.dma("gpsimd", sb5[:, 4096 + 2048 * half:4096 + 2048 * (half + 1)],
                       w2t_g[g, half], inc="qG")
            w2mk[g] = pb.cnt["qG"]
        ld_w1(0)
        ld_w2(0)
        for g in range(8):
            pb.wait(S, "qG", w1mk[g])
            if g:
                pb.wait(S, "mten", hmk[g - 1])
            for half in range(2):
                for c in range(8):
                    pb.emit(S, lambda E, c=c, half=half: E.sign(
                        sw1[:, 512 * c + 256 * half:512 * c + 256 * (half + 1)],
                        sb5[:, 2048 * half + 256 * c:2048 * half + 256 * (c + 1)]),
                        inc="mscl")
            sgmk["w1"] = pb.cnt["mscl"]
            if g:
                pb.wait(S, "mten", dnmk[g - 1])
            pb.wait(Tn, "mscl")
            for hc in range(4):
                for ts in range(TS):
                    bk = (hc * TS + ts) % 2
                    if bk in p5_marks:
                        pb.wait(Tn, "mscl", p5_marks[bk])
                    for dc in range(NC):
                        pb.emit(Tn, lambda E, bk=bk, hc=hc, dc=dc, ts=ts: E.matmul(
                            pbank[bk],
                            sw1[:, 512 * dc + 128 * hc:512 * dc + 128 * (hc + 1)],
                            xnT[:, T * dc + 512 * ts:T * dc + 512 * (ts + 1)],
                            start=(dc == 0), stop=(dc == 7)), inc="mten")
                    pb.wait(S, "mten")
                    pb.emit(S, lambda E, hc=hc, ts=ts, bk=bk, g=g: E.activation(
                        hbuf[:, T * hc + 512 * ts:T * hc + 512 * (ts + 1)],
                        pbank[bk], AF.Gelu, bias=pcol(COL_B1 + 4 * g + hc),
                        scale=alpha1[:, 4 * g + hc:4 * g + hc + 1]),
                        inc="mscl")
                    p5_marks[bk] = pb.cnt["mscl"]
            hmk[g] = pb.cnt["mten"]
            if g + 1 < 8:
                ld_w1(g + 1)
            pb.wait(S, "qG", w2mk[g])
            if g:
                pb.wait(S, "mten", dnmk[g - 1])
            for half in range(2):
                for c in range(2):
                    pb.emit(S, lambda E, c=c, half=half: E.sign(
                        sw2[:, 1024 * (2 * half + c):1024 * (2 * half + c + 1)],
                        sb5[:, 4096 + 2048 * half + 1024 * c:
                             4096 + 2048 * half + 1024 * (c + 1)]), inc="mscl")
            sgmk["w2"] = pb.cnt["mscl"]
            pb.wait(Tn, "mscl")
            for ts in range(TS):
                for oc in range(NC):
                    bk = 2 + (ts * NC + oc) % 2
                    if bk in p5_marks:
                        pb.wait(Tn, "mvec", p5_marks[bk])
                    for hc in range(4):
                        pb.emit(Tn, lambda E, bk=bk, oc=oc, hc=hc, ts=ts: E.matmul(
                            pbank[bk],
                            sw2[:, 1024 * hc + 128 * oc:1024 * hc + 128 * (oc + 1)],
                            hbuf[:, T * hc + 512 * ts:T * hc + 512 * (ts + 1)],
                            start=(hc == 0), stop=(hc == 3)), inc="mten")
                    pb.wait(V, "mten")
                    dd = dsl(oc, 512 * ts, 512 * (ts + 1))
                    if g == 0:
                        pb.emit(V, lambda E, dd=dd, oc=oc, bk=bk: E.tensor_scalar(
                            dd, pbank[bk], alpha2[:, oc:oc + 1],
                            pcol(COL_B2 + oc), ALU.mult, ALU.add), inc="mvec")
                    else:
                        pb.emit(V, lambda E, oc=oc, bk=bk: E.tensor_scalar(
                            tmp5[:], pbank[bk], alpha2[:, oc:oc + 1], None,
                            ALU.mult))
                        pb.emit(V, lambda E, dd=dd: E.tensor_add(
                            dd, dd, tmp5[:]), inc="mvec")
                    p5_marks[bk] = pb.cnt["mvec"]
            dnmk[g] = pb.cnt["mten"]
            if g + 1 < 8:
                ld_w2(g + 1)
        pb.barrier("p5_groups")
        for tt in range(TT):
            pb.wait("sync", "mvec")
            pb.wait("sync", "qG")
            pb.dma("sync", xtt[:], src_x[s, 128 * tt:128 * (tt + 1), :],
                   inc="qS")
            pb.wait(Tn, "mvec")
            for oc in range(NC):
                pb.emit(Tn, lambda E, tt=tt, oc=oc: E.transpose(
                    pgrp[1][:, 128 * oc:128 * (oc + 1)],
                    dsl(oc, 128 * tt, 128 * (tt + 1)),
                    ident[:]), inc=("mten" if oc == 7 else None))
            qs_t = pb.cnt["qS"]
            pb.wait(V, "qS", qs_t)
            pb.wait(V, "mten")
            pb.emit(V, lambda E: E.tensor_add(xtt[:], xtt[:], pgrp[1][:, 0:1024]),
                    inc="mvec")
            pb.wait(G, "mvec")
            pb.dma("gpsimd", dst_x[s, 128 * tt:128 * (tt + 1), :], xtt[:],
                   inc="qG")
        pb.barrier("p5_end")
        if "x2f" in dbg_out and s == 0:
            pb.dma("gpsimd", dbg_out["x2f"], dst_x[0], inc="qO")
        pb.barrier("p5_dbg")
        st.close()

    def emit_P6(s, src_x, dst_x):
        V, S, Tn, G = "vector", "scalar", "tensor", "gpsimd"
        dxT = res_seq["dxT"]
        rT = res_seq["hrmsT"]
        stP = ExitStack()
        kT = sbt(stP, "p6_kT", [128, NC * T], BF16)
        t1b = sbt(stP, "p6_t1", [64, T], BF16)
        masksb = sbt(stP, "p6_mk", [128, 4 * 512], mybir.dt.uint8)
        dw2b = sbt(stP, "p6_dw2", [64, 1024], BF16)
        pb.dma("gpsimd", masksb[:].rearrange("p (j n) -> p j n", j=4),
               lion_masks.rearrange("j p n -> p j n"), inc="qG")

        stA = ExitStack()
        wtb = sbt(stA, "p6_wtb", [128, NC * 1024], BF16)
        xcon2 = sbt(stA, "p6_xc", [128, 2 * NC * 512], BF16)
        xcon = xcon2[:, 0:NC * 512]
        af32 = sbt(stA, "p6_af", [128, 512], F32)
        xxa = sbt(stA, "p6_xxa", [64, T], BF16)
        xxb = sbt(stA, "p6_xxb", [64, T], BF16)
        xxc = sbt(stA, "p6_xxc", [32, T], BF16)
        maw1b = sbt(stA, "p6_m1", [128, 8 * 160], BF16)
        maw2b = sbt(stA, "p6_m2", [64, 5 * 1024], BF16)
        dw1b = sbt(stA, "p6_dw1", [128, 8 * 64], BF16)
        vtmp = sbt(stA, "p6_vt", [128, 512], BF16)
        gtmp = sbt(stA, "p6_gt", [128, 512], BF16)

        pb.dma("gpsimd", maw1b[:], maa_w1, inc="qG")
        pb.dma("gpsimd", maw2b[0:32, :].rearrange("m (f d) -> m f d", f=5),
               maa_w2.rearrange("f m d -> m f d"), inc="qG")
        pb.dma("gpsimd", maw2b[32:64, :].rearrange("m (f d) -> m f d", f=5),
               maa_w2.rearrange("f m d -> m f d"), inc="qG")
        pb.dma("gpsimd", dw1b[:], dec_w1, inc="qG")
        pb.dma("gpsimd", dw2b[:], dec_w2, inc="qG")
        pb.wait(Tn, "qG")
        pb.wait("vector", "qG")

        MF = {1: 1, 2: 2, 3: 3, 4: 4}
        MAAC = {1: 2, 2: 3, 3: 4, 4: 5}
        XXROW = {1: (None, 32), 2: (None, 0), 3: (None, 32), 4: (None, 0)}

        for ts in range(TS):
            for dc in range(NC):
                pb.emit(V, lambda E, dc=dc, ts=ts: E.tensor_scalar(
                    af32[:], dxT[:, T * dc + 512 * ts:T * dc + 512 * (ts + 1)],
                    pcol(COL_MAA + 0 + dc), None, ALU.mult))
                pb.emit(V, lambda E, dc=dc, ts=ts: E.tensor_add(
                    xcon[:, 512 * dc:512 * (dc + 1)], af32[:],
                    xnT[:, T * dc + 512 * ts:T * dc + 512 * (ts + 1)]),
                    inc="mvec")
            pb.wait(Tn, "mvec")
            pb.wait(Tn, "mscl")
            for dc in range(NC):
                pb.emit(Tn, lambda E, dc=dc: E.matmul(
                    pbank[0], maw1b[:, 160 * dc:160 * dc + 128],
                    xcon[:, 512 * dc:512 * (dc + 1)],
                    start=(dc == 0), stop=(dc == 7)))
            for dc in range(NC):
                pb.emit(Tn, lambda E, dc=dc: E.matmul(
                    pbank[1][0:32, :], maw1b[:, 160 * dc + 128:160 * (dc + 1)],
                    xcon[:, 512 * dc:512 * (dc + 1)],
                    start=(dc == 0), stop=(dc == 7)), inc="mten")
            pb.wait(S, "mten")
            pb.emit(S, lambda E, ts=ts: E.activation(
                xxa[:, 512 * ts:512 * (ts + 1)], pbank[0][0:64, :], AF.Tanh))
            pb.emit(S, lambda E, ts=ts: E.activation(
                xxb[:, 512 * ts:512 * (ts + 1)], pbank[0][64:128, :], AF.Tanh))
            pb.emit(S, lambda E, ts=ts: E.activation(
                xxc[:, 512 * ts:512 * (ts + 1)], pbank[1][0:32, :], AF.Tanh),
                inc="mscl")
        pb.barrier("p6_xxx")

        for ts in range(TS):
            for dc in range(NC):
                pb.wait(Tn, "mvec")
                pb.emit(Tn, lambda E, dc=dc, ts=ts: E.matmul(
                    pgrp[1][:, 1536:2048],
                    maw2b[0:32, 128 * dc:128 * (dc + 1)],
                    xxa[0:32, 512 * ts:512 * (ts + 1)],
                    start=True, stop=True), inc="mten")
                pb.wait(V, "mten")
                pb.emit(V, lambda E, dc=dc: E.tensor_scalar(
                    af32[:], pgrp[1][:, 1536:2048],
                    pcol(COL_MAA + 8 + dc), None, ALU.add))
                pb.emit(V, lambda E, dc=dc, ts=ts: E.tensor_mul(
                    af32[:], af32[:],
                    dxT[:, T * dc + 512 * ts:T * dc + 512 * (ts + 1)]))
                pb.emit(V, lambda E, dc=dc, ts=ts: E.tensor_add(
                    xcon[:, 512 * dc:512 * (dc + 1)], af32[:],
                    xnT[:, T * dc + 512 * ts:T * dc + 512 * (ts + 1)]),
                    inc="mvec")
            pb.wait(Tn, "mvec")
            for dc in range(NC):
                pb.emit(Tn, lambda E, dc=dc: E.matmul(
                    pbank[2][0:64, :], dw1b[:, 64 * dc:64 * (dc + 1)],
                    xcon[:, 512 * dc:512 * (dc + 1)],
                    start=(dc == 0), stop=(dc == 7)), inc="mten")
            pb.wait(S, "mten")
            pb.emit(S, lambda E, ts=ts: E.activation(
                t1b[:, 512 * ts:512 * (ts + 1)], pbank[2][0:64, :], AF.Tanh),
                inc="mscl")
            pb.wait(Tn, "mscl")
        pb.barrier("p6_t1")
        if "t1d" in dbg_out and s == 0:
            pb.dma("gpsimd", dbg_out["t1d"], t1b[:], inc="qO")
            pb.barrier("p6_t1d")

        def load_wt(mat):
            pb.wait("gpsimd", "mten")
            pb.dma("gpsimd", wtb[:], mat, inc="qG")
            pb.wait(Tn, "qG")

        ep_marks = {}

        def build_half(vec_idx, xx_src, xx_row, ts, xcon, half):
            for dc in range(4 * half, 4 * half + 4):
                slot = pgrp[1][:, 512 * (dc % 4):512 * (dc % 4 + 1)]
                if (dc % 4) in ep_marks:
                    pb.wait(Tn, "mvec", ep_marks[dc % 4])
                pb.emit(Tn, lambda E, dc=dc, ts=ts, vec_idx=vec_idx,
                        xx_src=xx_src, xx_row=xx_row, slot=slot: E.matmul(
                    slot,
                    maw2b[xx_row:xx_row + 32,
                          1024 * MF[vec_idx] + 128 * dc:
                          1024 * MF[vec_idx] + 128 * (dc + 1)],
                    xx_src[xx_row:xx_row + 32, 512 * ts:512 * (ts + 1)],
                    start=True, stop=True), inc="mten")
                pb.wait(V, "mten")
                pb.emit(V, lambda E, dc=dc, vec_idx=vec_idx, slot=slot:
                        E.tensor_scalar(
                    af32[:], slot, pcol(COL_MAA + 8 * MAAC[vec_idx] + dc),
                    None, ALU.add))
                pb.emit(V, lambda E, dc=dc, ts=ts: E.tensor_mul(
                    af32[:], af32[:],
                    dxT[:, T * dc + 512 * ts:T * dc + 512 * (ts + 1)]))
                pb.emit(V, lambda E, dc=dc, ts=ts, xcon=xcon: E.tensor_add(
                    xcon[:, 512 * dc:512 * (dc + 1)], af32[:],
                    xnT[:, T * dc + 512 * ts:T * dc + 512 * (ts + 1)]),
                    inc="mvec")
                ep_marks[dc % 4] = pb.cnt["mvec"]

        def proj_part(kind, dstT, xcon, ts, part):
            if kind in ("r", "k", "g"):
                for oc in range(4 * part, 4 * part + 4):
                    bk = oc % 2
                    if bk in p6_marks:
                        pb.wait(Tn, *p6_marks[bk])
                    for dc in range(NC):
                        pb.emit(Tn, lambda E, bk=bk, oc=oc, dc=dc, xcon=xcon:
                                E.matmul(
                            pbank[bk],
                            wtb[:, 1024 * dc + 128 * oc:1024 * dc + 128 * (oc + 1)],
                            xcon[:, 512 * dc:512 * (dc + 1)],
                            start=(dc == 0), stop=(dc == 7)), inc="mten")
                    if kind == "g":
                        pb.wait(S, "mten")
                        pb.wait(S, "qG")
                        pb.emit(S, lambda E, oc=oc, bk=bk: E.activation(
                            gtmp[:], pbank[bk], AF.Silu), inc="mscl")
                        p6_marks[bk] = ("mscl", pb.cnt["mscl"])
                        pb.wait(G, "mscl")
                        pb.dma("gpsimd", gdr[oc, :, 512 * ts:512 * (ts + 1)],
                               gtmp[:], inc="qG")
                    else:
                        eng = S if bk == 0 else V
                        pb.wait(eng, "mten")
                        dsl6 = dstT[:, T * oc + 512 * ts:T * oc + 512 * (ts + 1)]
                        if bk == 0:
                            pb.emit(S, lambda E, d=dsl6, bk=bk: E.copy(
                                d, pbank[bk]), inc="mscl")
                            p6_marks[bk] = ("mscl", pb.cnt["mscl"])
                        else:
                            pb.emit(V, lambda E, d=dsl6, bk=bk: E.tensor_copy(
                                d, pbank[bk]), inc="mvec")
                            p6_marks[bk] = ("mvec", pb.cnt["mvec"])
            else:
                for jv in range(4 * part, 4 * part + 4):
                    j, vs = jv // 2, jv % 2
                    tt = 4 * ts + j
                    bk = jv % 2
                    if bk in p6_marks:
                        pb.wait(Tn, *p6_marks[bk])
                    for dc in range(NC):
                        pb.emit(Tn, lambda E, bk=bk, j=j, vs=vs, dc=dc,
                                xcon=xcon: E.matmul(
                            pbank[bk],
                            xcon[:, 512 * dc + 128 * j:512 * dc + 128 * (j + 1)],
                            wtb[:, 1024 * dc + 512 * vs:1024 * dc + 512 * (vs + 1)],
                            start=(dc == 0), stop=(dc == 7)), inc="mten")
                    pb.wait(V, "mten")
                    pb.wait(V, "qG")
                    pb.emit(V, lambda E, bk=bk: E.tensor_copy(
                        vtmp[:], pbank[bk]), inc="mvec")
                    p6_marks[bk] = ("mvec", pb.cnt["mvec"])
                    pb.wait(G, "mvec")
                    pb.dma("gpsimd",
                           vdr[4 * vs:4 * vs + 4, 128 * tt:128 * (tt + 1),
                               :].rearrange("c p n -> p c n"),
                           vtmp[:].rearrange("p (c n) -> p c n", n=128),
                           inc="qG")

        p6_marks = {}
        for (mat, vec_idx, kind) in ((WrT, 3, "r"), (WkT, 1, "k"),
                                     (WvT, 2, "v"), (WgT, 4, "g")):
            p6_marks = {}
            load_wt(mat)
            pb.wait(Tn, "mscl")
            pb.wait(Tn, "mvec")
            xx_src = {1: xxa, 2: xxb, 3: xxb, 4: xxc}[vec_idx]
            xx_row = {1: 32, 2: 0, 3: 32, 4: 0}[vec_idx]
            dstT = rT if kind == "r" else kT
            for half in range(2):
                build_half(vec_idx, xx_src, xx_row, 0, xcon2[:, 0:4096], half)
            for ts in range(TS):
                xcon = xcon2[:, 4096 * (ts % 2):4096 * (ts % 2 + 1)]
                nxt = xcon2[:, 4096 * ((ts + 1) % 2):4096 * ((ts + 1) % 2 + 1)]
                pb.wait(Tn, "mvec", ep_marks.get(3, 0))
                if ts + 1 < TS:
                    build_half(vec_idx, xx_src, xx_row, ts + 1, nxt, 0)
                proj_part(kind, dstT, xcon, ts, 0)
                if ts + 1 < TS:
                    build_half(vec_idx, xx_src, xx_row, ts + 1, nxt, 1)
                proj_part(kind, dstT, xcon, ts, 1)
        pb.barrier("p6_proj")
        if s == 0:
            if "rTd" in dbg_out:
                pb.dma("gpsimd", dbg_out["rTd"], rT[:], inc="qO")
            if "kTd" in dbg_out:
                pb.dma("gpsimd", dbg_out["kTd"], kT[:], inc="qO")
            if "gdrd" in dbg_out:
                pb.dma("gpsimd", dbg_out["gdrd"], gdr, inc="qO")
            if "vdrd" in dbg_out:
                pb.dma("gpsimd", dbg_out["vdrd"], vdr, inc="qO")
            pb.barrier("p6_projd")
        stA.close()

        stB = ExitStack()
        A_ = sbt(stB, "p6_A", [128, T], F32)
        B_ = sbt(stB, "p6_B", [128, T], F32)
        C_ = sbt(stB, "p6_C", [128, T], F32)
        zerosb = sbt(stB, "p6_z0", [128, T], BF16)
        pb.emit(G, lambda E: E.memset(zerosb[:], 0.0))
        vhp = sbt(stB, "p6_vhp", [128, TT * 128], BF16)
        ghp = sbt(stB, "p6_ghp", [128, T], BF16)
        ysq1 = sbt(stB, "p6_ysq", [128, 512], F32)
        afmp = sbt(stB, "p6_af2", [128, 512], BF16)
        afmp1 = sbt(stB, "p6_af3", [128, 512], BF16)
        o1b2 = sbt(stB, "p6_o1b", [128, 64], BF16)
        O2f = sbt(stB, "p6_O2", [128, 2], F32)
        pb.emit(G, lambda E: E.memset(o1b2[:], 1.0))
        pb.emit(G, lambda E: E.memset(O2f[:], 0.0))
        pb.emit(G, lambda E: E.memset(O2f[0:64, 0:1], 1.0))
        pb.emit(G, lambda E: E.memset(O2f[64:128, 1:2], 1.0))
        rfb = dxT[:, 0:T]
        kfb = dxT[:, T:2 * T]
        rbb = dxT[:, 2 * T:3 * T]
        kbb = dxT[:, 3 * T:4 * T]
        gnb2 = dxT[:, 4 * T:4 * T + 4096]
        ysbf = dxT[:, 6 * T:8 * T].bitcast(F32)
        AT = [xnT[:, 0:16 * 512], xnT[:, 16 * 512:32 * 512]]

        for hp in range(NC):
            pb.wait("gpsimd", "mten")
            pb.dma("gpsimd", vhp[:].rearrange("p (tt n) -> p tt n", n=128),
                   vdr[hp].rearrange("(tt p) n -> p tt n", p=128), inc="qG")
            pb.dma("gpsimd", ghp[:], gdr[hp], inc="qG")
            pb.wait(Tn, "mvec")
            pb.wait(Tn, "mscl")
            for ts in range(TS):
                pb.emit(Tn, lambda E, hp=hp, ts=ts: E.matmul(
                    pbank[0], dw2b[:, 128 * hp:128 * (hp + 1)],
                    t1b[:, 512 * ts:512 * (ts + 1)], start=True, stop=True),
                    inc="mten")
                pb.wait(S, "mten")
                pb.emit(S, lambda E, hp=hp, ts=ts: E.activation(
                    A_[:, 512 * ts:512 * (ts + 1)], pbank[0], AF.Exp,
                    bias=pcol(COL_TDEC + hp)), inc="mscl")
            pb.wait(V, "mscl")
            pb.emit(V, lambda E: E.tensor_tensor_scan(
                B_[:], A_[:], zerosb[:], 0.0, ALU.add, ALU.add))
            pb.emit(V, lambda E: E.tensor_sub(C_[:], B_[:], A_[:]))
            pb.emit(V, lambda E: E.tensor_scalar(
                A_[:], B_[:], B_[:, 1024:1025], 60.0, ALU.subtract, ALU.min))
            pb.emit(V, lambda E: E.tensor_scalar_max(A_[:], A_[:], -60.0))
            pb.emit(V, lambda E: E.tensor_scalar(
                B_[:], C_[:], C_[:, 1024:1025], 60.0, ALU.subtract, ALU.min))
            pb.emit(V, lambda E: E.tensor_scalar_max(B_[:], B_[:], -60.0),
                    inc="mvec")
            pb.wait(S, "mvec")
            for (dst, srcC, sc, base) in ((rfb, A_, -1.0, rT), (kfb, A_, 1.0, kT),
                                          (rbb, B_, 1.0, rT), (kbb, B_, -1.0, kT)):
                pb.emit(S, lambda E, srcC=srcC, sc=sc: E.activation(
                    C_[:], srcC[:], AF.Exp, scale=sc), inc="mscl")
                pb.wait(V, "mscl")
                pb.emit(V, lambda E, dst=dst, base=base, hp=hp: E.tensor_mul(
                    dst, C_[:], base[:, T * hp:T * (hp + 1)]), inc="mvec")
                pb.wait(S, "mvec")
            if hp == 0 and s == 0 and "cfpd" in dbg_out:
                pb.wait("gpsimd", "mvec")
                pb.dma("gpsimd", dbg_out["cfpd"], A_[:], inc="qO")
                pb.dma("gpsimd", dbg_out["cbpd"], B_[:], inc="qO")
                pb.dma("gpsimd", dbg_out["rfbd"], rfb, inc="qO")
                pb.dma("gpsimd", dbg_out["kfbd"], kfb, inc="qO")
                pb.wait("gpsimd", "qO")
            pb.wait(Tn, "mvec")
            pb.wait(Tn, "qG")
            bkfree = {}
            sel_prev = {}
            yev_mark = [0]
            sq_guard = [0]
            apply_mark = [0]

            def emit_y(cy, atrdy):
                for mk in atrdy[cy]:
                    pb.wait(Tn, *mk)
                if cy == 0 and yev_mark[0]:
                    pb.wait(Tn, "mvec", yev_mark[0])
                for hl in (0, 1):
                    yb = (pgrp[1][0:64, 1024:1536] if hl == 0
                          else pgrp[1][64:128, 1536:2048])
                    pb.emit(Tn, lambda E, yb=yb, hl=hl, cy=cy: E.matmul(
                        yb, vhp[:, 128 * cy + 64 * hl:128 * cy + 64 * (hl + 1)],
                        AT[hl][:, 512 * cy:512 * (cy + 1)],
                        start=(cy == 0), stop=(cy == 15)), inc="mten")

            for qs in range(TS):
                pb.wait(S, "mten")
                pb.wait(V, "mten")
                atrdy = {}
                for c in range(16):
                    diag = 4 * qs <= c <= 4 * qs + 3
                    bkA, bkB = c % 2, 2 + c % 2
                    for bk in (bkA, bkB):
                        if bk in bkfree:
                            pb.wait(Tn, *bkfree[bk])
                    f_c = c < 4 * qs or diag
                    b_c = c > 4 * qs + 3 or diag
                    if f_c:
                        for hl, bk in ((0, bkA), (1, bkB)):
                            r0 = 64 * hl
                            pb.emit(Tn, lambda E, bk=bk, c=c, qs=qs, r0=r0:
                                    E.matmul(
                                pbank[bk], kfb[r0:r0 + 64, 128 * c:128 * (c + 1)],
                                rfb[r0:r0 + 64, 512 * qs:512 * (qs + 1)],
                                start=True, stop=True), inc="mten")
                    if b_c:
                        for hl, bk in ((0, bkA), (1, bkB)):
                            r0 = 64 * hl
                            if diag:
                                if hl in sel_prev:
                                    pb.wait(Tn, "mvec", sel_prev[hl])
                                ob = pgrp[1][:, 512 * hl:512 * (hl + 1)]
                            else:
                                ob = pbank[bk]
                            pb.emit(Tn, lambda E, ob=ob, c=c, qs=qs, r0=r0:
                                    E.matmul(
                                ob, kbb[r0:r0 + 64, 128 * c:128 * (c + 1)],
                                rbb[r0:r0 + 64, 512 * qs:512 * (qs + 1)],
                                start=True, stop=True), inc="mten")
                    if diag:
                        j = c - 4 * qs
                        if sel_prev:
                            pb.wait(S, "mvec", max(sel_prev.values()))
                        pb.wait(S, "mten")
                        pb.emit(S, lambda E, bkA=bkA: E.copy(afmp[:], pbank[bkA]),
                                inc="mscl")
                        bkfree[bkA] = ("mscl", pb.cnt["mscl"])
                        pb.emit(S, lambda E, bkB=bkB: E.copy(afmp1[:], pbank[bkB]),
                                inc="mscl")
                        bkfree[bkB] = ("mscl", pb.cnt["mscl"])
                        pb.wait(V, "mscl")
                        pb.wait(V, "mten")
                        pb.emit(V, lambda E, c=c, j=j: E.select(
                            AT[0][:, 512 * c:512 * (c + 1)],
                            masksb[:, 512 * j:512 * (j + 1)],
                            afmp[:], pgrp[1][:, 0:512]), inc="mvec")
                        sel_prev[0] = pb.cnt["mvec"]
                        pb.emit(V, lambda E, c=c, j=j: E.select(
                            AT[1][:, 512 * c:512 * (c + 1)],
                            masksb[:, 512 * j:512 * (j + 1)],
                            afmp1[:], pgrp[1][:, 512:1024]), inc="mvec")
                        sel_prev[1] = pb.cnt["mvec"]
                        atrdy[c] = [("mvec", sel_prev[1])]
                    else:
                        pb.wait(S, "mten")
                        pb.emit(S, lambda E, c=c, bkA=bkA: E.copy(
                            AT[0][:, 512 * c:512 * (c + 1)], pbank[bkA]),
                            inc="mscl")
                        bkfree[bkA] = ("mscl", pb.cnt["mscl"])
                        pb.wait(V, "mten")
                        pb.emit(V, lambda E, c=c, bkB=bkB: E.tensor_copy(
                            AT[1][:, 512 * c:512 * (c + 1)], pbank[bkB]),
                            inc="mvec")
                        bkfree[bkB] = ("mvec", pb.cnt["mvec"])
                        atrdy[c] = [bkfree[bkA], bkfree[bkB]]
                    if c >= 2:
                        emit_y(c - 2, atrdy)
                if (hp == 0 and qs == 0 and s == 0 and "ATd" in dbg_out):
                    pb.wait("gpsimd", "mscl")
                    pb.wait("gpsimd", "mvec")
                    pb.dma("gpsimd", dbg_out["ATd"], AT[0], inc="qO")
                    pb.wait("gpsimd", "qO")
                emit_y(14, atrdy)
                emit_y(15, atrdy)
                pb.wait(V, "mten")
                pb.emit(V, lambda E, qs=qs: E.tensor_copy(
                    ysbf[0:64, 512 * qs:512 * (qs + 1)],
                    pgrp[1][0:64, 1024:1536]))
                pb.emit(V, lambda E, qs=qs: E.tensor_copy(
                    ysbf[64:128, 512 * qs:512 * (qs + 1)],
                    pgrp[1][64:128, 1536:2048]), inc="mvec")
                if (hp == 0 and qs == 0 and s == 0 and "yd" in dbg_out):
                    pb.wait("gpsimd", "mvec")
                    pb.dma("gpsimd", dbg_out["yd"], ysbf[0:64, 0:512], inc="qO")
                    pb.wait("gpsimd", "qO")
                if sq_guard[0]:
                    pb.wait(S, "mten", sq_guard[0])
                pb.wait(S, "mvec")
                pb.emit(S, lambda E, qs=qs: E.activation(
                    ysq1[:], ysbf[:, 512 * qs:512 * (qs + 1)], AF.Square),
                    inc="mscl")
                pb.wait(Tn, "mvec")
                pb.wait(Tn, "mscl")
                pb.emit(Tn, lambda E, qs=qs: E.matmul(
                    pgrp[1][64:66, 1024:1536], O2f[:],
                    ysbf[:, 512 * qs:512 * (qs + 1)],
                    start=True, stop=True), inc="mten")
                pb.emit(Tn, lambda E: E.matmul(
                    pgrp[1][64:66, 1536:2048], O2f[:], ysq1[:],
                    start=True, stop=True), inc="mten")
                sq_guard[0] = pb.cnt["mten"]
                pb.wait(V, "mten")
                pb.emit(V, lambda E, qs=qs: E.tensor_copy(
                    A_[64:66, 512 * qs:512 * (qs + 1)],
                    pgrp[1][64:66, 1024:1536]))
                pb.emit(V, lambda E, qs=qs: E.tensor_copy(
                    B_[64:66, 512 * qs:512 * (qs + 1)],
                    pgrp[1][64:66, 1536:2048]), inc="mvec")
                yev_mark[0] = pb.cnt["mvec"]
            pb.emit(V, lambda E: E.tensor_scalar(
                C_[64:66, 0:T], A_[64:66, 0:T], 1.0 / 64, None, ALU.mult),
                inc="mvec")
            pb.wait(S, "mvec")
            pb.emit(S, lambda E: E.activation(
                A_[64:66, 0:T], C_[64:66, 0:T], AF.Square), inc="mscl")
            pb.emit(V, lambda E: E.tensor_scalar(
                B_[64:66, 0:T], B_[64:66, 0:T], 1.0 / 64, None, ALU.mult))
            pb.wait(V, "mscl")
            pb.emit(V, lambda E: E.tensor_sub(
                B_[64:66, 0:T], B_[64:66, 0:T], A_[64:66, 0:T]), inc="mvec")
            pb.wait(S, "mvec")
            pb.emit(S, lambda E: E.activation(
                A_[64:66, 0:T], B_[64:66, 0:T], AF.Sqrt, bias=6.4e-4),
                inc="mscl")
            pb.wait(V, "mscl")
            pb.emit(V, lambda E: E.reciprocal(A_[64:66, 0:T], A_[64:66, 0:T]))
            pb.emit(V, lambda E: E.tensor_copy(gnb2[64:66, 0:2048],
                                               C_[64:66, 0:T]))
            pb.emit(V, lambda E: E.tensor_copy(gnb2[64:66, 2048:4096],
                                               A_[64:66, 0:T]), inc="mvec")
            pb.wait(G, "mvec")
            pb.dma("gpsimd", gnb2[0:1, 0:4096], gnb2[64:65, 0:4096], inc="qG")
            pb.dma("gpsimd", gnb2[32:33, 0:4096], gnb2[65:66, 0:4096], inc="qG")
            pb.wait(Tn, "qG")
            for qs in range(TS):
                if apply_mark[0]:
                    pb.wait(Tn, "mvec", apply_mark[0])
                for hl, rb in ((0, 0), (1, 32)):
                    ro = slice(64 * hl, 64 * (hl + 1))
                    pb.emit(Tn, lambda E, ro=ro, rb=rb, qs=qs: E.matmul(
                        pgrp[1][ro, 1024:1536], o1b2[rb:rb + 1, :],
                        gnb2[rb:rb + 1, 512 * qs:512 * (qs + 1)],
                        start=True, stop=True))
                    pb.emit(Tn, lambda E, ro=ro, rb=rb, qs=qs: E.matmul(
                        pgrp[1][ro, 1536:2048], o1b2[rb:rb + 1, :],
                        gnb2[rb:rb + 1, 2048 + 512 * qs:2048 + 512 * (qs + 1)],
                        start=True, stop=True), inc="mten")
                pb.wait(V, "mten")
                for hl in (0, 1):
                    ro = slice(64 * hl, 64 * (hl + 1))
                    ysl = ysbf[64 * hl:64 * (hl + 1), 512 * qs:512 * (qs + 1)]
                    pb.emit(V, lambda E, ysl=ysl, ro=ro: E.tensor_sub(
                        ysl, ysl, pgrp[1][ro, 1024:1536]))
                    pb.emit(V, lambda E, ysl=ysl, ro=ro: E.tensor_mul(
                        ysl, ysl, pgrp[1][ro, 1536:2048]))
                    pb.emit(V, lambda E, ysl=ysl, hl=hl, hp=hp: E.tensor_scalar(
                        ysl, ysl,
                        packA_s[64 * hl:64 * (hl + 1),
                                COL_LNW + hp:COL_LNW + hp + 1],
                        packA_s[64 * hl:64 * (hl + 1),
                                COL_LNB + hp:COL_LNB + hp + 1],
                        ALU.mult, ALU.add))
                    pb.emit(V, lambda E, ysl=ysl, hl=hl, qs=qs: E.tensor_mul(
                        ghp[64 * hl:64 * (hl + 1), 512 * qs:512 * (qs + 1)],
                        ghp[64 * hl:64 * (hl + 1), 512 * qs:512 * (qs + 1)],
                        ysl), inc="mvec")
                apply_mark[0] = pb.cnt["mvec"]
            pb.wait(G, "mvec")
            pb.dma("gpsimd", gdr[hp], ghp[:], inc="qG")
        pb.barrier("p6_lion")
        if "zd" in dbg_out and s == 0:
            pb.dma("gpsimd", dbg_out["zd"], gdr, inc="qO")
            pb.barrier("p6_zd")
        stB.close()

        stC = ExitStack()
        WoTb = kT[:, 0:8192]
        dts = kT[:, 8192:16384].bitcast(F32)
        zts = rT[:, 0:4096]
        xtt = rT[:, 4096:6144].bitcast(F32)
        pb.dma("gpsimd", WoTb, WoT, inc="qG")
        pb.wait(Tn, "qG")
        op_marks = {}
        for ts in range(TS):
            pb.wait("sync", "mten")
            pb.dma("sync", zts[:].rearrange("p (c n) -> p c n", n=512),
                   gdr.rearrange("c p n -> p c n")[:, :, 512 * ts:512 * (ts + 1)],
                   inc="qS")
            pb.wait(Tn, "qS")
            for oc in range(NC):
                bk = oc % 2
                if bk in op_marks:
                    pb.wait(Tn, "mvec", op_marks[bk])
                for dc in range(NC):
                    pb.emit(Tn, lambda E, bk=bk, oc=oc, dc=dc: E.matmul(
                        pbank[bk],
                        WoTb[:, 1024 * dc + 128 * oc:1024 * dc + 128 * (oc + 1)],
                        zts[:, 512 * dc:512 * (dc + 1)],
                        start=(dc == 0), stop=(dc == 7)), inc="mten")
                pb.wait(V, "mten")
                pb.emit(V, lambda E, oc=oc, bk=bk: E.tensor_copy(
                    dts[:, 512 * oc:512 * (oc + 1)], pbank[bk]), inc="mvec")
                op_marks[bk] = pb.cnt["mvec"]
            pb.wait(Tn, "mvec")
            for j in range(4):
                tt = 4 * ts + j
                pb.wait("sync", "mvec")
                pb.wait("sync", "qO")
                pb.dma("sync", xtt[:], src_x[s, 128 * tt:128 * (tt + 1), :],
                       inc="qS")
                pb.wait(Tn, "mvec")
                for oc in range(NC):
                    pb.emit(Tn, lambda E, j=j, oc=oc: E.transpose(
                        pgrp[1][:, 128 * oc:128 * (oc + 1)],
                        dts[:, 512 * oc + 128 * j:512 * oc + 128 * (j + 1)],
                        ident[:]), inc=("mten" if oc == 7 else None))
                qs_t = pb.cnt["qS"]
                pb.wait(V, "qS", qs_t)
                pb.wait(V, "mten")
                pb.emit(V, lambda E: E.tensor_add(xtt[:], xtt[:],
                                                  pgrp[1][:, 0:1024]),
                        inc="mvec")
                pb.wait(G, "mvec")
                pb.dma("gpsimd", dst_x[s, 128 * tt:128 * (tt + 1), :], xtt[:],
                       inc="qO")
        pb.barrier("p6_end")
        stC.close()
        stP.close()
        res_seq.pop("dxT_st").close()
        res_seq.pop("dxT")

    for s in range(SEQS):
        if "P1" in phases:
            emit_P1(s, x_in)
        if "P2" in phases:
            emit_P2(s, x_in, xdr)
        if "P3" in phases:
            emit_P3(s, xdr, xdr)
        if "P4" in phases:
            emit_P4(s, xdr)
        if "P5" in phases:
            emit_P5(s, xdr, xdr)
        if "P6" in phases:
            emit_P6(s, xdr, out)

    if "P6" not in phases:
        with nc.sbuf_tensor("cp_buf", [128, 1024], F32) as cpb:
            for s in range(SEQS):
                for tt in range(TT):
                    pb.dma("sync", cpb[:], x_in[s, 128 * tt:128 * (tt + 1), :],
                           inc="qS")
                    pb.wait("sync", "qS")
                    pb.dma("sync", out[s, 128 * tt:128 * (tt + 1), :], cpb[:],
                           inc="qO")
                    pb.wait("sync", "qO")
            pb.barrier("out_done")
    pb.wait("gpsimd", "qO", pb.cnt["qO"])
    pb.flush()
    ctx.close()
    return nc




def _pack_cols(inp):
    pk = np.zeros((128, 120), np.float32)

    def put(base, vec, n):
        v = np.asarray(vec, np.float32).reshape(-1)
        pk[:, base:base + n] = v.reshape(n, 128).T
    put(0, inp["L0_attn_norm_w"], 8)
    put(8, inp["L1_attn_time_maa_x"], 8)
    put(16, inp["L1_attn_time_maa_w"], 8)
    put(24, inp["L1_attn_time_maa_k"], 8)
    put(32, inp["L1_attn_time_maa_v"], 8)
    put(40, inp["L1_attn_time_maa_r"], 8)
    put(48, inp["L1_attn_time_maa_g"], 8)
    put(56, inp["L1_attn_time_decay"], 8)
    put(64, inp["L1_attn_ln_w"], 8)
    put(72, inp["L1_attn_ln_b"], 8)
    put(80, inp["L1_ffn_b1"], 32)
    put(112, inp["L1_ffn_b2"], 8)
    return pk


def _consts():
    ident = np.eye(128, dtype=np.float32)
    masks = np.zeros((4, 128, 512), np.uint8)
    kp = np.arange(128)[:, None]
    qf = np.arange(512)[None, :]
    for j in range(4):
        masks[j] = (qf >= kp + 128 * j).astype(np.uint8)
    return ident, masks


def _pm(a):
    a = np.asarray(a, np.float32)
    C = a.shape[0] // 128
    return np.ascontiguousarray(
        a.reshape(C, 128, a.shape[1]).transpose(1, 0, 2).reshape(128, -1))


def _core_inputs(inp, core):
    ident, masks = _consts()
    f32c = np.ascontiguousarray
    wqkv = np.asarray(inp["L0_attn_w_qkv"], np.float32)
    wqkv_oc = np.ascontiguousarray(
        wqkv.reshape(8, 128, 24, 128).transpose(2, 1, 0, 3).reshape(
            24, 128, 1024))
    w1T = np.asarray(inp["L1_ffn_w1"], np.float32).T
    w1t_g = np.ascontiguousarray(
        w1T.reshape(8, 128, 8, 2, 256).transpose(2, 3, 1, 0, 4).reshape(
            8, 2, 128, 2048))
    w2T = np.asarray(inp["L1_ffn_w2"], np.float32).T
    w2t_g = np.ascontiguousarray(
        w2T.reshape(8, 2, 2, 128, 1024).transpose(0, 1, 3, 2, 4).reshape(
            8, 2, 128, 2048))
    gwT = np.asarray(inp["L0_ffn_gate_w"], np.float32).T
    gate_wT = np.ascontiguousarray(
        gwT.reshape(8, 128, 16).transpose(1, 0, 2).reshape(128, 128))
    m = {
        "x": f32c(inp["L0_attn_x"][2 * core:2 * core + 2]),
        "wqkv_oc": wqkv_oc,
        "wproj": _pm(inp["L0_attn_w_proj"]),
        "gate_w": inp["L0_ffn_gate_w"],
        "gate_wT": gate_wT,
        "wg_e": np.stack([_pm(w) for w in inp["L0_ffn_w_gate"]]),
        "wu_e": np.stack([_pm(w) for w in inp["L0_ffn_w_up"]]),
        "wd_e": np.stack([_pm(w) for w in inp["L0_ffn_w_down"]]),
        "sw_g": _pm(inp["L0_ffn_sw_gate"]),
        "sw_u": _pm(inp["L0_ffn_sw_up"]),
        "sw_d": _pm(inp["L0_ffn_sw_down"]),
        "maa_w1": _pm(inp["L1_attn_time_maa_w1"]),
        "maa_w2": inp["L1_attn_time_maa_w2"],
        "dec_w1": _pm(inp["L1_attn_time_decay_w1"]),
        "dec_w2": inp["L1_attn_time_decay_w2"],
        "WrT": _pm(np.asarray(inp["L1_attn_W_r"], np.float32).T),
        "WkT": _pm(np.asarray(inp["L1_attn_W_k"], np.float32).T),
        "WvT": _pm(np.asarray(inp["L1_attn_W_v"], np.float32).T),
        "WgT": _pm(np.asarray(inp["L1_attn_W_g"], np.float32).T),
        "WoT": _pm(np.asarray(inp["L1_attn_W_o"], np.float32).T),
        "w1": inp["L1_ffn_w1"],
        "w1t_g": w1t_g,
        "w2": inp["L1_ffn_w2"],
        "w2t_g": w2t_g,
        "packA": _pack_cols(inp),
        "ehot": np.kron(np.eye(16), np.ones((1, 128))).astype(ml_dtypes.bfloat16),
        "ident_f32": ident,
        "lion_masks": masks,
    }
    return {k: np.ascontiguousarray(v) if k in ("ehot", "lion_masks") else
            np.ascontiguousarray(np.asarray(v, np.float32)) for k, v in m.items()}


def kernel(**inputs):
    nc = build_program()
    in_maps = [_core_inputs(inputs, c) for c in range(NCORES)]
    res = run_bass_kernel_spmd(nc, in_maps, core_ids=list(range(NCORES)))
    outs = [r["out"] for r in res.results]
    return np.concatenate(outs, axis=0).astype(np.float32)
